# version 01
# baseline (speedup 1.0000x reference)
import contextlib
import numpy as np
import ml_dtypes
import concourse.bass as bass
import concourse.mybir as mybir
from concourse.bass_utils import run_bass_kernel_spmd

F32 = mybir.dt.float32
BF16 = mybir.dt.bfloat16
I32 = mybir.dt.int32
AF = mybir.ActivationFunctionType
ALU = mybir.AluOpType
AXX = mybir.AxisListType.X
bfloat16 = ml_dtypes.bfloat16

TOK, D, HID = 2048, 1024, 4096
NTT = TOK // 128
NH, FD, HD = 8, 16, 64
EPS = 1e-5
SEQ, NCH = 4096, 32
ENGS = ("sync", "scalar", "vector", "tensor", "gpsimd")
CH = ("ld", "tp", "gd", "v", "s", "t", "g", "cc")


def build(stage="full", dbg=False):
    nc = bass.Bass("TRN2", target_bir_lowering=False, detect_race_conditions=False)
    ops = {e: [] for e in ENGS}
    cnt = {c: 0 for c in CH}
    sems = {}

    def op(eng, make, waits=(), inc=None, amt=1):
        waits = tuple((n, v) for (n, v) in waits if v > 0)
        if inc is not None:
            cnt[inc] += amt
        def fn(e, _m=make, _w=waits, _i=inc, _a=amt):
            for n, v in _w:
                e.wait_ge(sems[n], v)
            ins = _m(e)
            if _i is not None:
                ins.then_inc(sems[_i], _a)
        ops[eng].append(fn)

    def now(*names):
        return [(n, cnt[n]) for n in names]

    def dep_all():
        return now(*CH)

    def barrier():
        w = dep_all()
        for eng in ENGS:
            if eng in ("vector", "gpsimd"):
                op(eng, lambda e: e.engine_nop(), waits=w)
            else:
                op(eng, lambda e: e.nop(), waits=w)

    def dma(eng, out, in_, waits=()):
        chn = {"sync": "ld", "scalar": "tp", "gpsimd": "gd"}[eng]
        op(eng, lambda e, o=out, i=in_: e.dma_start(out=o, in_=i), waits=waits, inc=chn, amt=16)

    def dmaT(out, in_, waits=()):
        op("scalar", lambda e, o=out, i=in_: e.dma_start_transpose(out=o, in_=i),
           waits=waits, inc="tp", amt=16)

    def v_op(make, waits=(), pad=False):
        op("vector", make, waits=waits, inc="v")
        if pad:
            ops["vector"].append(lambda e: e.tensor_copy(out=_vpad[0], in_=_vpad[1]))
    def s_op(make, waits=(), pad=False):
        op("scalar", make, waits=waits, inc="s")
        if pad:
            ops["scalar"].append(lambda e: e.activation(out=_spad[0], in_=_spad[1], func=AF.Copy))
    t_op = lambda make, waits=(): op("tensor", make, waits=waits, inc="t")
    g_op = lambda make, waits=(): op("gpsimd", make, waits=waits, inc="g")

    dt = nc.dram_tensor
    x_in = dt("x_in", [TOK, D], F32, kind="ExternalInput")
    w1 = dt("w1", [HID, D], F32, kind="ExternalInput")
    w2 = dt("w2", [D, HID], F32, kind="ExternalInput")
    b1d = dt("b1", [1, HID], F32, kind="ExternalInput")
    b2d = dt("b2", [1, D], F32, kind="ExternalInput")
    lnwd = dt("lnw", [1, D], F32, kind="ExternalInput")
    lnbd = dt("lnb", [1, D], F32, kind="ExternalInput")
    ln2wd = dt("ln2w", [1, HID], F32, kind="ExternalInput")
    ln2bd = dt("ln2b", [1, HID], F32, kind="ExternalInput")
    wqd = dt("wq", [NH * FD, D], F32, kind="ExternalInput")
    wkd = dt("wk", [NH * FD, D], F32, kind="ExternalInput")
    wvd = dt("wv", [NH * HD, D], F32, kind="ExternalInput")
    wod = dt("wo", [D, D], F32, kind="ExternalInput")
    gamd = dt("gam", [1, FD], F32, kind="ExternalInput")
    betd = dt("bet", [1, FD], F32, kind="ExternalInput")
    m0d = dt("m0", [128, 1], F32, kind="ExternalInput")
    m1d = dt("m1", [128, 1], F32, kind="ExternalInput")
    outd = dt("out", [TOK, D], F32, kind="ExternalOutput")

    x1d = dt("x1d", [TOK, D], F32)
    x2d = dt("x2d", [TOK, D], F32)
    g_std = dt("g_st", [TOK, HID], BF16)
    w1qt_st = dt("w1qt_st", [128, 8 * HID], BF16)
    w2qt_st = dt("w2qt_st", [128, 32 * D], BF16)
    ag1_in = dt("ag1_in", [TOK, D], BF16)
    ag1_out = dt("ag1_out", [SEQ, D], BF16)
    ag2_in = dt("ag2_in", [SEQ, NH * HD], BF16)
    ag2_out = dt("ag2_out", [2 * SEQ, NH * HD], BF16)

    tlv = lambda ap: ap.rearrange("(t p) f -> p t f", p=128)
    xin_t, x1_t, x2_t, out_t = tlv(x_in.ap()), tlv(x1d.ap()), tlv(x2d.ap()), tlv(outd.ap())
    gst_t = tlv(g_std.ap())
    ag1in_t, ag1out_t = tlv(ag1_in.ap()), tlv(ag1_out.ap())
    ag2in_t, ag2out_t = tlv(ag2_in.ap()), tlv(ag2_out.ap())
    w1_t = tlv(w1.ap())
    w2_t = tlv(w2.ap())
    wv_t = tlv(wvd.ap())
    wo_t = tlv(wod.ap())

    es = contextlib.ExitStack()
    sb = lambda name, shape, dtp: es.enter_context(nc.sbuf_tensor(name, shape, dtp))

    _vpadt = sb("_vpad", [128, 256], F32)
    _vpad = (_vpadt[:, 0:128], _vpadt[:, 128:256])
    _spadt = sb("_spad", [128, 256], F32)
    _spad = (_spadt[:, 0:128], _spadt[:, 128:256])
    wbuf = sb("wbuf", [128, 32768], BF16)
    w1qT = wbuf.ap().rearrange("p (c n) -> p c n", c=8)
    w2qT = wbuf.ap().rearrange("p (c n) -> p c n", c=32)
    wqT = wbuf[:, 0:1024].rearrange("p (c n) -> p c n", c=8)
    wkT = wbuf[:, 1024:2048].rearrange("p (c n) -> p c n", c=8)
    wvT = wbuf[:, 2048:6144].rearrange("p (c n) -> p c n", c=8)
    woT = wbuf[:, 6144:14336].rearrange("p (c n) -> p c n", c=8)
    v1t = wbuf[:, 14336:14336 + 32 * 520].rearrange("p (t h n) -> p t h n", t=32, h=8)

    NB = 20480
    bbuf = sb("bbuf", [128, NB], BF16)
    g_ring = [bbuf[:, i * HID:(i + 1) * HID] for i in range(2)]
    prepP = bbuf[:, 0:4096]
    prepN = bbuf[:, 4096:8192]
    tern_ring = [bbuf[:, 8192:12288], bbuf[:, 12288:16384]]
    xq2b = bbuf[:, 8192:12288]
    xq2T = bbuf[:, 12288:16384].rearrange("p (c n) -> p c n", c=32)
    xq1b = bbuf[:, 16384:17408]
    xq1T = bbuf[:, 17408:18432].rearrange("p (c n) -> p c n", c=8)
    xlnc_ring = [bbuf[:, i * 1024:(i + 1) * 1024] for i in range(2)]
    xlncT_ring = [bbuf[:, 2048 + i * 1024: 3072 + i * 1024].rearrange("p (c n) -> p c n", c=8)
                  for i in range(2)]
    qraw = bbuf[:, 4096:8192].rearrange("p (t n) -> p t n", t=32)
    kraw = bbuf[:, 8192:12288].rearrange("p (t n) -> p t n", t=32)
    qf_ring = [bbuf[:, 12288 + i * 128: 12416 + i * 128] for i in range(2)]
    kf_ring = [bbuf[:, 12544 + i * 128: 12672 + i * 128] for i in range(2)]
    phiqb = bbuf[:, 12800:14848]
    phikb = bbuf[:, 14848:16896]
    phiqT = bbuf[:, 16896:18944].rearrange("p (j n) -> p j n", j=16)
    ssnap = bbuf[:, 18944:18944 + 1040].rearrange("p (fh h n) -> p fh h n", fh=2, h=8)

    bigA = sb("bigA", [128, HID], F32)
    bigB = sb("bigB", [128, HID], F32)
    bigBbf = bigB.ap().bitcast(BF16)
    quadA = sb("quadA", [128, 512], F32)
    quadI = sb("quadI", [128, 512], I32)
    quadC = sb("quadC", [128, 512], F32)
    xrow = sb("xrow", [128, D], F32)
    phikT_s = sb("phikT", [128, 2048], BF16)
    phikT = phikT_s.ap().rearrange("p (j n) -> p j n", j=16)
    sTev = sb("sTev", [128, 128], BF16)
    qfp = sb("qfp", [128, 256], BF16)
    kfp = sb("kfp", [128, 256], BF16)
    qfpT_s = sb("qfpT", [128, 256], BF16)
    kfpT_s = sb("kfpT", [128, 256], BF16)
    qfpT = qfpT_s.ap().rearrange("p (b n) -> p b n", b=2)
    kfpT = kfpT_s.ap().rearrange("p (b n) -> p b n", b=2)
    sTm = sb("sTm", [128, 128], BF16)
    o_outb = sb("o_outb", [128, 512], BF16)
    ocat = sb("ocat", [128, 1024], BF16)
    ocatT_s = sb("ocatT", [128, 1024], BF16)
    ocatT = ocatT_s.ap().rearrange("p (c n) -> p c n", c=8)
    oslot = [sb(f"oslot{i}", [128, 512], BF16) for i in range(2)]

    lnw_bc = sb("lnw_bc", [128, D], F32)
    lnb_bc = sb("lnb_bc", [128, D], F32)
    ln2w_bc = sb("ln2w_bc", [128, HID], BF16)
    ln2b_bc = sb("ln2b_bc", [128, HID], BF16)
    gamq_bc = sb("gamq_bc", [128, 128], F32)
    betq_bc = sb("betq_bc", [128, 128], F32)
    gamk_bc = sb("gamk_bc", [128, 128], F32)
    betk_bc = sb("betk_bc", [128, 128], F32)
    identq = sb("identq", [128, 128], F32)
    onesq = sb("onesq", [128, 128], F32)

    rows1 = sb("rows1", [97, 512], BF16)
    rows2 = sb("rows2", [97, 512], BF16)
    rows3 = sb("rows3", [97, 512], BF16)
    rows4 = sb("rows4", [97, 512], BF16)
    rowst = sb("rowst", [97, 512], F32)

    def rowslice(t, k, n=512):
        return t[32 * k: 32 * k + 1, 0:n]

    st = lambda name, n=16: sb(name, [128, n], F32)
    sxs, sxq = st("sxs"), st("sxq")
    rr1, nmr1 = st("rr1"), st("nmr1")
    gs8 = sb("gs8", [128, 128], F32)
    gq8 = sb("gq8", [128, 128], F32)
    rr2, nmr2 = st("rr2"), st("nmr2")
    tA, tB, tC, tD = st("tA"), st("tB"), st("tC"), st("tD")
    am1, es1v, ar1 = st("am1", 1), st("es1", 1), st("ar1", 1)
    am2, es2v, ar2 = st("am2", 1), st("es2", 1), st("ar2", 1)
    wsum = st("wsum", 32)
    wthr, wnthr = st("wthr", 1), st("wnthr", 1)
    wsc1, wsc2 = st("wsc1", 1), st("wsc2", 1)
    onecol = st("onecol", 1)
    arow_s = sb("arow", [97, 128], BF16)
    arow = lambda k: arow_s[32 * k:32 * k + 1, :]
    onesrow_s = sb("onesrow", [97, 128], F32)
    onesrow = lambda k: onesrow_s[32 * k:32 * k + 1, :]
    scal = sb("scal", [1, 16], F32)
    m0t, m1t = st("m0t", 1), st("m1t", 1)
    asx, asq = st("asx"), st("asq")
    arr_, anmr = st("arr"), st("anmr")
    qsum = sb("qsum", [128, 256], F32)
    qsq = sb("qsq", [128, 256], F32)
    ksum = sb("ksum", [128, 256], F32)
    ksq = sb("ksq", [128, 256], F32)
    qmu = sb("qmu", [128, 256], F32)
    qrs = sb("qrs", [128, 256], F32)
    kmu = sb("kmu", [128, 256], F32)
    krs = sb("krs", [128, 256], F32)
    fmw = sb("fmw", [128, 128], F32)
    fmw2 = sb("fmw2", [128, 128], F32)
    fmw3 = sb("fmw3", [128, 128], F32)
    zt, zrt = st("zt", 8), st("zrt", 8)
    onebf = sb("onebf", [128, 1], BF16)

    pb = [es.enter_context(nc.psum_tensor(f"pb{i}", [128, 512], F32)) for i in range(8)]

    v_op(lambda e: e.memset(_vpadt[:], 0.0))
    s_op(lambda e: e.activation(out=_spadt[:], in_=_vpadt[:], func=AF.Copy), waits=now("v"))
    v_op(lambda e: e.memset(onecol[:], 1.0))
    v_op(lambda e: e.memset(onesq[:], 1.0))
    v_op(lambda e: e.memset(onebf[:], 1.0))
    v_op(lambda e: e.memset(onesrow_s[:], 1.0))
    dma("sync", m0t[:], m0d.ap())
    dma("sync", m1t[:], m1d.ap())
    g_op(lambda e: e.affine_select(out=identq[:], in_=onesq[:], pattern=[[1, 128]],
                                   compare_op=ALU.is_equal, fill=0.0, base=0,
                                   channel_multiplier=-1), waits=now("v"))

    def bc_load(dst_bc, src_ap, n):
        for ci in range(0, n, 512):
            w = min(512, n - ci)
            sl = (ci // 512) % 3
            dma("sync", rowslice(rowst, sl, w), src_ap[0:1, ci:ci + w], waits=now("t"))
            t_op(lambda e, w=w, sl=sl: e.matmul(out=pb[7][:, 0:w], lhsT=onesrow(sl),
                                                rhs=rowslice(rowst, sl, w),
                                                start=True, stop=True),
                 waits=now("ld", "v"))
            v_op(lambda e, ci=ci, w=w, d=dst_bc: e.tensor_copy(out=d[:, ci:ci + w],
                                                               in_=pb[7][:, 0:w]),
                 waits=now("t"))

    bc_load(lnw_bc, lnwd.ap(), D)
    bc_load(lnb_bc, lnbd.ap(), D)
    bc_load(ln2w_bc, ln2wd.ap(), HID)
    bc_load(ln2b_bc, ln2bd.ap(), HID)

    dma("sync", rowslice(rowst, 0, FD), gamd.ap(), waits=now("t"))
    dma("sync", rowslice(rowst, 1, FD), betd.ap(), waits=now("t"))
    t_op(lambda e: e.matmul(out=pb[7][:, 0:128], lhsT=onesrow(0),
                            rhs=rowst[0:1, 0:FD].unsqueeze(1).broadcast_to([1, 8, FD]),
                            start=True, stop=True), waits=now("ld", "v"))
    v_op(lambda e: e.tensor_copy(out=gamk_bc[:], in_=pb[7][:, 0:128]), waits=now("t"))
    t_op(lambda e: e.matmul(out=pb[7][:, 0:128], lhsT=onesrow(1),
                            rhs=rowst[32:33, 0:FD].unsqueeze(1).broadcast_to([1, 8, FD]),
                            start=True, stop=True), waits=now("v"))
    v_op(lambda e: e.tensor_copy(out=betk_bc[:], in_=pb[7][:, 0:128]), waits=now("t"))
    v_op(lambda e: e.tensor_scalar_mul(out=gamq_bc[:], in0=gamk_bc[:], scalar1=0.25))
    v_op(lambda e: e.tensor_scalar_mul(out=betq_bc[:], in0=betk_bc[:], scalar1=0.25))

    _b1tensors = [rows1, rows1, rows1, rows2, rows2, rows2, rows3, rows3]
    for n in range(8):
        dma("gpsimd", rowslice(_b1tensors[n], n % 3), b1d.ap()[0:1, 512 * n:512 * (n + 1)])
    dma("gpsimd", rowslice(rows3, 2), b2d.ap()[0:1, 0:512])
    dma("gpsimd", rowslice(rows4, 0), b2d.ap()[0:1, 512:1024])
    b1chunk = lambda n: rowslice(_b1tensors[n], n % 3)
    b2chunk = lambda n: rowslice(rows3, 2) if n == 0 else rowslice(rows4, 0)
    b2base = lambda n: 2 if n == 0 else 0
    barrier()

    def prep_w(w_tiles, ntiles, freedim, wsc_dst, qT_dst_fn, totelem):
        for i in range(ntiles):
            buf = bigA if i % 2 == 0 else bigB
            dma("sync", buf[:, 0:freedim], w_tiles[:, i, :], waits=now("s"))
            s_op(lambda e, b=buf, i=i, f=freedim: e.activation(
                out=prepP.bitcast(BF16)[:, 0:f] if False else prepP[:, 0:f],
                in_=b[:, 0:f], func=AF.Abs, accum_out=wsum[:, i:i + 1]),
                waits=now("ld"))
        v_op(lambda e, n=ntiles: e.reduce_sum(out=wthr[:], in_=wsum[:, 0:n], axis=AXX),
             waits=now("s"), pad=True)
        t_op(lambda e: e.matmul(out=pb[7][0:1, 0:1], lhsT=wthr[:], rhs=onecol[:],
                                start=True, stop=True), waits=now("v"))
        v_op(lambda e: e.tensor_copy(out=scal[0:1, 0:1], in_=pb[7][0:1, 0:1]), waits=now("t"), pad=True)
        t_op(lambda e: e.matmul(out=pb[7][:, 0:1], lhsT=onesrow(0), rhs=scal[0:1, 0:1],
                                start=True, stop=True), waits=now("v"))
        v_op(lambda e, n=totelem, d=wsc_dst: e.tensor_scalar_mul(out=d[:], in0=pb[7][:, 0:1],
                                                                 scalar1=1.0 / n), waits=now("t"), pad=True)
        v_op(lambda e, d=wsc_dst: e.tensor_scalar_mul(out=wthr[:], in0=d[:], scalar1=0.5), pad=True)
        v_op(lambda e: e.tensor_scalar_mul(out=wnthr[:], in0=wthr[:], scalar1=-1.0), pad=True)
        for i in range(ntiles):
            buf = bigA if i % 2 == 0 else bigB
            tern = tern_ring[i % 2]
            dma("sync", buf[:, 0:freedim], w_tiles[:, i, :], waits=now("g", "tp"))
            g_op(lambda e, b=buf, f=freedim: e.tensor_single_scalar(
                out=prepP[:, 0:f], in_=b[:, 0:f], scalar=wthr[:], op=ALU.is_gt),
                waits=now("ld", "v"))
            g_op(lambda e, b=buf, f=freedim: e.tensor_single_scalar(
                out=prepN[:, 0:f], in_=b[:, 0:f], scalar=wnthr[:], op=ALU.is_lt))
            g_op(lambda e, t_=tern, f=freedim: e.tensor_sub(out=t_[:, 0:f], in0=prepP[:, 0:f],
                                                            in1=prepN[:, 0:f]))
            dmaT(qT_dst_fn(i), tern[:, 0:freedim], waits=now("g"))

    prep_w(w2_t, 8, HID, wsc2, lambda i: w2qT[:, :, 128 * i:128 * (i + 1)], D * HID)
    dma("sync", w2qt_st.ap(), wbuf[:, :], waits=now("tp"))
    barrier()
    prep_w(w1_t, 32, D, wsc1, lambda i: w1qT[:, :, 128 * i:128 * (i + 1)], D * HID)
    dma("sync", w1qt_st.ap(), wbuf[:, :], waits=now("tp"))
    barrier()

    def ln_chain(sum_t, sq_t, n, rr_dst, nmr_dst, double_ln):
        v_op(lambda e: e.tensor_scalar_mul(out=tA[:], in0=sum_t[:], scalar1=1.0 / n), waits=now("s"), pad=True)
        v_op(lambda e: e.tensor_scalar_mul(out=tB[:], in0=sq_t[:], scalar1=1.0 / n), pad=True)
        v_op(lambda e: e.tensor_mul(out=tC[:], in0=tA[:], in1=tA[:]), pad=True)
        v_op(lambda e: e.tensor_sub(out=tB[:], in0=tB[:], in1=tC[:]), pad=True)
        if double_ln:
            v_op(lambda e: e.tensor_scalar_add(out=tC[:], in0=tB[:], scalar1=EPS), pad=True)
            v_op(lambda e: e.reciprocal(out=tD[:], in_=tC[:]), pad=True)
            v_op(lambda e: e.tensor_mul(out=tB[:], in0=tB[:], in1=tD[:]), pad=True)
            v_op(lambda e: e.tensor_scalar_add(out=tB[:], in0=tB[:], scalar1=EPS), pad=True)
            v_op(lambda e: e.tensor_mul(out=tB[:], in0=tB[:], in1=tC[:]), pad=True)
            v_op(lambda e: e.reciprocal(out=tB[:], in_=tB[:]), pad=True)
        else:
            v_op(lambda e: e.tensor_scalar_add(out=tB[:], in0=tB[:], scalar1=EPS), pad=True)
            v_op(lambda e: e.reciprocal(out=tB[:], in_=tB[:]), pad=True)
        s_op(lambda e, r=rr_dst: e.activation(out=r[:], in_=tB[:], func=AF.Sqrt), waits=now("v"), pad=True)
        v_op(lambda e, r=rr_dst, m=nmr_dst: e.tensor_mul(out=m[:], in0=tA[:], in1=r[:]),
             waits=now("s"), pad=True)
        v_op(lambda e, m=nmr_dst: e.tensor_scalar_mul(out=m[:], in0=m[:], scalar1=-1.0), pad=True)

    def quant_to(dst_bf, src_f32, width, s_scale_ap):
        for q in range(width // 512):
            sl = slice(512 * q, 512 * (q + 1))
            v_op(lambda e, sl=sl: e.tensor_scalar(out=quadA[:], in0=src_f32[:, sl],
                                                  scalar1=s_scale_ap, scalar2=128.0,
                                                  op0=ALU.mult, op1=ALU.add))
            v_op(lambda e: e.tensor_copy(out=quadI[:], in_=quadA[:]))
            v_op(lambda e: e.scalar_tensor_tensor(out=quadC[:], in0=quadA[:], scalar=0.5,
                                                  in1=quadI[:], op0=ALU.subtract, op1=ALU.is_ge))
            v_op(lambda e, sl=sl: e.scalar_tensor_tensor(out=dst_bf[:, sl], in0=quadI[:],
                                                         scalar=-128.0, in1=quadC[:],
                                                         op0=ALU.add, op1=ALU.add))

    dbgS = dt("dbg_S", [32 * 128, 1040], F32, kind="ExternalOutput") if dbg else None
    dbgP = dt("dbg_P", [32 * 128, 2048], BF16, kind="ExternalOutput") if dbg else None
    dbgPt = tlv(dbgP.ap()) if dbg else None
    dbgK = dt("dbg_K", [32 * 128, 128], BF16, kind="ExternalOutput") if dbg else None
    dbgKt = tlv(dbgK.ap()) if dbg else None
    dbgQ = dt("dbg_Q", [32 * 128, 128], BF16, kind="ExternalOutput") if dbg else None
    dbgQt = tlv(dbgQ.ap()) if dbg else None
    dbgSt = tlv(dbgS.ap()) if dbg else None
    dbgx = dt("dbg_x", [TOK, D], BF16, kind="ExternalOutput") if dbg else None
    dbgxt = tlv(dbgx.ap()) if dbg else None
    amdbg = sb("amdbg", [128, 16], F32)
    esdbg = sb("esdbg", [128, 16], F32)

    def ffn(xin_tiles, xout_tiles):
        for t in range(NTT):
            dma("sync", xrow[:], xin_tiles[:, t, :], waits=now("v", "s"))
            v_op(lambda e, t=t: e.reduce_sum(out=sxs[:, t:t + 1], in_=xrow[:], axis=AXX),
                 waits=now("ld"))
            s_op(lambda e, t=t: e.activation(out=bigA[:, 0:D], in_=xrow[:], func=AF.Square,
                                             accum_out=sxq[:, t:t + 1]), waits=now("ld", "v"))
        ln_chain(sxs, sxq, D, rr1, nmr1, double_ln=True)

        for t in range(NTT):
            gbuf = g_ring[t % 2]
            dma("sync", xrow[:], xin_tiles[:, t, :], waits=now("v", "s"))
            s_op(lambda e, t=t: e.activation(out=bigB[:, 0:D], in_=xrow[:], func=AF.Identity,
                                             bias=nmr1[:, t:t + 1], scale=rr1[:, t:t + 1]),
                 waits=now("ld", "v"))
            v_op(lambda e: e.tensor_mul(out=bigA[:, 0:D], in0=bigB[:, 0:D], in1=lnw_bc[:]),
                 waits=now("s"))
            v_op(lambda e: e.tensor_add(out=bigA[:, 0:D], in0=bigA[:, 0:D], in1=lnb_bc[:]))
            v_op(lambda e: e.tensor_reduce(out=am1[:], in_=bigA[:, 0:D], axis=AXX, op=ALU.max,
                                           apply_absolute_value=True), pad=True)
            v_op(lambda e: e.tensor_scalar_max(out=am1[:], in0=am1[:], scalar1=EPS), pad=True)
            v_op(lambda e: e.reciprocal(out=tD[:, 0:1], in_=am1[:]), pad=True)
            v_op(lambda e: e.tensor_scalar_mul(out=tD[:, 0:1], in0=tD[:, 0:1], scalar1=127.0), pad=True)
            v_op(lambda e: e.tensor_mul(out=es1v[:], in0=am1[:], in1=wsc1[:]), pad=True)
            v_op(lambda e: e.tensor_scalar_mul(out=es1v[:], in0=es1v[:], scalar1=1.0 / 127.0), pad=True)
            v_op(lambda e: e.reciprocal(out=ar1[:], in_=es1v[:]), pad=True)
            v_op(lambda e, t=t: e.tensor_copy(out=amdbg[:, t:t + 1], in_=am1[:]), pad=True)
            v_op(lambda e, t=t: e.tensor_copy(out=esdbg[:, t:t + 1], in_=es1v[:]), pad=True)
            quant_to(xq1b, bigA, D, tD[:, 0:1])
            if dbg:
                dma("sync", dbgxt[:, t, :], xq1b[:], waits=now("v"))
            dmaT(xq1T, xq1b, waits=now("v", "t"))
            t_op(lambda e: e.transpose(out=pb[7][0:1, 0:128], in_=ar1[:], identity=identq[:]),
                 waits=now("v", "s"))
            for k in range(3):
                v_op(lambda e, k=k: e.tensor_copy(out=arow(k), in_=pb[7][0:1, 0:128]),
                     waits=(now("t") if k == 0 else ()), pad=True)
            wt = now("tp", "v", "s")
            for kc in range(8):
                for n in range(8):
                    t_op(lambda e, kc=kc, n=n: e.matmul(
                        out=pb[n][:, 0:512], lhsT=xq1T[:, kc, :],
                        rhs=w1qT[:, kc, 512 * n:512 * (n + 1)],
                        start=(kc == 0), stop=False),
                        waits=(wt if (kc == 0 and n == 0) else ()))
            for n in range(8):
                t_op(lambda e, n=n: e.matmul(out=pb[n][:, 0:512], lhsT=arow(n % 3),
                                             rhs=b1chunk(n), start=False, stop=True),
                     waits=(now("v") if n == 0 else ()))
            for n in range(8):
                s_op(lambda e, n=n, g=gbuf, t=t: e.activation(
                    out=g[:, 512 * n:512 * (n + 1)], in_=pb[n][:, 0:512], func=AF.Gelu,
                    scale=es1v[:], accum_out=gs8[:, 8 * t + n: 8 * t + n + 1]),
                    waits=(now("t", "ld") if n == 0 else ()))
                s_op(lambda e, n=n, g=gbuf, t=t: e.activation(
                    out=bigA[:, 512 * n:512 * (n + 1)], in_=g[:, 512 * n:512 * (n + 1)],
                    func=AF.Square, accum_out=gq8[:, 8 * t + n: 8 * t + n + 1]))
            dma("sync", gst_t[:, t, :], gbuf[:], waits=now("s"))

        v_op(lambda e: e.tensor_reduce(out=sxs[:], in_=gs8.ap().rearrange("p (t n) -> p t n", n=8),
                                       axis=AXX, op=ALU.add), waits=now("s"))
        v_op(lambda e: e.tensor_reduce(out=sxq[:], in_=gq8.ap().rearrange("p (t n) -> p t n", n=8),
                                       axis=AXX, op=ALU.add))
        ln_chain(sxs, sxq, HID, rr2, nmr2, double_ln=False)

        dma("sync", wbuf[:, :], w2qt_st.ap(), waits=now("t"))

        for t in range(NTT):
            dma("sync", bbuf[:, 0:HID], gst_t[:, t, :], waits=now("v", "s"))
            s_op(lambda e, t=t: e.activation(out=bigB[:, 0:HID], in_=bbuf[:, 0:HID], func=AF.Identity,
                                             bias=nmr2[:, t:t + 1], scale=rr2[:, t:t + 1]),
                 waits=now("ld", "v"))
            v_op(lambda e: e.tensor_mul(out=bigA[:, 0:HID], in0=bigB[:, 0:HID], in1=ln2w_bc[:]),
                 waits=now("s"))
            v_op(lambda e: e.tensor_add(out=bigA[:, 0:HID], in0=bigA[:, 0:HID], in1=ln2b_bc[:]))
            v_op(lambda e: e.tensor_reduce(out=am2[:], in_=bigA[:, 0:HID], axis=AXX, op=ALU.max,
                                           apply_absolute_value=True), pad=True)
            v_op(lambda e: e.tensor_scalar_max(out=am2[:], in0=am2[:], scalar1=EPS), pad=True)
            v_op(lambda e: e.reciprocal(out=tD[:, 0:1], in_=am2[:]), pad=True)
            v_op(lambda e: e.tensor_scalar_mul(out=tD[:, 0:1], in0=tD[:, 0:1], scalar1=127.0), pad=True)
            v_op(lambda e: e.tensor_mul(out=es2v[:], in0=am2[:], in1=wsc2[:]), pad=True)
            v_op(lambda e: e.tensor_scalar_mul(out=es2v[:], in0=es2v[:], scalar1=0.5 / 127.0), pad=True)
            v_op(lambda e: e.reciprocal(out=ar2[:], in_=es2v[:]), pad=True)
            v_op(lambda e: e.tensor_scalar_mul(out=ar2[:], in0=ar2[:], scalar1=0.5), pad=True)
            quant_to(xq2b, bigA, HID, tD[:, 0:1])
            dmaT(xq2T, xq2b, waits=now("v", "t"))
            t_op(lambda e: e.transpose(out=pb[7][0:1, 0:128], in_=ar2[:], identity=identq[:]),
                 waits=now("v", "s"))
            for k in range(3):
                v_op(lambda e, k=k: e.tensor_copy(out=arow(k), in_=pb[7][0:1, 0:128]),
                     waits=(now("t") if k == 0 else ()), pad=True)
            wt = now("tp", "v", "s")
            for kc in range(32):
                for n in range(2):
                    t_op(lambda e, kc=kc, n=n: e.matmul(
                        out=pb[n][:, 0:512], lhsT=xq2T[:, kc, :],
                        rhs=w2qT[:, kc, 512 * n:512 * (n + 1)],
                        start=(kc == 0), stop=False),
                        waits=(wt if (kc == 0 and n == 0) else ()))
            for n in range(2):
                t_op(lambda e, n=n: e.matmul(out=pb[n][:, 0:512], lhsT=arow(b2base(n)),
                                             rhs=b2chunk(n), start=False, stop=True),
                     waits=(now("v") if n == 0 else ()))
            dma("sync", xrow[:], xin_tiles[:, t, :], waits=now("v"))
            for n in range(2):
                v_op(lambda e, n=n: e.scalar_tensor_tensor(
                    out=xrow[:, 512 * n:512 * (n + 1)], in0=pb[n][:, 0:512], scalar=es2v[:],
                    in1=xrow[:, 512 * n:512 * (n + 1)], op0=ALU.mult, op1=ALU.add),
                    waits=(now("t", "ld") if n == 0 else ()))
            dma("sync", xout_tiles[:, t, :], xrow[:], waits=now("v"))
        barrier()

    def attention():
        stg = bbuf[:, 0:1024]
        dma("gpsimd", stg, wqd.ap(), waits=now("tp", "t", "s", "v"))
        dmaT(wqT, stg, waits=now("gd"))
        dma("gpsimd", stg, wkd.ap(), waits=now("tp"))
        dmaT(wkT, stg, waits=now("gd"))
        for j in range(4):
            dma("gpsimd", stg, wv_t[:, j, :], waits=now("tp"))
            dmaT(wvT[:, :, 128 * j:128 * (j + 1)], stg, waits=now("gd"))
        for j in range(8):
            dma("gpsimd", stg, wo_t[:, j, :], waits=now("tp"))
            dmaT(woT[:, :, 128 * j:128 * (j + 1)], stg, waits=now("gd"))
        barrier()
        v_op(lambda e: e.memset(v1t[:, :, :, 64:65], 1.0))

        for t in range(NTT):
            dma("sync", xrow[:], x1_t[:, t, :], waits=now("v", "s"))
            v_op(lambda e, t=t: e.reduce_sum(out=asx[:, t:t + 1], in_=xrow[:], axis=AXX),
                 waits=now("ld"))
            s_op(lambda e, t=t: e.activation(out=bigA[:, 0:D], in_=xrow[:], func=AF.Square,
                                             accum_out=asq[:, t:t + 1]), waits=now("ld", "v"))
        ln_chain(asx, asq, D, arr_, anmr, double_ln=False)
        for t in range(NTT):
            dma("sync", xrow[:], x1_t[:, t, :], waits=now("v", "s"))
            s_op(lambda e, t=t: e.activation(out=bigBbf[:, 0:D], in_=xrow[:], func=AF.Identity,
                                             bias=anmr[:, t:t + 1], scale=arr_[:, t:t + 1]),
                 waits=now("ld"))
            dma("sync", ag1in_t[:, t, :], bigBbf[:, 0:D], waits=now("s"))
        op("gpsimd", lambda e: e.collective_compute(
            "AllGather", ALU.bypass, replica_groups=[[0, 1], [2, 3], [4, 5], [6, 7]],
            ins=[ag1_in.ap().opt()], outs=[ag1_out.ap().opt()]),
            waits=now("ld"), inc="cc")
        barrier()

        for c in range(NCH):
            r = c % 2
            dma("sync", xlnc_ring[r], ag1out_t[:, c, :], waits=now("cc", "tp", "t"))
            dmaT(xlncT_ring[r], xlnc_ring[r], waits=now("ld"))
            wt = now("tp", "s", "v")
            for kc in range(8):
                t_op(lambda e, kc=kc, r=r: e.matmul(out=pb[0][:, 0:128],
                                                    lhsT=xlncT_ring[r][:, kc, :],
                                                    rhs=wqT[:, kc, :], start=(kc == 0),
                                                    stop=(kc == 7)),
                     waits=(wt if kc == 0 else ()))
            for kc in range(8):
                t_op(lambda e, kc=kc, r=r: e.matmul(out=pb[1][:, 0:128],
                                                    lhsT=xlncT_ring[r][:, kc, :],
                                                    rhs=wkT[:, kc, :], start=(kc == 0),
                                                    stop=(kc == 7)))
            for kc in range(8):
                t_op(lambda e, kc=kc, r=r: e.matmul(out=pb[2][:, 0:512],
                                                    lhsT=xlncT_ring[r][:, kc, :],
                                                    rhs=wvT[:, kc, :], start=(kc == 0),
                                                    stop=(kc == 7)))
            s_op(lambda e: e.activation(out=fmw[:], in_=pb[0][:, 0:128], func=AF.Copy),
                 waits=now("t", "v"))
            s_op(lambda e: e.activation(out=fmw2[:], in_=pb[1][:, 0:128], func=AF.Copy))
            s_op(lambda e, c=c: e.activation(
                out=v1t[:, c, :, 0:64], in_=pb[2][:, 0:512].rearrange("p (h n) -> p h n", h=8),
                func=AF.Copy))
            v_op(lambda e, c=c: e.tensor_copy(out=qraw[:, c, :], in_=fmw[:]), waits=now("s"))
            v_op(lambda e, c=c: e.tensor_copy(out=kraw[:, c, :], in_=fmw2[:]))
            v_op(lambda e, c=c: e.reduce_sum(out=qsum[:, 8 * c:8 * c + 8],
                                             in_=fmw.ap().rearrange("p (h f) -> p h f", f=16),
                                             axis=AXX))
            v_op(lambda e, c=c: e.reduce_sum(out=ksum[:, 8 * c:8 * c + 8],
                                             in_=fmw2.ap().rearrange("p (h f) -> p h f", f=16),
                                             axis=AXX))
            s_op(lambda e: e.activation(out=fmw3[:], in_=fmw[:], func=AF.Square), waits=now("v"))
            v_op(lambda e, c=c: e.reduce_sum(out=qsq[:, 8 * c:8 * c + 8],
                                             in_=fmw3.ap().rearrange("p (h f) -> p h f", f=16),
                                             axis=AXX), waits=now("s"))
            s_op(lambda e: e.activation(out=fmw3[:], in_=fmw2[:], func=AF.Square), waits=now("v"))
            v_op(lambda e, c=c: e.reduce_sum(out=ksq[:, 8 * c:8 * c + 8],
                                             in_=fmw3.ap().rearrange("p (h f) -> p h f", f=16),
                                             axis=AXX), waits=now("s"))

        def fm_stats(sumt, sqt_, mut, rst):
            v_op(lambda e: e.tensor_scalar_mul(out=mut[:], in0=sumt[:], scalar1=1.0 / FD),
                 waits=now("s"), pad=True)
            v_op(lambda e: e.tensor_scalar_mul(out=rst[:], in0=sqt_[:], scalar1=1.0 / FD), pad=True)
            v_op(lambda e: e.tensor_mul(out=sqt_[:], in0=mut[:], in1=mut[:]))
            v_op(lambda e: e.tensor_sub(out=rst[:], in0=rst[:], in1=sqt_[:]), pad=True)
            v_op(lambda e: e.tensor_scalar_add(out=rst[:], in0=rst[:], scalar1=EPS), pad=True)
            v_op(lambda e: e.reciprocal(out=rst[:], in_=rst[:]), pad=True)
            s_op(lambda e: e.activation(out=rst[:], in_=rst[:], func=AF.Sqrt), waits=now("v"), pad=True)
        fm_stats(qsum, qsq, qmu, qrs)
        fm_stats(ksum, ksq, kmu, krs)
        barrier()

        Sf = [bigA[:, 0:512], bigA[:, 512:1024]]
        zf = bigA[:, 1024:1040]
        v_op(lambda e: e.memset(Sf[0], 0.0))
        v_op(lambda e: e.memset(Sf[1], 0.0))
        v_op(lambda e: e.memset(zf, 0.0), pad=True)
        for c in range(NCH):
            rq, rk = qf_ring[c % 2], kf_ring[c % 2]

            def fm_apply(raw, mu, rs, gbc, bbc, dstr):
                mu_b = mu[:, 8 * c:8 * c + 8].unsqueeze(2).broadcast_to([128, 8, 16])
                rs_b = rs[:, 8 * c:8 * c + 8].unsqueeze(2).broadcast_to([128, 8, 16])
                src3 = raw[:, c, :].rearrange("p (h f) -> p h f", f=16)
                v_op(lambda e, s3=src3, mb=mu_b: e.tensor_sub(
                    out=fmw.ap().rearrange("p (h f) -> p h f", f=16),
                    in0=s3, in1=mb), waits=now("t", "tp"))
                v_op(lambda e: e.tensor_mul(out=fmw.ap().rearrange("p (h f) -> p h f", f=16),
                                            in0=fmw.ap().rearrange("p (h f) -> p h f", f=16),
                                            in1=rs_b))
                v_op(lambda e: e.tensor_mul(out=fmw[:], in0=fmw[:], in1=gbc[:]))
                v_op(lambda e, d=dstr: e.tensor_add(out=d, in0=fmw[:], in1=bbc[:]))
            fm_apply(qraw, qmu, qrs, gamq_bc, betq_bc, rq)
            fm_apply(kraw, kmu, krs, gamk_bc, betk_bc, rk)

            def phi_build(src, dst):
                i0 = src.rearrange("p (h f) -> p h f", f=16).unsqueeze(3).broadcast_to([128, 8, 16, 16])
                i1 = src.rearrange("p (h f) -> p h f", f=16).unsqueeze(2).broadcast_to([128, 8, 16, 16])
                v_op(lambda e, d=dst: e.tensor_tensor(
                    out=d.rearrange("p (h a b) -> p h a b", a=16, b=16), in0=i0, in1=i1,
                    op=ALU.mult))
            phi_build(rq, phiqb)
            phi_build(rk, phikb)
            v_op(lambda e, rq=rq: e.tensor_copy(
                out=qfp.ap().rearrange("p (h n) -> p h n", h=8)[:, :, 0:16],
                in_=rq.rearrange("p (h f) -> p h f", f=16)))
            v_op(lambda e, rk=rk: e.tensor_copy(
                out=kfp.ap().rearrange("p (h n) -> p h n", h=8)[:, :, 0:16],
                in_=rk.rearrange("p (h f) -> p h f", f=16)))
            dmaT(phiqT, phiqb, waits=now("v", "t"))
            dmaT(phikT, phikb, waits=now("v", "t"))
            dmaT(qfpT, qfp[:], waits=now("v", "t"))
            dmaT(kfpT, kfp[:], waits=now("v", "t"))

            if c > 0:
                s_op(lambda e: e.activation(
                    out=ssnap[:, 0, :, 0:64], in_=Sf[0].rearrange("p (h n) -> p h n", h=8),
                    func=AF.Copy), waits=now("v"))
                s_op(lambda e: e.activation(
                    out=ssnap[:, 1, :, 0:64], in_=Sf[1].rearrange("p (h n) -> p h n", h=8),
                    func=AF.Copy))
                v_op(lambda e: e.tensor_copy(
                    out=ssnap[:, :, :, 64:65],
                    in_=zf.rearrange("p (fh h) -> p fh h", fh=2).unsqueeze(3)))
            snap_w = now("s", "v")

            for h in range(NH):
                sbk = pb[3] if h % 2 == 0 else pb[4]
                blk = h // 4
                base = 32 * (h % 4)
                tpos = (96, 0) if base == 96 else None
                t_op(lambda e, h=h, sbk=sbk, blk=blk, base=base, tpos=tpos: e.matmul(
                    out=sbk[:, 0:128],
                    lhsT=kfpT[base:base + 16, blk, :],
                    rhs=qfpT[base:base + 16, blk, :],
                    start=True, stop=True, tile_position=tpos),
                     waits=now("tp", "s", "v", "g"))
                s_op(lambda e, sbk=sbk: e.activation(out=sTev[:], in_=sbk[:, 0:128], func=AF.Square),
                     waits=now("t"))
                g_op(lambda e: e.affine_select(out=sTm[:], in_=sTev[:], pattern=[[1, 128]],
                                               compare_op=ALU.is_ge, fill=0.0, base=0,
                                               channel_multiplier=-1), waits=now("s"))
                ob = pb[5] if h < 4 else pb[6]
                col = 65 * (h % 4)
                bank_first = (h % 4 == 0)
                bank_last = (h % 4 == 3)
                nmm_bank = 1 if c == 0 else 3
                t_op(lambda e, ob=ob, col=col, c=c, h=h, bf_=bank_first, bl=bank_last:
                     e.matmul(out=ob[:, col:col + 65], lhsT=sTm[:],
                              rhs=v1t[:, c, h, 0:65],
                              start=bf_, stop=(bl and c == 0)),
                     waits=now("g"))
                if c > 0:
                    for fh in range(2):
                        t_op(lambda e, ob=ob, col=col, h=h, fh=fh, bl=bank_last:
                             e.matmul(out=ob[:, col:col + 65], lhsT=phiqT[:, 2 * h + fh, :],
                                      rhs=ssnap[:, fh, h, :],
                                      start=False, stop=(bl and fh == 1)),
                             waits=(snap_w if (h == 0 and fh == 0) else ()))
                for fh in range(2):
                    t_op(lambda e, h=h, c=c, fh=fh: e.matmul(
                        out=pb[fh][:, 64 * h:64 * h + 64],
                        lhsT=phikb[:, 256 * h + 128 * fh: 256 * h + 128 * (fh + 1)],
                        rhs=v1t[:, c, h, 0:64],
                        start=(h == 0), stop=(h == NH - 1)),
                        waits=(snap_w if (h == 0 and fh == 0) else ()))
                    t_op(lambda e, h=h, c=c, fh=fh: e.matmul(
                        out=pb[2][:, (8 * fh + h):(8 * fh + h) + 1],
                        lhsT=phikb[:, 256 * h + 128 * fh: 256 * h + 128 * (fh + 1)],
                        rhs=onebf[:], start=(h == 0 and fh == 0),
                        stop=(h == NH - 1 and fh == 1)))

            v_op(lambda e: e.tensor_add(out=Sf[0], in0=Sf[0], in1=pb[0][:, 0:512]),
                 waits=now("t"))
            v_op(lambda e: e.tensor_add(out=Sf[1], in0=Sf[1], in1=pb[1][:, 0:512]))
            v_op(lambda e: e.tensor_add(out=zf, in0=zf, in1=pb[2][:, 0:16]), pad=True)
            if dbg:
                dma("sync", dbgSt[:, c, :], bigA[:, 0:1040], waits=now("v"))
                dma("sync", dbgPt[:, c, :], phikb, waits=now("v"))
                dma("sync", dbgKt[:, c, :], kf_ring[c % 2], waits=now("v"))
                dma("sync", dbgKt[:, c, 128:256] if False else dbgQt[:, c, :], qf_ring[c % 2], waits=now("v"))
                v_op(lambda e: e.engine_nop(), waits=now("ld"))
            v_op(lambda e: e.tensor_copy(
                out=zt[:, 0:4],
                in_=pb[5][:, 0:260].rearrange("p (h n) -> p h n", n=65)[:, :, 64:65]),
                waits=now("t"), pad=True)
            v_op(lambda e: e.tensor_copy(
                out=zt[:, 4:8],
                in_=pb[6][:, 0:260].rearrange("p (h n) -> p h n", n=65)[:, :, 64:65]), pad=True)
            v_op(lambda e: e.tensor_scalar_add(out=zt[:], in0=zt[:], scalar1=EPS), pad=True)
            v_op(lambda e: e.reciprocal(out=zrt[:], in_=zt[:]), pad=True)
            for h in range(NH):
                ob = pb[5] if h < 4 else pb[6]
                col = 65 * (h % 4)
                s_op(lambda e, h=h, ob=ob, col=col: e.activation(
                    out=o_outb[:, 64 * h:64 * h + 64], in_=ob[:, col:col + 64], func=AF.Copy,
                    scale=zrt[:, h:h + 1]),
                    waits=(now("v", "ld") if h == 0 else ()))
            dma("sync", ag2in_t[:, c, :], o_outb[:], waits=now("s"))

        op("gpsimd", lambda e: e.collective_compute(
            "AllGather", ALU.bypass, replica_groups=[[0, 1], [2, 3], [4, 5], [6, 7]],
            ins=[ag2_in.ap().opt()], outs=[ag2_out.ap().opt()]),
            waits=now("ld"), inc="cc")
        barrier()

        for t in range(NTT):
            dma("sync", oslot[0][:], ag2out_t[:, t, :], waits=now("v", "cc"))
            dma("sync", oslot[1][:], ag2out_t[:, t + 16, :])
            v_op(lambda e: e.tensor_single_scalar(out=ocat[:, 0:512], in_=oslot[0][:],
                                                  scalar=m0t[:], op=ALU.mult),
                 waits=now("ld", "tp"))
            v_op(lambda e: e.scalar_tensor_tensor(out=ocat[:, 0:512], in0=oslot[1][:],
                                                  scalar=m1t[:], in1=ocat[:, 0:512],
                                                  op0=ALU.mult, op1=ALU.add))
            dma("sync", oslot[0][:], ag2out_t[:, 32 + t, :], waits=now("v"))
            dma("sync", oslot[1][:], ag2out_t[:, 32 + t + 16, :])
            v_op(lambda e: e.tensor_single_scalar(out=ocat[:, 512:1024], in_=oslot[0][:],
                                                  scalar=m0t[:], op=ALU.mult), waits=now("ld"))
            v_op(lambda e: e.scalar_tensor_tensor(out=ocat[:, 512:1024], in0=oslot[1][:],
                                                  scalar=m1t[:], in1=ocat[:, 512:1024],
                                                  op0=ALU.mult, op1=ALU.add))
            dmaT(ocatT, ocat[:], waits=now("v", "t"))
            wt = now("tp", "v")
            for kc in range(8):
                for n in range(2):
                    t_op(lambda e, kc=kc, n=n: e.matmul(
                        out=pb[n][:, 0:512], lhsT=ocatT[:, kc, :],
                        rhs=woT[:, kc, 512 * n:512 * (n + 1)],
                        start=(kc == 0), stop=(kc == 7)),
                        waits=(wt if (kc == 0 and n == 0) else ()))
            dma("sync", xrow[:], x1_t[:, t, :], waits=now("v"))
            for n in range(2):
                v_op(lambda e, n=n: e.tensor_add(out=xrow[:, 512 * n:512 * (n + 1)],
                                                 in0=pb[n][:, 0:512],
                                                 in1=xrow[:, 512 * n:512 * (n + 1)]),
                     waits=(now("t", "ld") if n == 0 else ()))
            dma("sync", x2_t[:, t, :], xrow[:], waits=now("v"))
        barrier()

    if stage == "prep":
        pass
    else:
        ffn(xin_t, x1_t)
    if stage in ("attn", "full"):
        attention()
    if stage == "prep":
        pass
    elif stage == "full":
        dma("sync", wbuf[:, :], w1qt_st.ap(), waits=now("t", "s", "v", "tp", "g"))
        barrier()
        ffn(x2_t, out_t)
    elif stage != "prep":
        src_t = x1_t if stage == "ffn1" else x2_t
        for t in range(NTT):
            dma("sync", xrow[:], src_t[:, t, :], waits=now("v"))
            v_op(lambda e: e.tensor_copy(out=bigA[:, 0:D], in_=xrow[:]), waits=now("ld"))
            dma("sync", out_t[:, t, :], bigA[:, 0:D], waits=now("v"))
    else:
        for t in range(NTT):
            dma("sync", xrow[:], xin_t[:, t, :], waits=now("v"))
            v_op(lambda e: e.tensor_copy(out=bigA[:, 0:D], in_=xrow[:]), waits=now("ld"))
            dma("sync", out_t[:, t, :], bigA[:, 0:D], waits=now("v"))

    if dbg:
        dbgg = dt("dbg_g", [TOK, HID], BF16, kind="ExternalOutput")
        dbgw = dt("dbg_w", [128, 8 * HID], BF16, kind="ExternalOutput")
        dbgs = dt("dbg_s", [128, 64], F32, kind="ExternalOutput")
        dma("sync", dbgg.ap(), g_std.ap(), waits=dep_all())
        dma("sync", dbgw.ap(), w1qt_st.ap())
        v_op(lambda e: e.tensor_copy(out=bigA[:, 0:16], in_=sxs[:]), waits=now("ld"))
        v_op(lambda e: e.tensor_copy(out=bigA[:, 16:32], in_=rr1[:]))
        v_op(lambda e: e.tensor_copy(out=bigA[:, 32:48], in_=rr2[:]))
        v_op(lambda e: e.tensor_copy(out=bigA[:, 48:49], in_=wsc1[:]))
        v_op(lambda e: e.tensor_copy(out=bigA[:, 49:50], in_=wsc2[:]))
        v_op(lambda e: e.tensor_copy(out=bigA[:, 50:51], in_=wthr[:]))
        v_op(lambda e: e.tensor_copy(out=bigA[:, 51:52], in_=wnthr[:]))
        v_op(lambda e: e.tensor_copy(out=bigA[:, 52:53], in_=wsc1[:]))
        v_op(lambda e: e.tensor_copy(out=bigA[:, 0:16], in_=amdbg[:]))
        v_op(lambda e: e.tensor_copy(out=bigA[:, 16:32], in_=esdbg[:]))
        dma("sync", dbgs.ap(), bigA[:, 0:64], waits=now("v"))
        dbgp = dt("dbg_p", [128, 12288], BF16, kind="ExternalOutput")
        dma("sync", tlv(dbgp.ap())[:, 0, :] if False else dbgp.ap()[:, 0:4096], prepP[:, 0:4096], waits=now("v"))
        dma("sync", dbgp.ap()[:, 4096:8192], prepN[:, 0:4096])
        dma("sync", dbgp.ap()[:, 8192:12288], tern_ring[1][:, 0:4096])
    op("sync", lambda e: e.nop(), waits=dep_all())

    with contextlib.ExitStack() as fes:
        for n in CH:
            sems[n] = fes.enter_context(nc.semaphore("sem_" + n))
        block = fes.enter_context(nc.Block())
        for eng in ENGS:
            getattr(block, eng)(lambda e, _f=ops[eng]: [f(e) for f in _f])
    es.close()
    return nc


def shard_inputs(inputs):
    x = np.ascontiguousarray(np.asarray(inputs["L0_attn_x"], dtype=np.float32))
    maps = []
    for core in range(8):
        b, hg = core // 2, core % 2
        maps.append({
            "x_in": np.ascontiguousarray(x[b, hg * 2048:(hg + 1) * 2048, :]),
            "w1": np.ascontiguousarray(np.asarray(inputs["L0_ffn_w1"], np.float32)),
            "w2": np.ascontiguousarray(np.asarray(inputs["L0_ffn_w2"], np.float32)),
            "b1": np.asarray(inputs["L0_ffn_b1"], np.float32).reshape(1, HID),
            "b2": np.asarray(inputs["L0_ffn_b2"], np.float32).reshape(1, D),
            "lnw": np.asarray(inputs["L0_ffn_ln1_w"], np.float32).reshape(1, D),
            "lnb": np.asarray(inputs["L0_ffn_ln1_b"], np.float32).reshape(1, D),
            "ln2w": np.asarray(inputs["L0_ffn_ln2_w"], np.float32).reshape(1, HID),
            "ln2b": np.asarray(inputs["L0_ffn_ln2_b"], np.float32).reshape(1, HID),
            "wq": np.ascontiguousarray(np.asarray(inputs["L0_attn_Wq"], np.float32)[hg * 128:(hg + 1) * 128, :]),
            "wk": np.ascontiguousarray(np.asarray(inputs["L0_attn_Wk"], np.float32)[hg * 128:(hg + 1) * 128, :]),
            "wv": np.ascontiguousarray(np.asarray(inputs["L0_attn_Wv"], np.float32)[hg * 512:(hg + 1) * 512, :]),
            "wo": np.ascontiguousarray(np.asarray(inputs["L0_attn_Wo"], np.float32)),
            "gam": np.asarray(inputs["L0_attn_gamma"], np.float32).reshape(1, FD),
            "bet": np.asarray(inputs["L0_attn_beta"], np.float32).reshape(1, FD),
            "m0": np.full((128, 1), 1.0 if hg == 0 else 0.0, np.float32),
            "m1": np.full((128, 1), 0.0 if hg == 0 else 1.0, np.float32),
        })
    return maps


def assemble(results):
    out = np.empty((4, 4096, 1024), np.float32)
    for core in range(8):
        b, hg = core // 2, core % 2
        out[b, hg * 2048:(hg + 1) * 2048, :] = results[core]["out"]
    return out


def kernel(**inputs):
    nc = build("full")
    maps = shard_inputs(inputs)
    res = run_bass_kernel_spmd(nc, maps, core_ids=list(range(8)))
    return assemble(res.results)


# version 02
# speedup vs baseline: 1.2959x; 1.2959x over previous
import contextlib
import numpy as np
import ml_dtypes
import concourse.bass as bass
import concourse.mybir as mybir
from concourse.bass_utils import run_bass_kernel_spmd

F32 = mybir.dt.float32
BF16 = mybir.dt.bfloat16
I32 = mybir.dt.int32
AF = mybir.ActivationFunctionType
ALU = mybir.AluOpType
AXX = mybir.AxisListType.X
bfloat16 = ml_dtypes.bfloat16

TOK, D, HID = 2048, 1024, 4096
NTT = TOK // 128
NH, FD, HD = 8, 16, 64
EPS = 1e-5
SEQ, NCH = 4096, 32
ENGS = ("sync", "scalar", "vector", "tensor", "gpsimd")
CH = ("ld", "tp", "gd", "v", "s", "t", "g", "cc")


def build(stage="full", dbg=False):
    nc = bass.Bass("TRN2", target_bir_lowering=False, detect_race_conditions=False)
    ops = {e: [] for e in ENGS}
    cnt = {c: 0 for c in CH}
    sems = {}

    def op(eng, make, waits=(), inc=None, amt=1):
        waits = tuple((n, v) for (n, v) in waits if v > 0)
        if inc is not None:
            cnt[inc] += amt
        def fn(e, _m=make, _w=waits, _i=inc, _a=amt):
            for n, v in _w:
                e.wait_ge(sems[n], v)
            ins = _m(e)
            if _i is not None:
                ins.then_inc(sems[_i], _a)
        ops[eng].append(fn)

    def now(*names):
        return [(n, cnt[n]) for n in names]

    def dep_all():
        return now(*CH)

    def barrier():
        w = dep_all()
        for eng in ENGS:
            if eng in ("vector", "gpsimd"):
                op(eng, lambda e: e.engine_nop(), waits=w)
            else:
                op(eng, lambda e: e.nop(), waits=w)

    def dma(eng, out, in_, waits=()):
        chn = {"sync": "ld", "scalar": "tp", "gpsimd": "gd"}[eng]
        op(eng, lambda e, o=out, i=in_: e.dma_start(out=o, in_=i), waits=waits, inc=chn, amt=16)

    def dmaT(out, in_, waits=()):
        op("sync", lambda e, o=out, i=in_: e.dma_start_transpose(out=o, in_=i),
           waits=waits, inc="tp", amt=16)

    def v_op(make, waits=(), pad=False):
        op("vector", make, waits=waits, inc="v")
        if pad:
            ops["vector"].append(lambda e: e.tensor_copy(out=_vpad[0], in_=_vpad[1]))
    def s_op(make, waits=(), pad=False):
        op("scalar", make, waits=waits, inc="s")
        if pad:
            ops["scalar"].append(lambda e: e.activation(out=_spad[0], in_=_spad[1], func=AF.Copy))
    t_op = lambda make, waits=(): op("tensor", make, waits=waits, inc="t")
    g_op = lambda make, waits=(): op("gpsimd", make, waits=waits, inc="g")

    dt = nc.dram_tensor
    x_in = dt("x_in", [TOK, D], F32, kind="ExternalInput")
    w1 = dt("w1", [HID, D], F32, kind="ExternalInput")
    w2 = dt("w2", [D, HID], F32, kind="ExternalInput")
    b1d = dt("b1", [1, HID], F32, kind="ExternalInput")
    b2d = dt("b2", [1, D], F32, kind="ExternalInput")
    lnwd = dt("lnw", [1, D], F32, kind="ExternalInput")
    lnbd = dt("lnb", [1, D], F32, kind="ExternalInput")
    ln2wd = dt("ln2w", [1, HID], F32, kind="ExternalInput")
    ln2bd = dt("ln2b", [1, HID], F32, kind="ExternalInput")
    wqd = dt("wq", [NH * FD, D], F32, kind="ExternalInput")
    wkd = dt("wk", [NH * FD, D], F32, kind="ExternalInput")
    wvd = dt("wv", [NH * HD, D], F32, kind="ExternalInput")
    wod = dt("wo", [D, D], F32, kind="ExternalInput")
    gamd = dt("gam", [1, FD], F32, kind="ExternalInput")
    betd = dt("bet", [1, FD], F32, kind="ExternalInput")
    m0d = dt("m0", [128, 1], F32, kind="ExternalInput")
    m1d = dt("m1", [128, 1], F32, kind="ExternalInput")
    outd = dt("out", [TOK, D], F32, kind="ExternalOutput")

    x1d = dt("x1d", [TOK, D], F32)
    x2d = dt("x2d", [TOK, D], F32)
    g_std = dt("g_st", [TOK, HID], BF16)
    w1qt_st = dt("w1qt_st", [128, 8 * HID], BF16)
    w2qt_st = dt("w2qt_st", [128, 32 * D], BF16)
    ag1_in = dt("ag1_in", [TOK, D], BF16)
    ag1_out = dt("ag1_out", [SEQ, D], BF16)
    ag2_in = dt("ag2_in", [SEQ, NH * HD], BF16)
    ag2_out = dt("ag2_out", [2 * SEQ, NH * HD], BF16)

    tlv = lambda ap: ap.rearrange("(t p) f -> p t f", p=128)
    xin_t, x1_t, x2_t, out_t = tlv(x_in.ap()), tlv(x1d.ap()), tlv(x2d.ap()), tlv(outd.ap())
    gst_t = tlv(g_std.ap())
    ag1in_t, ag1out_t = tlv(ag1_in.ap()), tlv(ag1_out.ap())
    ag2in_t, ag2out_t = tlv(ag2_in.ap()), tlv(ag2_out.ap())
    w1_t = tlv(w1.ap())
    w2_t = tlv(w2.ap())
    wv_t = tlv(wvd.ap())
    wo_t = tlv(wod.ap())

    es = contextlib.ExitStack()
    sb = lambda name, shape, dtp: es.enter_context(nc.sbuf_tensor(name, shape, dtp))

    _vpadt = sb("_vpad", [128, 256], F32)
    _vpad = (_vpadt[:, 0:128], _vpadt[:, 128:256])
    _spadt = sb("_spad", [128, 256], F32)
    _spad = (_spadt[:, 0:128], _spadt[:, 128:256])
    wbuf = sb("wbuf", [128, 32768], BF16)
    w1qT = wbuf.ap().rearrange("p (c n) -> p c n", c=8)
    w2qT = wbuf.ap().rearrange("p (c n) -> p c n", c=32)
    wqT = wbuf[:, 0:1024].rearrange("p (c n) -> p c n", c=8)
    wkT = wbuf[:, 1024:2048].rearrange("p (c n) -> p c n", c=8)
    wvT = wbuf[:, 2048:6144].rearrange("p (c n) -> p c n", c=8)
    woT = wbuf[:, 6144:14336].rearrange("p (c n) -> p c n", c=8)
    v1t = wbuf[:, 14336:14336 + 32 * 520].rearrange("p (t h n) -> p t h n", t=32, h=8)

    NB = 20480
    bbuf = sb("bbuf", [128, NB], BF16)
    g_ring = [bbuf[:, i * HID:(i + 1) * HID] for i in range(2)]
    prepP = bbuf[:, 0:4096]
    prepN = bbuf[:, 4096:8192]
    tern_ring = [bbuf[:, 8192:12288], bbuf[:, 12288:16384]]
    xq2b = bbuf[:, 8192:12288]
    xq2T = bbuf[:, 12288:16384].rearrange("p (c n) -> p c n", c=32)
    xq1b = bbuf[:, 16384:17408]
    xq1T = bbuf[:, 17408:18432].rearrange("p (c n) -> p c n", c=8)
    xlnc_ring = [bbuf[:, i * 1024:(i + 1) * 1024] for i in range(2)]
    xlncT_ring = [bbuf[:, 2048 + i * 1024: 3072 + i * 1024].rearrange("p (c n) -> p c n", c=8)
                  for i in range(2)]
    qraw = bbuf[:, 4096:8192].rearrange("p (t n) -> p t n", t=32)
    kraw = bbuf[:, 8192:12288].rearrange("p (t n) -> p t n", t=32)
    qf_ring = [bbuf[:, 12288 + i * 128: 12416 + i * 128] for i in range(2)]
    kf_ring = [bbuf[:, 12544 + i * 128: 12672 + i * 128] for i in range(2)]
    phiqb = bbuf[:, 12800:14848]
    phikb = bbuf[:, 14848:16896]
    phiqT = bbuf[:, 16896:18944].rearrange("p (j n) -> p j n", j=16)
    ssnap = bbuf[:, 18944:18944 + 1040].rearrange("p (fh h n) -> p fh h n", fh=2, h=8)

    bigA = sb("bigA", [128, HID], F32)
    bigB = sb("bigB", [128, HID], F32)
    bigBbf = bigB.ap().bitcast(BF16)
    quadA = sb("quadA", [128, 512], F32)
    quadI = sb("quadI", [128, 512], I32)
    quadC = sb("quadC", [128, 512], F32)
    xrow = sb("xrow", [128, D], F32)
    phikT_s = sb("phikT", [128, 2048], BF16)
    phikT = phikT_s.ap().rearrange("p (j n) -> p j n", j=16)
    sTev = sb("sTev", [128, 128], BF16)
    qfp = sb("qfp", [128, 256], BF16)
    kfp = sb("kfp", [128, 256], BF16)
    qfpT_s = sb("qfpT", [128, 256], BF16)
    kfpT_s = sb("kfpT", [128, 256], BF16)
    qfpT = qfpT_s.ap().rearrange("p (b n) -> p b n", b=2)
    kfpT = kfpT_s.ap().rearrange("p (b n) -> p b n", b=2)
    sTm = sb("sTm", [128, 128], BF16)
    o_outb = sb("o_outb", [128, 512], BF16)
    ocat = sb("ocat", [128, 1024], BF16)
    ocatT_s = sb("ocatT", [128, 1024], BF16)
    ocatT = ocatT_s.ap().rearrange("p (c n) -> p c n", c=8)
    oslot = [sb(f"oslot{i}", [128, 512], BF16) for i in range(2)]

    lnw_bc = sb("lnw_bc", [128, D], F32)
    lnb_bc = sb("lnb_bc", [128, D], F32)
    ln2w_bc = sb("ln2w_bc", [128, HID], BF16)
    ln2b_bc = sb("ln2b_bc", [128, HID], BF16)
    gamq_bc = sb("gamq_bc", [128, 128], F32)
    betq_bc = sb("betq_bc", [128, 128], F32)
    gamk_bc = sb("gamk_bc", [128, 128], F32)
    betk_bc = sb("betk_bc", [128, 128], F32)
    identq = sb("identq", [128, 128], F32)
    onesq = sb("onesq", [128, 128], F32)

    rows1 = sb("rows1", [97, 512], BF16)
    rows2 = sb("rows2", [97, 512], BF16)
    rows3 = sb("rows3", [97, 512], BF16)
    rows4 = sb("rows4", [97, 512], BF16)
    rowst = sb("rowst", [97, 512], F32)

    def rowslice(t, k, n=512):
        return t[32 * k: 32 * k + 1, 0:n]

    st = lambda name, n=16: sb(name, [128, n], F32)
    sxs, sxq = st("sxs"), st("sxq")
    rr1, nmr1 = st("rr1"), st("nmr1")
    gs8 = sb("gs8", [128, 128], F32)
    gq8 = sb("gq8", [128, 128], F32)
    rr2, nmr2 = st("rr2"), st("nmr2")
    tA, tB, tC, tD = st("tA"), st("tB"), st("tC"), st("tD")
    am1, es1v, ar1 = st("am1", 1), st("es1", 1), st("ar1", 1)
    am2, es2v, ar2 = st("am2", 1), st("es2", 1), st("ar2", 1)
    wsum = st("wsum", 32)
    wthr, wnthr = st("wthr", 1), st("wnthr", 1)
    wsc1, wsc2 = st("wsc1", 1), st("wsc2", 1)
    onecol = st("onecol", 1)
    arow_s = sb("arow", [97, 128], BF16)
    arow = lambda k: arow_s[32 * k:32 * k + 1, :]
    onesrow_s = sb("onesrow", [97, 128], F32)
    onesrow = lambda k: onesrow_s[32 * k:32 * k + 1, :]
    scal = sb("scal", [1, 16], F32)
    m0t, m1t = st("m0t", 1), st("m1t", 1)
    asx, asq = st("asx"), st("asq")
    arr_, anmr = st("arr"), st("anmr")
    qsum = sb("qsum", [128, 256], F32)
    qsq = sb("qsq", [128, 256], F32)
    ksum = sb("ksum", [128, 256], F32)
    ksq = sb("ksq", [128, 256], F32)
    qmu = sb("qmu", [128, 256], F32)
    qrs = sb("qrs", [128, 256], F32)
    kmu = sb("kmu", [128, 256], F32)
    krs = sb("krs", [128, 256], F32)
    fmw = sb("fmw", [128, 128], F32)
    fmw2 = sb("fmw2", [128, 128], F32)
    fmw3 = sb("fmw3", [128, 128], F32)
    zt, zrt = st("zt", 8), st("zrt", 8)
    onebf = sb("onebf", [128, 1], BF16)

    pb = [es.enter_context(nc.psum_tensor(f"pb{i}", [128, 512], F32)) for i in range(8)]

    v_op(lambda e: e.memset(_vpadt[:], 0.0))
    s_op(lambda e: e.activation(out=_spadt[:], in_=_vpadt[:], func=AF.Copy), waits=now("v"))
    v_op(lambda e: e.memset(onecol[:], 1.0))
    v_op(lambda e: e.memset(onesq[:], 1.0))
    v_op(lambda e: e.memset(onebf[:], 1.0))
    v_op(lambda e: e.memset(onesrow_s[:], 1.0))
    dma("sync", m0t[:], m0d.ap())
    dma("sync", m1t[:], m1d.ap())
    g_op(lambda e: e.affine_select(out=identq[:], in_=onesq[:], pattern=[[1, 128]],
                                   compare_op=ALU.is_equal, fill=0.0, base=0,
                                   channel_multiplier=-1), waits=now("v"))

    def bc_load(dst_bc, src_ap, n):
        for ci in range(0, n, 512):
            w = min(512, n - ci)
            sl = (ci // 512) % 3
            dma("sync", rowslice(rowst, sl, w), src_ap[0:1, ci:ci + w], waits=now("t"))
            t_op(lambda e, w=w, sl=sl: e.matmul(out=pb[7][:, 0:w], lhsT=onesrow(sl),
                                                rhs=rowslice(rowst, sl, w),
                                                start=True, stop=True),
                 waits=now("ld", "v"))
            v_op(lambda e, ci=ci, w=w, d=dst_bc: e.tensor_copy(out=d[:, ci:ci + w],
                                                               in_=pb[7][:, 0:w]),
                 waits=now("t"))

    bc_load(lnw_bc, lnwd.ap(), D)
    bc_load(lnb_bc, lnbd.ap(), D)
    bc_load(ln2w_bc, ln2wd.ap(), HID)
    bc_load(ln2b_bc, ln2bd.ap(), HID)

    dma("sync", rowslice(rowst, 0, FD), gamd.ap(), waits=now("t"))
    dma("sync", rowslice(rowst, 1, FD), betd.ap(), waits=now("t"))
    t_op(lambda e: e.matmul(out=pb[7][:, 0:128], lhsT=onesrow(0),
                            rhs=rowst[0:1, 0:FD].unsqueeze(1).broadcast_to([1, 8, FD]),
                            start=True, stop=True), waits=now("ld", "v"))
    v_op(lambda e: e.tensor_copy(out=gamk_bc[:], in_=pb[7][:, 0:128]), waits=now("t"))
    t_op(lambda e: e.matmul(out=pb[7][:, 0:128], lhsT=onesrow(1),
                            rhs=rowst[32:33, 0:FD].unsqueeze(1).broadcast_to([1, 8, FD]),
                            start=True, stop=True), waits=now("v"))
    v_op(lambda e: e.tensor_copy(out=betk_bc[:], in_=pb[7][:, 0:128]), waits=now("t"))
    v_op(lambda e: e.tensor_scalar_mul(out=gamq_bc[:], in0=gamk_bc[:], scalar1=0.25))
    v_op(lambda e: e.tensor_scalar_mul(out=betq_bc[:], in0=betk_bc[:], scalar1=0.25))

    _b1tensors = [rows1, rows1, rows1, rows2, rows2, rows2, rows3, rows3]
    for n in range(8):
        dma("gpsimd", rowslice(_b1tensors[n], n % 3), b1d.ap()[0:1, 512 * n:512 * (n + 1)])
    dma("gpsimd", rowslice(rows3, 2), b2d.ap()[0:1, 0:512])
    dma("gpsimd", rowslice(rows4, 0), b2d.ap()[0:1, 512:1024])
    b1chunk = lambda n: rowslice(_b1tensors[n], n % 3)
    b2chunk = lambda n: rowslice(rows3, 2) if n == 0 else rowslice(rows4, 0)
    b2base = lambda n: 2 if n == 0 else 0
    barrier()

    def prep_w(w_tiles, ntiles, freedim, wsc_dst, qT_dst_fn, totelem):
        for i in range(ntiles):
            buf = bigA if i % 2 == 0 else bigB
            dma("sync", buf[:, 0:freedim], w_tiles[:, i, :], waits=now("s"))
            s_op(lambda e, b=buf, i=i, f=freedim: e.activation(
                out=prepP.bitcast(BF16)[:, 0:f] if False else prepP[:, 0:f],
                in_=b[:, 0:f], func=AF.Abs, accum_out=wsum[:, i:i + 1]),
                waits=now("ld"))
        v_op(lambda e, n=ntiles: e.reduce_sum(out=wthr[:], in_=wsum[:, 0:n], axis=AXX),
             waits=now("s"), pad=True)
        t_op(lambda e: e.matmul(out=pb[7][0:1, 0:1], lhsT=wthr[:], rhs=onecol[:],
                                start=True, stop=True), waits=now("v"))
        v_op(lambda e: e.tensor_copy(out=scal[0:1, 0:1], in_=pb[7][0:1, 0:1]), waits=now("t"), pad=True)
        t_op(lambda e: e.matmul(out=pb[7][:, 0:1], lhsT=onesrow(0), rhs=scal[0:1, 0:1],
                                start=True, stop=True), waits=now("v"))
        v_op(lambda e, n=totelem, d=wsc_dst: e.tensor_scalar_mul(out=d[:], in0=pb[7][:, 0:1],
                                                                 scalar1=1.0 / n), waits=now("t"), pad=True)
        v_op(lambda e, d=wsc_dst: e.tensor_scalar_mul(out=wthr[:], in0=d[:], scalar1=0.5), pad=True)
        v_op(lambda e: e.tensor_scalar_mul(out=wnthr[:], in0=wthr[:], scalar1=-1.0), pad=True)
        for i in range(ntiles):
            buf = bigA if i % 2 == 0 else bigB
            tern = tern_ring[i % 2]
            dma("sync", buf[:, 0:freedim], w_tiles[:, i, :], waits=now("v", "tp"))
            v_op(lambda e, b=buf, f=freedim: e.tensor_single_scalar(
                out=prepP[:, 0:f], in_=b[:, 0:f], scalar=wthr[:], op=ALU.is_gt),
                waits=now("ld"))
            v_op(lambda e, b=buf, f=freedim: e.tensor_single_scalar(
                out=prepN[:, 0:f], in_=b[:, 0:f], scalar=wnthr[:], op=ALU.is_lt))
            v_op(lambda e, t_=tern, f=freedim: e.tensor_sub(out=t_[:, 0:f], in0=prepP[:, 0:f],
                                                            in1=prepN[:, 0:f]))
            dmaT(qT_dst_fn(i), tern[:, 0:freedim], waits=now("v"))

    prep_w(w2_t, 8, HID, wsc2, lambda i: w2qT[:, :, 128 * i:128 * (i + 1)], D * HID)
    dma("sync", w2qt_st.ap(), wbuf[:, :], waits=now("tp"))
    barrier()
    prep_w(w1_t, 32, D, wsc1, lambda i: w1qT[:, :, 128 * i:128 * (i + 1)], D * HID)
    dma("sync", w1qt_st.ap(), wbuf[:, :], waits=now("tp"))
    barrier()

    def ln_chain(sum_t, sq_t, n, rr_dst, nmr_dst, double_ln):
        v_op(lambda e: e.tensor_scalar_mul(out=tA[:], in0=sum_t[:], scalar1=1.0 / n), waits=now("s"), pad=True)
        v_op(lambda e: e.tensor_scalar_mul(out=tB[:], in0=sq_t[:], scalar1=1.0 / n), pad=True)
        v_op(lambda e: e.tensor_mul(out=tC[:], in0=tA[:], in1=tA[:]), pad=True)
        v_op(lambda e: e.tensor_sub(out=tB[:], in0=tB[:], in1=tC[:]), pad=True)
        if double_ln:
            v_op(lambda e: e.tensor_scalar_add(out=tC[:], in0=tB[:], scalar1=EPS), pad=True)
            v_op(lambda e: e.reciprocal(out=tD[:], in_=tC[:]), pad=True)
            v_op(lambda e: e.tensor_mul(out=tB[:], in0=tB[:], in1=tD[:]), pad=True)
            v_op(lambda e: e.tensor_scalar_add(out=tB[:], in0=tB[:], scalar1=EPS), pad=True)
            v_op(lambda e: e.tensor_mul(out=tB[:], in0=tB[:], in1=tC[:]), pad=True)
            v_op(lambda e: e.reciprocal(out=tB[:], in_=tB[:]), pad=True)
        else:
            v_op(lambda e: e.tensor_scalar_add(out=tB[:], in0=tB[:], scalar1=EPS), pad=True)
            v_op(lambda e: e.reciprocal(out=tB[:], in_=tB[:]), pad=True)
        s_op(lambda e, r=rr_dst: e.activation(out=r[:], in_=tB[:], func=AF.Sqrt), waits=now("v"), pad=True)
        v_op(lambda e, r=rr_dst, m=nmr_dst: e.tensor_mul(out=m[:], in0=tA[:], in1=r[:]),
             waits=now("s"), pad=True)
        v_op(lambda e, m=nmr_dst: e.tensor_scalar_mul(out=m[:], in0=m[:], scalar1=-1.0), pad=True)

    def quant_to(dst_bf, src_f32, width, s_scale_ap):
        for q in range(width // 512):
            sl = slice(512 * q, 512 * (q + 1))
            v_op(lambda e, sl=sl: e.tensor_scalar(out=quadA[:], in0=src_f32[:, sl],
                                                  scalar1=s_scale_ap, scalar2=128.0,
                                                  op0=ALU.mult, op1=ALU.add))
            v_op(lambda e: e.tensor_copy(out=quadI[:], in_=quadA[:]))
            v_op(lambda e: e.scalar_tensor_tensor(out=quadC[:], in0=quadA[:], scalar=0.5,
                                                  in1=quadI[:], op0=ALU.subtract, op1=ALU.is_ge))
            v_op(lambda e, sl=sl: e.scalar_tensor_tensor(out=dst_bf[:, sl], in0=quadI[:],
                                                         scalar=-128.0, in1=quadC[:],
                                                         op0=ALU.add, op1=ALU.add))

    dbgS = dt("dbg_S", [32 * 128, 1040], F32, kind="ExternalOutput") if dbg else None
    dbgP = dt("dbg_P", [32 * 128, 2048], BF16, kind="ExternalOutput") if dbg else None
    dbgPt = tlv(dbgP.ap()) if dbg else None
    dbgK = dt("dbg_K", [32 * 128, 128], BF16, kind="ExternalOutput") if dbg else None
    dbgKt = tlv(dbgK.ap()) if dbg else None
    dbgQ = dt("dbg_Q", [32 * 128, 128], BF16, kind="ExternalOutput") if dbg else None
    dbgQt = tlv(dbgQ.ap()) if dbg else None
    dbgSt = tlv(dbgS.ap()) if dbg else None
    dbgx = dt("dbg_x", [TOK, D], BF16, kind="ExternalOutput") if dbg else None
    dbgxt = tlv(dbgx.ap()) if dbg else None
    amdbg = sb("amdbg", [128, 16], F32)
    esdbg = sb("esdbg", [128, 16], F32)

    def ffn(xin_tiles, xout_tiles):
        for t in range(NTT):
            dma("sync", xrow[:], xin_tiles[:, t, :], waits=now("v", "s"))
            v_op(lambda e, t=t: e.reduce_sum(out=sxs[:, t:t + 1], in_=xrow[:], axis=AXX),
                 waits=now("ld"))
            s_op(lambda e, t=t: e.activation(out=bigA[:, 0:D], in_=xrow[:], func=AF.Square,
                                             accum_out=sxq[:, t:t + 1]), waits=now("ld", "v"))
        ln_chain(sxs, sxq, D, rr1, nmr1, double_ln=True)

        for t in range(NTT):
            gbuf = g_ring[t % 2]
            dma("sync", xrow[:], xin_tiles[:, t, :], waits=now("v", "s"))
            s_op(lambda e, t=t: e.activation(out=bigB[:, 0:D], in_=xrow[:], func=AF.Identity,
                                             bias=nmr1[:, t:t + 1], scale=rr1[:, t:t + 1]),
                 waits=now("ld", "v"))
            v_op(lambda e: e.tensor_mul(out=bigA[:, 0:D], in0=bigB[:, 0:D], in1=lnw_bc[:]),
                 waits=now("s"))
            v_op(lambda e: e.tensor_add(out=bigA[:, 0:D], in0=bigA[:, 0:D], in1=lnb_bc[:]))
            v_op(lambda e: e.tensor_reduce(out=am1[:], in_=bigA[:, 0:D], axis=AXX, op=ALU.max,
                                           apply_absolute_value=True), pad=True)
            v_op(lambda e: e.tensor_scalar_max(out=am1[:], in0=am1[:], scalar1=EPS), pad=True)
            v_op(lambda e: e.reciprocal(out=tD[:, 0:1], in_=am1[:]), pad=True)
            v_op(lambda e: e.tensor_scalar_mul(out=tD[:, 0:1], in0=tD[:, 0:1], scalar1=127.0), pad=True)
            v_op(lambda e: e.tensor_mul(out=es1v[:], in0=am1[:], in1=wsc1[:]), pad=True)
            v_op(lambda e: e.tensor_scalar_mul(out=es1v[:], in0=es1v[:], scalar1=1.0 / 127.0), pad=True)
            v_op(lambda e: e.reciprocal(out=ar1[:], in_=es1v[:]), pad=True)
            if dbg:
                v_op(lambda e, t=t: e.tensor_copy(out=amdbg[:, t:t + 1], in_=am1[:]), pad=True)
                v_op(lambda e, t=t: e.tensor_copy(out=esdbg[:, t:t + 1], in_=es1v[:]), pad=True)
            quant_to(xq1b, bigA, D, tD[:, 0:1])
            if dbg:
                dma("sync", dbgxt[:, t, :], xq1b[:], waits=now("v"))
            dmaT(xq1T, xq1b, waits=now("v", "t"))
            t_op(lambda e: e.transpose(out=pb[7][0:1, 0:128], in_=ar1[:], identity=identq[:]),
                 waits=now("v", "s"))
            for k in range(3):
                v_op(lambda e, k=k: e.tensor_copy(out=arow(k), in_=pb[7][0:1, 0:128]),
                     waits=(now("t") if k == 0 else ()), pad=True)
            wt = now("tp", "v", "s")
            for kc in range(8):
                for n in range(8):
                    t_op(lambda e, kc=kc, n=n: e.matmul(
                        out=pb[n][:, 0:512], lhsT=xq1T[:, kc, :],
                        rhs=w1qT[:, kc, 512 * n:512 * (n + 1)],
                        start=(kc == 0), stop=False),
                        waits=(wt if (kc == 0 and n == 0) else ()))
            for n in range(8):
                t_op(lambda e, n=n: e.matmul(out=pb[n][:, 0:512], lhsT=arow(n % 3),
                                             rhs=b1chunk(n), start=False, stop=True),
                     waits=(now("v") if n == 0 else ()))
            for n in range(8):
                s_op(lambda e, n=n, g=gbuf, t=t: e.activation(
                    out=g[:, 512 * n:512 * (n + 1)], in_=pb[n][:, 0:512], func=AF.Gelu,
                    scale=es1v[:], accum_out=gs8[:, 8 * t + n: 8 * t + n + 1]),
                    waits=(now("t", "ld") if n == 0 else ()))
                s_op(lambda e, n=n, g=gbuf, t=t: e.activation(
                    out=bigA[:, 512 * n:512 * (n + 1)], in_=g[:, 512 * n:512 * (n + 1)],
                    func=AF.Square, accum_out=gq8[:, 8 * t + n: 8 * t + n + 1]))
            dma("sync", gst_t[:, t, :], gbuf[:], waits=now("s"))

        v_op(lambda e: e.tensor_reduce(out=sxs[:], in_=gs8.ap().rearrange("p (t n) -> p t n", n=8),
                                       axis=AXX, op=ALU.add), waits=now("s"))
        v_op(lambda e: e.tensor_reduce(out=sxq[:], in_=gq8.ap().rearrange("p (t n) -> p t n", n=8),
                                       axis=AXX, op=ALU.add))
        ln_chain(sxs, sxq, HID, rr2, nmr2, double_ln=False)

        dma("sync", wbuf[:, :], w2qt_st.ap(), waits=now("t"))

        for t in range(NTT):
            dma("sync", bbuf[:, 0:HID], gst_t[:, t, :], waits=now("v", "s"))
            s_op(lambda e, t=t: e.activation(out=bigB[:, 0:HID], in_=bbuf[:, 0:HID], func=AF.Identity,
                                             bias=nmr2[:, t:t + 1], scale=rr2[:, t:t + 1]),
                 waits=now("ld", "v"))
            v_op(lambda e: e.tensor_mul(out=bigA[:, 0:HID], in0=bigB[:, 0:HID], in1=ln2w_bc[:]),
                 waits=now("s"))
            v_op(lambda e: e.tensor_add(out=bigA[:, 0:HID], in0=bigA[:, 0:HID], in1=ln2b_bc[:]))
            v_op(lambda e: e.tensor_reduce(out=am2[:], in_=bigA[:, 0:HID], axis=AXX, op=ALU.max,
                                           apply_absolute_value=True), pad=True)
            v_op(lambda e: e.tensor_scalar_max(out=am2[:], in0=am2[:], scalar1=EPS), pad=True)
            v_op(lambda e: e.reciprocal(out=tD[:, 0:1], in_=am2[:]), pad=True)
            v_op(lambda e: e.tensor_scalar_mul(out=tD[:, 0:1], in0=tD[:, 0:1], scalar1=127.0), pad=True)
            v_op(lambda e: e.tensor_mul(out=es2v[:], in0=am2[:], in1=wsc2[:]), pad=True)
            v_op(lambda e: e.tensor_scalar_mul(out=es2v[:], in0=es2v[:], scalar1=0.5 / 127.0), pad=True)
            v_op(lambda e: e.reciprocal(out=ar2[:], in_=es2v[:]), pad=True)
            v_op(lambda e: e.tensor_scalar_mul(out=ar2[:], in0=ar2[:], scalar1=0.5), pad=True)
            quant_to(xq2b, bigA, HID, tD[:, 0:1])
            dmaT(xq2T, xq2b, waits=now("v", "t"))
            t_op(lambda e: e.transpose(out=pb[7][0:1, 0:128], in_=ar2[:], identity=identq[:]),
                 waits=now("v", "s"))
            for k in range(3):
                v_op(lambda e, k=k: e.tensor_copy(out=arow(k), in_=pb[7][0:1, 0:128]),
                     waits=(now("t") if k == 0 else ()), pad=True)
            wt = now("tp", "v", "s")
            for kc in range(32):
                for n in range(2):
                    t_op(lambda e, kc=kc, n=n: e.matmul(
                        out=pb[n][:, 0:512], lhsT=xq2T[:, kc, :],
                        rhs=w2qT[:, kc, 512 * n:512 * (n + 1)],
                        start=(kc == 0), stop=False),
                        waits=(wt if (kc == 0 and n == 0) else ()))
            for n in range(2):
                t_op(lambda e, n=n: e.matmul(out=pb[n][:, 0:512], lhsT=arow(b2base(n)),
                                             rhs=b2chunk(n), start=False, stop=True),
                     waits=(now("v") if n == 0 else ()))
            dma("sync", xrow[:], xin_tiles[:, t, :], waits=now("v"))
            for n in range(2):
                v_op(lambda e, n=n: e.scalar_tensor_tensor(
                    out=xrow[:, 512 * n:512 * (n + 1)], in0=pb[n][:, 0:512], scalar=es2v[:],
                    in1=xrow[:, 512 * n:512 * (n + 1)], op0=ALU.mult, op1=ALU.add),
                    waits=(now("t", "ld") if n == 0 else ()))
            dma("sync", xout_tiles[:, t, :], xrow[:], waits=now("v"))
        barrier()

    def attention():
        stg = bbuf[:, 0:1024]
        dma("gpsimd", stg, wqd.ap(), waits=now("tp", "t", "s", "v"))
        dmaT(wqT, stg, waits=now("gd"))
        dma("gpsimd", stg, wkd.ap(), waits=now("tp"))
        dmaT(wkT, stg, waits=now("gd"))
        for j in range(4):
            dma("gpsimd", stg, wv_t[:, j, :], waits=now("tp"))
            dmaT(wvT[:, :, 128 * j:128 * (j + 1)], stg, waits=now("gd"))
        for j in range(8):
            dma("gpsimd", stg, wo_t[:, j, :], waits=now("tp"))
            dmaT(woT[:, :, 128 * j:128 * (j + 1)], stg, waits=now("gd"))
        barrier()
        v_op(lambda e: e.memset(v1t[:, :, :, 64:65], 1.0))

        for t in range(NTT):
            dma("sync", xrow[:], x1_t[:, t, :], waits=now("v", "s"))
            v_op(lambda e, t=t: e.reduce_sum(out=asx[:, t:t + 1], in_=xrow[:], axis=AXX),
                 waits=now("ld"))
            s_op(lambda e, t=t: e.activation(out=bigA[:, 0:D], in_=xrow[:], func=AF.Square,
                                             accum_out=asq[:, t:t + 1]), waits=now("ld", "v"))
        ln_chain(asx, asq, D, arr_, anmr, double_ln=False)
        for t in range(NTT):
            dma("sync", xrow[:], x1_t[:, t, :], waits=now("v", "s"))
            s_op(lambda e, t=t: e.activation(out=bigBbf[:, 0:D], in_=xrow[:], func=AF.Identity,
                                             bias=anmr[:, t:t + 1], scale=arr_[:, t:t + 1]),
                 waits=now("ld"))
            dma("sync", ag1in_t[:, t, :], bigBbf[:, 0:D], waits=now("s"))
        op("gpsimd", lambda e: e.collective_compute(
            "AllGather", ALU.bypass, replica_groups=[[0, 1], [2, 3], [4, 5], [6, 7]],
            ins=[ag1_in.ap().opt()], outs=[ag1_out.ap().opt()]),
            waits=now("ld"), inc="cc")
        barrier()

        for c in range(NCH):
            r = c % 2
            dma("sync", xlnc_ring[r], ag1out_t[:, c, :], waits=now("cc", "tp", "t"))
            dmaT(xlncT_ring[r], xlnc_ring[r], waits=now("ld"))
            wt = now("tp", "s", "v")
            for kc in range(8):
                t_op(lambda e, kc=kc, r=r: e.matmul(out=pb[0][:, 0:128],
                                                    lhsT=xlncT_ring[r][:, kc, :],
                                                    rhs=wqT[:, kc, :], start=(kc == 0),
                                                    stop=(kc == 7)),
                     waits=(wt if kc == 0 else ()))
            for kc in range(8):
                t_op(lambda e, kc=kc, r=r: e.matmul(out=pb[1][:, 0:128],
                                                    lhsT=xlncT_ring[r][:, kc, :],
                                                    rhs=wkT[:, kc, :], start=(kc == 0),
                                                    stop=(kc == 7)))
            for kc in range(8):
                t_op(lambda e, kc=kc, r=r: e.matmul(out=pb[2][:, 0:512],
                                                    lhsT=xlncT_ring[r][:, kc, :],
                                                    rhs=wvT[:, kc, :], start=(kc == 0),
                                                    stop=(kc == 7)))
            s_op(lambda e: e.activation(out=fmw[:], in_=pb[0][:, 0:128], func=AF.Copy),
                 waits=now("t", "v"))
            s_op(lambda e: e.activation(out=fmw2[:], in_=pb[1][:, 0:128], func=AF.Copy))
            s_op(lambda e, c=c: e.activation(
                out=v1t[:, c, :, 0:64], in_=pb[2][:, 0:512].rearrange("p (h n) -> p h n", h=8),
                func=AF.Copy))
            v_op(lambda e, c=c: e.tensor_copy(out=qraw[:, c, :], in_=fmw[:]), waits=now("s"))
            v_op(lambda e, c=c: e.tensor_copy(out=kraw[:, c, :], in_=fmw2[:]))
            v_op(lambda e, c=c: e.reduce_sum(out=qsum[:, 8 * c:8 * c + 8],
                                             in_=fmw.ap().rearrange("p (h f) -> p h f", f=16),
                                             axis=AXX))
            v_op(lambda e, c=c: e.reduce_sum(out=ksum[:, 8 * c:8 * c + 8],
                                             in_=fmw2.ap().rearrange("p (h f) -> p h f", f=16),
                                             axis=AXX))
            s_op(lambda e: e.activation(out=fmw3[:], in_=fmw[:], func=AF.Square), waits=now("v"))
            v_op(lambda e, c=c: e.reduce_sum(out=qsq[:, 8 * c:8 * c + 8],
                                             in_=fmw3.ap().rearrange("p (h f) -> p h f", f=16),
                                             axis=AXX), waits=now("s"))
            s_op(lambda e: e.activation(out=fmw3[:], in_=fmw2[:], func=AF.Square), waits=now("v"))
            v_op(lambda e, c=c: e.reduce_sum(out=ksq[:, 8 * c:8 * c + 8],
                                             in_=fmw3.ap().rearrange("p (h f) -> p h f", f=16),
                                             axis=AXX), waits=now("s"))

        def fm_stats(sumt, sqt_, mut, rst):
            v_op(lambda e: e.tensor_scalar_mul(out=mut[:], in0=sumt[:], scalar1=1.0 / FD),
                 waits=now("s"), pad=True)
            v_op(lambda e: e.tensor_scalar_mul(out=rst[:], in0=sqt_[:], scalar1=1.0 / FD), pad=True)
            v_op(lambda e: e.tensor_mul(out=sqt_[:], in0=mut[:], in1=mut[:]))
            v_op(lambda e: e.tensor_sub(out=rst[:], in0=rst[:], in1=sqt_[:]), pad=True)
            v_op(lambda e: e.tensor_scalar_add(out=rst[:], in0=rst[:], scalar1=EPS), pad=True)
            v_op(lambda e: e.reciprocal(out=rst[:], in_=rst[:]), pad=True)
            s_op(lambda e: e.activation(out=rst[:], in_=rst[:], func=AF.Sqrt), waits=now("v"), pad=True)
        fm_stats(qsum, qsq, qmu, qrs)
        fm_stats(ksum, ksq, kmu, krs)
        barrier()

        Sf = [bigA[:, 0:512], bigA[:, 512:1024]]
        zf = bigA[:, 1024:1040]
        v_op(lambda e: e.memset(Sf[0], 0.0))
        v_op(lambda e: e.memset(Sf[1], 0.0))
        v_op(lambda e: e.memset(zf, 0.0), pad=True)
        for c in range(NCH):
            rq, rk = qf_ring[c % 2], kf_ring[c % 2]

            def fm_apply(raw, mu, rs, gbc, bbc, dstr):
                mu_b = mu[:, 8 * c:8 * c + 8].unsqueeze(2).broadcast_to([128, 8, 16])
                rs_b = rs[:, 8 * c:8 * c + 8].unsqueeze(2).broadcast_to([128, 8, 16])
                src3 = raw[:, c, :].rearrange("p (h f) -> p h f", f=16)
                v_op(lambda e, s3=src3, mb=mu_b: e.tensor_sub(
                    out=fmw.ap().rearrange("p (h f) -> p h f", f=16),
                    in0=s3, in1=mb), waits=now("t", "tp"))
                v_op(lambda e: e.tensor_mul(out=fmw.ap().rearrange("p (h f) -> p h f", f=16),
                                            in0=fmw.ap().rearrange("p (h f) -> p h f", f=16),
                                            in1=rs_b))
                v_op(lambda e: e.tensor_mul(out=fmw[:], in0=fmw[:], in1=gbc[:]))
                v_op(lambda e, d=dstr: e.tensor_add(out=d, in0=fmw[:], in1=bbc[:]))
            fm_apply(qraw, qmu, qrs, gamq_bc, betq_bc, rq)
            fm_apply(kraw, kmu, krs, gamk_bc, betk_bc, rk)

            def phi_build(src, dst):
                i0 = src.rearrange("p (h f) -> p h f", f=16).unsqueeze(3).broadcast_to([128, 8, 16, 16])
                i1 = src.rearrange("p (h f) -> p h f", f=16).unsqueeze(2).broadcast_to([128, 8, 16, 16])
                v_op(lambda e, d=dst: e.tensor_tensor(
                    out=d.rearrange("p (h a b) -> p h a b", a=16, b=16), in0=i0, in1=i1,
                    op=ALU.mult))
            phi_build(rq, phiqb)
            phi_build(rk, phikb)
            v_op(lambda e, rq=rq: e.tensor_copy(
                out=qfp.ap().rearrange("p (h n) -> p h n", h=8)[:, :, 0:16],
                in_=rq.rearrange("p (h f) -> p h f", f=16)))
            v_op(lambda e, rk=rk: e.tensor_copy(
                out=kfp.ap().rearrange("p (h n) -> p h n", h=8)[:, :, 0:16],
                in_=rk.rearrange("p (h f) -> p h f", f=16)))
            dmaT(phiqT, phiqb, waits=now("v", "t"))
            dmaT(phikT, phikb, waits=now("v", "t"))
            dmaT(qfpT, qfp[:], waits=now("v", "t"))
            dmaT(kfpT, kfp[:], waits=now("v", "t"))

            if c > 0:
                s_op(lambda e: e.activation(
                    out=ssnap[:, 0, :, 0:64], in_=Sf[0].rearrange("p (h n) -> p h n", h=8),
                    func=AF.Copy), waits=now("v"))
                s_op(lambda e: e.activation(
                    out=ssnap[:, 1, :, 0:64], in_=Sf[1].rearrange("p (h n) -> p h n", h=8),
                    func=AF.Copy))
                v_op(lambda e: e.tensor_copy(
                    out=ssnap[:, :, :, 64:65],
                    in_=zf.rearrange("p (fh h) -> p fh h", fh=2).unsqueeze(3)))
            snap_w = now("s", "v")

            for h in range(NH):
                sbk = pb[3] if h % 2 == 0 else pb[4]
                blk = h // 4
                base = 32 * (h % 4)
                tpos = (96, 0) if base == 96 else None
                t_op(lambda e, h=h, sbk=sbk, blk=blk, base=base, tpos=tpos: e.matmul(
                    out=sbk[:, 0:128],
                    lhsT=kfpT[base:base + 16, blk, :],
                    rhs=qfpT[base:base + 16, blk, :],
                    start=True, stop=True, tile_position=tpos),
                     waits=now("tp", "s", "v", "g"))
                s_op(lambda e, sbk=sbk: e.activation(out=sTev[:], in_=sbk[:, 0:128], func=AF.Square),
                     waits=now("t"))
                g_op(lambda e: e.affine_select(out=sTm[:], in_=sTev[:], pattern=[[1, 128]],
                                               compare_op=ALU.is_ge, fill=0.0, base=0,
                                               channel_multiplier=-1), waits=now("s"))
                ob = pb[5] if h < 4 else pb[6]
                col = 65 * (h % 4)
                bank_first = (h % 4 == 0)
                bank_last = (h % 4 == 3)
                nmm_bank = 1 if c == 0 else 3
                t_op(lambda e, ob=ob, col=col, c=c, h=h, bf_=bank_first, bl=bank_last:
                     e.matmul(out=ob[:, col:col + 65], lhsT=sTm[:],
                              rhs=v1t[:, c, h, 0:65],
                              start=bf_, stop=(bl and c == 0)),
                     waits=now("g"))
                if c > 0:
                    for fh in range(2):
                        t_op(lambda e, ob=ob, col=col, h=h, fh=fh, bl=bank_last:
                             e.matmul(out=ob[:, col:col + 65], lhsT=phiqT[:, 2 * h + fh, :],
                                      rhs=ssnap[:, fh, h, :],
                                      start=False, stop=(bl and fh == 1)),
                             waits=(snap_w if (h == 0 and fh == 0) else ()))
                for fh in range(2):
                    t_op(lambda e, h=h, c=c, fh=fh: e.matmul(
                        out=pb[fh][:, 64 * h:64 * h + 64],
                        lhsT=phikb[:, 256 * h + 128 * fh: 256 * h + 128 * (fh + 1)],
                        rhs=v1t[:, c, h, 0:64],
                        start=(h == 0), stop=(h == NH - 1)),
                        waits=(snap_w if (h == 0 and fh == 0) else ()))
                    t_op(lambda e, h=h, c=c, fh=fh: e.matmul(
                        out=pb[2][:, (8 * fh + h):(8 * fh + h) + 1],
                        lhsT=phikb[:, 256 * h + 128 * fh: 256 * h + 128 * (fh + 1)],
                        rhs=onebf[:], start=(h == 0 and fh == 0),
                        stop=(h == NH - 1 and fh == 1)))

            v_op(lambda e: e.tensor_add(out=Sf[0], in0=Sf[0], in1=pb[0][:, 0:512]),
                 waits=now("t"))
            v_op(lambda e: e.tensor_add(out=Sf[1], in0=Sf[1], in1=pb[1][:, 0:512]))
            v_op(lambda e: e.tensor_add(out=zf, in0=zf, in1=pb[2][:, 0:16]), pad=True)
            if dbg:
                dma("sync", dbgSt[:, c, :], bigA[:, 0:1040], waits=now("v"))
                dma("sync", dbgPt[:, c, :], phikb, waits=now("v"))
                dma("sync", dbgKt[:, c, :], kf_ring[c % 2], waits=now("v"))
                dma("sync", dbgKt[:, c, 128:256] if False else dbgQt[:, c, :], qf_ring[c % 2], waits=now("v"))
                v_op(lambda e: e.engine_nop(), waits=now("ld"))
            v_op(lambda e: e.tensor_copy(
                out=zt[:, 0:4],
                in_=pb[5][:, 0:260].rearrange("p (h n) -> p h n", n=65)[:, :, 64:65]),
                waits=now("t"), pad=True)
            v_op(lambda e: e.tensor_copy(
                out=zt[:, 4:8],
                in_=pb[6][:, 0:260].rearrange("p (h n) -> p h n", n=65)[:, :, 64:65]), pad=True)
            v_op(lambda e: e.tensor_scalar_add(out=zt[:], in0=zt[:], scalar1=EPS), pad=True)
            v_op(lambda e: e.reciprocal(out=zrt[:], in_=zt[:]), pad=True)
            for h in range(NH):
                ob = pb[5] if h < 4 else pb[6]
                col = 65 * (h % 4)
                s_op(lambda e, h=h, ob=ob, col=col: e.activation(
                    out=o_outb[:, 64 * h:64 * h + 64], in_=ob[:, col:col + 64], func=AF.Copy,
                    scale=zrt[:, h:h + 1]),
                    waits=(now("v", "ld") if h == 0 else ()))
            dma("sync", ag2in_t[:, c, :], o_outb[:], waits=now("s"))

        op("gpsimd", lambda e: e.collective_compute(
            "AllGather", ALU.bypass, replica_groups=[[0, 1], [2, 3], [4, 5], [6, 7]],
            ins=[ag2_in.ap().opt()], outs=[ag2_out.ap().opt()]),
            waits=now("ld"), inc="cc")
        barrier()

        for t in range(NTT):
            dma("sync", oslot[0][:], ag2out_t[:, t, :], waits=now("v", "cc"))
            dma("sync", oslot[1][:], ag2out_t[:, t + 16, :])
            v_op(lambda e: e.tensor_single_scalar(out=ocat[:, 0:512], in_=oslot[0][:],
                                                  scalar=m0t[:], op=ALU.mult),
                 waits=now("ld", "tp"))
            v_op(lambda e: e.scalar_tensor_tensor(out=ocat[:, 0:512], in0=oslot[1][:],
                                                  scalar=m1t[:], in1=ocat[:, 0:512],
                                                  op0=ALU.mult, op1=ALU.add))
            dma("sync", oslot[0][:], ag2out_t[:, 32 + t, :], waits=now("v"))
            dma("sync", oslot[1][:], ag2out_t[:, 32 + t + 16, :])
            v_op(lambda e: e.tensor_single_scalar(out=ocat[:, 512:1024], in_=oslot[0][:],
                                                  scalar=m0t[:], op=ALU.mult), waits=now("ld"))
            v_op(lambda e: e.scalar_tensor_tensor(out=ocat[:, 512:1024], in0=oslot[1][:],
                                                  scalar=m1t[:], in1=ocat[:, 512:1024],
                                                  op0=ALU.mult, op1=ALU.add))
            dmaT(ocatT, ocat[:], waits=now("v", "t"))
            wt = now("tp", "v")
            for kc in range(8):
                for n in range(2):
                    t_op(lambda e, kc=kc, n=n: e.matmul(
                        out=pb[n][:, 0:512], lhsT=ocatT[:, kc, :],
                        rhs=woT[:, kc, 512 * n:512 * (n + 1)],
                        start=(kc == 0), stop=(kc == 7)),
                        waits=(wt if (kc == 0 and n == 0) else ()))
            dma("sync", xrow[:], x1_t[:, t, :], waits=now("v"))
            for n in range(2):
                v_op(lambda e, n=n: e.tensor_add(out=xrow[:, 512 * n:512 * (n + 1)],
                                                 in0=pb[n][:, 0:512],
                                                 in1=xrow[:, 512 * n:512 * (n + 1)]),
                     waits=(now("t", "ld") if n == 0 else ()))
            dma("sync", x2_t[:, t, :], xrow[:], waits=now("v"))
        barrier()

    if stage == "prep":
        pass
    else:
        ffn(xin_t, x1_t)
    if stage in ("attn", "full"):
        attention()
    if stage == "prep":
        pass
    elif stage == "full":
        dma("sync", wbuf[:, :], w1qt_st.ap(), waits=now("t", "s", "v", "tp", "g"))
        barrier()
        ffn(x2_t, out_t)
    elif stage != "prep":
        src_t = x1_t if stage == "ffn1" else x2_t
        for t in range(NTT):
            dma("sync", xrow[:], src_t[:, t, :], waits=now("v"))
            v_op(lambda e: e.tensor_copy(out=bigA[:, 0:D], in_=xrow[:]), waits=now("ld"))
            dma("sync", out_t[:, t, :], bigA[:, 0:D], waits=now("v"))
    else:
        for t in range(NTT):
            dma("sync", xrow[:], xin_t[:, t, :], waits=now("v"))
            v_op(lambda e: e.tensor_copy(out=bigA[:, 0:D], in_=xrow[:]), waits=now("ld"))
            dma("sync", out_t[:, t, :], bigA[:, 0:D], waits=now("v"))

    if dbg:
        dbgg = dt("dbg_g", [TOK, HID], BF16, kind="ExternalOutput")
        dbgw = dt("dbg_w", [128, 8 * HID], BF16, kind="ExternalOutput")
        dbgs = dt("dbg_s", [128, 64], F32, kind="ExternalOutput")
        dma("sync", dbgg.ap(), g_std.ap(), waits=dep_all())
        dma("sync", dbgw.ap(), w1qt_st.ap())
        v_op(lambda e: e.tensor_copy(out=bigA[:, 0:16], in_=sxs[:]), waits=now("ld"))
        v_op(lambda e: e.tensor_copy(out=bigA[:, 16:32], in_=rr1[:]))
        v_op(lambda e: e.tensor_copy(out=bigA[:, 32:48], in_=rr2[:]))
        v_op(lambda e: e.tensor_copy(out=bigA[:, 48:49], in_=wsc1[:]))
        v_op(lambda e: e.tensor_copy(out=bigA[:, 49:50], in_=wsc2[:]))
        v_op(lambda e: e.tensor_copy(out=bigA[:, 50:51], in_=wthr[:]))
        v_op(lambda e: e.tensor_copy(out=bigA[:, 51:52], in_=wnthr[:]))
        v_op(lambda e: e.tensor_copy(out=bigA[:, 52:53], in_=wsc1[:]))
        v_op(lambda e: e.tensor_copy(out=bigA[:, 0:16], in_=amdbg[:]))
        v_op(lambda e: e.tensor_copy(out=bigA[:, 16:32], in_=esdbg[:]))
        dma("sync", dbgs.ap(), bigA[:, 0:64], waits=now("v"))
        dbgp = dt("dbg_p", [128, 12288], BF16, kind="ExternalOutput")
        dma("sync", tlv(dbgp.ap())[:, 0, :] if False else dbgp.ap()[:, 0:4096], prepP[:, 0:4096], waits=now("v"))
        dma("sync", dbgp.ap()[:, 4096:8192], prepN[:, 0:4096])
        dma("sync", dbgp.ap()[:, 8192:12288], tern_ring[1][:, 0:4096])
    op("sync", lambda e: e.nop(), waits=dep_all())

    with contextlib.ExitStack() as fes:
        for n in CH:
            sems[n] = fes.enter_context(nc.semaphore("sem_" + n))
        block = fes.enter_context(nc.Block())
        for eng in ENGS:
            getattr(block, eng)(lambda e, _f=ops[eng]: [f(e) for f in _f])
    es.close()
    return nc


def shard_inputs(inputs):
    x = np.ascontiguousarray(np.asarray(inputs["L0_attn_x"], dtype=np.float32))
    maps = []
    for core in range(8):
        b, hg = core // 2, core % 2
        maps.append({
            "x_in": np.ascontiguousarray(x[b, hg * 2048:(hg + 1) * 2048, :]),
            "w1": np.ascontiguousarray(np.asarray(inputs["L0_ffn_w1"], np.float32)),
            "w2": np.ascontiguousarray(np.asarray(inputs["L0_ffn_w2"], np.float32)),
            "b1": np.asarray(inputs["L0_ffn_b1"], np.float32).reshape(1, HID),
            "b2": np.asarray(inputs["L0_ffn_b2"], np.float32).reshape(1, D),
            "lnw": np.asarray(inputs["L0_ffn_ln1_w"], np.float32).reshape(1, D),
            "lnb": np.asarray(inputs["L0_ffn_ln1_b"], np.float32).reshape(1, D),
            "ln2w": np.asarray(inputs["L0_ffn_ln2_w"], np.float32).reshape(1, HID),
            "ln2b": np.asarray(inputs["L0_ffn_ln2_b"], np.float32).reshape(1, HID),
            "wq": np.ascontiguousarray(np.asarray(inputs["L0_attn_Wq"], np.float32)[hg * 128:(hg + 1) * 128, :]),
            "wk": np.ascontiguousarray(np.asarray(inputs["L0_attn_Wk"], np.float32)[hg * 128:(hg + 1) * 128, :]),
            "wv": np.ascontiguousarray(np.asarray(inputs["L0_attn_Wv"], np.float32)[hg * 512:(hg + 1) * 512, :]),
            "wo": np.ascontiguousarray(np.asarray(inputs["L0_attn_Wo"], np.float32)),
            "gam": np.asarray(inputs["L0_attn_gamma"], np.float32).reshape(1, FD),
            "bet": np.asarray(inputs["L0_attn_beta"], np.float32).reshape(1, FD),
            "m0": np.full((128, 1), 1.0 if hg == 0 else 0.0, np.float32),
            "m1": np.full((128, 1), 0.0 if hg == 0 else 1.0, np.float32),
        })
    return maps


def assemble(results):
    out = np.empty((4, 4096, 1024), np.float32)
    for core in range(8):
        b, hg = core // 2, core % 2
        out[b, hg * 2048:(hg + 1) * 2048, :] = results[core]["out"]
    return out


def kernel(**inputs):
    nc = build("full")
    maps = shard_inputs(inputs)
    res = run_bass_kernel_spmd(nc, maps, core_ids=list(range(8)))
    return assemble(res.results)


# version 03
# speedup vs baseline: 1.4754x; 1.1385x over previous
import contextlib
import numpy as np
import ml_dtypes
import concourse.bass as bass
import concourse.mybir as mybir
from concourse.bass_utils import run_bass_kernel_spmd

F32 = mybir.dt.float32
BF16 = mybir.dt.bfloat16
I32 = mybir.dt.int32
AF = mybir.ActivationFunctionType
ALU = mybir.AluOpType
AXX = mybir.AxisListType.X
bfloat16 = ml_dtypes.bfloat16

TOK, D, HID = 2048, 1024, 4096
NTT = TOK // 128
NH, FD, HD = 8, 16, 64
EPS = 1e-5
SEQ, NCH = 4096, 32
ENGS = ("sync", "scalar", "vector", "tensor", "gpsimd")
CH = ("ld", "tp", "gd", "v", "s", "t", "g", "cc")


def build(stage="full", dbg=False):
    nc = bass.Bass("TRN2", target_bir_lowering=False, detect_race_conditions=False)
    ops = {e: [] for e in ENGS}
    cnt = {c: 0 for c in CH}
    sems = {}

    def op(eng, make, waits=(), inc=None, amt=1):
        waits = tuple((n, v) for (n, v) in waits if v > 0)
        if inc is not None:
            cnt[inc] += amt
        def fn(e, _m=make, _w=waits, _i=inc, _a=amt):
            for n, v in _w:
                e.wait_ge(sems[n], v)
            ins = _m(e)
            if _i is not None:
                ins.then_inc(sems[_i], _a)
        ops[eng].append(fn)

    def now(*names):
        return [(n, cnt[n]) for n in names]

    def dep_all():
        return now(*CH)

    def barrier():
        w = dep_all()
        for eng in ENGS:
            if eng in ("vector", "gpsimd"):
                op(eng, lambda e: e.engine_nop(), waits=w)
            else:
                op(eng, lambda e: e.nop(), waits=w)

    def dma(eng, out, in_, waits=()):
        chn = {"sync": "ld", "scalar": "tp", "gpsimd": "gd"}[eng]
        op(eng, lambda e, o=out, i=in_: e.dma_start(out=o, in_=i), waits=waits, inc=chn, amt=16)

    def dmaT(out, in_, waits=()):
        op("sync", lambda e, o=out, i=in_: e.dma_start_transpose(out=o, in_=i),
           waits=waits, inc="tp", amt=16)

    def v_op(make, waits=(), pad=False):
        op("vector", make, waits=waits, inc="v")
        if pad:
            ops["vector"].append(lambda e: e.tensor_copy(out=_vpad[0], in_=_vpad[1]))
    def s_op(make, waits=(), pad=False):
        op("scalar", make, waits=waits, inc="s")
        if pad:
            ops["scalar"].append(lambda e: e.activation(out=_spad[0], in_=_spad[1], func=AF.Copy))
    t_op = lambda make, waits=(): op("tensor", make, waits=waits, inc="t")
    g_op = lambda make, waits=(): op("gpsimd", make, waits=waits, inc="g")

    dt = nc.dram_tensor
    x_in = dt("x_in", [TOK, D], F32, kind="ExternalInput")
    w1 = dt("w1", [HID, D], F32, kind="ExternalInput")
    w2 = dt("w2", [D, HID], F32, kind="ExternalInput")
    b1d = dt("b1", [1, HID], F32, kind="ExternalInput")
    b2d = dt("b2", [1, D], F32, kind="ExternalInput")
    lnwd = dt("lnw", [1, D], F32, kind="ExternalInput")
    lnbd = dt("lnb", [1, D], F32, kind="ExternalInput")
    ln2wd = dt("ln2w", [1, HID], F32, kind="ExternalInput")
    ln2bd = dt("ln2b", [1, HID], F32, kind="ExternalInput")
    wqd = dt("wq", [NH * FD, D], F32, kind="ExternalInput")
    wkd = dt("wk", [NH * FD, D], F32, kind="ExternalInput")
    wvd = dt("wv", [NH * HD, D], F32, kind="ExternalInput")
    wod = dt("wo", [D, D], F32, kind="ExternalInput")
    gamd = dt("gam", [1, FD], F32, kind="ExternalInput")
    betd = dt("bet", [1, FD], F32, kind="ExternalInput")
    m0d = dt("m0", [128, 1], F32, kind="ExternalInput")
    m1d = dt("m1", [128, 1], F32, kind="ExternalInput")
    outd = dt("out", [TOK, D], F32, kind="ExternalOutput")

    x1d = dt("x1d", [TOK, D], F32)
    x2d = dt("x2d", [TOK, D], F32)
    g_std = dt("g_st", [TOK, HID], BF16)
    w1qt_st = dt("w1qt_st", [128, 8 * HID], BF16)
    w2qt_st = dt("w2qt_st", [128, 32 * D], BF16)
    ag1_in = dt("ag1_in", [TOK, D], BF16)
    ag1_out = dt("ag1_out", [SEQ, D], BF16)
    ag2_in = dt("ag2_in", [SEQ, NH * HD], BF16)
    ag2_out = dt("ag2_out", [2 * SEQ, NH * HD], BF16)

    tlv = lambda ap: ap.rearrange("(t p) f -> p t f", p=128)
    xin_t, x1_t, x2_t, out_t = tlv(x_in.ap()), tlv(x1d.ap()), tlv(x2d.ap()), tlv(outd.ap())
    gst_t = tlv(g_std.ap())
    ag1in_t, ag1out_t = tlv(ag1_in.ap()), tlv(ag1_out.ap())
    ag2in_t, ag2out_t = tlv(ag2_in.ap()), tlv(ag2_out.ap())
    w1_t = tlv(w1.ap())
    w2_t = tlv(w2.ap())
    wv_t = tlv(wvd.ap())
    wo_t = tlv(wod.ap())

    es = contextlib.ExitStack()
    sb = lambda name, shape, dtp: es.enter_context(nc.sbuf_tensor(name, shape, dtp))

    _vpadt = sb("_vpad", [128, 128], F32)
    _vpad = (_vpadt[:, 0:64], _vpadt[:, 64:128])
    _spadt = sb("_spad", [128, 128], F32)
    _spad = (_spadt[:, 0:64], _spadt[:, 64:128])
    wbuf = sb("wbuf", [128, 32768], BF16)
    w1qT = wbuf.ap().rearrange("p (c n) -> p c n", c=8)
    w2qT = wbuf.ap().rearrange("p (c n) -> p c n", c=32)
    wqT = wbuf[:, 0:1024].rearrange("p (c n) -> p c n", c=8)
    wkT = wbuf[:, 1024:2048].rearrange("p (c n) -> p c n", c=8)
    wvT = wbuf[:, 2048:6144].rearrange("p (c n) -> p c n", c=8)
    woT = wbuf[:, 6144:14336].rearrange("p (c n) -> p c n", c=8)
    v1t = wbuf[:, 14336:14336 + 32 * 520].rearrange("p (t h n) -> p t h n", t=32, h=8)

    NB = 20480
    bbuf = sb("bbuf", [128, NB], BF16)
    g_ring = [bbuf[:, i * HID:(i + 1) * HID] for i in range(2)]
    prepP = bbuf[:, 0:4096]
    prepN = bbuf[:, 4096:8192]
    tern_ring = [bbuf[:, 8192:12288], bbuf[:, 12288:16384]]
    xq2b = bbuf[:, 8192:12288]
    xq2T = bbuf[:, 12288:16384].rearrange("p (c n) -> p c n", c=32)
    xq1b = bbuf[:, 16384:17408]
    xq1T = bbuf[:, 17408:18432].rearrange("p (c n) -> p c n", c=8)
    xlnc_ring = [bbuf[:, i * 1024:(i + 1) * 1024] for i in range(2)]
    xlncT_ring = [bbuf[:, 2048 + i * 1024: 3072 + i * 1024].rearrange("p (c n) -> p c n", c=8)
                  for i in range(2)]
    qraw = bbuf[:, 4096:8192].rearrange("p (t n) -> p t n", t=32)
    kraw = bbuf[:, 8192:12288].rearrange("p (t n) -> p t n", t=32)
    qf_ring = [bbuf[:, 12288 + i * 128: 12416 + i * 128] for i in range(2)]
    kf_ring = [bbuf[:, 12544 + i * 128: 12672 + i * 128] for i in range(2)]
    phiqb = bbuf[:, 12800:14848]
    phikb = bbuf[:, 14848:16896]
    phiqT = bbuf[:, 16896:18944].rearrange("p (j n) -> p j n", j=16)
    ssnap = bbuf[:, 18944:18944 + 1040].rearrange("p (fh h n) -> p fh h n", fh=2, h=8)

    bigA = sb("bigA", [128, HID], F32)
    bigB = sb("bigB", [128, HID], F32)
    bigBbf = bigB.ap().bitcast(BF16)
    quadA = sb("quadA", [128, 512], F32)
    quadI = sb("quadI", [128, 512], I32)
    quadC = sb("quadC", [128, 512], F32)
    xrow_a = sb("xrow", [128, D], F32)
    xrow_b = sb("xrowb", [128, D], F32)
    xrow2 = [xrow_a, xrow_b]
    xrow = xrow_a
    phikT_s = sb("phikT", [128, 2048], BF16)
    phikT = phikT_s.ap().rearrange("p (j n) -> p j n", j=16)
    sTev = sb("sTev", [128, 128], BF16)
    qfp = sb("qfp", [128, 256], BF16)
    kfp = sb("kfp", [128, 256], BF16)
    qfpT_s = sb("qfpT", [128, 256], BF16)
    kfpT_s = sb("kfpT", [128, 256], BF16)
    qfpT = qfpT_s.ap().rearrange("p (b n) -> p b n", b=2)
    kfpT = kfpT_s.ap().rearrange("p (b n) -> p b n", b=2)
    sTm = sb("sTm", [128, 128], BF16)
    o_outb = sb("o_outb", [128, 512], BF16)
    ocat = sb("ocat", [128, 1024], BF16)
    ocatT_s = sb("ocatT", [128, 1024], BF16)
    ocatT = ocatT_s.ap().rearrange("p (c n) -> p c n", c=8)
    oslot = [sb(f"oslot{i}", [128, 512], BF16) for i in range(2)]

    lnw_bc = sb("lnw_bc", [128, D], F32)
    lnb_bc = sb("lnb_bc", [128, D], F32)
    ln2w_bc = sb("ln2w_bc", [128, HID], BF16)
    ln2b_bc = sb("ln2b_bc", [128, HID], BF16)
    gamq_bc = sb("gamq_bc", [128, 128], F32)
    betq_bc = sb("betq_bc", [128, 128], F32)
    gamk_bc = sb("gamk_bc", [128, 128], F32)
    betk_bc = sb("betk_bc", [128, 128], F32)
    identq = sb("identq", [128, 128], F32)
    onesq = sb("onesq", [128, 128], F32)

    rows1 = sb("rows1", [97, 512], BF16)
    rows2 = sb("rows2", [97, 512], BF16)
    rows3 = sb("rows3", [97, 512], BF16)
    rows4 = sb("rows4", [97, 512], BF16)
    rowst = sb("rowst", [97, 512], F32)

    def rowslice(t, k, n=512):
        return t[32 * k: 32 * k + 1, 0:n]

    st = lambda name, n=16: sb(name, [128, n], F32)
    sxs, sxq = st("sxs"), st("sxq")
    rr1, nmr1 = st("rr1"), st("nmr1")
    gs8 = sb("gs8", [128, 128], F32)
    gq8 = sb("gq8", [128, 128], F32)
    rr2, nmr2 = st("rr2"), st("nmr2")
    tA, tB, tC, tD = st("tA"), st("tB"), st("tC"), st("tD")
    am1, es1v, ar1 = st("am1", 1), st("es1", 1), st("ar1", 1)
    amP = [st("amP0", 1), st("amP1", 1)]
    esP = [st("esP0", 1), st("esP1", 1)]
    arP = [st("arP0", 1), st("arP1", 1)]
    scP = [st("scP0", 1), st("scP1", 1)]
    am2, es2v, ar2 = st("am2", 1), st("es2", 1), st("ar2", 1)
    wsum = st("wsum", 32)
    wthr, wnthr = st("wthr", 1), st("wnthr", 1)
    wsc1, wsc2 = st("wsc1", 1), st("wsc2", 1)
    onecol = st("onecol", 1)
    arow_s = sb("arow", [97, 256], BF16)
    arow = lambda k, par=0: arow_s[32 * k:32 * k + 1, 128 * par:128 * par + 128]
    onesrow_s = sb("onesrow", [97, 128], F32)
    onesrow = lambda k: onesrow_s[32 * k:32 * k + 1, :]
    scal = sb("scal", [1, 16], F32)
    m0t, m1t = st("m0t", 1), st("m1t", 1)
    asx, asq = st("asx"), st("asq")
    arr_, anmr = st("arr"), st("anmr")
    qsum = bigA[:, 2048:2304]
    qsq = bigA[:, 2304:2560]
    ksum = bigA[:, 2560:2816]
    ksq = bigA[:, 2816:3072]
    qmu = bigA[:, 3072:3328]
    qrs = bigA[:, 3328:3584]
    kmu = bigA[:, 3584:3840]
    krs = bigA[:, 3840:4096]
    fmw = sb("fmw", [128, 128], F32)
    fmw2 = sb("fmw2", [128, 128], F32)
    fmw3 = sb("fmw3", [128, 128], F32)
    zt, zrt = st("zt", 8), st("zrt", 8)
    onebf = sb("onebf", [128, 1], BF16)

    pb = [es.enter_context(nc.psum_tensor(f"pb{i}", [128, 512], F32)) for i in range(8)]

    v_op(lambda e: e.memset(_vpadt[:], 0.0))
    s_op(lambda e: e.activation(out=_spadt[:], in_=_vpadt[:], func=AF.Copy), waits=now("v"))
    v_op(lambda e: e.memset(onecol[:], 1.0))
    v_op(lambda e: e.memset(onesq[:], 1.0))
    v_op(lambda e: e.memset(onebf[:], 1.0))
    v_op(lambda e: e.memset(onesrow_s[:], 1.0))
    dma("sync", m0t[:], m0d.ap())
    dma("sync", m1t[:], m1d.ap())
    g_op(lambda e: e.affine_select(out=identq[:], in_=onesq[:], pattern=[[1, 128]],
                                   compare_op=ALU.is_equal, fill=0.0, base=0,
                                   channel_multiplier=-1), waits=now("v"))

    def bc_load(dst_bc, src_ap, n):
        for ci in range(0, n, 512):
            w = min(512, n - ci)
            sl = (ci // 512) % 3
            dma("sync", rowslice(rowst, sl, w), src_ap[0:1, ci:ci + w], waits=now("t"))
            t_op(lambda e, w=w, sl=sl: e.matmul(out=pb[7][:, 0:w], lhsT=onesrow(sl),
                                                rhs=rowslice(rowst, sl, w),
                                                start=True, stop=True),
                 waits=now("ld", "v"))
            v_op(lambda e, ci=ci, w=w, d=dst_bc: e.tensor_copy(out=d[:, ci:ci + w],
                                                               in_=pb[7][:, 0:w]),
                 waits=now("t"))

    bc_load(lnw_bc, lnwd.ap(), D)
    bc_load(lnb_bc, lnbd.ap(), D)
    bc_load(ln2w_bc, ln2wd.ap(), HID)
    bc_load(ln2b_bc, ln2bd.ap(), HID)

    dma("sync", rowslice(rowst, 0, FD), gamd.ap(), waits=now("t"))
    dma("sync", rowslice(rowst, 1, FD), betd.ap(), waits=now("t"))
    t_op(lambda e: e.matmul(out=pb[7][:, 0:128], lhsT=onesrow(0),
                            rhs=rowst[0:1, 0:FD].unsqueeze(1).broadcast_to([1, 8, FD]),
                            start=True, stop=True), waits=now("ld", "v"))
    v_op(lambda e: e.tensor_copy(out=gamk_bc[:], in_=pb[7][:, 0:128]), waits=now("t"))
    t_op(lambda e: e.matmul(out=pb[7][:, 0:128], lhsT=onesrow(1),
                            rhs=rowst[32:33, 0:FD].unsqueeze(1).broadcast_to([1, 8, FD]),
                            start=True, stop=True), waits=now("v"))
    v_op(lambda e: e.tensor_copy(out=betk_bc[:], in_=pb[7][:, 0:128]), waits=now("t"))
    v_op(lambda e: e.tensor_scalar_mul(out=gamq_bc[:], in0=gamk_bc[:], scalar1=0.25))
    v_op(lambda e: e.tensor_scalar_mul(out=betq_bc[:], in0=betk_bc[:], scalar1=0.25))

    _b1tensors = [rows1, rows1, rows1, rows2, rows2, rows2, rows3, rows3]
    for n in range(8):
        dma("gpsimd", rowslice(_b1tensors[n], n % 3), b1d.ap()[0:1, 512 * n:512 * (n + 1)])
    dma("gpsimd", rowslice(rows3, 2), b2d.ap()[0:1, 0:512])
    dma("gpsimd", rowslice(rows4, 0), b2d.ap()[0:1, 512:1024])
    b1chunk = lambda n: rowslice(_b1tensors[n], n % 3)
    b2chunk = lambda n: rowslice(rows3, 2) if n == 0 else rowslice(rows4, 0)
    b2base = lambda n: 2 if n == 0 else 0
    barrier()

    def prep_w(w_tiles, ntiles, freedim, wsc_dst, qT_dst_fn, totelem):
        for i in range(ntiles):
            buf = bigA if i % 2 == 0 else bigB
            dma("sync", buf[:, 0:freedim], w_tiles[:, i, :], waits=now("s"))
            s_op(lambda e, b=buf, i=i, f=freedim: e.activation(
                out=prepP.bitcast(BF16)[:, 0:f] if False else prepP[:, 0:f],
                in_=b[:, 0:f], func=AF.Abs, accum_out=wsum[:, i:i + 1]),
                waits=now("ld"))
        v_op(lambda e, n=ntiles: e.reduce_sum(out=wthr[:], in_=wsum[:, 0:n], axis=AXX),
             waits=now("s"), pad=True)
        t_op(lambda e: e.matmul(out=pb[7][0:1, 0:1], lhsT=wthr[:], rhs=onecol[:],
                                start=True, stop=True), waits=now("v"))
        v_op(lambda e: e.tensor_copy(out=scal[0:1, 0:1], in_=pb[7][0:1, 0:1]), waits=now("t"), pad=True)
        t_op(lambda e: e.matmul(out=pb[7][:, 0:1], lhsT=onesrow(0), rhs=scal[0:1, 0:1],
                                start=True, stop=True), waits=now("v"))
        v_op(lambda e, n=totelem, d=wsc_dst: e.tensor_scalar_mul(out=d[:], in0=pb[7][:, 0:1],
                                                                 scalar1=1.0 / n), waits=now("t"), pad=True)
        v_op(lambda e, d=wsc_dst: e.tensor_scalar_mul(out=wthr[:], in0=d[:], scalar1=0.5), pad=True)
        v_op(lambda e: e.tensor_scalar_mul(out=wnthr[:], in0=wthr[:], scalar1=-1.0), pad=True)
        for i in range(ntiles):
            buf = bigA if i % 2 == 0 else bigB
            tern = tern_ring[i % 2]
            dma("sync", buf[:, 0:freedim], w_tiles[:, i, :], waits=now("v", "tp"))
            v_op(lambda e, b=buf, f=freedim: e.tensor_single_scalar(
                out=prepP[:, 0:f], in_=b[:, 0:f], scalar=wthr[:], op=ALU.is_gt),
                waits=now("ld"))
            v_op(lambda e, b=buf, f=freedim: e.tensor_single_scalar(
                out=prepN[:, 0:f], in_=b[:, 0:f], scalar=wnthr[:], op=ALU.is_lt))
            v_op(lambda e, t_=tern, f=freedim: e.tensor_sub(out=t_[:, 0:f], in0=prepP[:, 0:f],
                                                            in1=prepN[:, 0:f]))
            dmaT(qT_dst_fn(i), tern[:, 0:freedim], waits=now("v"))

    prep_w(w2_t, 8, HID, wsc2, lambda i: w2qT[:, :, 128 * i:128 * (i + 1)], D * HID)
    dma("sync", w2qt_st.ap(), wbuf[:, :], waits=now("tp"))
    barrier()
    prep_w(w1_t, 32, D, wsc1, lambda i: w1qT[:, :, 128 * i:128 * (i + 1)], D * HID)
    dma("sync", w1qt_st.ap(), wbuf[:, :], waits=now("tp"))
    barrier()

    def ln_chain(sum_t, sq_t, n, rr_dst, nmr_dst, double_ln):
        v_op(lambda e: e.tensor_scalar_mul(out=tA[:], in0=sum_t[:], scalar1=1.0 / n), waits=now("s"), pad=True)
        v_op(lambda e: e.tensor_scalar_mul(out=tB[:], in0=sq_t[:], scalar1=1.0 / n), pad=True)
        v_op(lambda e: e.tensor_mul(out=tC[:], in0=tA[:], in1=tA[:]), pad=True)
        v_op(lambda e: e.tensor_sub(out=tB[:], in0=tB[:], in1=tC[:]), pad=True)
        if double_ln:
            v_op(lambda e: e.tensor_scalar_add(out=tC[:], in0=tB[:], scalar1=EPS), pad=True)
            v_op(lambda e: e.reciprocal(out=tD[:], in_=tC[:]), pad=True)
            v_op(lambda e: e.tensor_mul(out=tB[:], in0=tB[:], in1=tD[:]), pad=True)
            v_op(lambda e: e.tensor_scalar_add(out=tB[:], in0=tB[:], scalar1=EPS), pad=True)
            v_op(lambda e: e.tensor_mul(out=tB[:], in0=tB[:], in1=tC[:]), pad=True)
            v_op(lambda e: e.reciprocal(out=tB[:], in_=tB[:]), pad=True)
        else:
            v_op(lambda e: e.tensor_scalar_add(out=tB[:], in0=tB[:], scalar1=EPS), pad=True)
            v_op(lambda e: e.reciprocal(out=tB[:], in_=tB[:]), pad=True)
        s_op(lambda e, r=rr_dst: e.activation(out=r[:], in_=tB[:], func=AF.Sqrt), waits=now("v"), pad=True)
        v_op(lambda e, r=rr_dst, m=nmr_dst: e.tensor_mul(out=m[:], in0=tA[:], in1=r[:]),
             waits=now("s"), pad=True)
        v_op(lambda e, m=nmr_dst: e.tensor_scalar_mul(out=m[:], in0=m[:], scalar1=-1.0), pad=True)

    def quant_to(dst_bf, src_f32, width, s_scale_ap, waits=()):
        for q in range(width // 512):
            sl = slice(512 * q, 512 * (q + 1))
            v_op(lambda e, sl=sl: e.tensor_scalar(out=quadA[:], in0=src_f32[:, sl],
                                                  scalar1=s_scale_ap, scalar2=128.0,
                                                  op0=ALU.mult, op1=ALU.add),
                 waits=(waits if q == 0 else ()))
            v_op(lambda e: e.tensor_copy(out=quadI[:], in_=quadA[:]))
            v_op(lambda e: e.scalar_tensor_tensor(out=quadC[:], in0=quadA[:], scalar=0.5,
                                                  in1=quadI[:], op0=ALU.subtract, op1=ALU.is_ge))
            v_op(lambda e, sl=sl: e.scalar_tensor_tensor(out=dst_bf[:, sl], in0=quadI[:],
                                                         scalar=-128.0, in1=quadC[:],
                                                         op0=ALU.add, op1=ALU.add))

    dbgS = dt("dbg_S", [32 * 128, 1040], F32, kind="ExternalOutput") if dbg else None
    dbgP = dt("dbg_P", [32 * 128, 2048], BF16, kind="ExternalOutput") if dbg else None
    dbgPt = tlv(dbgP.ap()) if dbg else None
    dbgK = dt("dbg_K", [32 * 128, 128], BF16, kind="ExternalOutput") if dbg else None
    dbgKt = tlv(dbgK.ap()) if dbg else None
    dbgQ = dt("dbg_Q", [32 * 128, 128], BF16, kind="ExternalOutput") if dbg else None
    dbgQt = tlv(dbgQ.ap()) if dbg else None
    dbgSt = tlv(dbgS.ap()) if dbg else None
    dbgx = dt("dbg_x", [TOK, D], BF16, kind="ExternalOutput") if dbg else None
    dbgxt = tlv(dbgx.ap()) if dbg else None
    amdbg = sb("amdbg", [128, 16], F32)
    esdbg = sb("esdbg", [128, 16], F32)

    SCR = bigA[:, 2048:2560]
    fa_t1 = [bigB[:, 0:D], bigB[:, D:2 * D]]
    fa_h1 = [bigA[:, 0:D], bigA[:, D:2 * D]]
    fb_h = [bigA, bigB]
    xq1b2 = [bbuf[:, 16384:17408], bbuf[:, 18432:19456]]
    xq1T2 = [bbuf[:, 17408:18432].rearrange("p (c n) -> p c n", c=8),
             bbuf[:, 19456:20480].rearrange("p (c n) -> p c n", c=8)]
    xq2b2 = [bbuf[:, 8192:12288], bbuf[:, 12288:16384]]
    xq2T2 = [bbuf[:, 0:4096].rearrange("p (c n) -> p c n", c=32),
             bbuf[:, 4096:8192].rearrange("p (c n) -> p c n", c=32)]
    wsc127_1, wsc127_2 = st("w127a", 1), st("w127b", 1)
    caps = {}
    _cid = [0]

    def ffn(xin_tiles, xout_tiles):
        cid = _cid[0]
        _cid[0] += 1

        def cap(tag, t, *chs):
            caps[(cid, tag, t)] = now(*chs)

        def get(tag, t, fb=None):
            k = (cid, tag, t)
            if k in caps:
                return caps[k]
            return dep_all() if fb is None else fb

        for t in range(NTT):
            par = t % 2
            dma("sync", xrow2[par][:], xin_tiles[:, t, :], waits=get("sts", t - 2))
            v_op(lambda e, t=t, par=par: e.reduce_sum(out=sxs[:, t:t + 1], in_=xrow2[par][:],
                                                      axis=AXX), waits=now("ld"))
            s_op(lambda e, t=t, par=par: e.activation(out=fa_h1[par], in_=xrow2[par][:],
                                                      func=AF.Square, accum_out=sxq[:, t:t + 1]),
                 waits=now("ld"))
            cap("sts", t, "v", "s")
        ln_chain(sxs, sxq, D, rr1, nmr1, double_ln=True)
        v_op(lambda e: e.tensor_scalar_mul(out=wsc127_1[:], in0=wsc1[:], scalar1=1.0 / 127.0), pad=True)
        v_op(lambda e: e.tensor_scalar_mul(out=wsc127_2[:], in0=wsc2[:], scalar1=0.5 / 127.0), pad=True)

        def prepA(t):
            par = t % 2
            dma("sync", xrow2[par][:], xin_tiles[:, t, :],
                waits=get("At1", t - 2, None if t < 2 else []))
            s_op(lambda e, t=t, par=par: e.activation(out=fa_t1[par], in_=xrow2[par][:],
                                                      func=AF.Identity, bias=nmr1[:, t:t + 1],
                                                      scale=rr1[:, t:t + 1]),
                 waits=now("ld") + get("Apv", t - 2, [] if t >= 2 else None))
            cap("At1", t, "s")
            v_op(lambda e, par=par: e.tensor_mul(out=fa_h1[par], in0=fa_t1[par], in1=lnw_bc[:]),
                 waits=now("s"))
            v_op(lambda e, par=par: e.tensor_add(out=fa_h1[par], in0=fa_h1[par], in1=lnb_bc[:]))
            v_op(lambda e, par=par: e.tensor_reduce(out=amP[par][:], in_=fa_h1[par], axis=AXX,
                                                    op=ALU.max, apply_absolute_value=True), pad=True)
            v_op(lambda e, par=par: e.tensor_scalar_max(out=amP[par][:], in0=amP[par][:],
                                                        scalar1=EPS), pad=True)
            v_op(lambda e, par=par: e.tensor_mul(out=esP[par][:], in0=amP[par][:], in1=wsc127_1[:]),
                 waits=get("AevAll", t - 2, []), pad=True)
            v_op(lambda e, par=par: e.reciprocal(out=arP[par][:], in_=esP[par][:]),
                 waits=get("AatT", t - 2, []), pad=True)
            v_op(lambda e, par=par: e.tensor_mul(out=scP[par][:], in0=arP[par][:], in1=wsc1[:]), pad=True)
            quant_to(xq1b2[par], fa_h1[par], D, scP[par][:], waits=get("Atp", t - 2, []))
            dmaT(xq1T2[par], xq1b2[par], waits=now("v") + get("Amm", t - 2, []))
            cap("Atp", t, "tp")
            cap("Apv", t, "v")

        def mmA(t):
            par = t % 2
            t_op(lambda e, par=par: e.transpose(out=pb[7][0:1, 0:128], in_=arP[par][:],
                                                identity=identq[:]),
                 waits=get("Apv", t) + get("Agel7", t - 1, []))
            cap("AatT", t, "t")
            for k in range(3):
                v_op(lambda e, k=k, par=par: e.tensor_copy(out=arow(k, par), in_=pb[7][0:1, 0:128]),
                     waits=(now("t") if k == 0 else ()))
            cap("Aar", t, "v")
            for n in range(8):
                bank = pb[7] if n == 7 else pb[n]
                if n == 0:
                    w0 = get("Atp", t) + get("Agel0", t - 1, [])
                elif n == 7:
                    w0 = get("Aar", t) + get("Agel7", t - 1, [])
                else:
                    w0 = get("Agel%d" % n, t - 1, [])
                for kc in range(8):
                    t_op(lambda e, kc=kc, n=n, bank=bank, par=par: e.matmul(
                        out=bank[:, 0:512], lhsT=xq1T2[par][:, kc, :],
                        rhs=w1qT[:, kc, 512 * n:512 * (n + 1)],
                        start=(kc == 0), stop=False),
                        waits=(w0 if kc == 0 else ()))
                t_op(lambda e, n=n, bank=bank, par=par: e.matmul(
                    out=bank[:, 0:512], lhsT=arow(n % 3, par), rhs=b1chunk(n),
                    start=False, stop=True),
                    waits=(get("Aar", t) if n == 0 else ()))
                wt_n = now("t")
                s_op(lambda e, n=n, bank=bank, par=par, t=t: e.activation(
                    out=g_ring[par][:, 512 * n:512 * (n + 1)], in_=bank[:, 0:512], func=AF.Gelu,
                    scale=esP[par][:], accum_out=gs8[:, 8 * t + n: 8 * t + n + 1]),
                    waits=wt_n + (get("Agst", t - 2, []) if n == 0 else []))
                s_op(lambda e, n=n, par=par, t=t: e.activation(
                    out=SCR, in_=g_ring[par][:, 512 * n:512 * (n + 1)],
                    func=AF.Square, accum_out=gq8[:, 8 * t + n: 8 * t + n + 1]))
                cap("Agel%d" % n, t, "s")
            cap("AevAll", t, "s")
            cap("Amm", t, "t")
            dma("sync", gst_t[:, t, :], g_ring[par][:], waits=now("s"))
            cap("Agst", t, "ld")

        prepA(0)
        prepA(1)
        for t in range(NTT):
            mmA(t)
            if t + 2 < NTT:
                prepA(t + 2)

        v_op(lambda e: e.tensor_reduce(out=sxs[:], in_=gs8.ap().rearrange("p (t n) -> p t n", n=8),
                                       axis=AXX, op=ALU.add), waits=now("s"))
        v_op(lambda e: e.tensor_reduce(out=sxq[:], in_=gq8.ap().rearrange("p (t n) -> p t n", n=8),
                                       axis=AXX, op=ALU.add))
        ln_chain(sxs, sxq, HID, rr2, nmr2, double_ln=False)
        dma("sync", wbuf[:, :], w2qt_st.ap(), waits=now("t"))

        def prepB(t):
            par = t % 2
            dma("sync", g_ring[par][:], gst_t[:, t, :], waits=get("Bmm", t - 2))
            dma("sync", xrow2[par][:], xin_tiles[:, t, :], waits=get("BevC1", t - 2))
            cap("Bxld", t, "ld")
            s_op(lambda e, t=t, par=par: e.activation(out=fb_h[par][:], in_=g_ring[par][:],
                                                      func=AF.Identity, bias=nmr2[:, t:t + 1],
                                                      scale=rr2[:, t:t + 1]),
                 waits=now("ld") + get("Bpv", t - 2, [] if t >= 2 else None))
            cap("Bt2", t, "s")
            v_op(lambda e, par=par: e.tensor_mul(out=fb_h[par][:], in0=fb_h[par][:], in1=ln2w_bc[:]),
                 waits=now("s"))
            v_op(lambda e, par=par: e.tensor_add(out=fb_h[par][:], in0=fb_h[par][:], in1=ln2b_bc[:]))
            v_op(lambda e, par=par: e.tensor_reduce(out=amP[par][:], in_=fb_h[par][:], axis=AXX,
                                                    op=ALU.max, apply_absolute_value=True), pad=True)
            v_op(lambda e, par=par: e.tensor_scalar_max(out=amP[par][:], in0=amP[par][:],
                                                        scalar1=EPS), pad=True)
            v_op(lambda e, par=par: e.tensor_mul(out=esP[par][:], in0=amP[par][:], in1=wsc127_2[:]), pad=True)
            v_op(lambda e, par=par: e.reciprocal(out=scP[par][:], in_=esP[par][:]), pad=True)
            v_op(lambda e, par=par: e.tensor_scalar_mul(out=arP[par][:], in0=scP[par][:],
                                                        scalar1=0.5),
                 waits=get("BatT", t - 2, []), pad=True)
            v_op(lambda e, par=par: e.tensor_mul(out=scP[par][:], in0=arP[par][:], in1=wsc2[:]), pad=True)
            quant_to(xq2b2[par], fb_h[par], HID, scP[par][:], waits=get("Btp", t - 2, []))
            dmaT(xq2T2[par], xq2b2[par], waits=now("v", "s") + get("Bmm", t - 2, []))
            cap("Btp", t, "tp")
            cap("Bpv", t, "v")

        def mmB(t):
            par = t % 2
            t_op(lambda e, par=par: e.transpose(out=pb[7][0:1, 0:128], in_=arP[par][:],
                                                identity=identq[:]),
                 waits=get("Bpv", t) + get("Bar", t - 1, []))
            cap("BatT", t, "t")
            for j, k in enumerate((0, 2)):
                v_op(lambda e, k=k, par=par: e.tensor_copy(out=arow(k, par), in_=pb[7][0:1, 0:128]),
                     waits=(now("t") if j == 0 else ()))
            cap("Bar", t, "v")
            for n in range(2):
                w0 = (get("Btp", t) if n == 0 else []) + get("BevC%d" % n, t - 1, [])
                for kc in range(32):
                    t_op(lambda e, kc=kc, n=n, par=par: e.matmul(
                        out=pb[n][:, 0:512], lhsT=xq2T2[par][:, kc, :],
                        rhs=w2qT[:, kc, 512 * n:512 * (n + 1)],
                        start=(kc == 0), stop=False),
                        waits=(w0 if kc == 0 else ()))
                t_op(lambda e, n=n, par=par: e.matmul(out=pb[n][:, 0:512],
                                                      lhsT=arow(b2base(n), par), rhs=b2chunk(n),
                                                      start=False, stop=True),
                     waits=(get("Bar", t) if n == 0 else ()))
                wt_n = now("t")
                v_op(lambda e, n=n, par=par: e.scalar_tensor_tensor(
                    out=xrow2[par][:, 512 * n:512 * (n + 1)], in0=pb[n][:, 0:512],
                    scalar=esP[par][:], in1=xrow2[par][:, 512 * n:512 * (n + 1)],
                    op0=ALU.mult, op1=ALU.add),
                    waits=wt_n + (get("Bxld", t) if n == 0 else []))
                cap("BevC%d" % n, t, "v")
            cap("Bmm", t, "t")
            dma("sync", xout_tiles[:, t, :], xrow2[par][:], waits=now("v"))

        prepB(0)
        prepB(1)
        for t in range(NTT):
            mmB(t)
            if t + 2 < NTT:
                prepB(t + 2)
        barrier()

    def attention():
        stg = bbuf[:, 0:1024]
        dma("gpsimd", stg, wqd.ap(), waits=now("tp", "t", "s", "v"))
        dmaT(wqT, stg, waits=now("gd"))
        dma("gpsimd", stg, wkd.ap(), waits=now("tp"))
        dmaT(wkT, stg, waits=now("gd"))
        for j in range(4):
            dma("gpsimd", stg, wv_t[:, j, :], waits=now("tp"))
            dmaT(wvT[:, :, 128 * j:128 * (j + 1)], stg, waits=now("gd"))
        for j in range(8):
            dma("gpsimd", stg, wo_t[:, j, :], waits=now("tp"))
            dmaT(woT[:, :, 128 * j:128 * (j + 1)], stg, waits=now("gd"))
        barrier()
        v_op(lambda e: e.memset(v1t[:, :, :, 64:65], 1.0))

        for t in range(NTT):
            dma("sync", xrow[:], x1_t[:, t, :], waits=now("v", "s"))
            v_op(lambda e, t=t: e.reduce_sum(out=asx[:, t:t + 1], in_=xrow[:], axis=AXX),
                 waits=now("ld"))
            s_op(lambda e, t=t: e.activation(out=bigA[:, 0:D], in_=xrow[:], func=AF.Square,
                                             accum_out=asq[:, t:t + 1]), waits=now("ld", "v"))
        ln_chain(asx, asq, D, arr_, anmr, double_ln=False)
        for t in range(NTT):
            dma("sync", xrow[:], x1_t[:, t, :], waits=now("v", "s"))
            s_op(lambda e, t=t: e.activation(out=bigBbf[:, 0:D], in_=xrow[:], func=AF.Identity,
                                             bias=anmr[:, t:t + 1], scale=arr_[:, t:t + 1]),
                 waits=now("ld"))
            dma("sync", ag1in_t[:, t, :], bigBbf[:, 0:D], waits=now("s"))
        op("gpsimd", lambda e: e.collective_compute(
            "AllGather", ALU.bypass, replica_groups=[[0, 1], [2, 3], [4, 5], [6, 7]],
            ins=[ag1_in.ap().opt()], outs=[ag1_out.ap().opt()]),
            waits=now("ld"), inc="cc")
        barrier()

        for c in range(NCH):
            r = c % 2
            dma("sync", xlnc_ring[r], ag1out_t[:, c, :], waits=now("cc", "tp", "t"))
            dmaT(xlncT_ring[r], xlnc_ring[r], waits=now("ld"))
            wt = now("tp", "s", "v")
            for kc in range(8):
                t_op(lambda e, kc=kc, r=r: e.matmul(out=pb[0][:, 0:128],
                                                    lhsT=xlncT_ring[r][:, kc, :],
                                                    rhs=wqT[:, kc, :], start=(kc == 0),
                                                    stop=(kc == 7)),
                     waits=(wt if kc == 0 else ()))
            for kc in range(8):
                t_op(lambda e, kc=kc, r=r: e.matmul(out=pb[1][:, 0:128],
                                                    lhsT=xlncT_ring[r][:, kc, :],
                                                    rhs=wkT[:, kc, :], start=(kc == 0),
                                                    stop=(kc == 7)))
            for kc in range(8):
                t_op(lambda e, kc=kc, r=r: e.matmul(out=pb[2][:, 0:512],
                                                    lhsT=xlncT_ring[r][:, kc, :],
                                                    rhs=wvT[:, kc, :], start=(kc == 0),
                                                    stop=(kc == 7)))
            s_op(lambda e: e.activation(out=fmw[:], in_=pb[0][:, 0:128], func=AF.Copy),
                 waits=now("t", "v"))
            s_op(lambda e: e.activation(out=fmw2[:], in_=pb[1][:, 0:128], func=AF.Copy))
            s_op(lambda e, c=c: e.activation(
                out=v1t[:, c, :, 0:64], in_=pb[2][:, 0:512].rearrange("p (h n) -> p h n", h=8),
                func=AF.Copy))
            v_op(lambda e, c=c: e.tensor_copy(out=qraw[:, c, :], in_=fmw[:]), waits=now("s"))
            v_op(lambda e, c=c: e.tensor_copy(out=kraw[:, c, :], in_=fmw2[:]))
            v_op(lambda e, c=c: e.reduce_sum(out=qsum[:, 8 * c:8 * c + 8],
                                             in_=fmw.ap().rearrange("p (h f) -> p h f", f=16),
                                             axis=AXX))
            v_op(lambda e, c=c: e.reduce_sum(out=ksum[:, 8 * c:8 * c + 8],
                                             in_=fmw2.ap().rearrange("p (h f) -> p h f", f=16),
                                             axis=AXX))
            s_op(lambda e: e.activation(out=fmw3[:], in_=fmw[:], func=AF.Square), waits=now("v"))
            v_op(lambda e, c=c: e.reduce_sum(out=qsq[:, 8 * c:8 * c + 8],
                                             in_=fmw3.ap().rearrange("p (h f) -> p h f", f=16),
                                             axis=AXX), waits=now("s"))
            s_op(lambda e: e.activation(out=fmw3[:], in_=fmw2[:], func=AF.Square), waits=now("v"))
            v_op(lambda e, c=c: e.reduce_sum(out=ksq[:, 8 * c:8 * c + 8],
                                             in_=fmw3.ap().rearrange("p (h f) -> p h f", f=16),
                                             axis=AXX), waits=now("s"))

        def fm_stats(sumt, sqt_, mut, rst):
            v_op(lambda e: e.tensor_scalar_mul(out=mut[:], in0=sumt[:], scalar1=1.0 / FD),
                 waits=now("s"), pad=True)
            v_op(lambda e: e.tensor_scalar_mul(out=rst[:], in0=sqt_[:], scalar1=1.0 / FD), pad=True)
            v_op(lambda e: e.tensor_mul(out=sqt_[:], in0=mut[:], in1=mut[:]))
            v_op(lambda e: e.tensor_sub(out=rst[:], in0=rst[:], in1=sqt_[:]), pad=True)
            v_op(lambda e: e.tensor_scalar_add(out=rst[:], in0=rst[:], scalar1=EPS), pad=True)
            v_op(lambda e: e.reciprocal(out=rst[:], in_=rst[:]), pad=True)
            s_op(lambda e: e.activation(out=rst[:], in_=rst[:], func=AF.Sqrt), waits=now("v"), pad=True)
        fm_stats(qsum, qsq, qmu, qrs)
        fm_stats(ksum, ksq, kmu, krs)
        barrier()

        Sf = [bigA[:, 0:512], bigA[:, 512:1024]]
        zf = bigA[:, 1024:1040]
        v_op(lambda e: e.memset(Sf[0], 0.0))
        v_op(lambda e: e.memset(Sf[1], 0.0))
        v_op(lambda e: e.memset(zf, 0.0), pad=True)
        for c in range(NCH):
            rq, rk = qf_ring[c % 2], kf_ring[c % 2]

            def fm_apply(raw, mu, rs, gbc, bbc, dstr):
                mu_b = mu[:, 8 * c:8 * c + 8].unsqueeze(2).broadcast_to([128, 8, 16])
                rs_b = rs[:, 8 * c:8 * c + 8].unsqueeze(2).broadcast_to([128, 8, 16])
                src3 = raw[:, c, :].rearrange("p (h f) -> p h f", f=16)
                v_op(lambda e, s3=src3, mb=mu_b: e.tensor_sub(
                    out=fmw.ap().rearrange("p (h f) -> p h f", f=16),
                    in0=s3, in1=mb), waits=now("t", "tp"))
                v_op(lambda e: e.tensor_mul(out=fmw.ap().rearrange("p (h f) -> p h f", f=16),
                                            in0=fmw.ap().rearrange("p (h f) -> p h f", f=16),
                                            in1=rs_b))
                v_op(lambda e: e.tensor_mul(out=fmw[:], in0=fmw[:], in1=gbc[:]))
                v_op(lambda e, d=dstr: e.tensor_add(out=d, in0=fmw[:], in1=bbc[:]))
            fm_apply(qraw, qmu, qrs, gamq_bc, betq_bc, rq)
            fm_apply(kraw, kmu, krs, gamk_bc, betk_bc, rk)

            def phi_build(src, dst):
                i0 = src.rearrange("p (h f) -> p h f", f=16).unsqueeze(3).broadcast_to([128, 8, 16, 16])
                i1 = src.rearrange("p (h f) -> p h f", f=16).unsqueeze(2).broadcast_to([128, 8, 16, 16])
                v_op(lambda e, d=dst: e.tensor_tensor(
                    out=d.rearrange("p (h a b) -> p h a b", a=16, b=16), in0=i0, in1=i1,
                    op=ALU.mult))
            phi_build(rq, phiqb)
            phi_build(rk, phikb)
            v_op(lambda e, rq=rq: e.tensor_copy(
                out=qfp.ap().rearrange("p (h n) -> p h n", h=8)[:, :, 0:16],
                in_=rq.rearrange("p (h f) -> p h f", f=16)))
            v_op(lambda e, rk=rk: e.tensor_copy(
                out=kfp.ap().rearrange("p (h n) -> p h n", h=8)[:, :, 0:16],
                in_=rk.rearrange("p (h f) -> p h f", f=16)))
            dmaT(phiqT, phiqb, waits=now("v", "t"))
            dmaT(phikT, phikb, waits=now("v", "t"))
            dmaT(qfpT, qfp[:], waits=now("v", "t"))
            dmaT(kfpT, kfp[:], waits=now("v", "t"))

            if c > 0:
                s_op(lambda e: e.activation(
                    out=ssnap[:, 0, :, 0:64], in_=Sf[0].rearrange("p (h n) -> p h n", h=8),
                    func=AF.Copy), waits=now("v"))
                s_op(lambda e: e.activation(
                    out=ssnap[:, 1, :, 0:64], in_=Sf[1].rearrange("p (h n) -> p h n", h=8),
                    func=AF.Copy))
                v_op(lambda e: e.tensor_copy(
                    out=ssnap[:, :, :, 64:65],
                    in_=zf.rearrange("p (fh h) -> p fh h", fh=2).unsqueeze(3)))
            snap_w = now("s", "v")

            for h in range(NH):
                sbk = pb[3] if h % 2 == 0 else pb[4]
                blk = h // 4
                base = 32 * (h % 4)
                tpos = (96, 0) if base == 96 else None
                t_op(lambda e, h=h, sbk=sbk, blk=blk, base=base, tpos=tpos: e.matmul(
                    out=sbk[:, 0:128],
                    lhsT=kfpT[base:base + 16, blk, :],
                    rhs=qfpT[base:base + 16, blk, :],
                    start=True, stop=True, tile_position=tpos),
                     waits=now("tp", "s", "v", "g"))
                s_op(lambda e, sbk=sbk: e.activation(out=sTev[:], in_=sbk[:, 0:128], func=AF.Square),
                     waits=now("t"))
                g_op(lambda e: e.affine_select(out=sTm[:], in_=sTev[:], pattern=[[1, 128]],
                                               compare_op=ALU.is_ge, fill=0.0, base=0,
                                               channel_multiplier=-1), waits=now("s"))
                ob = pb[5] if h < 4 else pb[6]
                col = 65 * (h % 4)
                bank_first = (h % 4 == 0)
                bank_last = (h % 4 == 3)
                nmm_bank = 1 if c == 0 else 3
                t_op(lambda e, ob=ob, col=col, c=c, h=h, bf_=bank_first, bl=bank_last:
                     e.matmul(out=ob[:, col:col + 65], lhsT=sTm[:],
                              rhs=v1t[:, c, h, 0:65],
                              start=bf_, stop=(bl and c == 0)),
                     waits=now("g"))
                if c > 0:
                    for fh in range(2):
                        t_op(lambda e, ob=ob, col=col, h=h, fh=fh, bl=bank_last:
                             e.matmul(out=ob[:, col:col + 65], lhsT=phiqT[:, 2 * h + fh, :],
                                      rhs=ssnap[:, fh, h, :],
                                      start=False, stop=(bl and fh == 1)),
                             waits=(snap_w if (h == 0 and fh == 0) else ()))
                for fh in range(2):
                    t_op(lambda e, h=h, c=c, fh=fh: e.matmul(
                        out=pb[fh][:, 64 * h:64 * h + 64],
                        lhsT=phikb[:, 256 * h + 128 * fh: 256 * h + 128 * (fh + 1)],
                        rhs=v1t[:, c, h, 0:64],
                        start=(h == 0), stop=(h == NH - 1)),
                        waits=(snap_w if (h == 0 and fh == 0) else ()))
                    t_op(lambda e, h=h, c=c, fh=fh: e.matmul(
                        out=pb[2][:, (8 * fh + h):(8 * fh + h) + 1],
                        lhsT=phikb[:, 256 * h + 128 * fh: 256 * h + 128 * (fh + 1)],
                        rhs=onebf[:], start=(h == 0 and fh == 0),
                        stop=(h == NH - 1 and fh == 1)))

            v_op(lambda e: e.tensor_add(out=Sf[0], in0=Sf[0], in1=pb[0][:, 0:512]),
                 waits=now("t"))
            v_op(lambda e: e.tensor_add(out=Sf[1], in0=Sf[1], in1=pb[1][:, 0:512]))
            v_op(lambda e: e.tensor_add(out=zf, in0=zf, in1=pb[2][:, 0:16]), pad=True)
            if dbg:
                dma("sync", dbgSt[:, c, :], bigA[:, 0:1040], waits=now("v"))
                dma("sync", dbgPt[:, c, :], phikb, waits=now("v"))
                dma("sync", dbgKt[:, c, :], kf_ring[c % 2], waits=now("v"))
                dma("sync", dbgKt[:, c, 128:256] if False else dbgQt[:, c, :], qf_ring[c % 2], waits=now("v"))
                v_op(lambda e: e.engine_nop(), waits=now("ld"))
            v_op(lambda e: e.tensor_copy(
                out=zt[:, 0:4],
                in_=pb[5][:, 0:260].rearrange("p (h n) -> p h n", n=65)[:, :, 64:65]),
                waits=now("t"), pad=True)
            v_op(lambda e: e.tensor_copy(
                out=zt[:, 4:8],
                in_=pb[6][:, 0:260].rearrange("p (h n) -> p h n", n=65)[:, :, 64:65]), pad=True)
            v_op(lambda e: e.tensor_scalar_add(out=zt[:], in0=zt[:], scalar1=EPS), pad=True)
            v_op(lambda e: e.reciprocal(out=zrt[:], in_=zt[:]), pad=True)
            for h in range(NH):
                ob = pb[5] if h < 4 else pb[6]
                col = 65 * (h % 4)
                s_op(lambda e, h=h, ob=ob, col=col: e.activation(
                    out=o_outb[:, 64 * h:64 * h + 64], in_=ob[:, col:col + 64], func=AF.Copy,
                    scale=zrt[:, h:h + 1]),
                    waits=(now("v", "ld") if h == 0 else ()))
            dma("sync", ag2in_t[:, c, :], o_outb[:], waits=now("s"))

        op("gpsimd", lambda e: e.collective_compute(
            "AllGather", ALU.bypass, replica_groups=[[0, 1], [2, 3], [4, 5], [6, 7]],
            ins=[ag2_in.ap().opt()], outs=[ag2_out.ap().opt()]),
            waits=now("ld"), inc="cc")
        barrier()

        for t in range(NTT):
            dma("sync", oslot[0][:], ag2out_t[:, t, :], waits=now("v", "cc"))
            dma("sync", oslot[1][:], ag2out_t[:, t + 16, :])
            v_op(lambda e: e.tensor_single_scalar(out=ocat[:, 0:512], in_=oslot[0][:],
                                                  scalar=m0t[:], op=ALU.mult),
                 waits=now("ld", "tp"))
            v_op(lambda e: e.scalar_tensor_tensor(out=ocat[:, 0:512], in0=oslot[1][:],
                                                  scalar=m1t[:], in1=ocat[:, 0:512],
                                                  op0=ALU.mult, op1=ALU.add))
            dma("sync", oslot[0][:], ag2out_t[:, 32 + t, :], waits=now("v"))
            dma("sync", oslot[1][:], ag2out_t[:, 32 + t + 16, :])
            v_op(lambda e: e.tensor_single_scalar(out=ocat[:, 512:1024], in_=oslot[0][:],
                                                  scalar=m0t[:], op=ALU.mult), waits=now("ld"))
            v_op(lambda e: e.scalar_tensor_tensor(out=ocat[:, 512:1024], in0=oslot[1][:],
                                                  scalar=m1t[:], in1=ocat[:, 512:1024],
                                                  op0=ALU.mult, op1=ALU.add))
            dmaT(ocatT, ocat[:], waits=now("v", "t"))
            wt = now("tp", "v")
            for kc in range(8):
                for n in range(2):
                    t_op(lambda e, kc=kc, n=n: e.matmul(
                        out=pb[n][:, 0:512], lhsT=ocatT[:, kc, :],
                        rhs=woT[:, kc, 512 * n:512 * (n + 1)],
                        start=(kc == 0), stop=(kc == 7)),
                        waits=(wt if (kc == 0 and n == 0) else ()))
            dma("sync", xrow[:], x1_t[:, t, :], waits=now("v"))
            for n in range(2):
                v_op(lambda e, n=n: e.tensor_add(out=xrow[:, 512 * n:512 * (n + 1)],
                                                 in0=pb[n][:, 0:512],
                                                 in1=xrow[:, 512 * n:512 * (n + 1)]),
                     waits=(now("t", "ld") if n == 0 else ()))
            dma("sync", x2_t[:, t, :], xrow[:], waits=now("v"))
        barrier()

    if stage == "prep":
        pass
    else:
        ffn(xin_t, x1_t)
    if stage in ("attn", "full"):
        attention()
    if stage == "prep":
        pass
    elif stage == "full":
        dma("sync", wbuf[:, :], w1qt_st.ap(), waits=now("t", "s", "v", "tp", "g"))
        barrier()
        ffn(x2_t, out_t)
    elif stage != "prep":
        src_t = x1_t if stage == "ffn1" else x2_t
        for t in range(NTT):
            dma("sync", xrow[:], src_t[:, t, :], waits=now("v"))
            v_op(lambda e: e.tensor_copy(out=bigA[:, 0:D], in_=xrow[:]), waits=now("ld"))
            dma("sync", out_t[:, t, :], bigA[:, 0:D], waits=now("v"))
    else:
        for t in range(NTT):
            dma("sync", xrow[:], xin_t[:, t, :], waits=now("v"))
            v_op(lambda e: e.tensor_copy(out=bigA[:, 0:D], in_=xrow[:]), waits=now("ld"))
            dma("sync", out_t[:, t, :], bigA[:, 0:D], waits=now("v"))

    if dbg:
        dbgg = dt("dbg_g", [TOK, HID], BF16, kind="ExternalOutput")
        dbgw = dt("dbg_w", [128, 8 * HID], BF16, kind="ExternalOutput")
        dbgs = dt("dbg_s", [128, 64], F32, kind="ExternalOutput")
        dma("sync", dbgg.ap(), g_std.ap(), waits=dep_all())
        dma("sync", dbgw.ap(), w1qt_st.ap())
        v_op(lambda e: e.tensor_copy(out=bigA[:, 0:16], in_=sxs[:]), waits=now("ld"))
        v_op(lambda e: e.tensor_copy(out=bigA[:, 16:32], in_=rr1[:]))
        v_op(lambda e: e.tensor_copy(out=bigA[:, 32:48], in_=rr2[:]))
        v_op(lambda e: e.tensor_copy(out=bigA[:, 48:49], in_=wsc1[:]))
        v_op(lambda e: e.tensor_copy(out=bigA[:, 49:50], in_=wsc2[:]))
        v_op(lambda e: e.tensor_copy(out=bigA[:, 50:51], in_=wthr[:]))
        v_op(lambda e: e.tensor_copy(out=bigA[:, 51:52], in_=wnthr[:]))
        v_op(lambda e: e.tensor_copy(out=bigA[:, 52:53], in_=wsc1[:]))
        v_op(lambda e: e.tensor_copy(out=bigA[:, 0:16], in_=amdbg[:]))
        v_op(lambda e: e.tensor_copy(out=bigA[:, 16:32], in_=esdbg[:]))
        dma("sync", dbgs.ap(), bigA[:, 0:64], waits=now("v"))
        dbgp = dt("dbg_p", [128, 12288], BF16, kind="ExternalOutput")
        dma("sync", tlv(dbgp.ap())[:, 0, :] if False else dbgp.ap()[:, 0:4096], prepP[:, 0:4096], waits=now("v"))
        dma("sync", dbgp.ap()[:, 4096:8192], prepN[:, 0:4096])
        dma("sync", dbgp.ap()[:, 8192:12288], tern_ring[1][:, 0:4096])
    op("sync", lambda e: e.nop(), waits=dep_all())

    with contextlib.ExitStack() as fes:
        for n in CH:
            sems[n] = fes.enter_context(nc.semaphore("sem_" + n))
        block = fes.enter_context(nc.Block())
        for eng in ENGS:
            getattr(block, eng)(lambda e, _f=ops[eng]: [f(e) for f in _f])
    es.close()
    return nc


def shard_inputs(inputs):
    x = np.ascontiguousarray(np.asarray(inputs["L0_attn_x"], dtype=np.float32))
    maps = []
    for core in range(8):
        b, hg = core // 2, core % 2
        maps.append({
            "x_in": np.ascontiguousarray(x[b, hg * 2048:(hg + 1) * 2048, :]),
            "w1": np.ascontiguousarray(np.asarray(inputs["L0_ffn_w1"], np.float32)),
            "w2": np.ascontiguousarray(np.asarray(inputs["L0_ffn_w2"], np.float32)),
            "b1": np.asarray(inputs["L0_ffn_b1"], np.float32).reshape(1, HID),
            "b2": np.asarray(inputs["L0_ffn_b2"], np.float32).reshape(1, D),
            "lnw": np.asarray(inputs["L0_ffn_ln1_w"], np.float32).reshape(1, D),
            "lnb": np.asarray(inputs["L0_ffn_ln1_b"], np.float32).reshape(1, D),
            "ln2w": np.asarray(inputs["L0_ffn_ln2_w"], np.float32).reshape(1, HID),
            "ln2b": np.asarray(inputs["L0_ffn_ln2_b"], np.float32).reshape(1, HID),
            "wq": np.ascontiguousarray(np.asarray(inputs["L0_attn_Wq"], np.float32)[hg * 128:(hg + 1) * 128, :]),
            "wk": np.ascontiguousarray(np.asarray(inputs["L0_attn_Wk"], np.float32)[hg * 128:(hg + 1) * 128, :]),
            "wv": np.ascontiguousarray(np.asarray(inputs["L0_attn_Wv"], np.float32)[hg * 512:(hg + 1) * 512, :]),
            "wo": np.ascontiguousarray(np.asarray(inputs["L0_attn_Wo"], np.float32)),
            "gam": np.asarray(inputs["L0_attn_gamma"], np.float32).reshape(1, FD),
            "bet": np.asarray(inputs["L0_attn_beta"], np.float32).reshape(1, FD),
            "m0": np.full((128, 1), 1.0 if hg == 0 else 0.0, np.float32),
            "m1": np.full((128, 1), 0.0 if hg == 0 else 1.0, np.float32),
        })
    return maps


def assemble(results):
    out = np.empty((4, 4096, 1024), np.float32)
    for core in range(8):
        b, hg = core // 2, core % 2
        out[b, hg * 2048:(hg + 1) * 2048, :] = results[core]["out"]
    return out


def kernel(**inputs):
    nc = build("full")
    maps = shard_inputs(inputs)
    res = run_bass_kernel_spmd(nc, maps, core_ids=list(range(8)))
    return assemble(res.results)


# version 05
# speedup vs baseline: 1.7928x; 1.2151x over previous
import contextlib
import numpy as np
import ml_dtypes
import concourse.bass as bass
import concourse.mybir as mybir
from concourse.bass_utils import run_bass_kernel_spmd

F32 = mybir.dt.float32
BF16 = mybir.dt.bfloat16
I32 = mybir.dt.int32
AF = mybir.ActivationFunctionType
ALU = mybir.AluOpType
AXX = mybir.AxisListType.X
bfloat16 = ml_dtypes.bfloat16

TOK, D, HID = 2048, 1024, 4096
NTT = TOK // 128
NH, FD, HD = 8, 16, 64
EPS = 1e-5
SEQ, NCH = 4096, 32
ENGS = ("sync", "scalar", "vector", "tensor", "gpsimd")
CH = ("ld", "tp", "gd", "v", "s", "t", "g", "cc")


def build(stage="full", dbg=False):
    nc = bass.Bass("TRN2", target_bir_lowering=False, detect_race_conditions=False)
    ops = {e: [] for e in ENGS}
    cnt = {c: 0 for c in CH}
    sems = {}

    def op(eng, make, waits=(), inc=None, amt=1):
        waits = tuple((n, v) for (n, v) in waits if v > 0)
        if inc is not None:
            cnt[inc] += amt
        def fn(e, _m=make, _w=waits, _i=inc, _a=amt):
            for n, v in _w:
                e.wait_ge(sems[n], v)
            ins = _m(e)
            if _i is not None:
                ins.then_inc(sems[_i], _a)
        ops[eng].append(fn)

    def now(*names):
        return [(n, cnt[n]) for n in names]

    def dep_all():
        return now(*CH)

    def barrier():
        w = dep_all()
        for eng in ENGS:
            if eng in ("vector", "gpsimd"):
                op(eng, lambda e: e.engine_nop(), waits=w)
            else:
                op(eng, lambda e: e.nop(), waits=w)

    def dma(eng, out, in_, waits=()):
        chn = {"sync": "ld", "scalar": "tp", "gpsimd": "gd"}[eng]
        op(eng, lambda e, o=out, i=in_: e.dma_start(out=o, in_=i), waits=waits, inc=chn, amt=16)

    def dmaT(out, in_, waits=()):
        op("sync", lambda e, o=out, i=in_: e.dma_start_transpose(out=o, in_=i),
           waits=waits, inc="tp", amt=16)

    def v_op(make, waits=(), pad=False):
        op("vector", make, waits=waits, inc="v")
        if pad:
            ops["vector"].append(lambda e: e.tensor_copy(out=_vpad[0], in_=_vpad[1]))
    def s_op(make, waits=(), pad=False):
        op("scalar", make, waits=waits, inc="s")
        if pad:
            ops["scalar"].append(lambda e: e.activation(out=_spad[0], in_=_spad[1], func=AF.Copy))
    t_op = lambda make, waits=(): op("tensor", make, waits=waits, inc="t")
    g_op = lambda make, waits=(): op("gpsimd", make, waits=waits, inc="g")

    dt = nc.dram_tensor
    x_in = dt("x_in", [TOK, D], F32, kind="ExternalInput")
    w1 = dt("w1", [HID, D], F32, kind="ExternalInput")
    w2 = dt("w2", [D, HID], F32, kind="ExternalInput")
    b1d = dt("b1", [1, HID], F32, kind="ExternalInput")
    b2d = dt("b2", [1, D], F32, kind="ExternalInput")
    lnwd = dt("lnw", [1, D], F32, kind="ExternalInput")
    lnbd = dt("lnb", [1, D], F32, kind="ExternalInput")
    ln2wd = dt("ln2w", [1, HID], F32, kind="ExternalInput")
    ln2bd = dt("ln2b", [1, HID], F32, kind="ExternalInput")
    wqd = dt("wq", [NH * FD, D], F32, kind="ExternalInput")
    wkd = dt("wk", [NH * FD, D], F32, kind="ExternalInput")
    wvd = dt("wv", [NH * HD, D], F32, kind="ExternalInput")
    wod = dt("wo", [D, D], F32, kind="ExternalInput")
    gamd = dt("gam", [1, FD], F32, kind="ExternalInput")
    betd = dt("bet", [1, FD], F32, kind="ExternalInput")
    m0d = dt("m0", [128, 1], F32, kind="ExternalInput")
    m1d = dt("m1", [128, 1], F32, kind="ExternalInput")
    outd = dt("out", [TOK, D], F32, kind="ExternalOutput")

    x1d = dt("x1d", [TOK, D], F32)
    x2d = dt("x2d", [TOK, D], F32)
    g_std = dt("g_st", [TOK, HID], BF16)
    w1qt_st = dt("w1qt_st", [128, 8 * HID], BF16)
    w2qt_st = dt("w2qt_st", [128, 32 * D], BF16)
    ag1_in = dt("ag1_in", [TOK, D], BF16)
    ag1_out = dt("ag1_out", [SEQ, D], BF16)
    ag2_in = dt("ag2_in", [SEQ, NH * HD], BF16)
    ag2_out = dt("ag2_out", [2 * SEQ, NH * HD], BF16)

    tlv = lambda ap: ap.rearrange("(t p) f -> p t f", p=128)
    xin_t, x1_t, x2_t, out_t = tlv(x_in.ap()), tlv(x1d.ap()), tlv(x2d.ap()), tlv(outd.ap())
    gst_t = tlv(g_std.ap())
    ag1in_t, ag1out_t = tlv(ag1_in.ap()), tlv(ag1_out.ap())
    ag2in_t, ag2out_t = tlv(ag2_in.ap()), tlv(ag2_out.ap())
    w1_t = tlv(w1.ap())
    w2_t = tlv(w2.ap())
    wv_t = tlv(wvd.ap())
    wo_t = tlv(wod.ap())

    es = contextlib.ExitStack()
    sb = lambda name, shape, dtp: es.enter_context(nc.sbuf_tensor(name, shape, dtp))

    _vpadt = sb("_vpad", [128, 128], F32)
    _vpad = (_vpadt[:, 0:64], _vpadt[:, 64:128])
    _spadt = sb("_spad", [128, 128], F32)
    _spad = (_spadt[:, 0:64], _spadt[:, 64:128])
    wbuf = sb("wbuf", [128, 32768], BF16)
    w1qT = wbuf.ap().rearrange("p (c n) -> p c n", c=8)
    w2qT = wbuf.ap().rearrange("p (c n) -> p c n", c=32)
    wqT = wbuf[:, 0:1024].rearrange("p (c n) -> p c n", c=8)
    wkT = wbuf[:, 1024:2048].rearrange("p (c n) -> p c n", c=8)
    wvT = wbuf[:, 2048:6144].rearrange("p (c n) -> p c n", c=8)
    woT = wbuf[:, 6144:14336].rearrange("p (c n) -> p c n", c=8)
    v1t = wbuf[:, 14336:14336 + 32 * 520].rearrange("p (t h n) -> p t h n", t=32, h=8)

    NB = 20480
    bbuf = sb("bbuf", [128, NB], BF16)
    g_ring = [bbuf[:, i * HID:(i + 1) * HID] for i in range(2)]
    prepP = bbuf[:, 0:4096]
    prepN = bbuf[:, 4096:8192]
    tern_ring = [bbuf[:, 8192:12288], bbuf[:, 12288:16384]]
    xq2b = bbuf[:, 8192:12288]
    xq2T = bbuf[:, 12288:16384].rearrange("p (c n) -> p c n", c=32)
    xq1b = bbuf[:, 16384:17408]
    xq1T = bbuf[:, 17408:18432].rearrange("p (c n) -> p c n", c=8)
    xlnc_ring = [bbuf[:, i * 1024:(i + 1) * 1024] for i in range(2)]
    xlncT_ring = [bbuf[:, 2048 + i * 1024: 3072 + i * 1024].rearrange("p (c n) -> p c n", c=8)
                  for i in range(2)]
    qraw = bbuf[:, 4096:8192].rearrange("p (t n) -> p t n", t=32)
    kraw = bbuf[:, 8192:12288].rearrange("p (t n) -> p t n", t=32)
    qf_ring = [bbuf[:, 12288 + i * 128: 12416 + i * 128] for i in range(2)]
    kf_ring = [bbuf[:, 12544 + i * 128: 12672 + i * 128] for i in range(2)]
    phiqb = bbuf[:, 12800:14848]
    phikb = bbuf[:, 14848:16896]
    phiqT = bbuf[:, 16896:18944].rearrange("p (j n) -> p j n", j=16)
    ssnap = bbuf[:, 18944:18944 + 1040].rearrange("p (fh h n) -> p fh h n", fh=2, h=8)

    bigA = sb("bigA", [128, HID], F32)
    bigB = sb("bigB", [128, HID], F32)
    bigBbf = bigB.ap().bitcast(BF16)
    quadA = sb("quadA", [128, 512], F32)
    quadI = sb("quadI", [128, 512], I32)
    quadC = sb("quadC", [128, 512], F32)
    xrow_a = sb("xrow", [128, D], F32)
    xrow_b = sb("xrowb", [128, D], F32)
    xrow2 = [xrow_a, xrow_b]
    xrow = xrow_a
    phikT_s = sb("phikT", [128, 2048], BF16)
    phikT = phikT_s.ap().rearrange("p (j n) -> p j n", j=16)
    sTev = sb("sTev", [128, 128], BF16)
    qfp = sb("qfp", [128, 256], BF16)
    kfp = sb("kfp", [128, 256], BF16)
    qfpT_s = sb("qfpT", [128, 256], BF16)
    kfpT_s = sb("kfpT", [128, 256], BF16)
    qfpT = qfpT_s.ap().rearrange("p (b n) -> p b n", b=2)
    kfpT = kfpT_s.ap().rearrange("p (b n) -> p b n", b=2)
    sTm = sb("sTm", [128, 128], BF16)
    o_outb = sb("o_outb", [128, 512], BF16)
    ocat = sb("ocat", [128, 1024], BF16)
    ocatT_s = sb("ocatT", [128, 1024], BF16)
    ocatT = ocatT_s.ap().rearrange("p (c n) -> p c n", c=8)
    oslot = [sb(f"oslot{i}", [128, 512], BF16) for i in range(2)]

    lnw_bc = sb("lnw_bc", [128, D], F32)
    lnb_bc = sb("lnb_bc", [128, D], F32)
    ln2w_bc = sb("ln2w_bc", [128, HID], BF16)
    ln2b_bc = sb("ln2b_bc", [128, HID], BF16)
    gamq_bc = sb("gamq_bc", [128, 128], F32)
    betq_bc = sb("betq_bc", [128, 128], F32)
    gamk_bc = sb("gamk_bc", [128, 128], F32)
    betk_bc = sb("betk_bc", [128, 128], F32)
    identq = sb("identq", [128, 128], F32)
    onesq = sb("onesq", [128, 128], F32)

    rows1 = sb("rows1", [97, 512], BF16)
    rows2 = sb("rows2", [97, 512], BF16)
    rows3 = sb("rows3", [97, 512], BF16)
    rows4 = sb("rows4", [97, 512], BF16)
    rowst = sb("rowst", [97, 512], F32)

    def rowslice(t, k, n=512):
        return t[32 * k: 32 * k + 1, 0:n]

    st = lambda name, n=16: sb(name, [128, n], F32)
    sxs, sxq = st("sxs"), st("sxq")
    rr1, nmr1 = st("rr1"), st("nmr1")
    gs8 = sb("gs8", [128, 128], F32)
    gq8 = sb("gq8", [128, 128], F32)
    rr2, nmr2 = st("rr2"), st("nmr2")
    tA, tB, tC, tD = st("tA"), st("tB"), st("tC"), st("tD")
    am1, es1v, ar1 = st("am1", 1), st("es1", 1), st("ar1", 1)
    amP = [st("amP0", 1), st("amP1", 1)]
    esP = [st("esP0", 1), st("esP1", 1)]
    arP = [st("arP0", 1), st("arP1", 1)]
    scP = [st("scP0", 1), st("scP1", 1)]
    am2, es2v, ar2 = st("am2", 1), st("es2", 1), st("ar2", 1)
    wsum = st("wsum", 32)
    wthr, wnthr = st("wthr", 1), st("wnthr", 1)
    wsc1, wsc2 = st("wsc1", 1), st("wsc2", 1)
    onecol = st("onecol", 1)
    arow_s = sb("arow", [97, 256], BF16)
    arow = lambda k, par=0: arow_s[32 * k:32 * k + 1, 128 * par:128 * par + 128]
    onesrow_s = sb("onesrow", [97, 128], F32)
    onesrow = lambda k: onesrow_s[32 * k:32 * k + 1, :]
    scal = sb("scal", [1, 16], F32)
    m0t, m1t = st("m0t", 1), st("m1t", 1)
    asx, asq = st("asx"), st("asq")
    arr_, anmr = st("arr"), st("anmr")
    qsum = bigA[:, 2048:2304]
    qsq = bigA[:, 2304:2560]
    ksum = bigA[:, 2560:2816]
    ksq = bigA[:, 2816:3072]
    qmu = bigA[:, 3072:3328]
    qrs = bigA[:, 3328:3584]
    kmu = bigA[:, 3584:3840]
    krs = bigA[:, 3840:4096]
    fmw = sb("fmw", [128, 128], F32)
    fmw2 = sb("fmw2", [128, 128], F32)
    fmw3 = sb("fmw3", [128, 128], F32)
    zt, zrt = st("zt", 8), st("zrt", 8)
    onebf = sb("onebf", [128, 1], BF16)

    pb = [es.enter_context(nc.psum_tensor(f"pb{i}", [128, 512], F32)) for i in range(8)]

    v_op(lambda e: e.memset(_vpadt[:], 0.0))
    s_op(lambda e: e.activation(out=_spadt[:], in_=_vpadt[:], func=AF.Copy), waits=now("v"))
    v_op(lambda e: e.memset(onecol[:], 1.0))
    v_op(lambda e: e.memset(onesq[:], 1.0))
    v_op(lambda e: e.memset(onebf[:], 1.0))
    v_op(lambda e: e.memset(onesrow_s[:], 1.0))
    dma("sync", m0t[:], m0d.ap())
    dma("sync", m1t[:], m1d.ap())
    g_op(lambda e: e.affine_select(out=identq[:], in_=onesq[:], pattern=[[1, 128]],
                                   compare_op=ALU.is_equal, fill=0.0, base=0,
                                   channel_multiplier=-1), waits=now("v"))

    def bc_load(dst_bc, src_ap, n):
        for ci in range(0, n, 512):
            w = min(512, n - ci)
            sl = (ci // 512) % 3
            dma("sync", rowslice(rowst, sl, w), src_ap[0:1, ci:ci + w], waits=now("t"))
            t_op(lambda e, w=w, sl=sl: e.matmul(out=pb[7][:, 0:w], lhsT=onesrow(sl),
                                                rhs=rowslice(rowst, sl, w),
                                                start=True, stop=True),
                 waits=now("ld", "v"))
            v_op(lambda e, ci=ci, w=w, d=dst_bc: e.tensor_copy(out=d[:, ci:ci + w],
                                                               in_=pb[7][:, 0:w]),
                 waits=now("t"))

    bc_load(lnw_bc, lnwd.ap(), D)
    bc_load(lnb_bc, lnbd.ap(), D)
    bc_load(ln2w_bc, ln2wd.ap(), HID)
    bc_load(ln2b_bc, ln2bd.ap(), HID)

    dma("sync", rowslice(rowst, 0, FD), gamd.ap(), waits=now("t"))
    dma("sync", rowslice(rowst, 1, FD), betd.ap(), waits=now("t"))
    t_op(lambda e: e.matmul(out=pb[7][:, 0:128], lhsT=onesrow(0),
                            rhs=rowst[0:1, 0:FD].unsqueeze(1).broadcast_to([1, 8, FD]),
                            start=True, stop=True), waits=now("ld", "v"))
    v_op(lambda e: e.tensor_copy(out=gamk_bc[:], in_=pb[7][:, 0:128]), waits=now("t"))
    t_op(lambda e: e.matmul(out=pb[7][:, 0:128], lhsT=onesrow(1),
                            rhs=rowst[32:33, 0:FD].unsqueeze(1).broadcast_to([1, 8, FD]),
                            start=True, stop=True), waits=now("v"))
    v_op(lambda e: e.tensor_copy(out=betk_bc[:], in_=pb[7][:, 0:128]), waits=now("t"))
    v_op(lambda e: e.tensor_scalar_mul(out=gamq_bc[:], in0=gamk_bc[:], scalar1=0.25))
    v_op(lambda e: e.tensor_scalar_mul(out=betq_bc[:], in0=betk_bc[:], scalar1=0.25))

    _b1tensors = [rows1, rows1, rows1, rows2, rows2, rows2, rows3, rows3]
    for n in range(8):
        dma("gpsimd", rowslice(_b1tensors[n], n % 3), b1d.ap()[0:1, 512 * n:512 * (n + 1)])
    dma("gpsimd", rowslice(rows3, 2), b2d.ap()[0:1, 0:512])
    dma("gpsimd", rowslice(rows4, 0), b2d.ap()[0:1, 512:1024])
    b1chunk = lambda n: rowslice(_b1tensors[n], n % 3)
    b2chunk = lambda n: rowslice(rows3, 2) if n == 0 else rowslice(rows4, 0)
    b2base = lambda n: 2 if n == 0 else 0
    barrier()

    def prep_w(w_tiles, ntiles, freedim, wsc_dst, qT_dst_fn, totelem):
        for i in range(ntiles):
            buf = bigA if i % 2 == 0 else bigB
            dma("sync", buf[:, 0:freedim], w_tiles[:, i, :], waits=now("s"))
            s_op(lambda e, b=buf, i=i, f=freedim: e.activation(
                out=prepP.bitcast(BF16)[:, 0:f] if False else prepP[:, 0:f],
                in_=b[:, 0:f], func=AF.Abs, accum_out=wsum[:, i:i + 1]),
                waits=now("ld"))
        v_op(lambda e, n=ntiles: e.reduce_sum(out=wthr[:], in_=wsum[:, 0:n], axis=AXX),
             waits=now("s"), pad=True)
        t_op(lambda e: e.matmul(out=pb[7][0:1, 0:1], lhsT=wthr[:], rhs=onecol[:],
                                start=True, stop=True), waits=now("v"))
        v_op(lambda e: e.tensor_copy(out=scal[0:1, 0:1], in_=pb[7][0:1, 0:1]), waits=now("t"), pad=True)
        t_op(lambda e: e.matmul(out=pb[7][:, 0:1], lhsT=onesrow(0), rhs=scal[0:1, 0:1],
                                start=True, stop=True), waits=now("v"))
        v_op(lambda e, n=totelem, d=wsc_dst: e.tensor_scalar_mul(out=d[:], in0=pb[7][:, 0:1],
                                                                 scalar1=1.0 / n), waits=now("t"), pad=True)
        v_op(lambda e, d=wsc_dst: e.tensor_scalar_mul(out=wthr[:], in0=d[:], scalar1=0.5), pad=True)
        v_op(lambda e: e.tensor_scalar_mul(out=wnthr[:], in0=wthr[:], scalar1=-1.0), pad=True)
        for i in range(ntiles):
            buf = bigA if i % 2 == 0 else bigB
            tern = tern_ring[i % 2]
            dma("sync", buf[:, 0:freedim], w_tiles[:, i, :], waits=now("v", "tp"))
            v_op(lambda e, b=buf, f=freedim: e.tensor_single_scalar(
                out=prepP[:, 0:f], in_=b[:, 0:f], scalar=wthr[:], op=ALU.is_gt),
                waits=now("ld"))
            v_op(lambda e, b=buf, f=freedim: e.tensor_single_scalar(
                out=prepN[:, 0:f], in_=b[:, 0:f], scalar=wnthr[:], op=ALU.is_lt))
            v_op(lambda e, t_=tern, f=freedim: e.tensor_sub(out=t_[:, 0:f], in0=prepP[:, 0:f],
                                                            in1=prepN[:, 0:f]))
            dmaT(qT_dst_fn(i), tern[:, 0:freedim], waits=now("v"))

    prep_w(w2_t, 8, HID, wsc2, lambda i: w2qT[:, :, 128 * i:128 * (i + 1)], D * HID)
    dma("sync", w2qt_st.ap(), wbuf[:, :], waits=now("tp"))
    barrier()
    prep_w(w1_t, 32, D, wsc1, lambda i: w1qT[:, :, 128 * i:128 * (i + 1)], D * HID)
    dma("sync", w1qt_st.ap(), wbuf[:, :], waits=now("tp"))
    barrier()

    def ln_chain(sum_t, sq_t, n, rr_dst, nmr_dst, double_ln):
        v_op(lambda e: e.tensor_scalar_mul(out=tA[:], in0=sum_t[:], scalar1=1.0 / n), waits=now("s"), pad=True)
        v_op(lambda e: e.tensor_scalar_mul(out=tB[:], in0=sq_t[:], scalar1=1.0 / n), pad=True)
        v_op(lambda e: e.tensor_mul(out=tC[:], in0=tA[:], in1=tA[:]), pad=True)
        v_op(lambda e: e.tensor_sub(out=tB[:], in0=tB[:], in1=tC[:]), pad=True)
        if double_ln:
            v_op(lambda e: e.tensor_scalar_add(out=tC[:], in0=tB[:], scalar1=EPS), pad=True)
            v_op(lambda e: e.reciprocal(out=tD[:], in_=tC[:]), pad=True)
            v_op(lambda e: e.tensor_mul(out=tB[:], in0=tB[:], in1=tD[:]), pad=True)
            v_op(lambda e: e.tensor_scalar_add(out=tB[:], in0=tB[:], scalar1=EPS), pad=True)
            v_op(lambda e: e.tensor_mul(out=tB[:], in0=tB[:], in1=tC[:]), pad=True)
            v_op(lambda e: e.reciprocal(out=tB[:], in_=tB[:]), pad=True)
        else:
            v_op(lambda e: e.tensor_scalar_add(out=tB[:], in0=tB[:], scalar1=EPS), pad=True)
            v_op(lambda e: e.reciprocal(out=tB[:], in_=tB[:]), pad=True)
        s_op(lambda e, r=rr_dst: e.activation(out=r[:], in_=tB[:], func=AF.Sqrt), waits=now("v"), pad=True)
        v_op(lambda e, r=rr_dst, m=nmr_dst: e.tensor_mul(out=m[:], in0=tA[:], in1=r[:]),
             waits=now("s"), pad=True)
        v_op(lambda e, m=nmr_dst: e.tensor_scalar_mul(out=m[:], in0=m[:], scalar1=-1.0), pad=True)

    def quant_to(dst_bf, src_f32, width, s_scale_ap, waits=()):
        for q in range(width // 512):
            sl = slice(512 * q, 512 * (q + 1))
            v_op(lambda e, sl=sl: e.tensor_scalar(out=quadA[:], in0=src_f32[:, sl],
                                                  scalar1=s_scale_ap, scalar2=128.0,
                                                  op0=ALU.mult, op1=ALU.add),
                 waits=(waits if q == 0 else ()))
            v_op(lambda e: e.tensor_copy(out=quadI[:], in_=quadA[:]))
            v_op(lambda e: e.scalar_tensor_tensor(out=quadC[:], in0=quadA[:], scalar=0.5,
                                                  in1=quadI[:], op0=ALU.subtract, op1=ALU.is_ge))
            v_op(lambda e, sl=sl: e.scalar_tensor_tensor(out=dst_bf[:, sl], in0=quadI[:],
                                                         scalar=-128.0, in1=quadC[:],
                                                         op0=ALU.add, op1=ALU.add))

    dbgS = dt("dbg_S", [32 * 128, 1040], F32, kind="ExternalOutput") if dbg else None
    dbgP = dt("dbg_P", [32 * 128, 2048], BF16, kind="ExternalOutput") if dbg else None
    dbgPt = tlv(dbgP.ap()) if dbg else None
    dbgK = dt("dbg_K", [32 * 128, 128], BF16, kind="ExternalOutput") if dbg else None
    dbgKt = tlv(dbgK.ap()) if dbg else None
    dbgQ = dt("dbg_Q", [32 * 128, 128], BF16, kind="ExternalOutput") if dbg else None
    dbgQt = tlv(dbgQ.ap()) if dbg else None
    dbgSt = tlv(dbgS.ap()) if dbg else None
    dbgx = dt("dbg_x", [TOK, D], BF16, kind="ExternalOutput") if dbg else None
    dbgxt = tlv(dbgx.ap()) if dbg else None
    amdbg = sb("amdbg", [128, 16], F32)
    esdbg = sb("esdbg", [128, 16], F32)

    SCR = bigA[:, 2048:2560]
    fa_t1 = [bigB[:, 0:D], bigB[:, D:2 * D]]
    fa_h1 = [bigA[:, 0:D], bigA[:, D:2 * D]]
    fb_h = [bigA, bigB]
    xq1b2 = [bbuf[:, 16384:17408], bbuf[:, 18432:19456]]
    xq1T2 = [bbuf[:, 17408:18432].rearrange("p (c n) -> p c n", c=8),
             bbuf[:, 19456:20480].rearrange("p (c n) -> p c n", c=8)]
    xq2b2 = [bbuf[:, 8192:12288], bbuf[:, 12288:16384]]
    xq2T2 = [bbuf[:, 0:4096].rearrange("p (c n) -> p c n", c=32),
             bbuf[:, 4096:8192].rearrange("p (c n) -> p c n", c=32)]
    wsc127_1, wsc127_2 = st("w127a", 1), st("w127b", 1)
    caps = {}
    _cid = [0]

    def ffn(xin_tiles, xout_tiles):
        cid = _cid[0]
        _cid[0] += 1

        def cap(tag, t, *chs):
            caps[(cid, tag, t)] = now(*chs)

        def get(tag, t, fb=None):
            k = (cid, tag, t)
            if k in caps:
                return caps[k]
            return dep_all() if fb is None else fb

        for t in range(NTT):
            par = t % 2
            dma("sync", xrow2[par][:], xin_tiles[:, t, :], waits=get("sts", t - 2))
            v_op(lambda e, t=t, par=par: e.reduce_sum(out=sxs[:, t:t + 1], in_=xrow2[par][:],
                                                      axis=AXX), waits=now("ld"))
            s_op(lambda e, t=t, par=par: e.activation(out=fa_h1[par], in_=xrow2[par][:],
                                                      func=AF.Square, accum_out=sxq[:, t:t + 1]),
                 waits=now("ld"))
            cap("sts", t, "v", "s")
        ln_chain(sxs, sxq, D, rr1, nmr1, double_ln=True)
        v_op(lambda e: e.tensor_scalar_mul(out=wsc127_1[:], in0=wsc1[:], scalar1=1.0 / 127.0), pad=True)
        v_op(lambda e: e.tensor_scalar_mul(out=wsc127_2[:], in0=wsc2[:], scalar1=0.5 / 127.0), pad=True)

        def prepA(t):
            par = t % 2
            dma("sync", xrow2[par][:], xin_tiles[:, t, :],
                waits=get("At1", t - 2, None if t < 2 else []))
            s_op(lambda e, t=t, par=par: e.activation(out=fa_t1[par], in_=xrow2[par][:],
                                                      func=AF.Identity, bias=nmr1[:, t:t + 1],
                                                      scale=rr1[:, t:t + 1]),
                 waits=now("ld") + get("Apv", t - 2, [] if t >= 2 else None))
            cap("At1", t, "s")
            v_op(lambda e, par=par: e.tensor_mul(out=fa_h1[par], in0=fa_t1[par], in1=lnw_bc[:]),
                 waits=now("s"))
            v_op(lambda e, par=par: e.tensor_add(out=fa_h1[par], in0=fa_h1[par], in1=lnb_bc[:]))
            v_op(lambda e, par=par: e.tensor_reduce(out=amP[par][:], in_=fa_h1[par], axis=AXX,
                                                    op=ALU.max, apply_absolute_value=True), pad=True)
            v_op(lambda e, par=par: e.tensor_scalar_max(out=amP[par][:], in0=amP[par][:],
                                                        scalar1=EPS), pad=True)
            v_op(lambda e, par=par: e.tensor_mul(out=esP[par][:], in0=amP[par][:], in1=wsc127_1[:]),
                 waits=get("AevAll", t - 2, []), pad=True)
            v_op(lambda e, par=par: e.reciprocal(out=arP[par][:], in_=esP[par][:]),
                 waits=get("AatT", t - 2, []), pad=True)
            v_op(lambda e, par=par: e.tensor_mul(out=scP[par][:], in0=arP[par][:], in1=wsc1[:]), pad=True)
            quant_to(xq1b2[par], fa_h1[par], D, scP[par][:], waits=get("Atp", t - 2, []))
            dmaT(xq1T2[par], xq1b2[par], waits=now("v") + get("Amm", t - 2, []))
            cap("Atp", t, "tp")
            cap("Apv", t, "v")

        def mmA(t):
            par = t % 2
            t_op(lambda e, par=par: e.transpose(out=pb[7][0:1, 0:128], in_=arP[par][:],
                                                identity=identq[:]),
                 waits=get("Apv", t) + get("Agel7", t - 1, []))
            cap("AatT", t, "t")
            for k in range(3):
                v_op(lambda e, k=k, par=par: e.tensor_copy(out=arow(k, par), in_=pb[7][0:1, 0:128]),
                     waits=(now("t") if k == 0 else ()))
            cap("Aar", t, "v")
            for n in range(8):
                bank = pb[7] if n == 7 else pb[n]
                if n == 0:
                    w0 = get("Atp", t) + get("Agel0", t - 1, [])
                elif n == 7:
                    w0 = get("Aar", t) + get("Agel7", t - 1, [])
                else:
                    w0 = get("Agel%d" % n, t - 1, [])
                for kc in range(8):
                    t_op(lambda e, kc=kc, n=n, bank=bank, par=par: e.matmul(
                        out=bank[:, 0:512], lhsT=xq1T2[par][:, kc, :],
                        rhs=w1qT[:, kc, 512 * n:512 * (n + 1)],
                        start=(kc == 0), stop=False),
                        waits=(w0 if kc == 0 else ()))
                t_op(lambda e, n=n, bank=bank, par=par: e.matmul(
                    out=bank[:, 0:512], lhsT=arow(n % 3, par), rhs=b1chunk(n),
                    start=False, stop=True),
                    waits=(get("Aar", t) if n == 0 else ()))
                wt_n = now("t")
                s_op(lambda e, n=n, bank=bank, par=par, t=t: e.activation(
                    out=g_ring[par][:, 512 * n:512 * (n + 1)], in_=bank[:, 0:512], func=AF.Gelu,
                    scale=esP[par][:], accum_out=gs8[:, 8 * t + n: 8 * t + n + 1]),
                    waits=wt_n + (get("Agst", t - 2, []) if n == 0 else []))
                s_op(lambda e, n=n, par=par, t=t: e.activation(
                    out=SCR, in_=g_ring[par][:, 512 * n:512 * (n + 1)],
                    func=AF.Square, accum_out=gq8[:, 8 * t + n: 8 * t + n + 1]))
                cap("Agel%d" % n, t, "s")
            cap("AevAll", t, "s")
            cap("Amm", t, "t")
            dma("sync", gst_t[:, t, :], g_ring[par][:], waits=now("s"))
            cap("Agst", t, "ld")

        prepA(0)
        prepA(1)
        for t in range(NTT):
            mmA(t)
            if t + 2 < NTT:
                prepA(t + 2)

        v_op(lambda e: e.tensor_reduce(out=sxs[:], in_=gs8.ap().rearrange("p (t n) -> p t n", n=8),
                                       axis=AXX, op=ALU.add), waits=now("s"))
        v_op(lambda e: e.tensor_reduce(out=sxq[:], in_=gq8.ap().rearrange("p (t n) -> p t n", n=8),
                                       axis=AXX, op=ALU.add))
        ln_chain(sxs, sxq, HID, rr2, nmr2, double_ln=False)
        dma("sync", wbuf[:, :], w2qt_st.ap(), waits=now("t"))

        def prepB(t):
            par = t % 2
            dma("sync", g_ring[par][:], gst_t[:, t, :], waits=get("Bmm", t - 2))
            dma("sync", xrow2[par][:], xin_tiles[:, t, :], waits=get("BevC1", t - 2))
            cap("Bxld", t, "ld")
            s_op(lambda e, t=t, par=par: e.activation(out=fb_h[par][:], in_=g_ring[par][:],
                                                      func=AF.Identity, bias=nmr2[:, t:t + 1],
                                                      scale=rr2[:, t:t + 1]),
                 waits=now("ld") + get("Bpv", t - 2, [] if t >= 2 else None))
            cap("Bt2", t, "s")
            v_op(lambda e, par=par: e.tensor_mul(out=fb_h[par][:], in0=fb_h[par][:], in1=ln2w_bc[:]),
                 waits=now("s"))
            v_op(lambda e, par=par: e.tensor_add(out=fb_h[par][:], in0=fb_h[par][:], in1=ln2b_bc[:]))
            v_op(lambda e, par=par: e.tensor_reduce(out=amP[par][:], in_=fb_h[par][:], axis=AXX,
                                                    op=ALU.max, apply_absolute_value=True), pad=True)
            v_op(lambda e, par=par: e.tensor_scalar_max(out=amP[par][:], in0=amP[par][:],
                                                        scalar1=EPS), pad=True)
            v_op(lambda e, par=par: e.tensor_mul(out=esP[par][:], in0=amP[par][:], in1=wsc127_2[:]), pad=True)
            v_op(lambda e, par=par: e.reciprocal(out=scP[par][:], in_=esP[par][:]), pad=True)
            v_op(lambda e, par=par: e.tensor_scalar_mul(out=arP[par][:], in0=scP[par][:],
                                                        scalar1=0.5),
                 waits=get("BatT", t - 2, []), pad=True)
            v_op(lambda e, par=par: e.tensor_mul(out=scP[par][:], in0=arP[par][:], in1=wsc2[:]), pad=True)
            quant_to(xq2b2[par], fb_h[par], HID, scP[par][:], waits=get("Btp", t - 2, []))
            dmaT(xq2T2[par], xq2b2[par], waits=now("v", "s") + get("Bmm", t - 2, []))
            cap("Btp", t, "tp")
            cap("Bpv", t, "v")

        def mmB(t):
            par = t % 2
            t_op(lambda e, par=par: e.transpose(out=pb[7][0:1, 0:128], in_=arP[par][:],
                                                identity=identq[:]),
                 waits=get("Bpv", t) + get("Bar", t - 1, []))
            cap("BatT", t, "t")
            for j, k in enumerate((0, 2)):
                v_op(lambda e, k=k, par=par: e.tensor_copy(out=arow(k, par), in_=pb[7][0:1, 0:128]),
                     waits=(now("t") if j == 0 else ()))
            cap("Bar", t, "v")
            for n in range(2):
                w0 = (get("Btp", t) if n == 0 else []) + get("BevC%d" % n, t - 1, [])
                for kc in range(32):
                    t_op(lambda e, kc=kc, n=n, par=par: e.matmul(
                        out=pb[n][:, 0:512], lhsT=xq2T2[par][:, kc, :],
                        rhs=w2qT[:, kc, 512 * n:512 * (n + 1)],
                        start=(kc == 0), stop=False),
                        waits=(w0 if kc == 0 else ()))
                t_op(lambda e, n=n, par=par: e.matmul(out=pb[n][:, 0:512],
                                                      lhsT=arow(b2base(n), par), rhs=b2chunk(n),
                                                      start=False, stop=True),
                     waits=(get("Bar", t) if n == 0 else ()))
                wt_n = now("t")
                v_op(lambda e, n=n, par=par: e.scalar_tensor_tensor(
                    out=xrow2[par][:, 512 * n:512 * (n + 1)], in0=pb[n][:, 0:512],
                    scalar=esP[par][:], in1=xrow2[par][:, 512 * n:512 * (n + 1)],
                    op0=ALU.mult, op1=ALU.add),
                    waits=wt_n + (get("Bxld", t) if n == 0 else []))
                cap("BevC%d" % n, t, "v")
            cap("Bmm", t, "t")
            dma("sync", xout_tiles[:, t, :], xrow2[par][:], waits=now("v"))

        prepB(0)
        prepB(1)
        for t in range(NTT):
            mmB(t)
            if t + 2 < NTT:
                prepB(t + 2)
        barrier()

    sTev_b = [sb("sTe0", [128, 512], BF16), sb("sTe1", [128, 512], BF16)]
    sTm_b = [sb("sTmA", [128, 512], BF16), sb("sTmB", [128, 512], BF16)]
    phiqb2 = [phiqb, bbuf[:, 0:2048]]
    phikb2 = [phikb, bbuf[:, 2048:4096]]
    phiqT2 = [phiqT, phikT_s.ap().rearrange("p (j n) -> p j n", j=16)]
    qfp2 = [qfp[:], wbuf[:, 30976:31232]]
    kfp2 = [kfp[:], wbuf[:, 31232:31488]]
    qfpT2 = [qfpT, wbuf[:, 31488:31744].rearrange("p (b n) -> p b n", b=2)]
    kfpT2 = [kfpT, wbuf[:, 31744:32000].rearrange("p (b n) -> p b n", b=2)]
    fmwQ2 = [fmw, fmw3]
    fmwK2 = [fmw2, quadA[:, 128:256]]
    sqscr = quadA[:, 0:128]
    o_outb2 = [o_outb, oslot[0]]
    zt2 = [zt, st("ztb", 8)]
    zrt2 = [zrt, st("zrtb", 8)]
    ocat2 = [ocat[:], quadA.ap().bitcast(BF16)]
    ocatT2 = [ocatT, quadC.ap().bitcast(BF16).rearrange("p (c n) -> p c n", c=8)]
    os4 = [oslot[0][:], oslot[1][:],
           quadI.ap().bitcast(BF16)[:, 0:512], quadI.ap().bitcast(BF16)[:, 512:1024]]

    def attention():
        acaps = {}

        def cap(tag, t, *chs):
            acaps[(tag, t)] = now(*chs)

        def get(tag, t, fb=None):
            k = (tag, t)
            if k in acaps:
                return acaps[k]
            return dep_all() if fb is None else fb

        stg = bbuf[:, 0:1024]
        dma("gpsimd", stg, wqd.ap(), waits=now("tp", "t", "s", "v"))
        dmaT(wqT, stg, waits=now("gd"))
        dma("gpsimd", stg, wkd.ap(), waits=now("tp"))
        dmaT(wkT, stg, waits=now("gd"))
        for j in range(4):
            dma("gpsimd", stg, wv_t[:, j, :], waits=now("tp"))
            dmaT(wvT[:, :, 128 * j:128 * (j + 1)], stg, waits=now("gd"))
        for j in range(8):
            dma("gpsimd", stg, wo_t[:, j, :], waits=now("tp"))
            dmaT(woT[:, :, 128 * j:128 * (j + 1)], stg, waits=now("gd"))
        barrier()
        v_op(lambda e: e.memset(v1t[:, :, :, 64:65], 1.0))

        for t in range(NTT):
            par = t % 2
            dma("sync", xrow2[par][:], x1_t[:, t, :], waits=get("Lst", t - 2))
            v_op(lambda e, t=t, par=par: e.reduce_sum(out=asx[:, t:t + 1], in_=xrow2[par][:],
                                                      axis=AXX), waits=now("ld"))
            s_op(lambda e, t=t, par=par: e.activation(out=bigA[:, 0:D], in_=xrow2[par][:],
                                                      func=AF.Square, accum_out=asq[:, t:t + 1]),
                 waits=now("ld"))
            cap("Lst", t, "v", "s")
        ln_chain(asx, asq, D, arr_, anmr, double_ln=False)
        for t in range(NTT):
            par = t % 2
            dma("sync", xrow2[par][:], x1_t[:, t, :], waits=get("Lap", t - 2, [] if t >= 2 else None))
            s_op(lambda e, t=t, par=par: e.activation(out=bigBbf[:, 1024 * par:1024 * par + D],
                                                      in_=xrow2[par][:], func=AF.Identity,
                                                      bias=anmr[:, t:t + 1], scale=arr_[:, t:t + 1]),
                 waits=now("ld") + get("Lsd", t - 2, []))
            cap("Lap", t, "s")
            dma("sync", ag1in_t[:, t, :], bigBbf[:, 1024 * par:1024 * par + D], waits=now("s"))
            cap("Lsd", t, "ld")
        op("gpsimd", lambda e: e.collective_compute(
            "AllGather", ALU.bypass, replica_groups=[[0, 1], [2, 3], [4, 5], [6, 7]],
            ins=[ag1_in.ap().opt()], outs=[ag1_out.ap().opt()]),
            waits=now("ld"), inc="cc")

        def pprep(c):
            par = c % 2
            dma("sync", xlnc_ring[par], ag1out_t[:, c, :],
                waits=get("Ptp", c - 2, [] if c >= 2 else None) + (now("cc") if c == 0 else []))
            dmaT(xlncT_ring[par], xlnc_ring[par], waits=now("ld") + get("Pmm", c - 2, []))
            cap("Ptp", c, "tp")

        def pmm(c):
            par = c % 2
            w0 = get("Ptp", c) + get("Pev", c - 1, [])
            for kc in range(8):
                t_op(lambda e, kc=kc, par=par: e.matmul(out=pb[0][:, 0:128],
                                                        lhsT=xlncT_ring[par][:, kc, :],
                                                        rhs=wqT[:, kc, :], start=(kc == 0),
                                                        stop=(kc == 7)),
                     waits=(w0 if kc == 0 else ()))
            for kc in range(8):
                t_op(lambda e, kc=kc, par=par: e.matmul(out=pb[1][:, 0:128],
                                                        lhsT=xlncT_ring[par][:, kc, :],
                                                        rhs=wkT[:, kc, :], start=(kc == 0),
                                                        stop=(kc == 7)))
            for kc in range(8):
                t_op(lambda e, kc=kc, par=par: e.matmul(out=pb[2][:, 0:512],
                                                        lhsT=xlncT_ring[par][:, kc, :],
                                                        rhs=wvT[:, kc, :], start=(kc == 0),
                                                        stop=(kc == 7)))
            cap("Pmm", c, "t")
            s_op(lambda e, par=par: e.activation(out=fmwQ2[par][:], in_=pb[0][:, 0:128],
                                                 func=AF.Copy),
                 waits=now("t") + get("Pst", c - 2, []))
            s_op(lambda e, par=par: e.activation(out=fmwK2[par][:], in_=pb[1][:, 0:128],
                                                 func=AF.Copy))
            s_op(lambda e, c=c: e.activation(
                out=v1t[:, c, :, 0:64], in_=pb[2][:, 0:512].rearrange("p (h n) -> p h n", h=8),
                func=AF.Copy))
            cap("Pev", c, "s")
            v_op(lambda e, c=c, par=par: e.tensor_copy(out=qraw[:, c, :], in_=fmwQ2[par][:]),
                 waits=now("s"))
            v_op(lambda e, c=c, par=par: e.tensor_copy(out=kraw[:, c, :], in_=fmwK2[par][:]))
            v_op(lambda e, c=c, par=par: e.reduce_sum(
                out=qsum[:, 8 * c:8 * c + 8],
                in_=fmwQ2[par].ap().rearrange("p (h f) -> p h f", f=16) if False
                else fmwQ2[par][:].rearrange("p (h f) -> p h f", f=16), axis=AXX))
            v_op(lambda e, c=c, par=par: e.reduce_sum(
                out=ksum[:, 8 * c:8 * c + 8],
                in_=fmwK2[par][:].rearrange("p (h f) -> p h f", f=16), axis=AXX))
            v_op(lambda e, par=par: e.tensor_mul(out=sqscr[:], in0=fmwQ2[par][:], in1=fmwQ2[par][:]))
            v_op(lambda e, c=c: e.reduce_sum(out=qsq[:, 8 * c:8 * c + 8],
                                             in_=sqscr[:].rearrange("p (h f) -> p h f", f=16),
                                             axis=AXX))
            v_op(lambda e, par=par: e.tensor_mul(out=sqscr[:], in0=fmwK2[par][:], in1=fmwK2[par][:]))
            v_op(lambda e, c=c: e.reduce_sum(out=ksq[:, 8 * c:8 * c + 8],
                                             in_=sqscr[:].rearrange("p (h f) -> p h f", f=16),
                                             axis=AXX))
            cap("Pst", c, "v")

        pprep(0)
        pprep(1)
        for c in range(NCH):
            pmm(c)
            if c + 2 < NCH:
                pprep(c + 2)

        def fm_stats(sumt, sqt_, mut, rst):
            v_op(lambda e: e.tensor_scalar_mul(out=mut[:], in0=sumt[:], scalar1=1.0 / FD),
                 waits=now("s", "v"), pad=True)
            v_op(lambda e: e.tensor_scalar_mul(out=rst[:], in0=sqt_[:], scalar1=1.0 / FD), pad=True)
            v_op(lambda e: e.tensor_mul(out=sqt_[:], in0=mut[:], in1=mut[:]))
            v_op(lambda e: e.tensor_sub(out=rst[:], in0=rst[:], in1=sqt_[:]), pad=True)
            v_op(lambda e: e.tensor_scalar_add(out=rst[:], in0=rst[:], scalar1=EPS), pad=True)
            v_op(lambda e: e.reciprocal(out=rst[:], in_=rst[:]), pad=True)
            s_op(lambda e: e.activation(out=rst[:], in_=rst[:], func=AF.Sqrt), waits=now("v"), pad=True)
        fm_stats(qsum, qsq, qmu, qrs)
        fm_stats(ksum, ksq, kmu, krs)
        barrier()

        Sf = [bigA[:, 0:520], bigA[:, 520:1040]]
        v_op(lambda e: e.memset(Sf[0], 0.0))
        v_op(lambda e: e.memset(Sf[1], 0.0), pad=True)
        for pp in range(2):
            v_op(lambda e, pp=pp: e.memset(qfp2[pp], 0.0))
            v_op(lambda e, pp=pp: e.memset(kfp2[pp], 0.0))

        def sprep(c):
            par = c % 2
            rq, rk = qf_ring[par], kf_ring[par]

            def fm_apply(raw, mu, rs, gbc, bbc, dstr):
                mu_b = mu[:, 8 * c:8 * c + 8].unsqueeze(2).broadcast_to([128, 8, 16])
                rs_b = rs[:, 8 * c:8 * c + 8].unsqueeze(2).broadcast_to([128, 8, 16])
                src3 = raw[:, c, :].rearrange("p (h f) -> p h f", f=16)
                v_op(lambda e, s3=src3, mb=mu_b: e.tensor_sub(
                    out=fmw.ap().rearrange("p (h f) -> p h f", f=16),
                    in0=s3, in1=mb))
                v_op(lambda e, rb=rs_b: e.tensor_mul(out=fmw.ap().rearrange("p (h f) -> p h f", f=16),
                                                     in0=fmw.ap().rearrange("p (h f) -> p h f", f=16),
                                                     in1=rb))
                v_op(lambda e, g=gbc: e.tensor_mul(out=fmw[:], in0=fmw[:], in1=g[:]))
                v_op(lambda e, d=dstr, b=bbc: e.tensor_add(out=d, in0=fmw[:], in1=b[:]))
            fm_apply(qraw, qmu, qrs, gamq_bc, betq_bc, rq)
            fm_apply(kraw, kmu, krs, gamk_bc, betk_bc, rk)

            def phi_build(src, dst, w):
                i0 = src.rearrange("p (h f) -> p h f", f=16).unsqueeze(3).broadcast_to([128, 8, 16, 16])
                i1 = src.rearrange("p (h f) -> p h f", f=16).unsqueeze(2).broadcast_to([128, 8, 16, 16])
                v_op(lambda e, d=dst: e.tensor_tensor(
                    out=d.rearrange("p (h a b) -> p h a b", a=16, b=16), in0=i0, in1=i1,
                    op=ALU.mult), waits=w)
            phi_build(rq, phiqb2[par], get("Stp", c - 2, []))
            phi_build(rk, phikb2[par], get("Smm", c - 2, []))
            v_op(lambda e, rq=rq, par=par: e.tensor_copy(
                out=qfp2[par].rearrange("p (h n) -> p h n", h=8)[:, :, 0:16],
                in_=rq.rearrange("p (h f) -> p h f", f=16)))
            v_op(lambda e, rk=rk, par=par: e.tensor_copy(
                out=kfp2[par].rearrange("p (h n) -> p h n", h=8)[:, :, 0:16],
                in_=rk.rearrange("p (h f) -> p h f", f=16)))
            dmaT(phiqT2[par], phiqb2[par], waits=now("v") + get("Smm", c - 2, []))
            dmaT(qfpT2[par], qfp2[par], waits=())
            dmaT(kfpT2[par], kfp2[par], waits=())
            cap("Spv", c, "v")
            cap("Stp", c, "tp")

        def smm(c):
            par = c % 2
            if c > 0:
                s_op(lambda e: e.activation(
                    out=ssnap[:, 0, :, :], in_=Sf[0].rearrange("p (h n) -> p h n", h=8),
                    func=AF.Copy), waits=get("Sadd", c - 1))
                s_op(lambda e: e.activation(
                    out=ssnap[:, 1, :, :], in_=Sf[1].rearrange("p (h n) -> p h n", h=8),
                    func=AF.Copy))
            cap("Ssnap", c, "s")

            sbanks = {(0, 0): pb[0], (1, 0): pb[1], (0, 1): pb[2], (1, 1): pb[7]}
            state_mms = []
            first_state = [True]
            for fh in range(2):
                for hh in range(2):
                    bank = sbanks[(fh, hh)]
                    for j in range(4):
                        h = 4 * hh + j
                        def emit_state(bank=bank, j=j, h=h, fh=fh):
                            w0 = []
                            if first_state[0]:
                                w0 = get("Sadd", c - 1) + get("Spv", c)
                                first_state[0] = False
                            t_op(lambda e, bank=bank, j=j, h=h, fh=fh, c=c, par=par: e.matmul(
                                out=bank[:, 65 * j:65 * j + 65],
                                lhsT=phikb2[par][:, 256 * h + 128 * fh: 256 * h + 128 * (fh + 1)],
                                rhs=v1t[:, c, h, 0:65],
                                start=(j == 0), stop=(j == 3)),
                                waits=w0)
                        state_mms.append(emit_state)
            sidx = 0
            for b in range(2):
                for j in range(4):
                    h = 4 * b + j
                    blk = h // 4
                    base = 32 * (h % 4)
                    tpos = (96, 0) if base == 96 else None
                    w0 = []
                    if j == 0:
                        w0 = get("Stp", c) + get("Ssq%d" % b, c - 1, [])
                    t_op(lambda e, j=j, b=b, blk=blk, base=base, tpos=tpos, par=par: e.matmul(
                        out=pb[3 + b][:, 128 * j:128 * j + 128],
                        lhsT=kfpT2[par][base:base + 16, blk, :],
                        rhs=qfpT2[par][base:base + 16, blk, :],
                        start=True, stop=True, tile_position=tpos),
                        waits=w0)
                    state_mms[sidx]()
                    sidx += 1
                tcap = now("t")
                s_op(lambda e, b=b: e.activation(out=sTev_b[b][:], in_=pb[3 + b][:, 0:512],
                                                 func=AF.Square),
                     waits=tcap + get("Smk%d" % b, c - 1, []))
                cap("Ssq%d" % b, c, "s")
                g_op(lambda e, b=b: e.affine_select(
                    out=sTm_b[b].ap().rearrange("p (h n) -> p h n", h=4),
                    in_=sTev_b[b].ap().rearrange("p (h n) -> p h n", h=4),
                    pattern=[[0, 4], [1, 128]], compare_op=ALU.is_ge, fill=0.0, base=0,
                    channel_multiplier=-1), waits=now("s") + get("Soi%d" % b, c - 1, []))
                cap("Smk%d" % b, c, "g")
            while sidx < 16:
                state_mms[sidx]()
                sidx += 1
            stcap = now("t")
            for b in range(2):
                ob = pb[5 + b]
                for j in range(4):
                    h = 4 * b + j
                    col = 65 * j
                    w0 = get("Smk%d" % b, c) + (get("Sonorm", c - 1, []) if j == 0 else [])
                    t_op(lambda e, ob=ob, col=col, b=b, j=j, c=c: e.matmul(
                        out=ob[:, col:col + 65], lhsT=sTm_b[b][:, 128 * j:128 * j + 128],
                        rhs=v1t[:, c, 4 * b + j, 0:65],
                        start=(j == 0), stop=(j == 3 and c == 0)),
                        waits=(w0 if j == 0 else ()))
                    if c > 0:
                        for fh in range(2):
                            t_op(lambda e, ob=ob, col=col, h=h, fh=fh, j=j, par=par: e.matmul(
                                out=ob[:, col:col + 65], lhsT=phiqT2[par][:, 2 * h + fh, :],
                                rhs=ssnap[:, fh, h, :],
                                start=False, stop=(j == 3 and fh == 1)),
                                waits=(get("Ssnap", c) if (b == 0 and j == 0 and fh == 0) else ()))
                cap("Soi%d" % b, c, "t")
            cap("Smm", c, "t")
            v_op(lambda e: e.tensor_add(out=Sf[0][:, 0:260], in0=Sf[0][:, 0:260],
                                        in1=pb[0][:, 0:260]), waits=stcap)
            v_op(lambda e: e.tensor_add(out=Sf[0][:, 260:520], in0=Sf[0][:, 260:520],
                                        in1=pb[2][:, 0:260]))
            v_op(lambda e: e.tensor_add(out=Sf[1][:, 0:260], in0=Sf[1][:, 0:260],
                                        in1=pb[1][:, 0:260]))
            v_op(lambda e: e.tensor_add(out=Sf[1][:, 260:520], in0=Sf[1][:, 260:520],
                                        in1=pb[7][:, 0:260]))
            cap("Sadd", c, "v")
            v_op(lambda e, par=par: e.tensor_copy(
                out=zt2[par][:, 0:4],
                in_=pb[5][:, 0:260].rearrange("p (h n) -> p h n", n=65)[:, :, 64:65]),
                waits=get("Soi1", c) + get("Sonorm", c - 2, []), pad=True)
            v_op(lambda e, par=par: e.tensor_copy(
                out=zt2[par][:, 4:8],
                in_=pb[6][:, 0:260].rearrange("p (h n) -> p h n", n=65)[:, :, 64:65]), pad=True)
            v_op(lambda e, par=par: e.tensor_scalar_add(out=zt2[par][:], in0=zt2[par][:],
                                                        scalar1=EPS), pad=True)
            v_op(lambda e, par=par: e.reciprocal(out=zrt2[par][:], in_=zt2[par][:]), pad=True)
            for h in range(NH):
                ob = pb[5] if h < 4 else pb[6]
                col = 65 * (h % 4)
                s_op(lambda e, h=h, ob=ob, col=col, par=par: e.activation(
                    out=o_outb2[par][:, 64 * h:64 * h + 64], in_=ob[:, col:col + 64],
                    func=AF.Copy, scale=zrt2[par][:, h:h + 1]),
                    waits=(now("v") + get("Sost", c - 2, []) if h == 0 else ()))
            cap("Sonorm", c, "s")
            dma("sync", ag2in_t[:, c, :], o_outb2[par][:], waits=now("s"))
            cap("Sost", c, "ld")

        sprep(0)
        sprep(1)
        for c in range(NCH):
            smm(c)
            if c + 2 < NCH:
                sprep(c + 2)

        op("gpsimd", lambda e: e.collective_compute(
            "AllGather", ALU.bypass, replica_groups=[[0, 1], [2, 3], [4, 5], [6, 7]],
            ins=[ag2_in.ap().opt()], outs=[ag2_out.ap().opt()]),
            waits=now("ld"), inc="cc")

        def wprep(t):
            par = t % 2
            rows = (t, t + 16, 32 + t, 48 + t)
            dma("sync", os4[0], ag2out_t[:, rows[0], :],
                waits=get("Wcmb", t - 1, [] if t >= 1 else None) + (now("cc") if t == 0 else []))
            for k in range(1, 4):
                dma("sync", os4[k], ag2out_t[:, rows[k], :])
            for half in range(2):
                sl = slice(512 * half, 512 * (half + 1))
                v_op(lambda e, sl=sl, par=par, half=half: e.tensor_single_scalar(
                    out=ocat2[par][:, sl], in_=os4[2 * half], scalar=m0t[:], op=ALU.mult),
                    waits=(now("ld") + get("Wtp", t - 2, []) if half == 0 else ()))
                v_op(lambda e, sl=sl, par=par, half=half: e.scalar_tensor_tensor(
                    out=ocat2[par][:, sl], in0=os4[2 * half + 1], scalar=m1t[:],
                    in1=ocat2[par][:, sl], op0=ALU.mult, op1=ALU.add))
            cap("Wcmb", t, "v")
            dmaT(ocatT2[par], ocat2[par], waits=now("v") + get("Wmm", t - 2, []))
            cap("Wtp", t, "tp")
            dma("sync", xrow2[par][:], x1_t[:, t, :], waits=get("Wadd", t - 2, []))
            cap("Wxld", t, "ld")

        def wmm(t):
            par = t % 2
            w0 = get("Wtp", t) + get("Wadd", t - 1, [])
            for kc in range(8):
                for n in range(2):
                    t_op(lambda e, kc=kc, n=n, par=par: e.matmul(
                        out=pb[n][:, 0:512], lhsT=ocatT2[par][:, kc, :],
                        rhs=woT[:, kc, 512 * n:512 * (n + 1)],
                        start=(kc == 0), stop=(kc == 7)),
                        waits=(w0 if (kc == 0 and n == 0) else ()))
            cap("Wmm", t, "t")
            for n in range(2):
                v_op(lambda e, n=n, par=par: e.tensor_add(
                    out=xrow2[par][:, 512 * n:512 * (n + 1)], in0=pb[n][:, 0:512],
                    in1=xrow2[par][:, 512 * n:512 * (n + 1)]),
                    waits=(now("t") + get("Wxld", t) if n == 0 else ()))
            cap("Wadd", t, "v")
            dma("sync", x2_t[:, t, :], xrow2[par][:], waits=now("v"))
            cap("Wst", t, "ld")

        wprep(0)
        wprep(1)
        for t in range(NTT):
            wmm(t)
            if t + 2 < NTT:
                wprep(t + 2)
        barrier()

    if stage == "prep":
        pass
    else:
        ffn(xin_t, x1_t)
    if stage in ("attn", "full"):
        attention()
    if stage == "prep":
        pass
    elif stage == "full":
        dma("sync", wbuf[:, :], w1qt_st.ap(), waits=now("t", "s", "v", "tp", "g"))
        barrier()
        ffn(x2_t, out_t)
    elif stage != "prep":
        src_t = x1_t if stage == "ffn1" else x2_t
        for t in range(NTT):
            dma("sync", xrow[:], src_t[:, t, :], waits=now("v"))
            v_op(lambda e: e.tensor_copy(out=bigA[:, 0:D], in_=xrow[:]), waits=now("ld"))
            dma("sync", out_t[:, t, :], bigA[:, 0:D], waits=now("v"))
    else:
        for t in range(NTT):
            dma("sync", xrow[:], xin_t[:, t, :], waits=now("v"))
            v_op(lambda e: e.tensor_copy(out=bigA[:, 0:D], in_=xrow[:]), waits=now("ld"))
            dma("sync", out_t[:, t, :], bigA[:, 0:D], waits=now("v"))

    if dbg:
        dbgg = dt("dbg_g", [TOK, HID], BF16, kind="ExternalOutput")
        dbgw = dt("dbg_w", [128, 8 * HID], BF16, kind="ExternalOutput")
        dbgs = dt("dbg_s", [128, 64], F32, kind="ExternalOutput")
        dma("sync", dbgg.ap(), g_std.ap(), waits=dep_all())
        dma("sync", dbgw.ap(), w1qt_st.ap())
        v_op(lambda e: e.tensor_copy(out=bigA[:, 0:16], in_=sxs[:]), waits=now("ld"))
        v_op(lambda e: e.tensor_copy(out=bigA[:, 16:32], in_=rr1[:]))
        v_op(lambda e: e.tensor_copy(out=bigA[:, 32:48], in_=rr2[:]))
        v_op(lambda e: e.tensor_copy(out=bigA[:, 48:49], in_=wsc1[:]))
        v_op(lambda e: e.tensor_copy(out=bigA[:, 49:50], in_=wsc2[:]))
        v_op(lambda e: e.tensor_copy(out=bigA[:, 50:51], in_=wthr[:]))
        v_op(lambda e: e.tensor_copy(out=bigA[:, 51:52], in_=wnthr[:]))
        v_op(lambda e: e.tensor_copy(out=bigA[:, 52:53], in_=wsc1[:]))
        v_op(lambda e: e.tensor_copy(out=bigA[:, 0:16], in_=amdbg[:]))
        v_op(lambda e: e.tensor_copy(out=bigA[:, 16:32], in_=esdbg[:]))
        dma("sync", dbgs.ap(), bigA[:, 0:64], waits=now("v"))
        dbgp = dt("dbg_p", [128, 12288], BF16, kind="ExternalOutput")
        dma("sync", tlv(dbgp.ap())[:, 0, :] if False else dbgp.ap()[:, 0:4096], prepP[:, 0:4096], waits=now("v"))
        dma("sync", dbgp.ap()[:, 4096:8192], prepN[:, 0:4096])
        dma("sync", dbgp.ap()[:, 8192:12288], tern_ring[1][:, 0:4096])
    op("sync", lambda e: e.nop(), waits=dep_all())

    with contextlib.ExitStack() as fes:
        for n in CH:
            sems[n] = fes.enter_context(nc.semaphore("sem_" + n))
        block = fes.enter_context(nc.Block())
        for eng in ENGS:
            getattr(block, eng)(lambda e, _f=ops[eng]: [f(e) for f in _f])
    es.close()
    return nc


def shard_inputs(inputs):
    x = np.ascontiguousarray(np.asarray(inputs["L0_attn_x"], dtype=np.float32))
    maps = []
    for core in range(8):
        b, hg = core // 2, core % 2
        maps.append({
            "x_in": np.ascontiguousarray(x[b, hg * 2048:(hg + 1) * 2048, :]),
            "w1": np.ascontiguousarray(np.asarray(inputs["L0_ffn_w1"], np.float32)),
            "w2": np.ascontiguousarray(np.asarray(inputs["L0_ffn_w2"], np.float32)),
            "b1": np.asarray(inputs["L0_ffn_b1"], np.float32).reshape(1, HID),
            "b2": np.asarray(inputs["L0_ffn_b2"], np.float32).reshape(1, D),
            "lnw": np.asarray(inputs["L0_ffn_ln1_w"], np.float32).reshape(1, D),
            "lnb": np.asarray(inputs["L0_ffn_ln1_b"], np.float32).reshape(1, D),
            "ln2w": np.asarray(inputs["L0_ffn_ln2_w"], np.float32).reshape(1, HID),
            "ln2b": np.asarray(inputs["L0_ffn_ln2_b"], np.float32).reshape(1, HID),
            "wq": np.ascontiguousarray(np.asarray(inputs["L0_attn_Wq"], np.float32)[hg * 128:(hg + 1) * 128, :]),
            "wk": np.ascontiguousarray(np.asarray(inputs["L0_attn_Wk"], np.float32)[hg * 128:(hg + 1) * 128, :]),
            "wv": np.ascontiguousarray(np.asarray(inputs["L0_attn_Wv"], np.float32)[hg * 512:(hg + 1) * 512, :]),
            "wo": np.ascontiguousarray(np.asarray(inputs["L0_attn_Wo"], np.float32)),
            "gam": np.asarray(inputs["L0_attn_gamma"], np.float32).reshape(1, FD),
            "bet": np.asarray(inputs["L0_attn_beta"], np.float32).reshape(1, FD),
            "m0": np.full((128, 1), 1.0 if hg == 0 else 0.0, np.float32),
            "m1": np.full((128, 1), 0.0 if hg == 0 else 1.0, np.float32),
        })
    return maps


def assemble(results):
    out = np.empty((4, 4096, 1024), np.float32)
    for core in range(8):
        b, hg = core // 2, core % 2
        out[b, hg * 2048:(hg + 1) * 2048, :] = results[core]["out"]
    return out


def kernel(**inputs):
    nc = build("full")
    maps = shard_inputs(inputs)
    res = run_bass_kernel_spmd(nc, maps, core_ids=list(range(8)))
    return assemble(res.results)


# version 06
# speedup vs baseline: 1.8780x; 1.0475x over previous
import contextlib
import numpy as np
import ml_dtypes
import concourse.bass as bass
import concourse.mybir as mybir
from concourse.bass_utils import run_bass_kernel_spmd

F32 = mybir.dt.float32
BF16 = mybir.dt.bfloat16
I32 = mybir.dt.int32
AF = mybir.ActivationFunctionType
ALU = mybir.AluOpType
AXX = mybir.AxisListType.X
bfloat16 = ml_dtypes.bfloat16

TOK, D, HID = 2048, 1024, 4096
NTT = TOK // 128
NH, FD, HD = 8, 16, 64
EPS = 1e-5
SEQ, NCH = 4096, 32
ENGS = ("sync", "scalar", "vector", "tensor", "gpsimd")
CH = ("ld", "tp", "gd", "v", "s", "t", "g", "cc")


def build(stage="full", dbg=False):
    nc = bass.Bass("TRN2", target_bir_lowering=False, detect_race_conditions=False)
    ops = {e: [] for e in ENGS}
    cnt = {c: 0 for c in CH}
    sems = {}

    def op(eng, make, waits=(), inc=None, amt=1):
        waits = tuple((n, v) for (n, v) in waits if v > 0)
        if inc is not None:
            cnt[inc] += amt
        def fn(e, _m=make, _w=waits, _i=inc, _a=amt):
            for n, v in _w:
                e.wait_ge(sems[n], v)
            ins = _m(e)
            if _i is not None:
                ins.then_inc(sems[_i], _a)
        ops[eng].append(fn)

    def now(*names):
        return [(n, cnt[n]) for n in names]

    def dep_all():
        return now(*CH)

    def barrier():
        w = dep_all()
        for eng in ENGS:
            if eng in ("vector", "gpsimd"):
                op(eng, lambda e: e.engine_nop(), waits=w)
            else:
                op(eng, lambda e: e.nop(), waits=w)

    def dma(eng, out, in_, waits=()):
        chn = {"sync": "ld", "scalar": "tp", "gpsimd": "gd"}[eng]
        op(eng, lambda e, o=out, i=in_: e.dma_start(out=o, in_=i), waits=waits, inc=chn, amt=16)

    def dmaT(out, in_, waits=()):
        op("sync", lambda e, o=out, i=in_: e.dma_start_transpose(out=o, in_=i),
           waits=waits, inc="tp", amt=16)

    def v_op(make, waits=(), pad=False):
        op("vector", make, waits=waits, inc="v")
        if pad:
            ops["vector"].append(lambda e: e.tensor_copy(out=_vpad[0], in_=_vpad[1]))
    def s_op(make, waits=(), pad=False):
        op("scalar", make, waits=waits, inc="s")
        if pad:
            ops["scalar"].append(lambda e: e.activation(out=_spad[0], in_=_spad[1], func=AF.Copy))
    t_op = lambda make, waits=(): op("tensor", make, waits=waits, inc="t")
    g_op = lambda make, waits=(): op("gpsimd", make, waits=waits, inc="g")

    dt = nc.dram_tensor
    x_in = dt("x_in", [TOK, D], F32, kind="ExternalInput")
    w1 = dt("w1", [HID, D], F32, kind="ExternalInput")
    w2 = dt("w2", [D, HID], F32, kind="ExternalInput")
    b1d = dt("b1", [1, HID], F32, kind="ExternalInput")
    b2d = dt("b2", [1, D], F32, kind="ExternalInput")
    lnwd = dt("lnw", [1, D], F32, kind="ExternalInput")
    lnbd = dt("lnb", [1, D], F32, kind="ExternalInput")
    ln2wd = dt("ln2w", [1, HID], F32, kind="ExternalInput")
    ln2bd = dt("ln2b", [1, HID], F32, kind="ExternalInput")
    wqd = dt("wq", [NH * FD, D], F32, kind="ExternalInput")
    wkd = dt("wk", [NH * FD, D], F32, kind="ExternalInput")
    wvd = dt("wv", [NH * HD, D], F32, kind="ExternalInput")
    wod = dt("wo", [D, D], F32, kind="ExternalInput")
    gamd = dt("gam", [1, FD], F32, kind="ExternalInput")
    betd = dt("bet", [1, FD], F32, kind="ExternalInput")
    m0d = dt("m0", [128, 1], F32, kind="ExternalInput")
    m1d = dt("m1", [128, 1], F32, kind="ExternalInput")
    outd = dt("out", [TOK, D], F32, kind="ExternalOutput")

    x1d = dt("x1d", [TOK, D], F32)
    x2d = dt("x2d", [TOK, D], F32)
    g_std = dt("g_st", [TOK, HID], BF16)
    w1qt_st = dt("w1qt_st", [128, 8 * HID], BF16)
    w2qt_st = dt("w2qt_st", [128, 32 * D], BF16)
    ag1_in = dt("ag1_in", [TOK, D], BF16)
    ag1_out = dt("ag1_out", [SEQ, D], BF16)
    ag2_in = dt("ag2_in", [SEQ, NH * HD], BF16)
    ag2_out = dt("ag2_out", [2 * SEQ, NH * HD], BF16)

    tlv = lambda ap: ap.rearrange("(t p) f -> p t f", p=128)
    xin_t, x1_t, x2_t, out_t = tlv(x_in.ap()), tlv(x1d.ap()), tlv(x2d.ap()), tlv(outd.ap())
    gst_t = tlv(g_std.ap())
    ag1in_t, ag1out_t = tlv(ag1_in.ap()), tlv(ag1_out.ap())
    ag2in_t, ag2out_t = tlv(ag2_in.ap()), tlv(ag2_out.ap())
    w1_t = tlv(w1.ap())
    w2_t = tlv(w2.ap())
    wv_t = tlv(wvd.ap())
    wo_t = tlv(wod.ap())

    es = contextlib.ExitStack()
    sb = lambda name, shape, dtp: es.enter_context(nc.sbuf_tensor(name, shape, dtp))

    _vpadt = sb("_vpad", [128, 128], F32)
    _vpad = (_vpadt[:, 0:64], _vpadt[:, 64:128])
    _spadt = sb("_spad", [128, 128], F32)
    _spad = (_spadt[:, 0:64], _spadt[:, 64:128])
    wbuf = sb("wbuf", [128, 32768], BF16)
    w1qT = wbuf.ap().rearrange("p (c n) -> p c n", c=8)
    w2qT = wbuf.ap().rearrange("p (c n) -> p c n", c=32)
    wqT = wbuf[:, 0:1024].rearrange("p (c n) -> p c n", c=8)
    wkT = wbuf[:, 1024:2048].rearrange("p (c n) -> p c n", c=8)
    wvT = wbuf[:, 2048:6144].rearrange("p (c n) -> p c n", c=8)
    woT = wbuf[:, 6144:14336].rearrange("p (c n) -> p c n", c=8)
    v1t = wbuf[:, 14336:14336 + 32 * 520].rearrange("p (t h n) -> p t h n", t=32, h=8)

    NB = 20480
    bbuf = sb("bbuf", [128, NB], BF16)
    g_ring = [bbuf[:, i * HID:(i + 1) * HID] for i in range(2)]
    prepP = bbuf[:, 0:4096]
    prepN = bbuf[:, 4096:8192]
    tern_ring = [bbuf[:, 8192:12288], bbuf[:, 12288:16384]]
    xq2b = bbuf[:, 8192:12288]
    xq2T = bbuf[:, 12288:16384].rearrange("p (c n) -> p c n", c=32)
    xq1b = bbuf[:, 16384:17408]
    xq1T = bbuf[:, 17408:18432].rearrange("p (c n) -> p c n", c=8)
    xlnc_ring = [bbuf[:, i * 1024:(i + 1) * 1024] for i in range(2)]
    xlncT_ring = [bbuf[:, 2048 + i * 1024: 3072 + i * 1024].rearrange("p (c n) -> p c n", c=8)
                  for i in range(2)]
    qraw = bbuf[:, 4096:8192].rearrange("p (t n) -> p t n", t=32)
    kraw = bbuf[:, 8192:12288].rearrange("p (t n) -> p t n", t=32)
    qf_ring = [bbuf[:, 12288 + i * 128: 12416 + i * 128] for i in range(2)]
    kf_ring = [bbuf[:, 12544 + i * 128: 12672 + i * 128] for i in range(2)]
    phiqb = bbuf[:, 12800:14848]
    phikb = bbuf[:, 14848:16896]
    phiqT = bbuf[:, 16896:18944].rearrange("p (j n) -> p j n", j=16)
    ssnap = bbuf[:, 18944:18944 + 1040].rearrange("p (fh h n) -> p fh h n", fh=2, h=8)

    bigA = sb("bigA", [128, HID], F32)
    bigB = sb("bigB", [128, HID], F32)
    bigBbf = bigB.ap().bitcast(BF16)
    quadA = sb("quadA", [128, 512], F32)
    quadI = sb("quadI", [128, 512], I32)
    quadC = sb("quadC", [128, 512], F32)
    b128 = sb("b128", [128, 1], F32)
    xrow_a = sb("xrow", [128, D], F32)
    xrow_b = sb("xrowb", [128, D], F32)
    xrow2 = [xrow_a, xrow_b]
    xrow = xrow_a
    phikT_s = sb("phikT", [128, 2048], BF16)
    phikT = phikT_s.ap().rearrange("p (j n) -> p j n", j=16)
    quadI2 = phikT_s.ap().bitcast(I32)[:, 0:512]
    qfp = sb("qfp", [128, 256], BF16)
    kfp = sb("kfp", [128, 256], BF16)
    qfpT_s = sb("qfpT", [128, 256], BF16)
    kfpT_s = sb("kfpT", [128, 256], BF16)
    qfpT = qfpT_s.ap().rearrange("p (b n) -> p b n", b=2)
    kfpT = kfpT_s.ap().rearrange("p (b n) -> p b n", b=2)
    o_outb = sb("o_outb", [128, 512], BF16)
    ocat = sb("ocat", [128, 1024], BF16)
    ocatT_s = sb("ocatT", [128, 1024], BF16)
    ocatT = ocatT_s.ap().rearrange("p (c n) -> p c n", c=8)
    oslot = [sb(f"oslot{i}", [128, 512], BF16) for i in range(2)]

    lnw_bc = sb("lnw_bc", [128, D], F32)
    lnb_bc = sb("lnb_bc", [128, D], F32)
    ln2w_bc = sb("ln2w_bc", [128, HID], BF16)
    ln2b_bc = sb("ln2b_bc", [128, HID], BF16)
    gamq_bc = sb("gamq_bc", [128, 128], F32)
    betq_bc = sb("betq_bc", [128, 128], F32)
    gamk_bc = sb("gamk_bc", [128, 128], F32)
    betk_bc = sb("betk_bc", [128, 128], F32)
    identq = sb("identq", [128, 128], F32)
    onesq = sb("onesq", [128, 128], F32)

    rows1 = sb("rows1", [97, 512], BF16)
    rows2 = sb("rows2", [97, 512], BF16)
    rows3 = sb("rows3", [97, 512], BF16)
    rows4 = sb("rows4", [97, 512], BF16)
    rowst = sb("rowst", [97, 512], F32)

    def rowslice(t, k, n=512):
        return t[32 * k: 32 * k + 1, 0:n]

    st = lambda name, n=16: sb(name, [128, n], F32)
    sxs, sxq = st("sxs"), st("sxq")
    rr1, nmr1 = st("rr1"), st("nmr1")
    gs8 = sb("gs8", [128, 128], F32)
    gq8 = sb("gq8", [128, 128], F32)
    rr2, nmr2 = st("rr2"), st("nmr2")
    tA, tB, tC, tD = st("tA"), st("tB"), st("tC"), st("tD")
    am1, es1v, ar1 = st("am1", 1), st("es1", 1), st("ar1", 1)
    amP = [st("amP0", 1), st("amP1", 1)]
    esP = [st("esP0", 1), st("esP1", 1)]
    arP = [st("arP0", 1), st("arP1", 1)]
    scP = [st("scP0", 1), st("scP1", 1)]
    am2, es2v, ar2 = st("am2", 1), st("es2", 1), st("ar2", 1)
    wsum = st("wsum", 32)
    wthr, wnthr = st("wthr", 1), st("wnthr", 1)
    wsc1, wsc2 = st("wsc1", 1), st("wsc2", 1)
    onecol = st("onecol", 1)
    arow_s = sb("arow", [97, 256], BF16)
    arow = lambda k, par=0: arow_s[32 * k:32 * k + 1, 128 * par:128 * par + 128]
    onesrow_s = sb("onesrow", [97, 128], F32)
    onesrow = lambda k: onesrow_s[32 * k:32 * k + 1, :]
    scal = sb("scal", [1, 16], F32)
    m0t, m1t = st("m0t", 1), st("m1t", 1)
    asx, asq = st("asx"), st("asq")
    arr_, anmr = st("arr"), st("anmr")
    qsum = bigA[:, 2048:2304]
    qsq = bigA[:, 2304:2560]
    ksum = bigA[:, 2560:2816]
    ksq = bigA[:, 2816:3072]
    qmu = bigA[:, 3072:3328]
    qrs = bigA[:, 3328:3584]
    kmu = bigA[:, 3584:3840]
    krs = bigA[:, 3840:4096]
    fmw = sb("fmw", [128, 128], F32)
    fmw2 = sb("fmw2", [128, 128], F32)
    fmw3 = sb("fmw3", [128, 128], F32)
    zt, zrt = st("zt", 8), st("zrt", 8)
    onebf = sb("onebf", [128, 1], BF16)

    pb = [es.enter_context(nc.psum_tensor(f"pb{i}", [128, 512], F32)) for i in range(8)]

    v_op(lambda e: e.memset(_vpadt[:], 0.0))
    s_op(lambda e: e.activation(out=_spadt[:], in_=_vpadt[:], func=AF.Copy), waits=now("v"))
    v_op(lambda e: e.memset(onecol[:], 1.0))
    v_op(lambda e: e.memset(onesq[:], 1.0))
    v_op(lambda e: e.memset(onebf[:], 1.0))
    v_op(lambda e: e.memset(onesrow_s[:], 1.0))
    v_op(lambda e: e.memset(b128[:], 128.0))
    dma("sync", m0t[:], m0d.ap())
    dma("sync", m1t[:], m1d.ap())
    g_op(lambda e: e.affine_select(out=identq[:], in_=onesq[:], pattern=[[1, 128]],
                                   compare_op=ALU.is_equal, fill=0.0, base=0,
                                   channel_multiplier=-1), waits=now("v"))

    def bc_load(dst_bc, src_ap, n):
        for ci in range(0, n, 512):
            w = min(512, n - ci)
            sl = (ci // 512) % 3
            dma("sync", rowslice(rowst, sl, w), src_ap[0:1, ci:ci + w], waits=now("t"))
            t_op(lambda e, w=w, sl=sl: e.matmul(out=pb[7][:, 0:w], lhsT=onesrow(sl),
                                                rhs=rowslice(rowst, sl, w),
                                                start=True, stop=True),
                 waits=now("ld", "v"))
            v_op(lambda e, ci=ci, w=w, d=dst_bc: e.tensor_copy(out=d[:, ci:ci + w],
                                                               in_=pb[7][:, 0:w]),
                 waits=now("t"))

    bc_load(lnw_bc, lnwd.ap(), D)
    bc_load(lnb_bc, lnbd.ap(), D)
    bc_load(ln2w_bc, ln2wd.ap(), HID)
    bc_load(ln2b_bc, ln2bd.ap(), HID)

    dma("sync", rowslice(rowst, 0, FD), gamd.ap(), waits=now("t"))
    dma("sync", rowslice(rowst, 1, FD), betd.ap(), waits=now("t"))
    t_op(lambda e: e.matmul(out=pb[7][:, 0:128], lhsT=onesrow(0),
                            rhs=rowst[0:1, 0:FD].unsqueeze(1).broadcast_to([1, 8, FD]),
                            start=True, stop=True), waits=now("ld", "v"))
    v_op(lambda e: e.tensor_copy(out=gamk_bc[:], in_=pb[7][:, 0:128]), waits=now("t"))
    t_op(lambda e: e.matmul(out=pb[7][:, 0:128], lhsT=onesrow(1),
                            rhs=rowst[32:33, 0:FD].unsqueeze(1).broadcast_to([1, 8, FD]),
                            start=True, stop=True), waits=now("v"))
    v_op(lambda e: e.tensor_copy(out=betk_bc[:], in_=pb[7][:, 0:128]), waits=now("t"))
    v_op(lambda e: e.tensor_scalar_mul(out=gamq_bc[:], in0=gamk_bc[:], scalar1=0.25))
    v_op(lambda e: e.tensor_scalar_mul(out=betq_bc[:], in0=betk_bc[:], scalar1=0.25))

    _b1tensors = [rows1, rows1, rows1, rows2, rows2, rows2, rows3, rows3]
    for n in range(8):
        dma("gpsimd", rowslice(_b1tensors[n], n % 3), b1d.ap()[0:1, 512 * n:512 * (n + 1)])
    dma("gpsimd", rowslice(rows3, 2), b2d.ap()[0:1, 0:512])
    dma("gpsimd", rowslice(rows4, 0), b2d.ap()[0:1, 512:1024])
    b1chunk = lambda n: rowslice(_b1tensors[n], n % 3)
    b2chunk = lambda n: rowslice(rows3, 2) if n == 0 else rowslice(rows4, 0)
    b2base = lambda n: 2 if n == 0 else 0
    barrier()

    def prep_w(w_tiles, ntiles, freedim, wsc_dst, qT_dst_fn, totelem):
        pc = {}
        bufs = [bigA, bigB]
        for i in range(ntiles):
            buf = bufs[i % 2]
            dma("sync", buf[:, 0:freedim], w_tiles[:, i, :],
                waits=pc.get(("s1", i - 2), now("s", "v", "tp")))
            s_op(lambda e, b=buf, f=freedim, i=i: e.activation(
                out=prepP[:, 0:f], in_=b[:, 0:f], func=AF.Abs, accum_out=wsum[:, i:i + 1]),
                waits=now("ld"))
            pc[("s1", i)] = now("s")
        v_op(lambda e, n=ntiles: e.reduce_sum(out=wthr[:], in_=wsum[:, 0:n], axis=AXX),
             waits=now("s"), pad=True)
        t_op(lambda e: e.matmul(out=pb[7][0:1, 0:1], lhsT=wthr[:], rhs=onecol[:],
                                start=True, stop=True), waits=now("v"))
        v_op(lambda e: e.tensor_copy(out=scal[0:1, 0:1], in_=pb[7][0:1, 0:1]), waits=now("t"), pad=True)
        t_op(lambda e: e.matmul(out=pb[7][:, 0:1], lhsT=onesrow(0), rhs=scal[0:1, 0:1],
                                start=True, stop=True), waits=now("v"))
        v_op(lambda e, n=totelem, d=wsc_dst: e.tensor_scalar_mul(out=d[:], in0=pb[7][:, 0:1],
                                                                 scalar1=1.0 / n), waits=now("t"), pad=True)
        v_op(lambda e, d=wsc_dst: e.tensor_scalar_mul(out=wthr[:], in0=d[:], scalar1=0.5), pad=True)
        v_op(lambda e: e.tensor_scalar_mul(out=wnthr[:], in0=wthr[:], scalar1=-1.0), pad=True)
        for i in range(ntiles):
            buf = bufs[i % 2]
            tern = tern_ring[i % 2]
            dma("sync", buf[:, 0:freedim], w_tiles[:, i, :],
                waits=pc.get(("v2", i - 2), now("v", "tp")))
            v_op(lambda e, b=buf, f=freedim: e.tensor_single_scalar(
                out=prepP[:, 0:f], in_=b[:, 0:f], scalar=wthr[:], op=ALU.is_gt),
                waits=now("ld"))
            v_op(lambda e, b=buf, f=freedim: e.tensor_single_scalar(
                out=prepN[:, 0:f], in_=b[:, 0:f], scalar=wnthr[:], op=ALU.is_lt))
            v_op(lambda e, t_=tern, f=freedim: e.tensor_sub(out=t_[:, 0:f], in0=prepP[:, 0:f],
                                                            in1=prepN[:, 0:f]),
                 waits=pc.get(("tp2", i - 2), []))
            pc[("v2", i)] = now("v")
            dmaT(qT_dst_fn(i), tern[:, 0:freedim],
                 waits=now("v") + (now("ld") if i == 0 else []))
            pc[("tp2", i)] = now("tp")

    prep_w(w2_t, 8, HID, wsc2, lambda i: w2qT[:, :, 128 * i:128 * (i + 1)], D * HID)
    dma("sync", w2qt_st.ap(), wbuf[:, :], waits=now("tp"))
    prep_w(w1_t, 32, D, wsc1, lambda i: w1qT[:, :, 128 * i:128 * (i + 1)], D * HID)
    dma("sync", w1qt_st.ap(), wbuf[:, :], waits=now("tp"))
    barrier()

    def ln_chain(sum_t, sq_t, n, rr_dst, nmr_dst, double_ln):
        v_op(lambda e: e.tensor_scalar_mul(out=tA[:], in0=sum_t[:], scalar1=1.0 / n), waits=now("s"), pad=True)
        v_op(lambda e: e.tensor_scalar_mul(out=tB[:], in0=sq_t[:], scalar1=1.0 / n), pad=True)
        v_op(lambda e: e.tensor_mul(out=tC[:], in0=tA[:], in1=tA[:]), pad=True)
        v_op(lambda e: e.tensor_sub(out=tB[:], in0=tB[:], in1=tC[:]), pad=True)
        if double_ln:
            v_op(lambda e: e.tensor_scalar_add(out=tC[:], in0=tB[:], scalar1=EPS), pad=True)
            v_op(lambda e: e.reciprocal(out=tD[:], in_=tC[:]), pad=True)
            v_op(lambda e: e.tensor_mul(out=tB[:], in0=tB[:], in1=tD[:]), pad=True)
            v_op(lambda e: e.tensor_scalar_add(out=tB[:], in0=tB[:], scalar1=EPS), pad=True)
            v_op(lambda e: e.tensor_mul(out=tB[:], in0=tB[:], in1=tC[:]), pad=True)
            v_op(lambda e: e.reciprocal(out=tB[:], in_=tB[:]), pad=True)
        else:
            v_op(lambda e: e.tensor_scalar_add(out=tB[:], in0=tB[:], scalar1=EPS), pad=True)
            v_op(lambda e: e.reciprocal(out=tB[:], in_=tB[:]), pad=True)
        s_op(lambda e, r=rr_dst: e.activation(out=r[:], in_=tB[:], func=AF.Sqrt), waits=now("v"), pad=True)
        v_op(lambda e, r=rr_dst, m=nmr_dst: e.tensor_mul(out=m[:], in0=tA[:], in1=r[:]),
             waits=now("s"), pad=True)
        v_op(lambda e, m=nmr_dst: e.tensor_scalar_mul(out=m[:], in0=m[:], scalar1=-1.0), pad=True)

    _qsets = [(quadA, quadI), (quadC, quadI2)]
    _qlastv = {0: [], 1: []}

    def quant_to(dst_bf, src_f32, width, s_scale_ap, waits=()):
        for q in range(width // 512):
            sl = slice(512 * q, 512 * (q + 1))
            A, I = _qsets[q % 2]
            s_op(lambda e, sl=sl, A=A: e.activation(out=A[:], in_=src_f32[:, sl],
                                                    func=AF.Identity, scale=s_scale_ap,
                                                    bias=b128[:]),
                 waits=tuple((list(waits) + now("v")) if q == 0 else ()) + tuple(_qlastv[q % 2]))
            s_op(lambda e, A=A, I=I: e.activation(out=I[:], in_=A[:], func=AF.Copy))
            sw = now("s")
            v_op(lambda e, A=A, I=I: e.scalar_tensor_tensor(out=A[:], in0=A[:], scalar=0.5,
                                                            in1=I[:], op0=ALU.subtract,
                                                            op1=ALU.is_ge), waits=sw)
            v_op(lambda e, sl=sl, A=A, I=I: e.scalar_tensor_tensor(out=dst_bf[:, sl], in0=I[:],
                                                                   scalar=-128.0, in1=A[:],
                                                                   op0=ALU.add, op1=ALU.add))
            _qlastv[q % 2] = now("v")

    dbgS = dt("dbg_S", [32 * 128, 1040], F32, kind="ExternalOutput") if dbg else None
    dbgP = dt("dbg_P", [32 * 128, 2048], BF16, kind="ExternalOutput") if dbg else None
    dbgPt = tlv(dbgP.ap()) if dbg else None
    dbgK = dt("dbg_K", [32 * 128, 128], BF16, kind="ExternalOutput") if dbg else None
    dbgKt = tlv(dbgK.ap()) if dbg else None
    dbgQ = dt("dbg_Q", [32 * 128, 128], BF16, kind="ExternalOutput") if dbg else None
    dbgQt = tlv(dbgQ.ap()) if dbg else None
    dbgSt = tlv(dbgS.ap()) if dbg else None
    dbgx = dt("dbg_x", [TOK, D], BF16, kind="ExternalOutput") if dbg else None
    dbgxt = tlv(dbgx.ap()) if dbg else None
    amdbg = sb("amdbg", [128, 16], F32)
    esdbg = sb("esdbg", [128, 16], F32)

    SCR = bigA[:, 2048:2560]
    fa_t1 = [bigB[:, 0:D], bigB[:, D:2 * D]]
    fa_h1 = [bigA[:, 0:D], bigA[:, D:2 * D]]
    fb_h = [bigA, bigB]
    xq1b2 = [bbuf[:, 16384:17408], bbuf[:, 18432:19456]]
    xq1T2 = [bbuf[:, 17408:18432].rearrange("p (c n) -> p c n", c=8),
             bbuf[:, 19456:20480].rearrange("p (c n) -> p c n", c=8)]
    xq2b2 = [bbuf[:, 8192:12288], bbuf[:, 12288:16384]]
    xq2T2 = [bbuf[:, 0:4096].rearrange("p (c n) -> p c n", c=32),
             bbuf[:, 4096:8192].rearrange("p (c n) -> p c n", c=32)]
    wsc127_1, wsc127_2 = st("w127a", 1), st("w127b", 1)
    caps = {}
    _cid = [0]

    def ffn(xin_tiles, xout_tiles):
        cid = _cid[0]
        _cid[0] += 1

        def cap(tag, t, *chs):
            caps[(cid, tag, t)] = now(*chs)

        def get(tag, t, fb=None):
            k = (cid, tag, t)
            if k in caps:
                return caps[k]
            return dep_all() if fb is None else fb

        for t in range(NTT):
            par = t % 2
            dma("sync", xrow2[par][:], xin_tiles[:, t, :], waits=get("sts", t - 2))
            v_op(lambda e, t=t, par=par: e.reduce_sum(out=sxs[:, t:t + 1], in_=xrow2[par][:],
                                                      axis=AXX), waits=now("ld"))
            s_op(lambda e, t=t, par=par: e.activation(out=fa_h1[par], in_=xrow2[par][:],
                                                      func=AF.Square, accum_out=sxq[:, t:t + 1]),
                 waits=now("ld"))
            cap("sts", t, "v", "s")
        ln_chain(sxs, sxq, D, rr1, nmr1, double_ln=True)
        _qlastv[0] = dep_all()
        _qlastv[1] = dep_all()
        v_op(lambda e: e.tensor_scalar_mul(out=wsc127_1[:], in0=wsc1[:], scalar1=1.0 / 127.0), pad=True)
        v_op(lambda e: e.tensor_scalar_mul(out=wsc127_2[:], in0=wsc2[:], scalar1=0.5 / 127.0), pad=True)

        def prepA(t):
            par = t % 2
            dma("sync", xrow2[par][:], xin_tiles[:, t, :],
                waits=get("At1", t - 2, None if t < 2 else []))
            s_op(lambda e, t=t, par=par: e.activation(out=fa_t1[par], in_=xrow2[par][:],
                                                      func=AF.Identity, bias=nmr1[:, t:t + 1],
                                                      scale=rr1[:, t:t + 1]),
                 waits=now("ld") + get("Apv", t - 2, [] if t >= 2 else None))
            cap("At1", t, "s")
            v_op(lambda e, par=par: e.tensor_mul(out=fa_h1[par], in0=fa_t1[par], in1=lnw_bc[:]),
                 waits=now("s"))
            v_op(lambda e, par=par: e.tensor_add(out=fa_h1[par], in0=fa_h1[par], in1=lnb_bc[:]))
            v_op(lambda e, par=par: e.tensor_reduce(out=amP[par][:], in_=fa_h1[par], axis=AXX,
                                                    op=ALU.max, apply_absolute_value=True), pad=True)
            v_op(lambda e, par=par: e.tensor_scalar_max(out=amP[par][:], in0=amP[par][:],
                                                        scalar1=EPS), pad=True)
            v_op(lambda e, par=par: e.tensor_mul(out=esP[par][:], in0=amP[par][:], in1=wsc127_1[:]),
                 waits=get("AevAll", t - 2, []), pad=True)
            v_op(lambda e, par=par: e.reciprocal(out=arP[par][:], in_=esP[par][:]),
                 waits=get("AatT", t - 2, []), pad=True)
            v_op(lambda e, par=par: e.tensor_mul(out=scP[par][:], in0=arP[par][:], in1=wsc1[:]), pad=True)
            quant_to(xq1b2[par], fa_h1[par], D, scP[par][:], waits=get("Atp", t - 2, []))
            dmaT(xq1T2[par], xq1b2[par], waits=now("v") + get("Amm", t - 2, []))
            cap("Atp", t, "tp")
            cap("Apv", t, "v")

        def mmA(t):
            par = t % 2
            t_op(lambda e, par=par: e.transpose(out=pb[7][0:1, 0:128], in_=arP[par][:],
                                                identity=identq[:]),
                 waits=get("Apv", t) + get("Agel7", t - 1, []))
            cap("AatT", t, "t")
            for k in range(3):
                v_op(lambda e, k=k, par=par: e.tensor_copy(out=arow(k, par), in_=pb[7][0:1, 0:128]),
                     waits=(now("t") if k == 0 else ()))
            cap("Aar", t, "v")
            for n in range(8):
                bank = pb[7] if n == 7 else pb[n]
                if n == 0:
                    w0 = get("Atp", t) + get("Agel0", t - 1, [])
                elif n == 7:
                    w0 = get("Aar", t) + get("Agel7", t - 1, [])
                else:
                    w0 = get("Agel%d" % n, t - 1, [])
                for kc in range(8):
                    t_op(lambda e, kc=kc, n=n, bank=bank, par=par: e.matmul(
                        out=bank[:, 0:512], lhsT=xq1T2[par][:, kc, :],
                        rhs=w1qT[:, kc, 512 * n:512 * (n + 1)],
                        start=(kc == 0), stop=False),
                        waits=(w0 if kc == 0 else ()))
                t_op(lambda e, n=n, bank=bank, par=par: e.matmul(
                    out=bank[:, 0:512], lhsT=arow(n % 3, par), rhs=b1chunk(n),
                    start=False, stop=True),
                    waits=(get("Aar", t) if n == 0 else ()))
                wt_n = now("t")
                s_op(lambda e, n=n, bank=bank, par=par, t=t: e.activation(
                    out=g_ring[par][:, 512 * n:512 * (n + 1)], in_=bank[:, 0:512], func=AF.Gelu,
                    scale=esP[par][:], accum_out=gs8[:, 8 * t + n: 8 * t + n + 1]),
                    waits=wt_n + (get("Agst", t - 2, []) if n == 0 else []))
                s_op(lambda e, n=n, par=par, t=t: e.activation(
                    out=SCR, in_=g_ring[par][:, 512 * n:512 * (n + 1)],
                    func=AF.Square, accum_out=gq8[:, 8 * t + n: 8 * t + n + 1]))
                cap("Agel%d" % n, t, "s")
            cap("AevAll", t, "s")
            cap("Amm", t, "t")
            dma("sync", gst_t[:, t, :], g_ring[par][:], waits=now("s"))
            cap("Agst", t, "ld")

        prepA(0)
        prepA(1)
        for t in range(NTT):
            mmA(t)
            if t + 2 < NTT:
                prepA(t + 2)

        v_op(lambda e: e.tensor_reduce(out=sxs[:], in_=gs8.ap().rearrange("p (t n) -> p t n", n=8),
                                       axis=AXX, op=ALU.add), waits=now("s"))
        v_op(lambda e: e.tensor_reduce(out=sxq[:], in_=gq8.ap().rearrange("p (t n) -> p t n", n=8),
                                       axis=AXX, op=ALU.add))
        ln_chain(sxs, sxq, HID, rr2, nmr2, double_ln=False)
        dma("sync", wbuf[:, :], w2qt_st.ap(), waits=now("t"))

        def prepB(t):
            par = t % 2
            dma("sync", g_ring[par][:], gst_t[:, t, :], waits=get("Bmm", t - 2))
            dma("sync", xrow2[par][:], xin_tiles[:, t, :], waits=get("BevC1", t - 2))
            cap("Bxld", t, "ld")
            s_op(lambda e, t=t, par=par: e.activation(out=fb_h[par][:], in_=g_ring[par][:],
                                                      func=AF.Identity, bias=nmr2[:, t:t + 1],
                                                      scale=rr2[:, t:t + 1]),
                 waits=now("ld") + get("Bpv", t - 2, [] if t >= 2 else None))
            cap("Bt2", t, "s")
            v_op(lambda e, par=par: e.tensor_mul(out=fb_h[par][:], in0=fb_h[par][:], in1=ln2w_bc[:]),
                 waits=now("s"))
            v_op(lambda e, par=par: e.tensor_add(out=fb_h[par][:], in0=fb_h[par][:], in1=ln2b_bc[:]))
            v_op(lambda e, par=par: e.tensor_reduce(out=amP[par][:], in_=fb_h[par][:], axis=AXX,
                                                    op=ALU.max, apply_absolute_value=True), pad=True)
            v_op(lambda e, par=par: e.tensor_scalar_max(out=amP[par][:], in0=amP[par][:],
                                                        scalar1=EPS), pad=True)
            v_op(lambda e, par=par: e.tensor_mul(out=esP[par][:], in0=amP[par][:], in1=wsc127_2[:]), pad=True)
            v_op(lambda e, par=par: e.reciprocal(out=scP[par][:], in_=esP[par][:]), pad=True)
            v_op(lambda e, par=par: e.tensor_scalar_mul(out=arP[par][:], in0=scP[par][:],
                                                        scalar1=0.5),
                 waits=get("BatT", t - 2, []), pad=True)
            v_op(lambda e, par=par: e.tensor_mul(out=scP[par][:], in0=arP[par][:], in1=wsc2[:]), pad=True)
            quant_to(xq2b2[par], fb_h[par], HID, scP[par][:], waits=get("Btp", t - 2, []))
            dmaT(xq2T2[par], xq2b2[par], waits=now("v", "s") + get("Bmm", t - 2, []))
            cap("Btp", t, "tp")
            cap("Bpv", t, "v")

        def mmB(t):
            par = t % 2
            t_op(lambda e, par=par: e.transpose(out=pb[7][0:1, 0:128], in_=arP[par][:],
                                                identity=identq[:]),
                 waits=get("Bpv", t) + get("Bar", t - 1, []))
            cap("BatT", t, "t")
            for j, k in enumerate((0, 2)):
                v_op(lambda e, k=k, par=par: e.tensor_copy(out=arow(k, par), in_=pb[7][0:1, 0:128]),
                     waits=(now("t") if j == 0 else ()))
            cap("Bar", t, "v")
            for n in range(2):
                w0 = (get("Btp", t) if n == 0 else []) + get("BevC%d" % n, t - 1, [])
                for kc in range(32):
                    t_op(lambda e, kc=kc, n=n, par=par: e.matmul(
                        out=pb[n][:, 0:512], lhsT=xq2T2[par][:, kc, :],
                        rhs=w2qT[:, kc, 512 * n:512 * (n + 1)],
                        start=(kc == 0), stop=False),
                        waits=(w0 if kc == 0 else ()))
                t_op(lambda e, n=n, par=par: e.matmul(out=pb[n][:, 0:512],
                                                      lhsT=arow(b2base(n), par), rhs=b2chunk(n),
                                                      start=False, stop=True),
                     waits=(get("Bar", t) if n == 0 else ()))
                wt_n = now("t")
                v_op(lambda e, n=n, par=par: e.scalar_tensor_tensor(
                    out=xrow2[par][:, 512 * n:512 * (n + 1)], in0=pb[n][:, 0:512],
                    scalar=esP[par][:], in1=xrow2[par][:, 512 * n:512 * (n + 1)],
                    op0=ALU.mult, op1=ALU.add),
                    waits=wt_n + (get("Bxld", t) if n == 0 else []))
                cap("BevC%d" % n, t, "v")
            cap("Bmm", t, "t")
            dma("sync", xout_tiles[:, t, :], xrow2[par][:], waits=now("v"))

        prepB(0)
        prepB(1)
        for t in range(NTT):
            mmB(t)
            if t + 2 < NTT:
                prepB(t + 2)
        barrier()

    sTev_b = [sb("sTe0", [128, 512], BF16), sb("sTe1", [128, 512], BF16)]
    sTm_b = [sb("sTmA", [128, 512], BF16), sb("sTmB", [128, 512], BF16)]
    phiqb2 = [phiqb, bbuf[:, 0:2048]]
    phikb2 = [phikb, bbuf[:, 2048:4096]]
    phiqT2 = [phiqT, phikT_s.ap().rearrange("p (j n) -> p j n", j=16)]
    qfp2 = [qfp[:], wbuf[:, 30976:31232]]
    kfp2 = [kfp[:], wbuf[:, 31232:31488]]
    qfpT2 = [qfpT, wbuf[:, 31488:31744].rearrange("p (b n) -> p b n", b=2)]
    kfpT2 = [kfpT, wbuf[:, 31744:32000].rearrange("p (b n) -> p b n", b=2)]
    fmwQ2 = [fmw, fmw3]
    fmwK2 = [fmw2, quadA[:, 128:256]]
    sqscr = quadA[:, 0:128]
    o_outb2 = [o_outb, oslot[0]]
    zt2 = [zt, st("ztb", 8)]
    zrt2 = [zrt, st("zrtb", 8)]
    ocat2 = [ocat[:], quadA.ap().bitcast(BF16)]
    ocatT2 = [ocatT, quadC.ap().bitcast(BF16).rearrange("p (c n) -> p c n", c=8)]
    os4 = [oslot[0][:], oslot[1][:],
           quadI.ap().bitcast(BF16)[:, 0:512], quadI.ap().bitcast(BF16)[:, 512:1024]]

    def attention():
        acaps = {}

        def cap(tag, t, *chs):
            acaps[(tag, t)] = now(*chs)

        def get(tag, t, fb=None):
            k = (tag, t)
            if k in acaps:
                return acaps[k]
            return dep_all() if fb is None else fb

        stg = bbuf[:, 0:1024]
        dma("gpsimd", stg, wqd.ap(), waits=now("tp", "t", "s", "v"))
        dmaT(wqT, stg, waits=now("gd"))
        dma("gpsimd", stg, wkd.ap(), waits=now("tp"))
        dmaT(wkT, stg, waits=now("gd"))
        for j in range(4):
            dma("gpsimd", stg, wv_t[:, j, :], waits=now("tp"))
            dmaT(wvT[:, :, 128 * j:128 * (j + 1)], stg, waits=now("gd"))
        for j in range(8):
            dma("gpsimd", stg, wo_t[:, j, :], waits=now("tp"))
            dmaT(woT[:, :, 128 * j:128 * (j + 1)], stg, waits=now("gd"))
        barrier()
        v_op(lambda e: e.memset(v1t[:, :, :, 64:65], 1.0))

        for t in range(NTT):
            par = t % 2
            dma("sync", xrow2[par][:], x1_t[:, t, :], waits=get("Lst", t - 2))
            v_op(lambda e, t=t, par=par: e.reduce_sum(out=asx[:, t:t + 1], in_=xrow2[par][:],
                                                      axis=AXX), waits=now("ld"))
            s_op(lambda e, t=t, par=par: e.activation(out=bigA[:, 0:D], in_=xrow2[par][:],
                                                      func=AF.Square, accum_out=asq[:, t:t + 1]),
                 waits=now("ld"))
            cap("Lst", t, "v", "s")
        ln_chain(asx, asq, D, arr_, anmr, double_ln=False)
        for t in range(NTT):
            par = t % 2
            dma("sync", xrow2[par][:], x1_t[:, t, :], waits=get("Lap", t - 2, [] if t >= 2 else None))
            s_op(lambda e, t=t, par=par: e.activation(out=bigBbf[:, 1024 * par:1024 * par + D],
                                                      in_=xrow2[par][:], func=AF.Identity,
                                                      bias=anmr[:, t:t + 1], scale=arr_[:, t:t + 1]),
                 waits=now("ld") + get("Lsd", t - 2, []))
            cap("Lap", t, "s")
            dma("sync", ag1in_t[:, t, :], bigBbf[:, 1024 * par:1024 * par + D], waits=now("s"))
            cap("Lsd", t, "ld")
        op("gpsimd", lambda e: e.collective_compute(
            "AllGather", ALU.bypass, replica_groups=[[0, 1], [2, 3], [4, 5], [6, 7]],
            ins=[ag1_in.ap().opt()], outs=[ag1_out.ap().opt()]),
            waits=now("ld"), inc="cc")

        def pprep(c):
            par = c % 2
            dma("sync", xlnc_ring[par], ag1out_t[:, c, :],
                waits=get("Ptp", c - 2, [] if c >= 2 else None) + (now("cc") if c == 0 else []))
            dmaT(xlncT_ring[par], xlnc_ring[par], waits=now("ld") + get("Pmm", c - 2, []))
            cap("Ptp", c, "tp")

        def pmm(c):
            par = c % 2
            w0 = get("Ptp", c) + get("Pev", c - 1, [])
            for kc in range(8):
                t_op(lambda e, kc=kc, par=par: e.matmul(out=pb[0][:, 0:128],
                                                        lhsT=xlncT_ring[par][:, kc, :],
                                                        rhs=wqT[:, kc, :], start=(kc == 0),
                                                        stop=(kc == 7)),
                     waits=(w0 if kc == 0 else ()))
            for kc in range(8):
                t_op(lambda e, kc=kc, par=par: e.matmul(out=pb[1][:, 0:128],
                                                        lhsT=xlncT_ring[par][:, kc, :],
                                                        rhs=wkT[:, kc, :], start=(kc == 0),
                                                        stop=(kc == 7)))
            for kc in range(8):
                t_op(lambda e, kc=kc, par=par: e.matmul(out=pb[2][:, 0:512],
                                                        lhsT=xlncT_ring[par][:, kc, :],
                                                        rhs=wvT[:, kc, :], start=(kc == 0),
                                                        stop=(kc == 7)))
            cap("Pmm", c, "t")
            s_op(lambda e, par=par: e.activation(out=fmwQ2[par][:], in_=pb[0][:, 0:128],
                                                 func=AF.Copy),
                 waits=now("t") + get("Pst", c - 2, []))
            s_op(lambda e, par=par: e.activation(out=fmwK2[par][:], in_=pb[1][:, 0:128],
                                                 func=AF.Copy))
            s_op(lambda e, c=c: e.activation(
                out=v1t[:, c, :, 0:64], in_=pb[2][:, 0:512].rearrange("p (h n) -> p h n", h=8),
                func=AF.Copy))
            cap("Pev", c, "s")
            v_op(lambda e, c=c, par=par: e.tensor_copy(out=qraw[:, c, :], in_=fmwQ2[par][:]),
                 waits=now("s"))
            v_op(lambda e, c=c, par=par: e.tensor_copy(out=kraw[:, c, :], in_=fmwK2[par][:]))
            v_op(lambda e, c=c, par=par: e.reduce_sum(
                out=qsum[:, 8 * c:8 * c + 8],
                in_=fmwQ2[par].ap().rearrange("p (h f) -> p h f", f=16) if False
                else fmwQ2[par][:].rearrange("p (h f) -> p h f", f=16), axis=AXX))
            v_op(lambda e, c=c, par=par: e.reduce_sum(
                out=ksum[:, 8 * c:8 * c + 8],
                in_=fmwK2[par][:].rearrange("p (h f) -> p h f", f=16), axis=AXX))
            v_op(lambda e, par=par: e.tensor_mul(out=sqscr[:], in0=fmwQ2[par][:], in1=fmwQ2[par][:]))
            v_op(lambda e, c=c: e.reduce_sum(out=qsq[:, 8 * c:8 * c + 8],
                                             in_=sqscr[:].rearrange("p (h f) -> p h f", f=16),
                                             axis=AXX))
            v_op(lambda e, par=par: e.tensor_mul(out=sqscr[:], in0=fmwK2[par][:], in1=fmwK2[par][:]))
            v_op(lambda e, c=c: e.reduce_sum(out=ksq[:, 8 * c:8 * c + 8],
                                             in_=sqscr[:].rearrange("p (h f) -> p h f", f=16),
                                             axis=AXX))
            cap("Pst", c, "v")

        pprep(0)
        pprep(1)
        for c in range(NCH):
            pmm(c)
            if c + 2 < NCH:
                pprep(c + 2)

        def fm_stats(sumt, sqt_, mut, rst):
            v_op(lambda e: e.tensor_scalar_mul(out=mut[:], in0=sumt[:], scalar1=1.0 / FD),
                 waits=now("s", "v"), pad=True)
            v_op(lambda e: e.tensor_scalar_mul(out=rst[:], in0=sqt_[:], scalar1=1.0 / FD), pad=True)
            v_op(lambda e: e.tensor_mul(out=sqt_[:], in0=mut[:], in1=mut[:]))
            v_op(lambda e: e.tensor_sub(out=rst[:], in0=rst[:], in1=sqt_[:]), pad=True)
            v_op(lambda e: e.tensor_scalar_add(out=rst[:], in0=rst[:], scalar1=EPS), pad=True)
            v_op(lambda e: e.reciprocal(out=rst[:], in_=rst[:]), pad=True)
            s_op(lambda e: e.activation(out=rst[:], in_=rst[:], func=AF.Sqrt), waits=now("v"), pad=True)
        fm_stats(qsum, qsq, qmu, qrs)
        fm_stats(ksum, ksq, kmu, krs)
        barrier()

        Sf = [bigA[:, 0:520], bigA[:, 520:1040]]
        v_op(lambda e: e.memset(Sf[0], 0.0))
        v_op(lambda e: e.memset(Sf[1], 0.0), pad=True)
        for pp in range(2):
            v_op(lambda e, pp=pp: e.memset(qfp2[pp], 0.0))
            v_op(lambda e, pp=pp: e.memset(kfp2[pp], 0.0))

        def sprep(c):
            par = c % 2
            rq, rk = qf_ring[par], kf_ring[par]

            def fm_apply(raw, mu, rs, gbc, bbc, dstr):
                mu_b = mu[:, 8 * c:8 * c + 8].unsqueeze(2).broadcast_to([128, 8, 16])
                rs_b = rs[:, 8 * c:8 * c + 8].unsqueeze(2).broadcast_to([128, 8, 16])
                src3 = raw[:, c, :].rearrange("p (h f) -> p h f", f=16)
                v_op(lambda e, s3=src3, mb=mu_b: e.tensor_sub(
                    out=fmw.ap().rearrange("p (h f) -> p h f", f=16),
                    in0=s3, in1=mb))
                v_op(lambda e, rb=rs_b: e.tensor_mul(out=fmw.ap().rearrange("p (h f) -> p h f", f=16),
                                                     in0=fmw.ap().rearrange("p (h f) -> p h f", f=16),
                                                     in1=rb))
                v_op(lambda e, g=gbc: e.tensor_mul(out=fmw[:], in0=fmw[:], in1=g[:]))
                v_op(lambda e, d=dstr, b=bbc: e.tensor_add(out=d, in0=fmw[:], in1=b[:]))
            fm_apply(qraw, qmu, qrs, gamq_bc, betq_bc, rq)
            fm_apply(kraw, kmu, krs, gamk_bc, betk_bc, rk)

            def phi_build(src, dst, w):
                i0 = src.rearrange("p (h f) -> p h f", f=16).unsqueeze(3).broadcast_to([128, 8, 16, 16])
                i1 = src.rearrange("p (h f) -> p h f", f=16).unsqueeze(2).broadcast_to([128, 8, 16, 16])
                v_op(lambda e, d=dst: e.tensor_tensor(
                    out=d.rearrange("p (h a b) -> p h a b", a=16, b=16), in0=i0, in1=i1,
                    op=ALU.mult), waits=w)
            phi_build(rq, phiqb2[par], get("Stp", c - 2, []))
            phi_build(rk, phikb2[par], get("Smm", c - 2, []))
            v_op(lambda e, rq=rq, par=par: e.tensor_copy(
                out=qfp2[par].rearrange("p (h n) -> p h n", h=8)[:, :, 0:16],
                in_=rq.rearrange("p (h f) -> p h f", f=16)))
            v_op(lambda e, rk=rk, par=par: e.tensor_copy(
                out=kfp2[par].rearrange("p (h n) -> p h n", h=8)[:, :, 0:16],
                in_=rk.rearrange("p (h f) -> p h f", f=16)))
            dmaT(phiqT2[par], phiqb2[par], waits=now("v") + get("Smm", c - 2, []))
            dmaT(qfpT2[par], qfp2[par], waits=())
            dmaT(kfpT2[par], kfp2[par], waits=())
            cap("Spv", c, "v")
            cap("Stp", c, "tp")

        def smm(c):
            par = c % 2
            if c > 0:
                s_op(lambda e: e.activation(
                    out=ssnap[:, 0, :, :], in_=Sf[0].rearrange("p (h n) -> p h n", h=8),
                    func=AF.Copy), waits=get("Sadd", c - 1))
                s_op(lambda e: e.activation(
                    out=ssnap[:, 1, :, :], in_=Sf[1].rearrange("p (h n) -> p h n", h=8),
                    func=AF.Copy))
            cap("Ssnap", c, "s")

            sbanks = {(0, 0): pb[0], (1, 0): pb[1], (0, 1): pb[2], (1, 1): pb[7]}
            state_mms = []
            first_state = [True]
            for fh in range(2):
                for hh in range(2):
                    bank = sbanks[(fh, hh)]
                    for j in range(4):
                        h = 4 * hh + j
                        def emit_state(bank=bank, j=j, h=h, fh=fh):
                            w0 = []
                            if first_state[0]:
                                w0 = get("Sadd", c - 1) + get("Spv", c)
                                first_state[0] = False
                            t_op(lambda e, bank=bank, j=j, h=h, fh=fh, c=c, par=par: e.matmul(
                                out=bank[:, 65 * j:65 * j + 65],
                                lhsT=phikb2[par][:, 256 * h + 128 * fh: 256 * h + 128 * (fh + 1)],
                                rhs=v1t[:, c, h, 0:65],
                                start=(j == 0), stop=(j == 3)),
                                waits=w0)
                        state_mms.append(emit_state)
            sidx = 0
            for b in range(2):
                for j in range(4):
                    h = 4 * b + j
                    blk = h // 4
                    base = 32 * (h % 4)
                    tpos = (96, 0) if base == 96 else None
                    w0 = []
                    if j == 0:
                        w0 = get("Stp", c) + get("Ssq%d" % b, c - 1, [])
                    t_op(lambda e, j=j, b=b, blk=blk, base=base, tpos=tpos, par=par: e.matmul(
                        out=pb[3 + b][:, 128 * j:128 * j + 128],
                        lhsT=kfpT2[par][base:base + 16, blk, :],
                        rhs=qfpT2[par][base:base + 16, blk, :],
                        start=True, stop=True, tile_position=tpos),
                        waits=w0)
                    state_mms[sidx]()
                    sidx += 1
                tcap = now("t")
                s_op(lambda e, b=b: e.activation(out=sTev_b[b][:], in_=pb[3 + b][:, 0:512],
                                                 func=AF.Square),
                     waits=tcap + get("Smk%d" % b, c - 1, []))
                cap("Ssq%d" % b, c, "s")
                g_op(lambda e, b=b: e.affine_select(
                    out=sTm_b[b].ap().rearrange("p (h n) -> p h n", h=4),
                    in_=sTev_b[b].ap().rearrange("p (h n) -> p h n", h=4),
                    pattern=[[0, 4], [1, 128]], compare_op=ALU.is_ge, fill=0.0, base=0,
                    channel_multiplier=-1), waits=now("s") + get("Soi%d" % b, c - 1, []))
                cap("Smk%d" % b, c, "g")
            while sidx < 16:
                state_mms[sidx]()
                sidx += 1
            stcap = now("t")
            for b in range(2):
                ob = pb[5 + b]
                for j in range(4):
                    h = 4 * b + j
                    col = 65 * j
                    w0 = get("Smk%d" % b, c) + (get("Sonorm", c - 1, []) if j == 0 else [])
                    t_op(lambda e, ob=ob, col=col, b=b, j=j, c=c: e.matmul(
                        out=ob[:, col:col + 65], lhsT=sTm_b[b][:, 128 * j:128 * j + 128],
                        rhs=v1t[:, c, 4 * b + j, 0:65],
                        start=(j == 0), stop=(j == 3 and c == 0)),
                        waits=(w0 if j == 0 else ()))
                    if c > 0:
                        for fh in range(2):
                            t_op(lambda e, ob=ob, col=col, h=h, fh=fh, j=j, par=par: e.matmul(
                                out=ob[:, col:col + 65], lhsT=phiqT2[par][:, 2 * h + fh, :],
                                rhs=ssnap[:, fh, h, :],
                                start=False, stop=(j == 3 and fh == 1)),
                                waits=(get("Ssnap", c) if (b == 0 and j == 0 and fh == 0) else ()))
                cap("Soi%d" % b, c, "t")
            cap("Smm", c, "t")
            v_op(lambda e: e.tensor_add(out=Sf[0][:, 0:260], in0=Sf[0][:, 0:260],
                                        in1=pb[0][:, 0:260]), waits=stcap)
            v_op(lambda e: e.tensor_add(out=Sf[0][:, 260:520], in0=Sf[0][:, 260:520],
                                        in1=pb[2][:, 0:260]))
            v_op(lambda e: e.tensor_add(out=Sf[1][:, 0:260], in0=Sf[1][:, 0:260],
                                        in1=pb[1][:, 0:260]))
            v_op(lambda e: e.tensor_add(out=Sf[1][:, 260:520], in0=Sf[1][:, 260:520],
                                        in1=pb[7][:, 0:260]))
            cap("Sadd", c, "v")
            v_op(lambda e, par=par: e.tensor_copy(
                out=zt2[par][:, 0:4],
                in_=pb[5][:, 0:260].rearrange("p (h n) -> p h n", n=65)[:, :, 64:65]),
                waits=get("Soi1", c) + get("Sonorm", c - 2, []), pad=True)
            v_op(lambda e, par=par: e.tensor_copy(
                out=zt2[par][:, 4:8],
                in_=pb[6][:, 0:260].rearrange("p (h n) -> p h n", n=65)[:, :, 64:65]), pad=True)
            v_op(lambda e, par=par: e.tensor_scalar_add(out=zt2[par][:], in0=zt2[par][:],
                                                        scalar1=EPS), pad=True)
            v_op(lambda e, par=par: e.reciprocal(out=zrt2[par][:], in_=zt2[par][:]), pad=True)
            for h in range(NH):
                ob = pb[5] if h < 4 else pb[6]
                col = 65 * (h % 4)
                s_op(lambda e, h=h, ob=ob, col=col, par=par: e.activation(
                    out=o_outb2[par][:, 64 * h:64 * h + 64], in_=ob[:, col:col + 64],
                    func=AF.Copy, scale=zrt2[par][:, h:h + 1]),
                    waits=(now("v") + get("Sost", c - 2, []) if h == 0 else ()))
            cap("Sonorm", c, "s")
            dma("sync", ag2in_t[:, c, :], o_outb2[par][:], waits=now("s"))
            cap("Sost", c, "ld")

        sprep(0)
        sprep(1)
        for c in range(NCH):
            smm(c)
            if c + 2 < NCH:
                sprep(c + 2)

        op("gpsimd", lambda e: e.collective_compute(
            "AllGather", ALU.bypass, replica_groups=[[0, 1], [2, 3], [4, 5], [6, 7]],
            ins=[ag2_in.ap().opt()], outs=[ag2_out.ap().opt()]),
            waits=now("ld"), inc="cc")

        def wprep(t):
            par = t % 2
            rows = (t, t + 16, 32 + t, 48 + t)
            dma("sync", os4[0], ag2out_t[:, rows[0], :],
                waits=get("Wcmb", t - 1, [] if t >= 1 else None) + (now("cc") if t == 0 else []))
            for k in range(1, 4):
                dma("sync", os4[k], ag2out_t[:, rows[k], :])
            for half in range(2):
                sl = slice(512 * half, 512 * (half + 1))
                v_op(lambda e, sl=sl, par=par, half=half: e.tensor_single_scalar(
                    out=ocat2[par][:, sl], in_=os4[2 * half], scalar=m0t[:], op=ALU.mult),
                    waits=(now("ld") + get("Wtp", t - 2, []) if half == 0 else ()))
                v_op(lambda e, sl=sl, par=par, half=half: e.scalar_tensor_tensor(
                    out=ocat2[par][:, sl], in0=os4[2 * half + 1], scalar=m1t[:],
                    in1=ocat2[par][:, sl], op0=ALU.mult, op1=ALU.add))
            cap("Wcmb", t, "v")
            dmaT(ocatT2[par], ocat2[par], waits=now("v") + get("Wmm", t - 2, []))
            cap("Wtp", t, "tp")
            dma("sync", xrow2[par][:], x1_t[:, t, :], waits=get("Wadd", t - 2, []))
            cap("Wxld", t, "ld")

        def wmm(t):
            par = t % 2
            w0 = get("Wtp", t) + get("Wadd", t - 1, [])
            for kc in range(8):
                for n in range(2):
                    t_op(lambda e, kc=kc, n=n, par=par: e.matmul(
                        out=pb[n][:, 0:512], lhsT=ocatT2[par][:, kc, :],
                        rhs=woT[:, kc, 512 * n:512 * (n + 1)],
                        start=(kc == 0), stop=(kc == 7)),
                        waits=(w0 if (kc == 0 and n == 0) else ()))
            cap("Wmm", t, "t")
            for n in range(2):
                v_op(lambda e, n=n, par=par: e.tensor_add(
                    out=xrow2[par][:, 512 * n:512 * (n + 1)], in0=pb[n][:, 0:512],
                    in1=xrow2[par][:, 512 * n:512 * (n + 1)]),
                    waits=(now("t") + get("Wxld", t) if n == 0 else ()))
            cap("Wadd", t, "v")
            dma("sync", x2_t[:, t, :], xrow2[par][:], waits=now("v"))
            cap("Wst", t, "ld")

        wprep(0)
        wprep(1)
        for t in range(NTT):
            wmm(t)
            if t + 2 < NTT:
                wprep(t + 2)
        barrier()

    if stage == "prep":
        pass
    else:
        ffn(xin_t, x1_t)
    if stage in ("attn", "full"):
        attention()
    if stage == "prep":
        pass
    elif stage == "full":
        dma("sync", wbuf[:, :], w1qt_st.ap(), waits=now("t", "s", "v", "tp", "g"))
        barrier()
        ffn(x2_t, out_t)
    elif stage != "prep":
        src_t = x1_t if stage == "ffn1" else x2_t
        for t in range(NTT):
            dma("sync", xrow[:], src_t[:, t, :], waits=now("v"))
            v_op(lambda e: e.tensor_copy(out=bigA[:, 0:D], in_=xrow[:]), waits=now("ld"))
            dma("sync", out_t[:, t, :], bigA[:, 0:D], waits=now("v"))
    else:
        for t in range(NTT):
            dma("sync", xrow[:], xin_t[:, t, :], waits=now("v"))
            v_op(lambda e: e.tensor_copy(out=bigA[:, 0:D], in_=xrow[:]), waits=now("ld"))
            dma("sync", out_t[:, t, :], bigA[:, 0:D], waits=now("v"))

    if dbg:
        dbgg = dt("dbg_g", [TOK, HID], BF16, kind="ExternalOutput")
        dbgw = dt("dbg_w", [128, 8 * HID], BF16, kind="ExternalOutput")
        dbgs = dt("dbg_s", [128, 64], F32, kind="ExternalOutput")
        dma("sync", dbgg.ap(), g_std.ap(), waits=dep_all())
        dma("sync", dbgw.ap(), w1qt_st.ap())
        v_op(lambda e: e.tensor_copy(out=bigA[:, 0:16], in_=sxs[:]), waits=now("ld"))
        v_op(lambda e: e.tensor_copy(out=bigA[:, 16:32], in_=rr1[:]))
        v_op(lambda e: e.tensor_copy(out=bigA[:, 32:48], in_=rr2[:]))
        v_op(lambda e: e.tensor_copy(out=bigA[:, 48:49], in_=wsc1[:]))
        v_op(lambda e: e.tensor_copy(out=bigA[:, 49:50], in_=wsc2[:]))
        v_op(lambda e: e.tensor_copy(out=bigA[:, 50:51], in_=wthr[:]))
        v_op(lambda e: e.tensor_copy(out=bigA[:, 51:52], in_=wnthr[:]))
        v_op(lambda e: e.tensor_copy(out=bigA[:, 52:53], in_=wsc1[:]))
        v_op(lambda e: e.tensor_copy(out=bigA[:, 0:16], in_=amdbg[:]))
        v_op(lambda e: e.tensor_copy(out=bigA[:, 16:32], in_=esdbg[:]))
        dma("sync", dbgs.ap(), bigA[:, 0:64], waits=now("v"))
        dbgp = dt("dbg_p", [128, 12288], BF16, kind="ExternalOutput")
        dma("sync", tlv(dbgp.ap())[:, 0, :] if False else dbgp.ap()[:, 0:4096], prepP[:, 0:4096], waits=now("v"))
        dma("sync", dbgp.ap()[:, 4096:8192], prepN[:, 0:4096])
        dma("sync", dbgp.ap()[:, 8192:12288], tern_ring[1][:, 0:4096])
    op("sync", lambda e: e.nop(), waits=dep_all())

    with contextlib.ExitStack() as fes:
        for n in CH:
            sems[n] = fes.enter_context(nc.semaphore("sem_" + n))
        block = fes.enter_context(nc.Block())
        for eng in ENGS:
            getattr(block, eng)(lambda e, _f=ops[eng]: [f(e) for f in _f])
    es.close()
    return nc


def shard_inputs(inputs):
    x = np.ascontiguousarray(np.asarray(inputs["L0_attn_x"], dtype=np.float32))
    maps = []
    for core in range(8):
        b, hg = core // 2, core % 2
        maps.append({
            "x_in": np.ascontiguousarray(x[b, hg * 2048:(hg + 1) * 2048, :]),
            "w1": np.ascontiguousarray(np.asarray(inputs["L0_ffn_w1"], np.float32)),
            "w2": np.ascontiguousarray(np.asarray(inputs["L0_ffn_w2"], np.float32)),
            "b1": np.asarray(inputs["L0_ffn_b1"], np.float32).reshape(1, HID),
            "b2": np.asarray(inputs["L0_ffn_b2"], np.float32).reshape(1, D),
            "lnw": np.asarray(inputs["L0_ffn_ln1_w"], np.float32).reshape(1, D),
            "lnb": np.asarray(inputs["L0_ffn_ln1_b"], np.float32).reshape(1, D),
            "ln2w": np.asarray(inputs["L0_ffn_ln2_w"], np.float32).reshape(1, HID),
            "ln2b": np.asarray(inputs["L0_ffn_ln2_b"], np.float32).reshape(1, HID),
            "wq": np.ascontiguousarray(np.asarray(inputs["L0_attn_Wq"], np.float32)[hg * 128:(hg + 1) * 128, :]),
            "wk": np.ascontiguousarray(np.asarray(inputs["L0_attn_Wk"], np.float32)[hg * 128:(hg + 1) * 128, :]),
            "wv": np.ascontiguousarray(np.asarray(inputs["L0_attn_Wv"], np.float32)[hg * 512:(hg + 1) * 512, :]),
            "wo": np.ascontiguousarray(np.asarray(inputs["L0_attn_Wo"], np.float32)),
            "gam": np.asarray(inputs["L0_attn_gamma"], np.float32).reshape(1, FD),
            "bet": np.asarray(inputs["L0_attn_beta"], np.float32).reshape(1, FD),
            "m0": np.full((128, 1), 1.0 if hg == 0 else 0.0, np.float32),
            "m1": np.full((128, 1), 0.0 if hg == 0 else 1.0, np.float32),
        })
    return maps


def assemble(results):
    out = np.empty((4, 4096, 1024), np.float32)
    for core in range(8):
        b, hg = core // 2, core % 2
        out[b, hg * 2048:(hg + 1) * 2048, :] = results[core]["out"]
    return out


def kernel(**inputs):
    nc = build("full")
    maps = shard_inputs(inputs)
    res = run_bass_kernel_spmd(nc, maps, core_ids=list(range(8)))
    return assemble(res.results)


# version 07
# speedup vs baseline: 1.8980x; 1.0106x over previous
import contextlib
import numpy as np
import ml_dtypes
import concourse.bass as bass
import concourse.mybir as mybir
from concourse.bass_utils import run_bass_kernel_spmd

F32 = mybir.dt.float32
BF16 = mybir.dt.bfloat16
I32 = mybir.dt.int32
AF = mybir.ActivationFunctionType
ALU = mybir.AluOpType
AXX = mybir.AxisListType.X
bfloat16 = ml_dtypes.bfloat16

TOK, D, HID = 2048, 1024, 4096
NTT = TOK // 128
NH, FD, HD = 8, 16, 64
EPS = 1e-5
SEQ, NCH = 4096, 32
ENGS = ("sync", "scalar", "vector", "tensor", "gpsimd")
CH = ("ld", "tp", "gd", "v", "s", "t", "g", "cc")


def build(stage="full", dbg=False):
    nc = bass.Bass("TRN2", target_bir_lowering=False, detect_race_conditions=False)
    ops = {e: [] for e in ENGS}
    cnt = {c: 0 for c in CH}
    sems = {}

    def op(eng, make, waits=(), inc=None, amt=1):
        waits = tuple((n, v) for (n, v) in waits if v > 0)
        if inc is not None:
            cnt[inc] += amt
        def fn(e, _m=make, _w=waits, _i=inc, _a=amt):
            for n, v in _w:
                e.wait_ge(sems[n], v)
            ins = _m(e)
            if _i is not None:
                ins.then_inc(sems[_i], _a)
        ops[eng].append(fn)

    def now(*names):
        return [(n, cnt[n]) for n in names]

    def dep_all():
        return now(*CH)

    def barrier():
        w = dep_all()
        for eng in ENGS:
            if eng in ("vector", "gpsimd"):
                op(eng, lambda e: e.engine_nop(), waits=w)
            else:
                op(eng, lambda e: e.nop(), waits=w)

    def dma(eng, out, in_, waits=()):
        chn = {"sync": "ld", "scalar": "tp", "gpsimd": "gd"}[eng]
        op(eng, lambda e, o=out, i=in_: e.dma_start(out=o, in_=i), waits=waits, inc=chn, amt=16)

    def dmaT(out, in_, waits=()):
        op("sync", lambda e, o=out, i=in_: e.dma_start_transpose(out=o, in_=i),
           waits=waits, inc="tp", amt=16)

    def v_op(make, waits=(), pad=False):
        op("vector", make, waits=waits, inc="v")
        if pad:
            ops["vector"].append(lambda e: e.tensor_copy(out=_vpad[0], in_=_vpad[1]))
    def s_op(make, waits=(), pad=False):
        op("scalar", make, waits=waits, inc="s")
        if pad:
            ops["scalar"].append(lambda e: e.activation(out=_spad[0], in_=_spad[1], func=AF.Copy))
    t_op = lambda make, waits=(): op("tensor", make, waits=waits, inc="t")
    g_op = lambda make, waits=(): op("gpsimd", make, waits=waits, inc="g")

    dt = nc.dram_tensor
    x_in = dt("x_in", [TOK, D], F32, kind="ExternalInput")
    w1 = dt("w1", [HID, D], F32, kind="ExternalInput")
    w2 = dt("w2", [D, HID], F32, kind="ExternalInput")
    b1d = dt("b1", [1, HID], F32, kind="ExternalInput")
    b2d = dt("b2", [1, D], F32, kind="ExternalInput")
    lnwd = dt("lnw", [1, D], F32, kind="ExternalInput")
    lnbd = dt("lnb", [1, D], F32, kind="ExternalInput")
    ln2wd = dt("ln2w", [1, HID], F32, kind="ExternalInput")
    ln2bd = dt("ln2b", [1, HID], F32, kind="ExternalInput")
    wqd = dt("wq", [NH * FD, D], F32, kind="ExternalInput")
    wkd = dt("wk", [NH * FD, D], F32, kind="ExternalInput")
    wvd = dt("wv", [NH * HD, D], F32, kind="ExternalInput")
    wod = dt("wo", [D, D], F32, kind="ExternalInput")
    gamd = dt("gam", [1, FD], F32, kind="ExternalInput")
    betd = dt("bet", [1, FD], F32, kind="ExternalInput")
    m0d = dt("m0", [128, 1], F32, kind="ExternalInput")
    m1d = dt("m1", [128, 1], F32, kind="ExternalInput")
    outd = dt("out", [TOK, D], F32, kind="ExternalOutput")

    x1d = dt("x1d", [TOK, D], F32)
    x2d = dt("x2d", [TOK, D], F32)
    g_std = dt("g_st", [TOK, HID], BF16)
    w1qt_st = dt("w1qt_st", [128, 8 * HID], BF16)
    w2qt_st = dt("w2qt_st", [128, 32 * D], BF16)
    ag1_in = dt("ag1_in", [TOK, D], BF16)
    ag1_out = dt("ag1_out", [SEQ, D], BF16)
    ag2_in = dt("ag2_in", [SEQ, NH * HD], BF16)
    ag2_out = dt("ag2_out", [2 * SEQ, NH * HD], BF16)

    tlv = lambda ap: ap.rearrange("(t p) f -> p t f", p=128)
    xin_t, x1_t, x2_t, out_t = tlv(x_in.ap()), tlv(x1d.ap()), tlv(x2d.ap()), tlv(outd.ap())
    gst_t = tlv(g_std.ap())
    ag1in_t, ag1out_t = tlv(ag1_in.ap()), tlv(ag1_out.ap())
    ag2in_t, ag2out_t = tlv(ag2_in.ap()), tlv(ag2_out.ap())
    w1_t = tlv(w1.ap())
    w2_t = tlv(w2.ap())
    wv_t = tlv(wvd.ap())
    wo_t = tlv(wod.ap())

    es = contextlib.ExitStack()
    sb = lambda name, shape, dtp: es.enter_context(nc.sbuf_tensor(name, shape, dtp))

    _vpadt = sb("_vpad", [128, 128], F32)
    _vpad = (_vpadt[:, 0:64], _vpadt[:, 64:128])
    _spadt = sb("_spad", [128, 128], F32)
    _spad = (_spadt[:, 0:64], _spadt[:, 64:128])
    wbuf = sb("wbuf", [128, 32768], BF16)
    w1qT = wbuf.ap().rearrange("p (c n) -> p c n", c=8)
    w2qT = wbuf.ap().rearrange("p (c n) -> p c n", c=32)
    wqT = wbuf[:, 0:1024].rearrange("p (c n) -> p c n", c=8)
    wkT = wbuf[:, 1024:2048].rearrange("p (c n) -> p c n", c=8)
    wvT = wbuf[:, 2048:6144].rearrange("p (c n) -> p c n", c=8)
    woT = wbuf[:, 6144:14336].rearrange("p (c n) -> p c n", c=8)
    v1t = wbuf[:, 14336:14336 + 32 * 520].rearrange("p (t h n) -> p t h n", t=32, h=8)

    NB = 20480
    bbuf = sb("bbuf", [128, NB], BF16)
    g_ring = [bbuf[:, i * HID:(i + 1) * HID] for i in range(2)]
    prepP = bbuf[:, 0:4096]
    prepN = bbuf[:, 4096:8192]
    tern_ring = [bbuf[:, 8192:12288], bbuf[:, 12288:16384]]
    xq2b = bbuf[:, 8192:12288]
    xq2T = bbuf[:, 12288:16384].rearrange("p (c n) -> p c n", c=32)
    xq1b = bbuf[:, 16384:17408]
    xq1T = bbuf[:, 17408:18432].rearrange("p (c n) -> p c n", c=8)
    xlnc_ring = [bbuf[:, i * 1024:(i + 1) * 1024] for i in range(2)]
    xlncT_ring = [bbuf[:, 2048 + i * 1024: 3072 + i * 1024].rearrange("p (c n) -> p c n", c=8)
                  for i in range(2)]
    qraw = bbuf[:, 4096:8192].rearrange("p (t n) -> p t n", t=32)
    kraw = bbuf[:, 8192:12288].rearrange("p (t n) -> p t n", t=32)
    qf_ring = [bbuf[:, 12288 + i * 128: 12416 + i * 128] for i in range(2)]
    kf_ring = [bbuf[:, 12544 + i * 128: 12672 + i * 128] for i in range(2)]
    phiqb = bbuf[:, 12800:14848]
    phikb = bbuf[:, 14848:16896]
    phiqT = bbuf[:, 16896:18944].rearrange("p (j n) -> p j n", j=16)
    ssnap = bbuf[:, 18944:18944 + 1040].rearrange("p (fh h n) -> p fh h n", fh=2, h=8)

    bigA = sb("bigA", [128, HID], F32)
    bigB = sb("bigB", [128, HID], F32)
    bigBbf = bigB.ap().bitcast(BF16)
    quadA = sb("quadA", [128, 512], F32)
    quadI = sb("quadI", [128, 512], I32)
    quadC = sb("quadC", [128, 512], F32)
    b128 = sb("b128", [128, 1], F32)
    xrow_a = sb("xrow", [128, D], F32)
    xrow_b = sb("xrowb", [128, D], F32)
    xrow2 = [xrow_a, xrow_b]
    xrow = xrow_a
    phikT_s = sb("phikT", [128, 2048], BF16)
    phikT = phikT_s.ap().rearrange("p (j n) -> p j n", j=16)
    quadI2 = phikT_s.ap().bitcast(I32)[:, 0:512]
    qfp = sb("qfp", [128, 256], BF16)
    kfp = sb("kfp", [128, 256], BF16)
    qfpT_s = sb("qfpT", [128, 256], BF16)
    kfpT_s = sb("kfpT", [128, 256], BF16)
    qfpT = qfpT_s.ap().rearrange("p (b n) -> p b n", b=2)
    kfpT = kfpT_s.ap().rearrange("p (b n) -> p b n", b=2)
    o_outb = sb("o_outb", [128, 512], BF16)
    ocat = sb("ocat", [128, 1024], BF16)
    ocatT_s = sb("ocatT", [128, 1024], BF16)
    ocatT = ocatT_s.ap().rearrange("p (c n) -> p c n", c=8)
    oslot = [sb(f"oslot{i}", [128, 512], BF16) for i in range(2)]

    lnw_bc = sb("lnw_bc", [128, D], F32)
    lnb_bc = sb("lnb_bc", [128, D], F32)
    ln2w_bc = sb("ln2w_bc", [128, HID], BF16)
    ln2b_bc = sb("ln2b_bc", [128, HID], BF16)
    gamq_bc = sb("gamq_bc", [128, 128], F32)
    betq_bc = sb("betq_bc", [128, 128], F32)
    gamk_bc = sb("gamk_bc", [128, 128], F32)
    betk_bc = sb("betk_bc", [128, 128], F32)
    identq = sb("identq", [128, 128], F32)
    onesq = sb("onesq", [128, 128], F32)

    rows1 = sb("rows1", [97, 512], BF16)
    rows2 = sb("rows2", [97, 512], BF16)
    rows3 = sb("rows3", [97, 512], BF16)
    rows4 = sb("rows4", [97, 512], BF16)
    rowst = sb("rowst", [97, 512], F32)

    def rowslice(t, k, n=512):
        return t[32 * k: 32 * k + 1, 0:n]

    st = lambda name, n=16: sb(name, [128, n], F32)
    sxs, sxq = st("sxs"), st("sxq")
    rr1, nmr1 = st("rr1"), st("nmr1")
    gs8 = sb("gs8", [128, 128], F32)
    gq8 = sb("gq8", [128, 128], F32)
    rr2, nmr2 = st("rr2"), st("nmr2")
    tA, tB, tC, tD = st("tA"), st("tB"), st("tC"), st("tD")
    am1, es1v, ar1 = st("am1", 1), st("es1", 1), st("ar1", 1)
    amP = [st("amP0", 1), st("amP1", 1)]
    esP = [st("esP0", 1), st("esP1", 1)]
    arP = [st("arP0", 1), st("arP1", 1)]
    scP = [st("scP0", 1), st("scP1", 1)]
    am2, es2v, ar2 = st("am2", 1), st("es2", 1), st("ar2", 1)
    wsum = st("wsum", 32)
    wthr, wnthr = st("wthr", 1), st("wnthr", 1)
    wsc1, wsc2 = st("wsc1", 1), st("wsc2", 1)
    onecol = st("onecol", 1)
    arow_s = sb("arow", [97, 256], BF16)
    arow = lambda k, par=0: arow_s[32 * k:32 * k + 1, 128 * par:128 * par + 128]
    onesrow_s = sb("onesrow", [97, 128], F32)
    onesrow = lambda k: onesrow_s[32 * k:32 * k + 1, :]
    scal = sb("scal", [1, 16], F32)
    m0t, m1t = st("m0t", 1), st("m1t", 1)
    asx, asq = st("asx"), st("asq")
    arr_, anmr = st("arr"), st("anmr")
    qsum = bigA[:, 2048:2304]
    qsq = bigA[:, 2304:2560]
    ksum = bigA[:, 2560:2816]
    ksq = bigA[:, 2816:3072]
    qmu = bigA[:, 3072:3328]
    qrs = bigA[:, 3328:3584]
    kmu = bigA[:, 3584:3840]
    krs = bigA[:, 3840:4096]
    fmw = sb("fmw", [128, 128], F32)
    fmw2 = sb("fmw2", [128, 128], F32)
    fmw3 = sb("fmw3", [128, 128], F32)
    zt, zrt = st("zt", 8), st("zrt", 8)
    onebf = sb("onebf", [128, 1], BF16)

    pb = [es.enter_context(nc.psum_tensor(f"pb{i}", [128, 512], F32)) for i in range(8)]

    v_op(lambda e: e.memset(_vpadt[:], 0.0))
    s_op(lambda e: e.activation(out=_spadt[:], in_=_vpadt[:], func=AF.Copy), waits=now("v"))
    v_op(lambda e: e.memset(onecol[:], 1.0))
    v_op(lambda e: e.memset(onesq[:], 1.0))
    v_op(lambda e: e.memset(onebf[:], 1.0))
    v_op(lambda e: e.memset(onesrow_s[:], 1.0))
    v_op(lambda e: e.memset(b128[:], 128.0))
    dma("sync", m0t[:], m0d.ap())
    dma("sync", m1t[:], m1d.ap())
    g_op(lambda e: e.affine_select(out=identq[:], in_=onesq[:], pattern=[[1, 128]],
                                   compare_op=ALU.is_equal, fill=0.0, base=0,
                                   channel_multiplier=-1), waits=now("v"))

    def bc_load(dst_bc, src_ap, n):
        for ci in range(0, n, 512):
            w = min(512, n - ci)
            sl = (ci // 512) % 3
            dma("sync", rowslice(rowst, sl, w), src_ap[0:1, ci:ci + w], waits=now("t"))
            t_op(lambda e, w=w, sl=sl: e.matmul(out=pb[7][:, 0:w], lhsT=onesrow(sl),
                                                rhs=rowslice(rowst, sl, w),
                                                start=True, stop=True),
                 waits=now("ld", "v"))
            v_op(lambda e, ci=ci, w=w, d=dst_bc: e.tensor_copy(out=d[:, ci:ci + w],
                                                               in_=pb[7][:, 0:w]),
                 waits=now("t"))

    bc_load(lnw_bc, lnwd.ap(), D)
    bc_load(lnb_bc, lnbd.ap(), D)
    bc_load(ln2w_bc, ln2wd.ap(), HID)
    bc_load(ln2b_bc, ln2bd.ap(), HID)

    dma("sync", rowslice(rowst, 0, FD), gamd.ap(), waits=now("t"))
    dma("sync", rowslice(rowst, 1, FD), betd.ap(), waits=now("t"))
    t_op(lambda e: e.matmul(out=pb[7][:, 0:128], lhsT=onesrow(0),
                            rhs=rowst[0:1, 0:FD].unsqueeze(1).broadcast_to([1, 8, FD]),
                            start=True, stop=True), waits=now("ld", "v"))
    v_op(lambda e: e.tensor_copy(out=gamk_bc[:], in_=pb[7][:, 0:128]), waits=now("t"))
    t_op(lambda e: e.matmul(out=pb[7][:, 0:128], lhsT=onesrow(1),
                            rhs=rowst[32:33, 0:FD].unsqueeze(1).broadcast_to([1, 8, FD]),
                            start=True, stop=True), waits=now("v"))
    v_op(lambda e: e.tensor_copy(out=betk_bc[:], in_=pb[7][:, 0:128]), waits=now("t"))
    v_op(lambda e: e.tensor_scalar_mul(out=gamq_bc[:], in0=gamk_bc[:], scalar1=0.25))
    v_op(lambda e: e.tensor_scalar_mul(out=betq_bc[:], in0=betk_bc[:], scalar1=0.25))

    _b1tensors = [rows1, rows1, rows1, rows2, rows2, rows2, rows3, rows3]
    for n in range(8):
        dma("gpsimd", rowslice(_b1tensors[n], n % 3), b1d.ap()[0:1, 512 * n:512 * (n + 1)])
    dma("gpsimd", rowslice(rows3, 2), b2d.ap()[0:1, 0:512])
    dma("gpsimd", rowslice(rows4, 0), b2d.ap()[0:1, 512:1024])
    b1chunk = lambda n: rowslice(_b1tensors[n], n % 3)
    b2chunk = lambda n: rowslice(rows3, 2) if n == 0 else rowslice(rows4, 0)
    b2base = lambda n: 2 if n == 0 else 0
    barrier()

    def prep_w(w_tiles, ntiles, freedim, wsc_dst, qT_dst_fn, totelem):
        pc = {}
        bufs = [bigA, bigB]
        for i in range(ntiles):
            buf = bufs[i % 2]
            dma("sync", buf[:, 0:freedim], w_tiles[:, i, :],
                waits=pc.get(("s1", i - 2), now("s", "v", "tp")))
            s_op(lambda e, b=buf, f=freedim, i=i: e.activation(
                out=prepP[:, 0:f], in_=b[:, 0:f], func=AF.Abs, accum_out=wsum[:, i:i + 1]),
                waits=now("ld"))
            pc[("s1", i)] = now("s")
        v_op(lambda e, n=ntiles: e.reduce_sum(out=wthr[:], in_=wsum[:, 0:n], axis=AXX),
             waits=now("s"), pad=True)
        t_op(lambda e: e.matmul(out=pb[7][0:1, 0:1], lhsT=wthr[:], rhs=onecol[:],
                                start=True, stop=True), waits=now("v"))
        v_op(lambda e: e.tensor_copy(out=scal[0:1, 0:1], in_=pb[7][0:1, 0:1]), waits=now("t"), pad=True)
        t_op(lambda e: e.matmul(out=pb[7][:, 0:1], lhsT=onesrow(0), rhs=scal[0:1, 0:1],
                                start=True, stop=True), waits=now("v"))
        v_op(lambda e, n=totelem, d=wsc_dst: e.tensor_scalar_mul(out=d[:], in0=pb[7][:, 0:1],
                                                                 scalar1=1.0 / n), waits=now("t"), pad=True)
        v_op(lambda e, d=wsc_dst: e.tensor_scalar_mul(out=wthr[:], in0=d[:], scalar1=0.5), pad=True)
        v_op(lambda e: e.tensor_scalar_mul(out=wnthr[:], in0=wthr[:], scalar1=-1.0), pad=True)
        for i in range(ntiles):
            buf = bufs[i % 2]
            tern = tern_ring[i % 2]
            dma("sync", buf[:, 0:freedim], w_tiles[:, i, :],
                waits=pc.get(("v2", i - 2), now("v", "tp")))
            v_op(lambda e, b=buf, f=freedim: e.tensor_single_scalar(
                out=prepP[:, 0:f], in_=b[:, 0:f], scalar=wthr[:], op=ALU.is_gt),
                waits=now("ld"))
            v_op(lambda e, b=buf, f=freedim: e.tensor_single_scalar(
                out=prepN[:, 0:f], in_=b[:, 0:f], scalar=wnthr[:], op=ALU.is_lt))
            v_op(lambda e, t_=tern, f=freedim: e.tensor_sub(out=t_[:, 0:f], in0=prepP[:, 0:f],
                                                            in1=prepN[:, 0:f]),
                 waits=pc.get(("tp2", i - 2), []))
            pc[("v2", i)] = now("v")
            dmaT(qT_dst_fn(i), tern[:, 0:freedim],
                 waits=now("v") + (now("ld") if i == 0 else []))
            pc[("tp2", i)] = now("tp")

    prep_w(w2_t, 8, HID, wsc2, lambda i: w2qT[:, :, 128 * i:128 * (i + 1)], D * HID)
    dma("sync", w2qt_st.ap(), wbuf[:, :], waits=now("tp"))
    prep_w(w1_t, 32, D, wsc1, lambda i: w1qT[:, :, 128 * i:128 * (i + 1)], D * HID)
    dma("sync", w1qt_st.ap(), wbuf[:, :], waits=now("tp"))
    barrier()

    def ln_chain(sum_t, sq_t, n, rr_dst, nmr_dst, double_ln):
        v_op(lambda e: e.tensor_scalar_mul(out=tA[:], in0=sum_t[:], scalar1=1.0 / n), waits=now("s"), pad=True)
        v_op(lambda e: e.tensor_scalar_mul(out=tB[:], in0=sq_t[:], scalar1=1.0 / n), pad=True)
        v_op(lambda e: e.tensor_mul(out=tC[:], in0=tA[:], in1=tA[:]), pad=True)
        v_op(lambda e: e.tensor_sub(out=tB[:], in0=tB[:], in1=tC[:]), pad=True)
        if double_ln:
            v_op(lambda e: e.tensor_scalar_add(out=tC[:], in0=tB[:], scalar1=EPS), pad=True)
            v_op(lambda e: e.reciprocal(out=tD[:], in_=tC[:]), pad=True)
            v_op(lambda e: e.tensor_mul(out=tB[:], in0=tB[:], in1=tD[:]), pad=True)
            v_op(lambda e: e.tensor_scalar_add(out=tB[:], in0=tB[:], scalar1=EPS), pad=True)
            v_op(lambda e: e.tensor_mul(out=tB[:], in0=tB[:], in1=tC[:]), pad=True)
            v_op(lambda e: e.reciprocal(out=tB[:], in_=tB[:]), pad=True)
        else:
            v_op(lambda e: e.tensor_scalar_add(out=tB[:], in0=tB[:], scalar1=EPS), pad=True)
            v_op(lambda e: e.reciprocal(out=tB[:], in_=tB[:]), pad=True)
        s_op(lambda e, r=rr_dst: e.activation(out=r[:], in_=tB[:], func=AF.Sqrt), waits=now("v"), pad=True)
        v_op(lambda e, r=rr_dst, m=nmr_dst: e.tensor_mul(out=m[:], in0=tA[:], in1=r[:]),
             waits=now("s"), pad=True)
        v_op(lambda e, m=nmr_dst: e.tensor_scalar_mul(out=m[:], in0=m[:], scalar1=-1.0), pad=True)

    _qsets = [(quadA, quadI), (quadC, quadI2)]
    _qlastv = {0: [], 1: []}

    def quant_to(dst_bf, src_f32, width, s_scale_ap, waits=()):
        for q in range(width // 512):
            sl = slice(512 * q, 512 * (q + 1))
            A, I = _qsets[q % 2]
            s_op(lambda e, sl=sl, A=A: e.activation(out=A[:], in_=src_f32[:, sl],
                                                    func=AF.Identity, scale=s_scale_ap,
                                                    bias=b128[:]),
                 waits=tuple((list(waits) + now("v")) if q == 0 else ()) + tuple(_qlastv[q % 2]))
            s_op(lambda e, A=A, I=I: e.activation(out=I[:], in_=A[:], func=AF.Copy))
            sw = now("s")
            v_op(lambda e, A=A, I=I: e.scalar_tensor_tensor(out=A[:], in0=A[:], scalar=0.5,
                                                            in1=I[:], op0=ALU.subtract,
                                                            op1=ALU.is_ge), waits=sw)
            v_op(lambda e, sl=sl, A=A, I=I: e.scalar_tensor_tensor(out=dst_bf[:, sl], in0=I[:],
                                                                   scalar=-128.0, in1=A[:],
                                                                   op0=ALU.add, op1=ALU.add))
            _qlastv[q % 2] = now("v")

    dbgS = dt("dbg_S", [32 * 128, 1040], F32, kind="ExternalOutput") if dbg else None
    dbgP = dt("dbg_P", [32 * 128, 2048], BF16, kind="ExternalOutput") if dbg else None
    dbgPt = tlv(dbgP.ap()) if dbg else None
    dbgK = dt("dbg_K", [32 * 128, 128], BF16, kind="ExternalOutput") if dbg else None
    dbgKt = tlv(dbgK.ap()) if dbg else None
    dbgQ = dt("dbg_Q", [32 * 128, 128], BF16, kind="ExternalOutput") if dbg else None
    dbgQt = tlv(dbgQ.ap()) if dbg else None
    dbgSt = tlv(dbgS.ap()) if dbg else None
    dbgx = dt("dbg_x", [TOK, D], BF16, kind="ExternalOutput") if dbg else None
    dbgxt = tlv(dbgx.ap()) if dbg else None
    amdbg = sb("amdbg", [128, 16], F32)
    esdbg = sb("esdbg", [128, 16], F32)

    SCR = bigA[:, 2048:2560]
    fa_t1 = [bigB[:, 0:D], bigB[:, D:2 * D]]
    fa_h1 = [bigA[:, 0:D], bigA[:, D:2 * D]]
    fb_h = [bigA, bigB]
    xq1b2 = [bbuf[:, 16384:17408], bbuf[:, 18432:19456]]
    xq1T2 = [bbuf[:, 17408:18432].rearrange("p (c n) -> p c n", c=8),
             bbuf[:, 19456:20480].rearrange("p (c n) -> p c n", c=8)]
    xq2b1 = bbuf[:, 8192:12288]
    xq2T2 = [bbuf[:, 12288:16384].rearrange("p (c n) -> p c n", c=32),
             bbuf[:, 16384:20480].rearrange("p (c n) -> p c n", c=32)]
    wsc127_1, wsc127_2 = st("w127a", 1), st("w127b", 1)
    caps = {}
    _cid = [0]

    def ffn(xin_tiles, xout_tiles):
        cid = _cid[0]
        _cid[0] += 1

        def cap(tag, t, *chs):
            caps[(cid, tag, t)] = now(*chs)

        def get(tag, t, fb=None):
            k = (cid, tag, t)
            if k in caps:
                return caps[k]
            return dep_all() if fb is None else fb

        for t in range(NTT):
            par = t % 2
            dma("sync", xrow2[par][:], xin_tiles[:, t, :], waits=get("sts", t - 2))
            v_op(lambda e, t=t, par=par: e.reduce_sum(out=sxs[:, t:t + 1], in_=xrow2[par][:],
                                                      axis=AXX), waits=now("ld"))
            s_op(lambda e, t=t, par=par: e.activation(out=fa_h1[par], in_=xrow2[par][:],
                                                      func=AF.Square, accum_out=sxq[:, t:t + 1]),
                 waits=now("ld"))
            cap("sts", t, "v", "s")
        ln_chain(sxs, sxq, D, rr1, nmr1, double_ln=True)
        _qlastv[0] = dep_all()
        _qlastv[1] = dep_all()
        v_op(lambda e: e.tensor_scalar_mul(out=wsc127_1[:], in0=wsc1[:], scalar1=1.0 / 127.0), pad=True)
        v_op(lambda e: e.tensor_scalar_mul(out=wsc127_2[:], in0=wsc2[:], scalar1=0.5 / 127.0), pad=True)

        def prepA(t):
            par = t % 2
            dma("sync", xrow2[par][:], xin_tiles[:, t, :],
                waits=get("At1", t - 2, None if t < 2 else []))
            s_op(lambda e, t=t, par=par: e.activation(out=fa_t1[par], in_=xrow2[par][:],
                                                      func=AF.Identity, bias=nmr1[:, t:t + 1],
                                                      scale=rr1[:, t:t + 1]),
                 waits=now("ld") + get("Apv", t - 2, [] if t >= 2 else None))
            cap("At1", t, "s")
            v_op(lambda e, par=par: e.tensor_mul(out=fa_h1[par], in0=fa_t1[par], in1=lnw_bc[:]),
                 waits=now("s"))
            v_op(lambda e, par=par: e.tensor_add(out=fa_h1[par], in0=fa_h1[par], in1=lnb_bc[:]))
            v_op(lambda e, par=par: e.tensor_reduce(out=amP[par][:], in_=fa_h1[par], axis=AXX,
                                                    op=ALU.max, apply_absolute_value=True), pad=True)
            v_op(lambda e, par=par: e.tensor_scalar_max(out=amP[par][:], in0=amP[par][:],
                                                        scalar1=EPS), pad=True)
            v_op(lambda e, par=par: e.tensor_mul(out=esP[par][:], in0=amP[par][:], in1=wsc127_1[:]),
                 waits=get("AevAll", t - 2, []), pad=True)
            v_op(lambda e, par=par: e.reciprocal(out=arP[par][:], in_=esP[par][:]),
                 waits=get("AatT", t - 2, []), pad=True)
            v_op(lambda e, par=par: e.tensor_mul(out=scP[par][:], in0=arP[par][:], in1=wsc1[:]), pad=True)
            quant_to(xq1b2[par], fa_h1[par], D, scP[par][:], waits=get("Atp", t - 2, []))
            dmaT(xq1T2[par], xq1b2[par], waits=now("v") + get("Amm", t - 2, []))
            cap("Atp", t, "tp")
            cap("Apv", t, "v")

        def mmA(t):
            par = t % 2
            t_op(lambda e, par=par: e.transpose(out=pb[7][0:1, 0:128], in_=arP[par][:],
                                                identity=identq[:]),
                 waits=get("AchV", t) + get("Agel7", t - 1, []))
            cap("AatT", t, "t")
            for k in range(3):
                v_op(lambda e, k=k, par=par: e.tensor_copy(out=arow(k, par), in_=pb[7][0:1, 0:128]),
                     waits=(now("t") if k == 0 else ()))
            cap("Aar", t, "v")
            for n in range(8):
                bank = pb[7] if n == 7 else pb[n]
                if n == 0:
                    w0 = get("Atp", t) + get("Agel0", t - 1, [])
                elif n == 7:
                    w0 = get("Aar", t) + get("Agel7", t - 1, [])
                else:
                    w0 = get("Agel%d" % n, t - 1, [])
                for kc in range(8):
                    t_op(lambda e, kc=kc, n=n, bank=bank, par=par: e.matmul(
                        out=bank[:, 0:512], lhsT=xq1T2[par][:, kc, :],
                        rhs=w1qT[:, kc, 512 * n:512 * (n + 1)],
                        start=(kc == 0), stop=False),
                        waits=(w0 if kc == 0 else ()))
                t_op(lambda e, n=n, bank=bank, par=par: e.matmul(
                    out=bank[:, 0:512], lhsT=arow(n % 3, par), rhs=b1chunk(n),
                    start=False, stop=True),
                    waits=(get("Aar", t) if n == 0 else ()))
                wt_n = now("t")
                s_op(lambda e, n=n, bank=bank, par=par, t=t: e.activation(
                    out=g_ring[par][:, 512 * n:512 * (n + 1)], in_=bank[:, 0:512], func=AF.Gelu,
                    scale=esP[par][:], accum_out=gs8[:, 8 * t + n: 8 * t + n + 1]),
                    waits=wt_n + (get("Agst", t - 2, []) if n == 0 else []))
                s_op(lambda e, n=n, par=par, t=t: e.activation(
                    out=SCR, in_=g_ring[par][:, 512 * n:512 * (n + 1)],
                    func=AF.Square, accum_out=gq8[:, 8 * t + n: 8 * t + n + 1]))
                cap("Agel%d" % n, t, "s")
            cap("AevAll", t, "s")
            cap("Amm", t, "t")
            dma("gpsimd", gst_t[:, t, :], g_ring[par][:], waits=now("s"))
            cap("Agst", t, "gd")

        prepA(0)
        prepA(1)
        for t in range(NTT):
            mmA(t)
            if t + 2 < NTT:
                prepA(t + 2)

        v_op(lambda e: e.tensor_reduce(out=sxs[:], in_=gs8.ap().rearrange("p (t n) -> p t n", n=8),
                                       axis=AXX, op=ALU.add), waits=now("s"))
        v_op(lambda e: e.tensor_reduce(out=sxq[:], in_=gq8.ap().rearrange("p (t n) -> p t n", n=8),
                                       axis=AXX, op=ALU.add))
        ln_chain(sxs, sxq, HID, rr2, nmr2, double_ln=False)
        dma("sync", wbuf[:, :], w2qt_st.ap(), waits=now("t"))

        def prepB(t):
            par = t % 2
            dma("sync", g_ring[par][:], gst_t[:, t, :],
                waits=get("Bt2", t - 2, [] if t >= 2 else None) + get("Agst", t, []))
            dma("sync", xrow2[par][:], xin_tiles[:, t, :], waits=get("BevC1", t - 2))
            cap("Bxld", t, "ld")
            s_op(lambda e, t=t, par=par: e.activation(out=fb_h[par][:], in_=g_ring[par][:],
                                                      func=AF.Identity, bias=nmr2[:, t:t + 1],
                                                      scale=rr2[:, t:t + 1]),
                 waits=now("ld") + get("Bpv", t - 2, [] if t >= 2 else None))
            cap("Bt2", t, "s")
            v_op(lambda e, par=par: e.tensor_mul(out=fb_h[par][:], in0=fb_h[par][:], in1=ln2w_bc[:]),
                 waits=now("s"))
            v_op(lambda e, par=par: e.tensor_add(out=fb_h[par][:], in0=fb_h[par][:], in1=ln2b_bc[:]))
            v_op(lambda e, par=par: e.tensor_reduce(out=amP[par][:], in_=fb_h[par][:], axis=AXX,
                                                    op=ALU.max, apply_absolute_value=True), pad=True)
            v_op(lambda e, par=par: e.tensor_scalar_max(out=amP[par][:], in0=amP[par][:],
                                                        scalar1=EPS), pad=True)
            v_op(lambda e, par=par: e.tensor_mul(out=esP[par][:], in0=amP[par][:], in1=wsc127_2[:]), pad=True)
            v_op(lambda e, par=par: e.reciprocal(out=scP[par][:], in_=esP[par][:]), pad=True)
            v_op(lambda e, par=par: e.tensor_scalar_mul(out=arP[par][:], in0=scP[par][:],
                                                        scalar1=0.5),
                 waits=get("BatT", t - 2, []), pad=True)
            v_op(lambda e, par=par: e.tensor_mul(out=scP[par][:], in0=arP[par][:], in1=wsc2[:]), pad=True)
            cap("BchV", t, "v")
            quant_to(xq2b1, fb_h[par], HID, scP[par][:], waits=get("Btp", t - 1, []))
            dmaT(xq2T2[par], xq2b1, waits=now("v", "s") + get("Bmm", t - 2, []))
            cap("Btp", t, "tp")
            cap("Bpv", t, "v")

        def mmB(t):
            par = t % 2
            t_op(lambda e, par=par: e.transpose(out=pb[7][0:1, 0:128], in_=arP[par][:],
                                                identity=identq[:]),
                 waits=get("BchV", t) + get("Bar", t - 1, []))
            cap("BatT", t, "t")
            for j, k in enumerate((0, 2)):
                v_op(lambda e, k=k, par=par: e.tensor_copy(out=arow(k, par), in_=pb[7][0:1, 0:128]),
                     waits=(now("t") if j == 0 else ()))
            cap("Bar", t, "v")
            for n in range(2):
                w0 = (get("Btp", t) if n == 0 else []) + get("BevC%d" % n, t - 1, [])
                for kc in range(32):
                    t_op(lambda e, kc=kc, n=n, par=par: e.matmul(
                        out=pb[n][:, 0:512], lhsT=xq2T2[par][:, kc, :],
                        rhs=w2qT[:, kc, 512 * n:512 * (n + 1)],
                        start=(kc == 0), stop=False),
                        waits=(w0 if kc == 0 else ()))
                t_op(lambda e, n=n, par=par: e.matmul(out=pb[n][:, 0:512],
                                                      lhsT=arow(b2base(n), par), rhs=b2chunk(n),
                                                      start=False, stop=True),
                     waits=(get("Bar", t) if n == 0 else ()))
                wt_n = now("t")
                v_op(lambda e, n=n, par=par: e.scalar_tensor_tensor(
                    out=xrow2[par][:, 512 * n:512 * (n + 1)], in0=pb[n][:, 0:512],
                    scalar=esP[par][:], in1=xrow2[par][:, 512 * n:512 * (n + 1)],
                    op0=ALU.mult, op1=ALU.add),
                    waits=wt_n + (get("Bxld", t) if n == 0 else []))
                cap("BevC%d" % n, t, "v")
            cap("Bmm", t, "t")
            dma("sync", xout_tiles[:, t, :], xrow2[par][:], waits=now("v"))

        prepB(0)
        prepB(1)
        for t in range(NTT):
            mmB(t)
            if t + 2 < NTT:
                prepB(t + 2)
        barrier()

    sTev_b = [sb("sTe0", [128, 512], BF16), sb("sTe1", [128, 512], BF16)]
    sTm_b = [sb("sTmA", [128, 512], BF16), sb("sTmB", [128, 512], BF16)]
    phiqb2 = [phiqb, bbuf[:, 0:2048]]
    phikb2 = [phikb, bbuf[:, 2048:4096]]
    phiqT2 = [phiqT, phikT_s.ap().rearrange("p (j n) -> p j n", j=16)]
    qfp2 = [qfp[:], wbuf[:, 30976:31232]]
    kfp2 = [kfp[:], wbuf[:, 31232:31488]]
    qfpT2 = [qfpT, wbuf[:, 31488:31744].rearrange("p (b n) -> p b n", b=2)]
    kfpT2 = [kfpT, wbuf[:, 31744:32000].rearrange("p (b n) -> p b n", b=2)]
    fmwQ2 = [fmw, fmw3]
    fmwK2 = [fmw2, quadA[:, 128:256]]
    sqscr = quadA[:, 0:128]
    o_outb2 = [o_outb, oslot[0]]
    zt2 = [zt, st("ztb", 8)]
    zrt2 = [zrt, st("zrtb", 8)]
    ocat2 = [ocat[:], quadA.ap().bitcast(BF16)]
    ocatT2 = [ocatT, quadC.ap().bitcast(BF16).rearrange("p (c n) -> p c n", c=8)]
    os4 = [oslot[0][:], oslot[1][:],
           quadI.ap().bitcast(BF16)[:, 0:512], quadI.ap().bitcast(BF16)[:, 512:1024]]

    def attention():
        acaps = {}

        def cap(tag, t, *chs):
            acaps[(tag, t)] = now(*chs)

        def get(tag, t, fb=None):
            k = (tag, t)
            if k in acaps:
                return acaps[k]
            return dep_all() if fb is None else fb

        stg = bbuf[:, 0:1024]
        dma("gpsimd", stg, wqd.ap(), waits=now("tp", "t", "s", "v"))
        dmaT(wqT, stg, waits=now("gd"))
        dma("gpsimd", stg, wkd.ap(), waits=now("tp"))
        dmaT(wkT, stg, waits=now("gd"))
        for j in range(4):
            dma("gpsimd", stg, wv_t[:, j, :], waits=now("tp"))
            dmaT(wvT[:, :, 128 * j:128 * (j + 1)], stg, waits=now("gd"))
        for j in range(8):
            dma("gpsimd", stg, wo_t[:, j, :], waits=now("tp"))
            dmaT(woT[:, :, 128 * j:128 * (j + 1)], stg, waits=now("gd"))
        barrier()
        v_op(lambda e: e.memset(v1t[:, :, :, 64:65], 1.0))

        for t in range(NTT):
            par = t % 2
            dma("sync", xrow2[par][:], x1_t[:, t, :], waits=get("Lst", t - 2))
            v_op(lambda e, t=t, par=par: e.reduce_sum(out=asx[:, t:t + 1], in_=xrow2[par][:],
                                                      axis=AXX), waits=now("ld"))
            s_op(lambda e, t=t, par=par: e.activation(out=bigA[:, 0:D], in_=xrow2[par][:],
                                                      func=AF.Square, accum_out=asq[:, t:t + 1]),
                 waits=now("ld"))
            cap("Lst", t, "v", "s")
        ln_chain(asx, asq, D, arr_, anmr, double_ln=False)
        for t in range(NTT):
            par = t % 2
            dma("sync", xrow2[par][:], x1_t[:, t, :], waits=get("Lap", t - 2, [] if t >= 2 else None))
            s_op(lambda e, t=t, par=par: e.activation(out=bigBbf[:, 1024 * par:1024 * par + D],
                                                      in_=xrow2[par][:], func=AF.Identity,
                                                      bias=anmr[:, t:t + 1], scale=arr_[:, t:t + 1]),
                 waits=now("ld") + get("Lsd", t - 2, []))
            cap("Lap", t, "s")
            dma("sync", ag1in_t[:, t, :], bigBbf[:, 1024 * par:1024 * par + D], waits=now("s"))
            cap("Lsd", t, "ld")
        op("gpsimd", lambda e: e.collective_compute(
            "AllGather", ALU.bypass, replica_groups=[[0, 1], [2, 3], [4, 5], [6, 7]],
            ins=[ag1_in.ap().opt()], outs=[ag1_out.ap().opt()]),
            waits=now("ld"), inc="cc")

        def pprep(c):
            par = c % 2
            dma("sync", xlnc_ring[par], ag1out_t[:, c, :],
                waits=get("Ptp", c - 2, [] if c >= 2 else None) + (now("cc") if c == 0 else []))
            dmaT(xlncT_ring[par], xlnc_ring[par], waits=now("ld") + get("Pmm", c - 2, []))
            cap("Ptp", c, "tp")

        def pmm(c):
            par = c % 2
            w0 = get("Ptp", c) + get("Pev", c - 1, [])
            for kc in range(8):
                t_op(lambda e, kc=kc, par=par: e.matmul(out=pb[0][:, 0:128],
                                                        lhsT=xlncT_ring[par][:, kc, :],
                                                        rhs=wqT[:, kc, :], start=(kc == 0),
                                                        stop=(kc == 7)),
                     waits=(w0 if kc == 0 else ()))
            for kc in range(8):
                t_op(lambda e, kc=kc, par=par: e.matmul(out=pb[1][:, 0:128],
                                                        lhsT=xlncT_ring[par][:, kc, :],
                                                        rhs=wkT[:, kc, :], start=(kc == 0),
                                                        stop=(kc == 7)))
            for kc in range(8):
                t_op(lambda e, kc=kc, par=par: e.matmul(out=pb[2][:, 0:512],
                                                        lhsT=xlncT_ring[par][:, kc, :],
                                                        rhs=wvT[:, kc, :], start=(kc == 0),
                                                        stop=(kc == 7)))
            cap("Pmm", c, "t")
            s_op(lambda e, par=par: e.activation(out=fmwQ2[par][:], in_=pb[0][:, 0:128],
                                                 func=AF.Copy),
                 waits=now("t") + get("Pst", c - 2, []))
            s_op(lambda e, par=par: e.activation(out=fmwK2[par][:], in_=pb[1][:, 0:128],
                                                 func=AF.Copy))
            s_op(lambda e, c=c: e.activation(
                out=v1t[:, c, :, 0:64], in_=pb[2][:, 0:512].rearrange("p (h n) -> p h n", h=8),
                func=AF.Copy))
            cap("Pev", c, "s")
            v_op(lambda e, c=c, par=par: e.tensor_copy(out=qraw[:, c, :], in_=fmwQ2[par][:]),
                 waits=now("s"))
            v_op(lambda e, c=c, par=par: e.tensor_copy(out=kraw[:, c, :], in_=fmwK2[par][:]))
            v_op(lambda e, c=c, par=par: e.reduce_sum(
                out=qsum[:, 8 * c:8 * c + 8],
                in_=fmwQ2[par].ap().rearrange("p (h f) -> p h f", f=16) if False
                else fmwQ2[par][:].rearrange("p (h f) -> p h f", f=16), axis=AXX))
            v_op(lambda e, c=c, par=par: e.reduce_sum(
                out=ksum[:, 8 * c:8 * c + 8],
                in_=fmwK2[par][:].rearrange("p (h f) -> p h f", f=16), axis=AXX))
            v_op(lambda e, par=par: e.tensor_mul(out=sqscr[:], in0=fmwQ2[par][:], in1=fmwQ2[par][:]))
            v_op(lambda e, c=c: e.reduce_sum(out=qsq[:, 8 * c:8 * c + 8],
                                             in_=sqscr[:].rearrange("p (h f) -> p h f", f=16),
                                             axis=AXX))
            v_op(lambda e, par=par: e.tensor_mul(out=sqscr[:], in0=fmwK2[par][:], in1=fmwK2[par][:]))
            v_op(lambda e, c=c: e.reduce_sum(out=ksq[:, 8 * c:8 * c + 8],
                                             in_=sqscr[:].rearrange("p (h f) -> p h f", f=16),
                                             axis=AXX))
            cap("Pst", c, "v")

        pprep(0)
        pprep(1)
        for c in range(NCH):
            pmm(c)
            if c + 2 < NCH:
                pprep(c + 2)

        def fm_stats(sumt, sqt_, mut, rst):
            v_op(lambda e: e.tensor_scalar_mul(out=mut[:], in0=sumt[:], scalar1=1.0 / FD),
                 waits=now("s", "v"), pad=True)
            v_op(lambda e: e.tensor_scalar_mul(out=rst[:], in0=sqt_[:], scalar1=1.0 / FD), pad=True)
            v_op(lambda e: e.tensor_mul(out=sqt_[:], in0=mut[:], in1=mut[:]))
            v_op(lambda e: e.tensor_sub(out=rst[:], in0=rst[:], in1=sqt_[:]), pad=True)
            v_op(lambda e: e.tensor_scalar_add(out=rst[:], in0=rst[:], scalar1=EPS), pad=True)
            v_op(lambda e: e.reciprocal(out=rst[:], in_=rst[:]), pad=True)
            s_op(lambda e: e.activation(out=rst[:], in_=rst[:], func=AF.Sqrt), waits=now("v"), pad=True)
        fm_stats(qsum, qsq, qmu, qrs)
        fm_stats(ksum, ksq, kmu, krs)
        barrier()

        Sf = [bigA[:, 0:520], bigA[:, 520:1040]]
        v_op(lambda e: e.memset(Sf[0], 0.0))
        v_op(lambda e: e.memset(Sf[1], 0.0), pad=True)
        for pp in range(2):
            v_op(lambda e, pp=pp: e.memset(qfp2[pp], 0.0))
            v_op(lambda e, pp=pp: e.memset(kfp2[pp], 0.0))

        def sprep(c):
            par = c % 2
            rq, rk = qf_ring[par], kf_ring[par]

            def fm_apply(raw, mu, rs, gbc, bbc, dstr):
                mu_b = mu[:, 8 * c:8 * c + 8].unsqueeze(2).broadcast_to([128, 8, 16])
                rs_b = rs[:, 8 * c:8 * c + 8].unsqueeze(2).broadcast_to([128, 8, 16])
                src3 = raw[:, c, :].rearrange("p (h f) -> p h f", f=16)
                v_op(lambda e, s3=src3, mb=mu_b: e.tensor_sub(
                    out=fmw.ap().rearrange("p (h f) -> p h f", f=16),
                    in0=s3, in1=mb))
                v_op(lambda e, rb=rs_b: e.tensor_mul(out=fmw.ap().rearrange("p (h f) -> p h f", f=16),
                                                     in0=fmw.ap().rearrange("p (h f) -> p h f", f=16),
                                                     in1=rb))
                v_op(lambda e, g=gbc: e.tensor_mul(out=fmw[:], in0=fmw[:], in1=g[:]))
                v_op(lambda e, d=dstr, b=bbc: e.tensor_add(out=d, in0=fmw[:], in1=b[:]))
            fm_apply(qraw, qmu, qrs, gamq_bc, betq_bc, rq)
            fm_apply(kraw, kmu, krs, gamk_bc, betk_bc, rk)

            def phi_build(src, dst, w):
                i0 = src.rearrange("p (h f) -> p h f", f=16).unsqueeze(3).broadcast_to([128, 8, 16, 16])
                i1 = src.rearrange("p (h f) -> p h f", f=16).unsqueeze(2).broadcast_to([128, 8, 16, 16])
                v_op(lambda e, d=dst: e.tensor_tensor(
                    out=d.rearrange("p (h a b) -> p h a b", a=16, b=16), in0=i0, in1=i1,
                    op=ALU.mult), waits=w)
            phi_build(rq, phiqb2[par], get("Stp", c - 2, []))
            phi_build(rk, phikb2[par], get("Smm", c - 2, []))
            v_op(lambda e, rq=rq, par=par: e.tensor_copy(
                out=qfp2[par].rearrange("p (h n) -> p h n", h=8)[:, :, 0:16],
                in_=rq.rearrange("p (h f) -> p h f", f=16)))
            v_op(lambda e, rk=rk, par=par: e.tensor_copy(
                out=kfp2[par].rearrange("p (h n) -> p h n", h=8)[:, :, 0:16],
                in_=rk.rearrange("p (h f) -> p h f", f=16)))
            dmaT(phiqT2[par], phiqb2[par], waits=now("v") + get("Smm", c - 2, []))
            dmaT(qfpT2[par], qfp2[par], waits=())
            dmaT(kfpT2[par], kfp2[par], waits=())
            cap("Spv", c, "v")
            cap("Stp", c, "tp")

        def smm(c):
            par = c % 2
            if c > 0:
                s_op(lambda e: e.activation(
                    out=ssnap[:, 0, :, :], in_=Sf[0].rearrange("p (h n) -> p h n", h=8),
                    func=AF.Copy), waits=get("Sadd", c - 1))
                s_op(lambda e: e.activation(
                    out=ssnap[:, 1, :, :], in_=Sf[1].rearrange("p (h n) -> p h n", h=8),
                    func=AF.Copy))
            cap("Ssnap", c, "s")

            sbanks = {(0, 0): pb[0], (1, 0): pb[1], (0, 1): pb[2], (1, 1): pb[7]}
            state_mms = []
            first_state = [True]
            for fh in range(2):
                for hh in range(2):
                    bank = sbanks[(fh, hh)]
                    for j in range(4):
                        h = 4 * hh + j
                        def emit_state(bank=bank, j=j, h=h, fh=fh):
                            w0 = []
                            if first_state[0]:
                                w0 = get("Sadd", c - 1) + get("Spv", c)
                                first_state[0] = False
                            t_op(lambda e, bank=bank, j=j, h=h, fh=fh, c=c, par=par: e.matmul(
                                out=bank[:, 65 * j:65 * j + 65],
                                lhsT=phikb2[par][:, 256 * h + 128 * fh: 256 * h + 128 * (fh + 1)],
                                rhs=v1t[:, c, h, 0:65],
                                start=(j == 0), stop=(j == 3)),
                                waits=w0)
                        state_mms.append(emit_state)
            sidx = 0
            for b in range(2):
                for j in range(4):
                    h = 4 * b + j
                    blk = h // 4
                    base = 32 * (h % 4)
                    tpos = (96, 0) if base == 96 else None
                    w0 = []
                    if j == 0:
                        w0 = get("Stp", c) + get("Ssq%d" % b, c - 1, [])
                    t_op(lambda e, j=j, b=b, blk=blk, base=base, tpos=tpos, par=par: e.matmul(
                        out=pb[3 + b][:, 128 * j:128 * j + 128],
                        lhsT=kfpT2[par][base:base + 16, blk, :],
                        rhs=qfpT2[par][base:base + 16, blk, :],
                        start=True, stop=True, tile_position=tpos),
                        waits=w0)
                    state_mms[sidx]()
                    sidx += 1
                tcap = now("t")
                s_op(lambda e, b=b: e.activation(out=sTev_b[b][:], in_=pb[3 + b][:, 0:512],
                                                 func=AF.Square),
                     waits=tcap + get("Smk%d" % b, c - 1, []))
                cap("Ssq%d" % b, c, "s")
                g_op(lambda e, b=b: e.affine_select(
                    out=sTm_b[b].ap().rearrange("p (h n) -> p h n", h=4),
                    in_=sTev_b[b].ap().rearrange("p (h n) -> p h n", h=4),
                    pattern=[[0, 4], [1, 128]], compare_op=ALU.is_ge, fill=0.0, base=0,
                    channel_multiplier=-1), waits=now("s") + get("Soi%d" % b, c - 1, []))
                cap("Smk%d" % b, c, "g")
            while sidx < 16:
                state_mms[sidx]()
                sidx += 1
            stcap = now("t")
            for b in range(2):
                ob = pb[5 + b]
                for j in range(4):
                    h = 4 * b + j
                    col = 65 * j
                    w0 = get("Smk%d" % b, c) + (get("Sonorm", c - 1, []) if j == 0 else [])
                    t_op(lambda e, ob=ob, col=col, b=b, j=j, c=c: e.matmul(
                        out=ob[:, col:col + 65], lhsT=sTm_b[b][:, 128 * j:128 * j + 128],
                        rhs=v1t[:, c, 4 * b + j, 0:65],
                        start=(j == 0), stop=(j == 3 and c == 0)),
                        waits=(w0 if j == 0 else ()))
                    if c > 0:
                        for fh in range(2):
                            t_op(lambda e, ob=ob, col=col, h=h, fh=fh, j=j, par=par: e.matmul(
                                out=ob[:, col:col + 65], lhsT=phiqT2[par][:, 2 * h + fh, :],
                                rhs=ssnap[:, fh, h, :],
                                start=False, stop=(j == 3 and fh == 1)),
                                waits=(get("Ssnap", c) if (b == 0 and j == 0 and fh == 0) else ()))
                cap("Soi%d" % b, c, "t")
            cap("Smm", c, "t")
            v_op(lambda e: e.tensor_add(out=Sf[0][:, 0:260], in0=Sf[0][:, 0:260],
                                        in1=pb[0][:, 0:260]), waits=stcap)
            v_op(lambda e: e.tensor_add(out=Sf[0][:, 260:520], in0=Sf[0][:, 260:520],
                                        in1=pb[2][:, 0:260]))
            v_op(lambda e: e.tensor_add(out=Sf[1][:, 0:260], in0=Sf[1][:, 0:260],
                                        in1=pb[1][:, 0:260]))
            v_op(lambda e: e.tensor_add(out=Sf[1][:, 260:520], in0=Sf[1][:, 260:520],
                                        in1=pb[7][:, 0:260]))
            cap("Sadd", c, "v")
            v_op(lambda e, par=par: e.tensor_copy(
                out=zt2[par][:, 0:4],
                in_=pb[5][:, 0:260].rearrange("p (h n) -> p h n", n=65)[:, :, 64:65]),
                waits=get("Soi1", c) + get("Sonorm", c - 2, []), pad=True)
            v_op(lambda e, par=par: e.tensor_copy(
                out=zt2[par][:, 4:8],
                in_=pb[6][:, 0:260].rearrange("p (h n) -> p h n", n=65)[:, :, 64:65]), pad=True)
            v_op(lambda e, par=par: e.tensor_scalar_add(out=zt2[par][:], in0=zt2[par][:],
                                                        scalar1=EPS), pad=True)
            v_op(lambda e, par=par: e.reciprocal(out=zrt2[par][:], in_=zt2[par][:]), pad=True)
            for h in range(NH):
                ob = pb[5] if h < 4 else pb[6]
                col = 65 * (h % 4)
                s_op(lambda e, h=h, ob=ob, col=col, par=par: e.activation(
                    out=o_outb2[par][:, 64 * h:64 * h + 64], in_=ob[:, col:col + 64],
                    func=AF.Copy, scale=zrt2[par][:, h:h + 1]),
                    waits=(now("v") + get("Sost", c - 2, []) if h == 0 else ()))
            cap("Sonorm", c, "s")
            dma("sync", ag2in_t[:, c, :], o_outb2[par][:], waits=now("s"))
            cap("Sost", c, "ld")

        sprep(0)
        sprep(1)
        for c in range(NCH):
            smm(c)
            if c + 2 < NCH:
                sprep(c + 2)

        op("gpsimd", lambda e: e.collective_compute(
            "AllGather", ALU.bypass, replica_groups=[[0, 1], [2, 3], [4, 5], [6, 7]],
            ins=[ag2_in.ap().opt()], outs=[ag2_out.ap().opt()]),
            waits=now("ld"), inc="cc")

        def wprep(t):
            par = t % 2
            rows = (t, t + 16, 32 + t, 48 + t)
            dma("sync", os4[0], ag2out_t[:, rows[0], :],
                waits=get("Wcmb", t - 1, [] if t >= 1 else None) + (now("cc") if t == 0 else []))
            for k in range(1, 4):
                dma("sync", os4[k], ag2out_t[:, rows[k], :])
            for half in range(2):
                sl = slice(512 * half, 512 * (half + 1))
                v_op(lambda e, sl=sl, par=par, half=half: e.tensor_single_scalar(
                    out=ocat2[par][:, sl], in_=os4[2 * half], scalar=m0t[:], op=ALU.mult),
                    waits=(now("ld") + get("Wtp", t - 2, []) if half == 0 else ()))
                v_op(lambda e, sl=sl, par=par, half=half: e.scalar_tensor_tensor(
                    out=ocat2[par][:, sl], in0=os4[2 * half + 1], scalar=m1t[:],
                    in1=ocat2[par][:, sl], op0=ALU.mult, op1=ALU.add))
            cap("Wcmb", t, "v")
            dmaT(ocatT2[par], ocat2[par], waits=now("v") + get("Wmm", t - 2, []))
            cap("Wtp", t, "tp")
            dma("sync", xrow2[par][:], x1_t[:, t, :], waits=get("Wadd", t - 2, []))
            cap("Wxld", t, "ld")

        def wmm(t):
            par = t % 2
            w0 = get("Wtp", t) + get("Wadd", t - 1, [])
            for kc in range(8):
                for n in range(2):
                    t_op(lambda e, kc=kc, n=n, par=par: e.matmul(
                        out=pb[n][:, 0:512], lhsT=ocatT2[par][:, kc, :],
                        rhs=woT[:, kc, 512 * n:512 * (n + 1)],
                        start=(kc == 0), stop=(kc == 7)),
                        waits=(w0 if (kc == 0 and n == 0) else ()))
            cap("Wmm", t, "t")
            for n in range(2):
                v_op(lambda e, n=n, par=par: e.tensor_add(
                    out=xrow2[par][:, 512 * n:512 * (n + 1)], in0=pb[n][:, 0:512],
                    in1=xrow2[par][:, 512 * n:512 * (n + 1)]),
                    waits=(now("t") + get("Wxld", t) if n == 0 else ()))
            cap("Wadd", t, "v")
            dma("sync", x2_t[:, t, :], xrow2[par][:], waits=now("v"))
            cap("Wst", t, "ld")

        wprep(0)
        wprep(1)
        for t in range(NTT):
            wmm(t)
            if t + 2 < NTT:
                wprep(t + 2)
        barrier()

    if stage == "prep":
        pass
    else:
        ffn(xin_t, x1_t)
    if stage in ("attn", "full"):
        attention()
    if stage == "prep":
        pass
    elif stage == "full":
        dma("sync", wbuf[:, :], w1qt_st.ap(), waits=now("t", "s", "v", "tp", "g"))
        barrier()
        ffn(x2_t, out_t)
    elif stage != "prep":
        src_t = x1_t if stage == "ffn1" else x2_t
        for t in range(NTT):
            dma("sync", xrow[:], src_t[:, t, :], waits=now("v"))
            v_op(lambda e: e.tensor_copy(out=bigA[:, 0:D], in_=xrow[:]), waits=now("ld"))
            dma("sync", out_t[:, t, :], bigA[:, 0:D], waits=now("v"))
    else:
        for t in range(NTT):
            dma("sync", xrow[:], xin_t[:, t, :], waits=now("v"))
            v_op(lambda e: e.tensor_copy(out=bigA[:, 0:D], in_=xrow[:]), waits=now("ld"))
            dma("sync", out_t[:, t, :], bigA[:, 0:D], waits=now("v"))

    if dbg:
        dbgg = dt("dbg_g", [TOK, HID], BF16, kind="ExternalOutput")
        dbgw = dt("dbg_w", [128, 8 * HID], BF16, kind="ExternalOutput")
        dbgs = dt("dbg_s", [128, 64], F32, kind="ExternalOutput")
        dma("sync", dbgg.ap(), g_std.ap(), waits=dep_all())
        dma("sync", dbgw.ap(), w1qt_st.ap())
        v_op(lambda e: e.tensor_copy(out=bigA[:, 0:16], in_=sxs[:]), waits=now("ld"))
        v_op(lambda e: e.tensor_copy(out=bigA[:, 16:32], in_=rr1[:]))
        v_op(lambda e: e.tensor_copy(out=bigA[:, 32:48], in_=rr2[:]))
        v_op(lambda e: e.tensor_copy(out=bigA[:, 48:49], in_=wsc1[:]))
        v_op(lambda e: e.tensor_copy(out=bigA[:, 49:50], in_=wsc2[:]))
        v_op(lambda e: e.tensor_copy(out=bigA[:, 50:51], in_=wthr[:]))
        v_op(lambda e: e.tensor_copy(out=bigA[:, 51:52], in_=wnthr[:]))
        v_op(lambda e: e.tensor_copy(out=bigA[:, 52:53], in_=wsc1[:]))
        v_op(lambda e: e.tensor_copy(out=bigA[:, 0:16], in_=amdbg[:]))
        v_op(lambda e: e.tensor_copy(out=bigA[:, 16:32], in_=esdbg[:]))
        dma("sync", dbgs.ap(), bigA[:, 0:64], waits=now("v"))
        dbgp = dt("dbg_p", [128, 12288], BF16, kind="ExternalOutput")
        dma("sync", tlv(dbgp.ap())[:, 0, :] if False else dbgp.ap()[:, 0:4096], prepP[:, 0:4096], waits=now("v"))
        dma("sync", dbgp.ap()[:, 4096:8192], prepN[:, 0:4096])
        dma("sync", dbgp.ap()[:, 8192:12288], tern_ring[1][:, 0:4096])
    op("sync", lambda e: e.nop(), waits=dep_all())

    with contextlib.ExitStack() as fes:
        for n in CH:
            sems[n] = fes.enter_context(nc.semaphore("sem_" + n))
        block = fes.enter_context(nc.Block())
        for eng in ENGS:
            getattr(block, eng)(lambda e, _f=ops[eng]: [f(e) for f in _f])
    es.close()
    return nc


def shard_inputs(inputs):
    x = np.ascontiguousarray(np.asarray(inputs["L0_attn_x"], dtype=np.float32))
    maps = []
    for core in range(8):
        b, hg = core // 2, core % 2
        maps.append({
            "x_in": np.ascontiguousarray(x[b, hg * 2048:(hg + 1) * 2048, :]),
            "w1": np.ascontiguousarray(np.asarray(inputs["L0_ffn_w1"], np.float32)),
            "w2": np.ascontiguousarray(np.asarray(inputs["L0_ffn_w2"], np.float32)),
            "b1": np.asarray(inputs["L0_ffn_b1"], np.float32).reshape(1, HID),
            "b2": np.asarray(inputs["L0_ffn_b2"], np.float32).reshape(1, D),
            "lnw": np.asarray(inputs["L0_ffn_ln1_w"], np.float32).reshape(1, D),
            "lnb": np.asarray(inputs["L0_ffn_ln1_b"], np.float32).reshape(1, D),
            "ln2w": np.asarray(inputs["L0_ffn_ln2_w"], np.float32).reshape(1, HID),
            "ln2b": np.asarray(inputs["L0_ffn_ln2_b"], np.float32).reshape(1, HID),
            "wq": np.ascontiguousarray(np.asarray(inputs["L0_attn_Wq"], np.float32)[hg * 128:(hg + 1) * 128, :]),
            "wk": np.ascontiguousarray(np.asarray(inputs["L0_attn_Wk"], np.float32)[hg * 128:(hg + 1) * 128, :]),
            "wv": np.ascontiguousarray(np.asarray(inputs["L0_attn_Wv"], np.float32)[hg * 512:(hg + 1) * 512, :]),
            "wo": np.ascontiguousarray(np.asarray(inputs["L0_attn_Wo"], np.float32)),
            "gam": np.asarray(inputs["L0_attn_gamma"], np.float32).reshape(1, FD),
            "bet": np.asarray(inputs["L0_attn_beta"], np.float32).reshape(1, FD),
            "m0": np.full((128, 1), 1.0 if hg == 0 else 0.0, np.float32),
            "m1": np.full((128, 1), 0.0 if hg == 0 else 1.0, np.float32),
        })
    return maps


def assemble(results):
    out = np.empty((4, 4096, 1024), np.float32)
    for core in range(8):
        b, hg = core // 2, core % 2
        out[b, hg * 2048:(hg + 1) * 2048, :] = results[core]["out"]
    return out


def kernel(**inputs):
    nc = build("full")
    maps = shard_inputs(inputs)
    res = run_bass_kernel_spmd(nc, maps, core_ids=list(range(8)))
    return assemble(res.results)


# version 08
# speedup vs baseline: 2.0538x; 1.0820x over previous
import contextlib
import numpy as np
import ml_dtypes
import concourse.bass as bass
import concourse.mybir as mybir
from concourse.bass_utils import run_bass_kernel_spmd

F32 = mybir.dt.float32
BF16 = mybir.dt.bfloat16
I32 = mybir.dt.int32
AF = mybir.ActivationFunctionType
ALU = mybir.AluOpType
AXX = mybir.AxisListType.X
bfloat16 = ml_dtypes.bfloat16

TOK, D, HID = 2048, 1024, 4096
NTT = TOK // 128
NH, FD, HD = 8, 16, 64
EPS = 1e-5
SEQ, NCH = 4096, 32
ENGS = ("sync", "scalar", "vector", "tensor", "gpsimd")
CH = ("ld", "tp", "gd", "v", "s", "t", "g", "cc")


def build(stage="full", dbg=False):
    nc = bass.Bass("TRN2", target_bir_lowering=False, detect_race_conditions=False)
    ops = {e: [] for e in ENGS}
    cnt = {c: 0 for c in CH}
    sems = {}

    def op(eng, make, waits=(), inc=None, amt=1):
        waits = tuple((n, v) for (n, v) in waits if v > 0)
        if inc is not None:
            cnt[inc] += amt
        def fn(e, _m=make, _w=waits, _i=inc, _a=amt):
            for n, v in _w:
                e.wait_ge(sems[n], v)
            ins = _m(e)
            if _i is not None:
                ins.then_inc(sems[_i], _a)
        ops[eng].append(fn)

    def now(*names):
        return [(n, cnt[n]) for n in names]

    def dep_all():
        return now(*CH)

    def barrier():
        w = dep_all()
        for eng in ENGS:
            if eng in ("vector", "gpsimd"):
                op(eng, lambda e: e.engine_nop(), waits=w)
            else:
                op(eng, lambda e: e.nop(), waits=w)

    def dma(eng, out, in_, waits=()):
        chn = {"sync": "ld", "scalar": "tp", "gpsimd": "gd"}[eng]
        op(eng, lambda e, o=out, i=in_: e.dma_start(out=o, in_=i), waits=waits, inc=chn, amt=16)

    def dmaT(out, in_, waits=()):
        op("sync", lambda e, o=out, i=in_: e.dma_start_transpose(out=o, in_=i),
           waits=waits, inc="tp", amt=16)

    def v_op(make, waits=(), pad=False):
        op("vector", make, waits=waits, inc="v")
        if pad:
            ops["vector"].append(lambda e: e.tensor_copy(out=_vpad[0], in_=_vpad[1]))
    def s_op(make, waits=(), pad=False):
        op("scalar", make, waits=waits, inc="s")
        if pad:
            ops["scalar"].append(lambda e: e.activation(out=_spad[0], in_=_spad[1], func=AF.Copy))
    t_op = lambda make, waits=(): op("tensor", make, waits=waits, inc="t")
    g_op = lambda make, waits=(): op("gpsimd", make, waits=waits, inc="g")

    dt = nc.dram_tensor
    x_in = dt("x_in", [TOK, D], F32, kind="ExternalInput")
    w1 = dt("w1", [HID, D], F32, kind="ExternalInput")
    w2 = dt("w2", [D, HID], F32, kind="ExternalInput")
    b1d = dt("b1", [1, HID], F32, kind="ExternalInput")
    b2d = dt("b2", [1, D], F32, kind="ExternalInput")
    lnwd = dt("lnw", [1, D], F32, kind="ExternalInput")
    lnbd = dt("lnb", [1, D], F32, kind="ExternalInput")
    ln2wd = dt("ln2w", [1, HID], F32, kind="ExternalInput")
    ln2bd = dt("ln2b", [1, HID], F32, kind="ExternalInput")
    wqd = dt("wq", [NH * FD, D], F32, kind="ExternalInput")
    wkd = dt("wk", [NH * FD, D], F32, kind="ExternalInput")
    wvd = dt("wv", [NH * HD, D], F32, kind="ExternalInput")
    wod = dt("wo", [D, D], F32, kind="ExternalInput")
    gamd = dt("gam", [1, FD], F32, kind="ExternalInput")
    betd = dt("bet", [1, FD], F32, kind="ExternalInput")
    m0d = dt("m0", [128, 1], F32, kind="ExternalInput")
    m1d = dt("m1", [128, 1], F32, kind="ExternalInput")
    outd = dt("out", [TOK, D], F32, kind="ExternalOutput")

    x1d = dt("x1d", [TOK, D], F32)
    x2d = dt("x2d", [TOK, D], F32)
    g_std = dt("g_st", [TOK, HID], BF16)
    w1qt_st = dt("w1qt_st", [128, 8 * HID], BF16)
    w2qt_st = dt("w2qt_st", [128, 32 * D], BF16)
    ag1_in = dt("ag1_in", [TOK, D], BF16)
    ag1_out = dt("ag1_out", [SEQ, D], BF16)
    ag2_in = dt("ag2_in", [SEQ, NH * HD], BF16)
    ag2_out = dt("ag2_out", [2 * SEQ, NH * HD], BF16)

    tlv = lambda ap: ap.rearrange("(t p) f -> p t f", p=128)
    xin_t, x1_t, x2_t, out_t = tlv(x_in.ap()), tlv(x1d.ap()), tlv(x2d.ap()), tlv(outd.ap())
    gst_t = tlv(g_std.ap())
    ag1in_t, ag1out_t = tlv(ag1_in.ap()), tlv(ag1_out.ap())
    ag2in_t, ag2out_t = tlv(ag2_in.ap()), tlv(ag2_out.ap())
    w1_t = tlv(w1.ap())
    w2_t = tlv(w2.ap())
    wv_t = tlv(wvd.ap())
    wo_t = tlv(wod.ap())

    es = contextlib.ExitStack()
    sb = lambda name, shape, dtp: es.enter_context(nc.sbuf_tensor(name, shape, dtp))

    _vpadt = sb("_vpad", [128, 128], F32)
    _vpad = (_vpadt[:, 0:64], _vpadt[:, 64:128])
    _spadt = sb("_spad", [128, 128], F32)
    _spad = (_spadt[:, 0:64], _spadt[:, 64:128])
    wbuf = sb("wbuf", [128, 32768], BF16)
    w1qT = wbuf.ap().rearrange("p (c n) -> p c n", c=8)
    w2qT = wbuf.ap().rearrange("p (c n) -> p c n", c=32)
    wqT = wbuf[:, 0:1024].rearrange("p (c n) -> p c n", c=8)
    wkT = wbuf[:, 1024:2048].rearrange("p (c n) -> p c n", c=8)
    wvT = wbuf[:, 2048:6144].rearrange("p (c n) -> p c n", c=8)
    woT = wbuf[:, 6144:14336].rearrange("p (c n) -> p c n", c=8)
    v1t = wbuf[:, 14336:14336 + 32 * 520].rearrange("p (t h n) -> p t h n", t=32, h=8)

    NB = 20480
    bbuf = sb("bbuf", [128, NB], BF16)
    g_ring = [bbuf[:, i * HID:(i + 1) * HID] for i in range(2)]
    prepP = bbuf[:, 0:4096]
    prepN = bbuf[:, 4096:8192]
    tern_ring = [bbuf[:, 8192:12288], bbuf[:, 12288:16384]]
    xq2b = bbuf[:, 8192:12288]
    xq2T = bbuf[:, 12288:16384].rearrange("p (c n) -> p c n", c=32)
    xq1b = bbuf[:, 16384:17408]
    xq1T = bbuf[:, 17408:18432].rearrange("p (c n) -> p c n", c=8)
    xlnc_ring = [bbuf[:, i * 1024:(i + 1) * 1024] for i in range(2)]
    xlncT_ring = [bbuf[:, 2048 + i * 1024: 3072 + i * 1024].rearrange("p (c n) -> p c n", c=8)
                  for i in range(2)]
    qraw = bbuf[:, 4096:8192].rearrange("p (t n) -> p t n", t=32)
    kraw = bbuf[:, 8192:12288].rearrange("p (t n) -> p t n", t=32)
    qf_ring = [bbuf[:, 12288 + i * 128: 12416 + i * 128] for i in range(2)]
    kf_ring = [bbuf[:, 12544 + i * 128: 12672 + i * 128] for i in range(2)]
    phiqb = bbuf[:, 12800:14848]
    phikb = bbuf[:, 14848:16896]
    phiqT = bbuf[:, 16896:18944].rearrange("p (j n) -> p j n", j=16)
    ssnap = bbuf[:, 18944:18944 + 1040].rearrange("p (fh h n) -> p fh h n", fh=2, h=8)

    bigA = sb("bigA", [128, HID], F32)
    bigB = sb("bigB", [128, HID], F32)
    bigBbf = bigB.ap().bitcast(BF16)
    quadA = sb("quadA", [128, 512], F32)
    quadI = sb("quadI", [128, 512], I32)
    quadC = sb("quadC", [128, 512], F32)
    b128 = sb("b128", [128, 1], F32)
    xrow_a = sb("xrow", [128, D], F32)
    xrow_b = sb("xrowb", [128, D], F32)
    xrow2 = [xrow_a, xrow_b]
    xrow = xrow_a
    phikT_s = sb("phikT", [128, 2048], BF16)
    phikT = phikT_s.ap().rearrange("p (j n) -> p j n", j=16)
    quadI2 = phikT_s.ap().bitcast(I32)[:, 0:512]
    qfp = sb("qfp", [128, 256], BF16)
    kfp = sb("kfp", [128, 256], BF16)
    qfpT_s = sb("qfpT", [128, 256], BF16)
    kfpT_s = sb("kfpT", [128, 256], BF16)
    qfpT = qfpT_s.ap().rearrange("p (b n) -> p b n", b=2)
    kfpT = kfpT_s.ap().rearrange("p (b n) -> p b n", b=2)
    o_outb = sb("o_outb", [128, 512], BF16)
    ocat = sb("ocat", [128, 1024], BF16)
    ocatT_s = sb("ocatT", [128, 1024], BF16)
    ocatT = ocatT_s.ap().rearrange("p (c n) -> p c n", c=8)
    oslot = [sb(f"oslot{i}", [128, 512], BF16) for i in range(2)]

    lnw_bc = sb("lnw_bc", [128, D], F32)
    lnb_bc = sb("lnb_bc", [128, D], F32)
    ln2w_bc = sb("ln2w_bc", [128, HID], BF16)
    ln2b_bc = sb("ln2b_bc", [128, HID], BF16)
    gamq_bc = sb("gamq_bc", [128, 128], F32)
    betq_bc = sb("betq_bc", [128, 128], F32)
    gamk_bc = sb("gamk_bc", [128, 128], F32)
    betk_bc = sb("betk_bc", [128, 128], F32)
    identq = sb("identq", [128, 128], F32)
    onesq = sb("onesq", [128, 128], F32)

    rows1 = sb("rows1", [97, 512], BF16)
    rows2 = sb("rows2", [97, 512], BF16)
    rows3 = sb("rows3", [97, 512], BF16)
    rows4 = sb("rows4", [97, 512], BF16)
    rowst = sb("rowst", [97, 512], F32)

    def rowslice(t, k, n=512):
        return t[32 * k: 32 * k + 1, 0:n]

    st = lambda name, n=16: sb(name, [128, n], F32)
    sxs, sxq = st("sxs"), st("sxq")
    rr1, nmr1 = st("rr1"), st("nmr1")
    gs8 = sb("gs8", [128, 128], F32)
    gq8 = sb("gq8", [128, 128], F32)
    rr2, nmr2 = st("rr2"), st("nmr2")
    tA, tB, tC, tD = st("tA"), st("tB"), st("tC"), st("tD")
    am1, es1v, ar1 = st("am1", 1), st("es1", 1), st("ar1", 1)
    amP = [st("amP0", 1), st("amP1", 1)]
    esP = [st("esP0", 1), st("esP1", 1)]
    arP = [st("arP0", 1), st("arP1", 1)]
    scP = [st("scP0", 1), st("scP1", 1)]
    am2, es2v, ar2 = st("am2", 1), st("es2", 1), st("ar2", 1)
    wsum = st("wsum", 32)
    wthr, wnthr = st("wthr", 1), st("wnthr", 1)
    wsc1, wsc2 = st("wsc1", 1), st("wsc2", 1)
    onecol = st("onecol", 1)
    arow_s = sb("arow", [97, 256], BF16)
    arow = lambda k, par=0: arow_s[32 * k:32 * k + 1, 128 * par:128 * par + 128]
    onesrow_s = sb("onesrow", [97, 128], F32)
    onesrow = lambda k: onesrow_s[32 * k:32 * k + 1, :]
    scal = sb("scal", [1, 16], F32)
    m0t, m1t = st("m0t", 1), st("m1t", 1)
    asx, asq = st("asx"), st("asq")
    arr_, anmr = st("arr"), st("anmr")
    qsum = bigA[:, 2048:2304]
    qsq = bigA[:, 2304:2560]
    ksum = bigA[:, 2560:2816]
    ksq = bigA[:, 2816:3072]
    qmu = bigA[:, 3072:3328]
    qrs = bigA[:, 3328:3584]
    kmu = bigA[:, 3584:3840]
    krs = bigA[:, 3840:4096]
    fmw = sb("fmw", [128, 128], F32)
    fmw2 = sb("fmw2", [128, 128], F32)
    fmw3 = sb("fmw3", [128, 128], F32)
    zt, zrt = st("zt", 8), st("zrt", 8)
    onebf = sb("onebf", [128, 1], BF16)

    pb = [es.enter_context(nc.psum_tensor(f"pb{i}", [128, 512], F32)) for i in range(8)]

    v_op(lambda e: e.memset(_vpadt[:], 0.0))
    s_op(lambda e: e.activation(out=_spadt[:], in_=_vpadt[:], func=AF.Copy), waits=now("v"))
    v_op(lambda e: e.memset(onecol[:], 1.0))
    v_op(lambda e: e.memset(onesq[:], 1.0))
    v_op(lambda e: e.memset(onebf[:], 1.0))
    v_op(lambda e: e.memset(onesrow_s[:], 1.0))
    v_op(lambda e: e.memset(b128[:], 128.0))
    dma("sync", m0t[:], m0d.ap())
    dma("sync", m1t[:], m1d.ap())
    g_op(lambda e: e.affine_select(out=identq[:], in_=onesq[:], pattern=[[1, 128]],
                                   compare_op=ALU.is_equal, fill=0.0, base=0,
                                   channel_multiplier=-1), waits=now("v"))

    def bc_load(dst_bc, src_ap, n):
        for ci in range(0, n, 512):
            w = min(512, n - ci)
            sl = (ci // 512) % 3
            dma("sync", rowslice(rowst, sl, w), src_ap[0:1, ci:ci + w], waits=now("t"))
            t_op(lambda e, w=w, sl=sl: e.matmul(out=pb[7][:, 0:w], lhsT=onesrow(sl),
                                                rhs=rowslice(rowst, sl, w),
                                                start=True, stop=True),
                 waits=now("ld", "v"))
            v_op(lambda e, ci=ci, w=w, d=dst_bc: e.tensor_copy(out=d[:, ci:ci + w],
                                                               in_=pb[7][:, 0:w]),
                 waits=now("t"))

    bc_load(lnw_bc, lnwd.ap(), D)
    bc_load(lnb_bc, lnbd.ap(), D)
    bc_load(ln2w_bc, ln2wd.ap(), HID)
    bc_load(ln2b_bc, ln2bd.ap(), HID)

    dma("sync", rowslice(rowst, 0, FD), gamd.ap(), waits=now("t"))
    dma("sync", rowslice(rowst, 1, FD), betd.ap(), waits=now("t"))
    t_op(lambda e: e.matmul(out=pb[7][:, 0:128], lhsT=onesrow(0),
                            rhs=rowst[0:1, 0:FD].unsqueeze(1).broadcast_to([1, 8, FD]),
                            start=True, stop=True), waits=now("ld", "v"))
    v_op(lambda e: e.tensor_copy(out=gamk_bc[:], in_=pb[7][:, 0:128]), waits=now("t"))
    t_op(lambda e: e.matmul(out=pb[7][:, 0:128], lhsT=onesrow(1),
                            rhs=rowst[32:33, 0:FD].unsqueeze(1).broadcast_to([1, 8, FD]),
                            start=True, stop=True), waits=now("v"))
    v_op(lambda e: e.tensor_copy(out=betk_bc[:], in_=pb[7][:, 0:128]), waits=now("t"))
    v_op(lambda e: e.tensor_scalar_mul(out=gamq_bc[:], in0=gamk_bc[:], scalar1=0.25))
    v_op(lambda e: e.tensor_scalar_mul(out=betq_bc[:], in0=betk_bc[:], scalar1=0.25))

    _b1tensors = [rows1, rows1, rows1, rows2, rows2, rows2, rows3, rows3]
    for n in range(8):
        dma("gpsimd", rowslice(_b1tensors[n], n % 3), b1d.ap()[0:1, 512 * n:512 * (n + 1)])
    dma("gpsimd", rowslice(rows3, 2), b2d.ap()[0:1, 0:512])
    dma("gpsimd", rowslice(rows4, 0), b2d.ap()[0:1, 512:1024])
    b1chunk = lambda n: rowslice(_b1tensors[n], n % 3)
    b2chunk = lambda n: rowslice(rows3, 2) if n == 0 else rowslice(rows4, 0)
    b2base = lambda n: 2 if n == 0 else 0
    barrier()

    def prep_w(w_tiles, ntiles, freedim, wsc_dst, qT_dst_fn, totelem):
        pc = {}
        bufs = [bigA, bigB]
        for i in range(ntiles):
            buf = bufs[i % 2]
            dma("sync", buf[:, 0:freedim], w_tiles[:, i, :],
                waits=pc.get(("s1", i - 2), now("s", "v", "tp")))
            s_op(lambda e, b=buf, f=freedim, i=i: e.activation(
                out=prepP[:, 0:f], in_=b[:, 0:f], func=AF.Abs, accum_out=wsum[:, i:i + 1]),
                waits=now("ld"))
            pc[("s1", i)] = now("s")
        v_op(lambda e, n=ntiles: e.reduce_sum(out=wthr[:], in_=wsum[:, 0:n], axis=AXX),
             waits=now("s"), pad=True)
        t_op(lambda e: e.matmul(out=pb[7][0:1, 0:1], lhsT=wthr[:], rhs=onecol[:],
                                start=True, stop=True), waits=now("v"))
        v_op(lambda e: e.tensor_copy(out=scal[0:1, 0:1], in_=pb[7][0:1, 0:1]), waits=now("t"), pad=True)
        t_op(lambda e: e.matmul(out=pb[7][:, 0:1], lhsT=onesrow(0), rhs=scal[0:1, 0:1],
                                start=True, stop=True), waits=now("v"))
        v_op(lambda e, n=totelem, d=wsc_dst: e.tensor_scalar_mul(out=d[:], in0=pb[7][:, 0:1],
                                                                 scalar1=1.0 / n), waits=now("t"), pad=True)
        v_op(lambda e, d=wsc_dst: e.tensor_scalar_mul(out=wthr[:], in0=d[:], scalar1=0.5), pad=True)
        v_op(lambda e: e.tensor_scalar_mul(out=wnthr[:], in0=wthr[:], scalar1=-1.0), pad=True)
        for i in range(ntiles):
            buf = bufs[i % 2]
            tern = tern_ring[i % 2]
            dma("sync", buf[:, 0:freedim], w_tiles[:, i, :],
                waits=pc.get(("v2", i - 2), now("v", "tp")))
            v_op(lambda e, b=buf, f=freedim: e.tensor_single_scalar(
                out=prepP[:, 0:f], in_=b[:, 0:f], scalar=wthr[:], op=ALU.is_gt),
                waits=now("ld"))
            v_op(lambda e, b=buf, f=freedim: e.tensor_single_scalar(
                out=prepN[:, 0:f], in_=b[:, 0:f], scalar=wnthr[:], op=ALU.is_lt))
            v_op(lambda e, t_=tern, f=freedim: e.tensor_sub(out=t_[:, 0:f], in0=prepP[:, 0:f],
                                                            in1=prepN[:, 0:f]),
                 waits=pc.get(("tp2", i - 2), []))
            pc[("v2", i)] = now("v")
            dmaT(qT_dst_fn(i), tern[:, 0:freedim],
                 waits=now("v") + (now("ld") if i == 0 else []))
            pc[("tp2", i)] = now("tp")

    prep_w(w2_t, 8, HID, wsc2, lambda i: w2qT[:, :, 128 * i:128 * (i + 1)], D * HID)
    dma("sync", w2qt_st.ap(), wbuf[:, :], waits=now("tp"))
    prep_w(w1_t, 32, D, wsc1, lambda i: w1qT[:, :, 128 * i:128 * (i + 1)], D * HID)
    dma("sync", w1qt_st.ap(), wbuf[:, :], waits=now("tp"))
    barrier()

    def ln_chain(sum_t, sq_t, n, rr_dst, nmr_dst, double_ln):
        v_op(lambda e: e.tensor_scalar_mul(out=tA[:], in0=sum_t[:], scalar1=1.0 / n), waits=now("s"), pad=True)
        v_op(lambda e: e.tensor_scalar_mul(out=tB[:], in0=sq_t[:], scalar1=1.0 / n), pad=True)
        v_op(lambda e: e.tensor_mul(out=tC[:], in0=tA[:], in1=tA[:]), pad=True)
        v_op(lambda e: e.tensor_sub(out=tB[:], in0=tB[:], in1=tC[:]), pad=True)
        if double_ln:
            v_op(lambda e: e.tensor_scalar_add(out=tC[:], in0=tB[:], scalar1=EPS), pad=True)
            v_op(lambda e: e.reciprocal(out=tD[:], in_=tC[:]), pad=True)
            v_op(lambda e: e.tensor_mul(out=tB[:], in0=tB[:], in1=tD[:]), pad=True)
            v_op(lambda e: e.tensor_scalar_add(out=tB[:], in0=tB[:], scalar1=EPS), pad=True)
            v_op(lambda e: e.tensor_mul(out=tB[:], in0=tB[:], in1=tC[:]), pad=True)
            v_op(lambda e: e.reciprocal(out=tB[:], in_=tB[:]), pad=True)
        else:
            v_op(lambda e: e.tensor_scalar_add(out=tB[:], in0=tB[:], scalar1=EPS), pad=True)
            v_op(lambda e: e.reciprocal(out=tB[:], in_=tB[:]), pad=True)
        s_op(lambda e, r=rr_dst: e.activation(out=r[:], in_=tB[:], func=AF.Sqrt), waits=now("v"), pad=True)
        v_op(lambda e, r=rr_dst, m=nmr_dst: e.tensor_mul(out=m[:], in0=tA[:], in1=r[:]),
             waits=now("s"), pad=True)
        v_op(lambda e, m=nmr_dst: e.tensor_scalar_mul(out=m[:], in0=m[:], scalar1=-1.0), pad=True)

    _qsets = [(quadA, quadI), (quadC, quadI2)]
    _qlastv = {0: [], 1: []}

    def quant_to(dst_bf, src_f32, width, s_scale_ap, waits=()):
        for q in range(width // 512):
            sl = slice(512 * q, 512 * (q + 1))
            A, I = _qsets[q % 2]
            s_op(lambda e, sl=sl, A=A: e.activation(out=A[:], in_=src_f32[:, sl],
                                                    func=AF.Identity, scale=s_scale_ap,
                                                    bias=b128[:]),
                 waits=tuple((list(waits) + now("v")) if q == 0 else ()) + tuple(_qlastv[q % 2]))
            s_op(lambda e, A=A, I=I: e.activation(out=I[:], in_=A[:], func=AF.Copy))
            sw = now("s")
            v_op(lambda e, A=A, I=I: e.scalar_tensor_tensor(out=A[:], in0=A[:], scalar=0.5,
                                                            in1=I[:], op0=ALU.subtract,
                                                            op1=ALU.is_ge), waits=sw)
            v_op(lambda e, sl=sl, A=A, I=I: e.scalar_tensor_tensor(out=dst_bf[:, sl], in0=I[:],
                                                                   scalar=-128.0, in1=A[:],
                                                                   op0=ALU.add, op1=ALU.add))
            _qlastv[q % 2] = now("v")

    dbgS = dt("dbg_S", [32 * 128, 1040], F32, kind="ExternalOutput") if dbg else None
    dbgP = dt("dbg_P", [32 * 128, 2048], BF16, kind="ExternalOutput") if dbg else None
    dbgPt = tlv(dbgP.ap()) if dbg else None
    dbgK = dt("dbg_K", [32 * 128, 128], BF16, kind="ExternalOutput") if dbg else None
    dbgKt = tlv(dbgK.ap()) if dbg else None
    dbgQ = dt("dbg_Q", [32 * 128, 128], BF16, kind="ExternalOutput") if dbg else None
    dbgQt = tlv(dbgQ.ap()) if dbg else None
    dbgSt = tlv(dbgS.ap()) if dbg else None
    dbgx = dt("dbg_x", [TOK, D], BF16, kind="ExternalOutput") if dbg else None
    dbgxt = tlv(dbgx.ap()) if dbg else None
    amdbg = sb("amdbg", [128, 16], F32)
    esdbg = sb("esdbg", [128, 16], F32)

    SCR = bigA[:, 2048:2560]
    fa_t1 = [bigB[:, 0:D], bigB[:, D:2 * D]]
    fa_h1 = [bigA[:, 0:D], bigA[:, D:2 * D]]
    fb_h = [bigA, bigB]
    fb_hb = [bigA.ap().bitcast(BF16)[:, 0:4096], bigB.ap().bitcast(BF16)[:, 0:4096]]
    xq1b2 = [bbuf[:, 16384:17408], bbuf[:, 18432:19456]]
    xq1T2 = [bbuf[:, 17408:18432].rearrange("p (c n) -> p c n", c=8),
             bbuf[:, 19456:20480].rearrange("p (c n) -> p c n", c=8)]
    xq2b1 = bbuf[:, 8192:12288]
    xq2T2 = [bbuf[:, 12288:16384].rearrange("p (c n) -> p c n", c=32),
             bbuf[:, 16384:20480].rearrange("p (c n) -> p c n", c=32)]
    wsc127_1, wsc127_2 = st("w127a", 1), st("w127b", 1)
    caps = {}
    _cid = [0]

    def ffn(xin_tiles, xout_tiles):
        cid = _cid[0]
        _cid[0] += 1

        def cap(tag, t, *chs):
            caps[(cid, tag, t)] = now(*chs)

        def get(tag, t, fb=None):
            k = (cid, tag, t)
            if k in caps:
                return caps[k]
            return dep_all() if fb is None else fb

        for t in range(NTT):
            par = t % 2
            dma("sync", xrow2[par][:], xin_tiles[:, t, :], waits=get("sts", t - 2))
            v_op(lambda e, t=t, par=par: e.reduce_sum(out=sxs[:, t:t + 1], in_=xrow2[par][:],
                                                      axis=AXX), waits=now("ld"))
            s_op(lambda e, t=t, par=par: e.activation(out=fa_h1[par], in_=xrow2[par][:],
                                                      func=AF.Square, accum_out=sxq[:, t:t + 1]),
                 waits=now("ld"))
            cap("sts", t, "v", "s")
        ln_chain(sxs, sxq, D, rr1, nmr1, double_ln=True)
        _qlastv[0] = dep_all()
        _qlastv[1] = dep_all()
        v_op(lambda e: e.tensor_scalar_mul(out=wsc127_1[:], in0=wsc1[:], scalar1=1.0 / 127.0), pad=True)
        v_op(lambda e: e.tensor_scalar_mul(out=wsc127_2[:], in0=wsc2[:], scalar1=0.5 / 127.0), pad=True)

        def prepA(t):
            par = t % 2
            dma("sync", xrow2[par][:], xin_tiles[:, t, :],
                waits=get("At1", t - 2, None if t < 2 else []))
            s_op(lambda e, t=t, par=par: e.activation(out=fa_t1[par], in_=xrow2[par][:],
                                                      func=AF.Identity, bias=nmr1[:, t:t + 1],
                                                      scale=rr1[:, t:t + 1]),
                 waits=now("ld") + get("Apv", t - 2, [] if t >= 2 else None))
            cap("At1", t, "s")
            v_op(lambda e, par=par: e.tensor_mul(out=fa_h1[par], in0=fa_t1[par], in1=lnw_bc[:]),
                 waits=now("s"))
            v_op(lambda e, par=par: e.tensor_add(out=fa_h1[par], in0=fa_h1[par], in1=lnb_bc[:]))
            v_op(lambda e, par=par: e.tensor_reduce(out=amP[par][:], in_=fa_h1[par], axis=AXX,
                                                    op=ALU.max, apply_absolute_value=True), pad=True)
            v_op(lambda e, par=par: e.tensor_scalar_max(out=amP[par][:], in0=amP[par][:],
                                                        scalar1=EPS), pad=True)
            v_op(lambda e, par=par: e.tensor_mul(out=esP[par][:], in0=amP[par][:], in1=wsc127_1[:]),
                 waits=get("AevAll", t - 2, []), pad=True)
            v_op(lambda e, par=par: e.reciprocal(out=arP[par][:], in_=esP[par][:]),
                 waits=get("AatT", t - 2, []), pad=True)
            v_op(lambda e, par=par: e.tensor_mul(out=scP[par][:], in0=arP[par][:], in1=wsc1[:]), pad=True)
            quant_to(xq1b2[par], fa_h1[par], D, scP[par][:], waits=get("Atp", t - 2, []))
            dmaT(xq1T2[par], xq1b2[par], waits=now("v") + get("Amm", t - 2, []))
            cap("Atp", t, "tp")
            cap("Apv", t, "v")

        def mmA(t):
            par = t % 2
            t_op(lambda e, par=par: e.transpose(out=pb[7][0:1, 0:128], in_=arP[par][:],
                                                identity=identq[:]),
                 waits=get("AchV", t) + get("Agel7", t - 1, []))
            cap("AatT", t, "t")
            for k in range(3):
                v_op(lambda e, k=k, par=par: e.tensor_copy(out=arow(k, par), in_=pb[7][0:1, 0:128]),
                     waits=(now("t") if k == 0 else ()))
            cap("Aar", t, "v")
            for n in range(8):
                bank = pb[7] if n == 7 else pb[n]
                if n == 0:
                    w0 = get("Atp", t) + get("Agel0", t - 1, [])
                elif n == 7:
                    w0 = get("Aar", t) + get("Agel7", t - 1, [])
                else:
                    w0 = get("Agel%d" % n, t - 1, [])
                for kc in range(8):
                    t_op(lambda e, kc=kc, n=n, bank=bank, par=par: e.matmul(
                        out=bank[:, 0:512], lhsT=xq1T2[par][:, kc, :],
                        rhs=w1qT[:, kc, 512 * n:512 * (n + 1)],
                        start=(kc == 0), stop=False),
                        waits=(w0 if kc == 0 else ()))
                t_op(lambda e, n=n, bank=bank, par=par: e.matmul(
                    out=bank[:, 0:512], lhsT=arow(n % 3, par), rhs=b1chunk(n),
                    start=False, stop=True),
                    waits=(get("Aar", t) if n == 0 else ()))
                wt_n = now("t")
                s_op(lambda e, n=n, bank=bank, par=par, t=t: e.activation(
                    out=g_ring[par][:, 512 * n:512 * (n + 1)], in_=bank[:, 0:512], func=AF.Gelu,
                    scale=esP[par][:], accum_out=gs8[:, 8 * t + n: 8 * t + n + 1]),
                    waits=wt_n + (get("Agst", t - 2, []) if n == 0 else []))
                s_op(lambda e, n=n, par=par, t=t: e.activation(
                    out=SCR, in_=g_ring[par][:, 512 * n:512 * (n + 1)],
                    func=AF.Square, accum_out=gq8[:, 8 * t + n: 8 * t + n + 1]))
                cap("Agel%d" % n, t, "s")
            cap("AevAll", t, "s")
            cap("Amm", t, "t")
            dma("gpsimd", gst_t[:, t, :], g_ring[par][:], waits=now("s"))
            cap("Agst", t, "gd")

        prepA(0)
        prepA(1)
        for t in range(NTT):
            mmA(t)
            if t + 2 < NTT:
                prepA(t + 2)

        v_op(lambda e: e.tensor_reduce(out=sxs[:], in_=gs8.ap().rearrange("p (t n) -> p t n", n=8),
                                       axis=AXX, op=ALU.add), waits=now("s"))
        v_op(lambda e: e.tensor_reduce(out=sxq[:], in_=gq8.ap().rearrange("p (t n) -> p t n", n=8),
                                       axis=AXX, op=ALU.add))
        ln_chain(sxs, sxq, HID, rr2, nmr2, double_ln=False)
        dma("sync", wbuf[:, :], w2qt_st.ap(), waits=now("t"))

        def prepB(t):
            par = t % 2
            dma("sync", g_ring[par][:], gst_t[:, t, :],
                waits=get("Bt2", t - 2, [] if t >= 2 else None) + get("Agst", t, []))
            cap("Bgld", t, "ld")
            dma("sync", xrow2[par][:], xin_tiles[:, t, :], waits=get("Bst", t - 2))
            cap("Bxld", t, "ld")
            s_op(lambda e, t=t, par=par: e.activation(out=fb_hb[par], in_=g_ring[par][:],
                                                      func=AF.Identity, bias=nmr2[:, t:t + 1],
                                                      scale=rr2[:, t:t + 1]),
                 waits=get("Bgld", t) + get("Bpv", t - 2, [] if t >= 2 else None))
            cap("Bt2", t, "s")
            v_op(lambda e, par=par: e.tensor_mul(out=fb_hb[par], in0=fb_hb[par], in1=ln2w_bc[:]),
                 waits=now("s"))
            v_op(lambda e, par=par: e.tensor_add(out=fb_hb[par], in0=fb_hb[par], in1=ln2b_bc[:]))
            v_op(lambda e, par=par: e.tensor_reduce(out=amP[par][:], in_=fb_hb[par], axis=AXX,
                                                    op=ALU.max, apply_absolute_value=True), pad=True)
            v_op(lambda e, par=par: e.tensor_scalar_max(out=amP[par][:], in0=amP[par][:],
                                                        scalar1=EPS), pad=True)
            v_op(lambda e, par=par: e.tensor_mul(out=esP[par][:], in0=amP[par][:], in1=wsc127_2[:]), pad=True)
            v_op(lambda e, par=par: e.reciprocal(out=scP[par][:], in_=esP[par][:]), pad=True)
            v_op(lambda e, par=par: e.tensor_scalar_mul(out=arP[par][:], in0=scP[par][:],
                                                        scalar1=0.5),
                 waits=get("BatT", t - 2, []), pad=True)
            v_op(lambda e, par=par: e.tensor_mul(out=scP[par][:], in0=arP[par][:], in1=wsc2[:]), pad=True)
            cap("BchV", t, "v")
            quant_to(xq2b1, fb_hb[par], HID, scP[par][:], waits=get("Btp", t - 1, []))
            dmaT(xq2T2[par], xq2b1, waits=now("v", "s") + get("Bmm", t - 2, []))
            cap("Btp", t, "tp")
            cap("Bpv", t, "v")

        def mmB(t):
            par = t % 2
            t_op(lambda e, par=par: e.transpose(out=pb[7][0:1, 0:128], in_=arP[par][:],
                                                identity=identq[:]),
                 waits=get("BchV", t) + get("Bar", t - 1, []))
            cap("BatT", t, "t")
            for j, k in enumerate((0, 2)):
                v_op(lambda e, k=k, par=par: e.tensor_copy(out=arow(k, par), in_=pb[7][0:1, 0:128]),
                     waits=(now("t") if j == 0 else ()))
            cap("Bar", t, "v")
            for n in range(2):
                w0 = (get("Btp", t) if n == 0 else []) + get("BevC%d" % n, t - 1, [])
                for kc in range(32):
                    t_op(lambda e, kc=kc, n=n, par=par: e.matmul(
                        out=pb[n][:, 0:512], lhsT=xq2T2[par][:, kc, :],
                        rhs=w2qT[:, kc, 512 * n:512 * (n + 1)],
                        start=(kc == 0), stop=False),
                        waits=(w0 if kc == 0 else ()))
                t_op(lambda e, n=n, par=par: e.matmul(out=pb[n][:, 0:512],
                                                      lhsT=arow(b2base(n), par), rhs=b2chunk(n),
                                                      start=False, stop=True),
                     waits=(get("Bar", t) if n == 0 else ()))
                wt_n = now("t")
                v_op(lambda e, n=n, par=par: e.scalar_tensor_tensor(
                    out=xrow2[par][:, 512 * n:512 * (n + 1)], in0=pb[n][:, 0:512],
                    scalar=esP[par][:], in1=xrow2[par][:, 512 * n:512 * (n + 1)],
                    op0=ALU.mult, op1=ALU.add),
                    waits=wt_n + (get("Bxld", t) if n == 0 else []))
                cap("BevC%d" % n, t, "v")
            cap("Bmm", t, "t")
            dma("gpsimd", xout_tiles[:, t, :], xrow2[par][:], waits=now("v"))
            cap("Bst", t, "gd")

        prepB(0)
        prepB(1)
        for t in range(NTT):
            mmB(t)
            if t + 2 < NTT:
                prepB(t + 2)
        barrier()

    sTev_b = [sb("sTe0", [128, 512], BF16), sb("sTe1", [128, 512], BF16)]
    sTm_b = [sb("sTmA", [128, 512], BF16), sb("sTmB", [128, 512], BF16)]
    phiqb2 = [phiqb, bbuf[:, 0:2048]]
    phikb2 = [phikb, bbuf[:, 2048:4096]]
    phiqT2 = [phiqT, phikT_s.ap().rearrange("p (j n) -> p j n", j=16)]
    qfp2 = [qfp[:], wbuf[:, 30976:31232]]
    kfp2 = [kfp[:], wbuf[:, 31232:31488]]
    qfpT2 = [qfpT, wbuf[:, 31488:31744].rearrange("p (b n) -> p b n", b=2)]
    kfpT2 = [kfpT, wbuf[:, 31744:32000].rearrange("p (b n) -> p b n", b=2)]
    fmwQ2 = [fmw, fmw3]
    fmwK2 = [fmw2, quadA[:, 128:256]]
    sqscr = quadA[:, 0:128]
    o_outb2 = [o_outb, oslot[0]]
    zt2 = [zt, st("ztb", 8)]
    zrt2 = [zrt, st("zrtb", 8)]
    ocat2 = [ocat[:], quadA.ap().bitcast(BF16)]
    ocatT2 = [ocatT, quadC.ap().bitcast(BF16).rearrange("p (c n) -> p c n", c=8)]
    os4 = [oslot[0][:], oslot[1][:],
           quadI.ap().bitcast(BF16)[:, 0:512], quadI.ap().bitcast(BF16)[:, 512:1024]]

    def attention():
        acaps = {}

        def cap(tag, t, *chs):
            acaps[(tag, t)] = now(*chs)

        def get(tag, t, fb=None):
            k = (tag, t)
            if k in acaps:
                return acaps[k]
            return dep_all() if fb is None else fb

        stg = bbuf[:, 0:1024]
        dma("gpsimd", stg, wqd.ap(), waits=now("tp", "t", "s", "v"))
        dmaT(wqT, stg, waits=now("gd"))
        dma("gpsimd", stg, wkd.ap(), waits=now("tp"))
        dmaT(wkT, stg, waits=now("gd"))
        for j in range(4):
            dma("gpsimd", stg, wv_t[:, j, :], waits=now("tp"))
            dmaT(wvT[:, :, 128 * j:128 * (j + 1)], stg, waits=now("gd"))
        for j in range(8):
            dma("gpsimd", stg, wo_t[:, j, :], waits=now("tp"))
            dmaT(woT[:, :, 128 * j:128 * (j + 1)], stg, waits=now("gd"))
        barrier()
        v_op(lambda e: e.memset(v1t[:, :, :, 64:65], 1.0))

        for t in range(NTT):
            par = t % 2
            dma("sync", xrow2[par][:], x1_t[:, t, :], waits=get("Lst", t - 2))
            v_op(lambda e, t=t, par=par: e.reduce_sum(out=asx[:, t:t + 1], in_=xrow2[par][:],
                                                      axis=AXX), waits=now("ld"))
            s_op(lambda e, t=t, par=par: e.activation(out=bigA[:, 0:D], in_=xrow2[par][:],
                                                      func=AF.Square, accum_out=asq[:, t:t + 1]),
                 waits=now("ld"))
            cap("Lst", t, "v", "s")
        ln_chain(asx, asq, D, arr_, anmr, double_ln=False)
        for t in range(NTT):
            par = t % 2
            dma("sync", xrow2[par][:], x1_t[:, t, :], waits=get("Lap", t - 2, [] if t >= 2 else None))
            s_op(lambda e, t=t, par=par: e.activation(out=bigBbf[:, 1024 * par:1024 * par + D],
                                                      in_=xrow2[par][:], func=AF.Identity,
                                                      bias=anmr[:, t:t + 1], scale=arr_[:, t:t + 1]),
                 waits=now("ld") + get("Lsd", t - 2, []))
            cap("Lap", t, "s")
            dma("sync", ag1in_t[:, t, :], bigBbf[:, 1024 * par:1024 * par + D], waits=now("s"))
            cap("Lsd", t, "ld")
        op("gpsimd", lambda e: e.collective_compute(
            "AllGather", ALU.bypass, replica_groups=[[0, 1], [2, 3], [4, 5], [6, 7]],
            ins=[ag1_in.ap().opt()], outs=[ag1_out.ap().opt()]),
            waits=now("ld"), inc="cc")

        def pprep(c):
            par = c % 2
            dma("sync", xlnc_ring[par], ag1out_t[:, c, :],
                waits=get("Ptp", c - 2, [] if c >= 2 else None) + (now("cc") if c == 0 else []))
            dmaT(xlncT_ring[par], xlnc_ring[par], waits=now("ld") + get("Pmm", c - 2, []))
            cap("Ptp", c, "tp")

        def pmm(c):
            par = c % 2
            w0 = get("Ptp", c) + get("Pev", c - 1, [])
            for kc in range(8):
                t_op(lambda e, kc=kc, par=par: e.matmul(out=pb[0][:, 0:128],
                                                        lhsT=xlncT_ring[par][:, kc, :],
                                                        rhs=wqT[:, kc, :], start=(kc == 0),
                                                        stop=(kc == 7)),
                     waits=(w0 if kc == 0 else ()))
            for kc in range(8):
                t_op(lambda e, kc=kc, par=par: e.matmul(out=pb[1][:, 0:128],
                                                        lhsT=xlncT_ring[par][:, kc, :],
                                                        rhs=wkT[:, kc, :], start=(kc == 0),
                                                        stop=(kc == 7)))
            for kc in range(8):
                t_op(lambda e, kc=kc, par=par: e.matmul(out=pb[2][:, 0:512],
                                                        lhsT=xlncT_ring[par][:, kc, :],
                                                        rhs=wvT[:, kc, :], start=(kc == 0),
                                                        stop=(kc == 7)))
            cap("Pmm", c, "t")
            s_op(lambda e, par=par: e.activation(out=fmwQ2[par][:], in_=pb[0][:, 0:128],
                                                 func=AF.Copy),
                 waits=now("t") + get("Pst", c - 2, []))
            s_op(lambda e, par=par: e.activation(out=fmwK2[par][:], in_=pb[1][:, 0:128],
                                                 func=AF.Copy))
            s_op(lambda e, c=c: e.activation(
                out=v1t[:, c, :, 0:64], in_=pb[2][:, 0:512].rearrange("p (h n) -> p h n", h=8),
                func=AF.Copy))
            cap("Pev", c, "s")
            v_op(lambda e, c=c, par=par: e.tensor_copy(out=qraw[:, c, :], in_=fmwQ2[par][:]),
                 waits=now("s"))
            v_op(lambda e, c=c, par=par: e.tensor_copy(out=kraw[:, c, :], in_=fmwK2[par][:]))
            v_op(lambda e, c=c, par=par: e.reduce_sum(
                out=qsum[:, 8 * c:8 * c + 8],
                in_=fmwQ2[par].ap().rearrange("p (h f) -> p h f", f=16) if False
                else fmwQ2[par][:].rearrange("p (h f) -> p h f", f=16), axis=AXX))
            v_op(lambda e, c=c, par=par: e.reduce_sum(
                out=ksum[:, 8 * c:8 * c + 8],
                in_=fmwK2[par][:].rearrange("p (h f) -> p h f", f=16), axis=AXX))
            v_op(lambda e, par=par: e.tensor_mul(out=sqscr[:], in0=fmwQ2[par][:], in1=fmwQ2[par][:]))
            v_op(lambda e, c=c: e.reduce_sum(out=qsq[:, 8 * c:8 * c + 8],
                                             in_=sqscr[:].rearrange("p (h f) -> p h f", f=16),
                                             axis=AXX))
            v_op(lambda e, par=par: e.tensor_mul(out=sqscr[:], in0=fmwK2[par][:], in1=fmwK2[par][:]))
            v_op(lambda e, c=c: e.reduce_sum(out=ksq[:, 8 * c:8 * c + 8],
                                             in_=sqscr[:].rearrange("p (h f) -> p h f", f=16),
                                             axis=AXX))
            cap("Pst", c, "v")

        pprep(0)
        pprep(1)
        for c in range(NCH):
            pmm(c)
            if c + 2 < NCH:
                pprep(c + 2)

        def fm_stats(sumt, sqt_, mut, rst):
            v_op(lambda e: e.tensor_scalar_mul(out=mut[:], in0=sumt[:], scalar1=1.0 / FD),
                 waits=now("s", "v"), pad=True)
            v_op(lambda e: e.tensor_scalar_mul(out=rst[:], in0=sqt_[:], scalar1=1.0 / FD), pad=True)
            v_op(lambda e: e.tensor_mul(out=sqt_[:], in0=mut[:], in1=mut[:]))
            v_op(lambda e: e.tensor_sub(out=rst[:], in0=rst[:], in1=sqt_[:]), pad=True)
            v_op(lambda e: e.tensor_scalar_add(out=rst[:], in0=rst[:], scalar1=EPS), pad=True)
            v_op(lambda e: e.reciprocal(out=rst[:], in_=rst[:]), pad=True)
            s_op(lambda e: e.activation(out=rst[:], in_=rst[:], func=AF.Sqrt), waits=now("v"), pad=True)
        fm_stats(qsum, qsq, qmu, qrs)
        fm_stats(ksum, ksq, kmu, krs)
        barrier()

        Sf = [bigA[:, 0:520], bigA[:, 520:1040]]
        v_op(lambda e: e.memset(Sf[0], 0.0))
        v_op(lambda e: e.memset(Sf[1], 0.0), pad=True)
        for pp in range(2):
            v_op(lambda e, pp=pp: e.memset(qfp2[pp], 0.0))
            v_op(lambda e, pp=pp: e.memset(kfp2[pp], 0.0))

        def sprep(c):
            par = c % 2
            rq, rk = qf_ring[par], kf_ring[par]

            def fm_apply(raw, mu, rs, gbc, bbc, dstr):
                mu_b = mu[:, 8 * c:8 * c + 8].unsqueeze(2).broadcast_to([128, 8, 16])
                rs_b = rs[:, 8 * c:8 * c + 8].unsqueeze(2).broadcast_to([128, 8, 16])
                src3 = raw[:, c, :].rearrange("p (h f) -> p h f", f=16)
                v_op(lambda e, s3=src3, mb=mu_b: e.tensor_sub(
                    out=fmw.ap().rearrange("p (h f) -> p h f", f=16),
                    in0=s3, in1=mb))
                v_op(lambda e, rb=rs_b: e.tensor_mul(out=fmw.ap().rearrange("p (h f) -> p h f", f=16),
                                                     in0=fmw.ap().rearrange("p (h f) -> p h f", f=16),
                                                     in1=rb))
                v_op(lambda e, g=gbc: e.tensor_mul(out=fmw[:], in0=fmw[:], in1=g[:]))
                v_op(lambda e, d=dstr, b=bbc: e.tensor_add(out=d, in0=fmw[:], in1=b[:]))
            fm_apply(qraw, qmu, qrs, gamq_bc, betq_bc, rq)
            fm_apply(kraw, kmu, krs, gamk_bc, betk_bc, rk)

            def phi_build(src, dst, w):
                i0 = src.rearrange("p (h f) -> p h f", f=16).unsqueeze(3).broadcast_to([128, 8, 16, 16])
                i1 = src.rearrange("p (h f) -> p h f", f=16).unsqueeze(2).broadcast_to([128, 8, 16, 16])
                v_op(lambda e, d=dst: e.tensor_tensor(
                    out=d.rearrange("p (h a b) -> p h a b", a=16, b=16), in0=i0, in1=i1,
                    op=ALU.mult), waits=w)
            phi_build(rq, phiqb2[par], get("Stp", c - 2, []))
            phi_build(rk, phikb2[par], get("Smm", c - 2, []))
            v_op(lambda e, rq=rq, par=par: e.tensor_copy(
                out=qfp2[par].rearrange("p (h n) -> p h n", h=8)[:, :, 0:16],
                in_=rq.rearrange("p (h f) -> p h f", f=16)))
            v_op(lambda e, rk=rk, par=par: e.tensor_copy(
                out=kfp2[par].rearrange("p (h n) -> p h n", h=8)[:, :, 0:16],
                in_=rk.rearrange("p (h f) -> p h f", f=16)))
            dmaT(phiqT2[par], phiqb2[par], waits=now("v") + get("Smm", c - 2, []))
            dmaT(qfpT2[par], qfp2[par], waits=())
            dmaT(kfpT2[par], kfp2[par], waits=())
            cap("Spv", c, "v")
            cap("Stp", c, "tp")

        def smm(c):
            par = c % 2
            if c > 0:
                s_op(lambda e: e.activation(
                    out=ssnap[:, 0, :, :], in_=Sf[0].rearrange("p (h n) -> p h n", h=8),
                    func=AF.Copy), waits=get("Sadd", c - 1))
                s_op(lambda e: e.activation(
                    out=ssnap[:, 1, :, :], in_=Sf[1].rearrange("p (h n) -> p h n", h=8),
                    func=AF.Copy))
            cap("Ssnap", c, "s")

            sbanks = {(0, 0): pb[0], (1, 0): pb[1], (0, 1): pb[2], (1, 1): pb[7]}
            state_mms = []
            first_state = [True]
            for fh in range(2):
                for hh in range(2):
                    bank = sbanks[(fh, hh)]
                    for j in range(4):
                        h = 4 * hh + j
                        def emit_state(bank=bank, j=j, h=h, fh=fh):
                            w0 = []
                            if first_state[0]:
                                w0 = get("Sadd", c - 1) + get("Spv", c)
                                first_state[0] = False
                            t_op(lambda e, bank=bank, j=j, h=h, fh=fh, c=c, par=par: e.matmul(
                                out=bank[:, 65 * j:65 * j + 65],
                                lhsT=phikb2[par][:, 256 * h + 128 * fh: 256 * h + 128 * (fh + 1)],
                                rhs=v1t[:, c, h, 0:65],
                                start=(j == 0), stop=(j == 3)),
                                waits=w0)
                        state_mms.append(emit_state)
            sidx = 0
            for b in range(2):
                for j in range(4):
                    h = 4 * b + j
                    blk = h // 4
                    base = 32 * (h % 4)
                    tpos = (96, 0) if base == 96 else None
                    w0 = []
                    if j == 0:
                        w0 = get("Stp", c) + get("Ssq%d" % b, c - 1, [])
                    t_op(lambda e, j=j, b=b, blk=blk, base=base, tpos=tpos, par=par: e.matmul(
                        out=pb[3 + b][:, 128 * j:128 * j + 128],
                        lhsT=kfpT2[par][base:base + 16, blk, :],
                        rhs=qfpT2[par][base:base + 16, blk, :],
                        start=True, stop=True, tile_position=tpos),
                        waits=w0)
                    state_mms[sidx]()
                    sidx += 1
                tcap = now("t")
                s_op(lambda e, b=b: e.activation(out=sTev_b[b][:], in_=pb[3 + b][:, 0:512],
                                                 func=AF.Square),
                     waits=tcap + get("Smk%d" % b, c - 1, []))
                cap("Ssq%d" % b, c, "s")
                g_op(lambda e, b=b: e.affine_select(
                    out=sTm_b[b].ap().rearrange("p (h n) -> p h n", h=4),
                    in_=sTev_b[b].ap().rearrange("p (h n) -> p h n", h=4),
                    pattern=[[0, 4], [1, 128]], compare_op=ALU.is_ge, fill=0.0, base=0,
                    channel_multiplier=-1), waits=now("s") + get("Soi%d" % b, c - 1, []))
                cap("Smk%d" % b, c, "g")
            while sidx < 16:
                state_mms[sidx]()
                sidx += 1
            stcap = now("t")
            for b in range(2):
                ob = pb[5 + b]
                for j in range(4):
                    h = 4 * b + j
                    col = 65 * j
                    w0 = get("Smk%d" % b, c) + (get("Sonorm", c - 1, []) if j == 0 else [])
                    t_op(lambda e, ob=ob, col=col, b=b, j=j, c=c: e.matmul(
                        out=ob[:, col:col + 65], lhsT=sTm_b[b][:, 128 * j:128 * j + 128],
                        rhs=v1t[:, c, 4 * b + j, 0:65],
                        start=(j == 0), stop=(j == 3 and c == 0)),
                        waits=(w0 if j == 0 else ()))
                    if c > 0:
                        for fh in range(2):
                            t_op(lambda e, ob=ob, col=col, h=h, fh=fh, j=j, par=par: e.matmul(
                                out=ob[:, col:col + 65], lhsT=phiqT2[par][:, 2 * h + fh, :],
                                rhs=ssnap[:, fh, h, :],
                                start=False, stop=(j == 3 and fh == 1)),
                                waits=(get("Ssnap", c) if (b == 0 and j == 0 and fh == 0) else ()))
                cap("Soi%d" % b, c, "t")
            cap("Smm", c, "t")
            v_op(lambda e: e.tensor_add(out=Sf[0][:, 0:260], in0=Sf[0][:, 0:260],
                                        in1=pb[0][:, 0:260]), waits=stcap)
            v_op(lambda e: e.tensor_add(out=Sf[0][:, 260:520], in0=Sf[0][:, 260:520],
                                        in1=pb[2][:, 0:260]))
            v_op(lambda e: e.tensor_add(out=Sf[1][:, 0:260], in0=Sf[1][:, 0:260],
                                        in1=pb[1][:, 0:260]))
            v_op(lambda e: e.tensor_add(out=Sf[1][:, 260:520], in0=Sf[1][:, 260:520],
                                        in1=pb[7][:, 0:260]))
            cap("Sadd", c, "v")
            v_op(lambda e, par=par: e.tensor_copy(
                out=zt2[par][:, 0:4],
                in_=pb[5][:, 0:260].rearrange("p (h n) -> p h n", n=65)[:, :, 64:65]),
                waits=get("Soi1", c) + get("Sonorm", c - 2, []), pad=True)
            v_op(lambda e, par=par: e.tensor_copy(
                out=zt2[par][:, 4:8],
                in_=pb[6][:, 0:260].rearrange("p (h n) -> p h n", n=65)[:, :, 64:65]), pad=True)
            v_op(lambda e, par=par: e.tensor_scalar_add(out=zt2[par][:], in0=zt2[par][:],
                                                        scalar1=EPS), pad=True)
            v_op(lambda e, par=par: e.reciprocal(out=zrt2[par][:], in_=zt2[par][:]), pad=True)
            for h in range(NH):
                ob = pb[5] if h < 4 else pb[6]
                col = 65 * (h % 4)
                s_op(lambda e, h=h, ob=ob, col=col, par=par: e.activation(
                    out=o_outb2[par][:, 64 * h:64 * h + 64], in_=ob[:, col:col + 64],
                    func=AF.Copy, scale=zrt2[par][:, h:h + 1]),
                    waits=(now("v") + get("Sost", c - 2, []) if h == 0 else ()))
            cap("Sonorm", c, "s")
            dma("sync", ag2in_t[:, c, :], o_outb2[par][:], waits=now("s"))
            cap("Sost", c, "ld")

        sprep(0)
        sprep(1)
        for c in range(NCH):
            smm(c)
            if c + 2 < NCH:
                sprep(c + 2)

        op("gpsimd", lambda e: e.collective_compute(
            "AllGather", ALU.bypass, replica_groups=[[0, 1], [2, 3], [4, 5], [6, 7]],
            ins=[ag2_in.ap().opt()], outs=[ag2_out.ap().opt()]),
            waits=now("ld"), inc="cc")

        def wprep(t):
            par = t % 2
            rows = (t, t + 16, 32 + t, 48 + t)
            dma("sync", os4[0], ag2out_t[:, rows[0], :],
                waits=get("Wcmb", t - 1, [] if t >= 1 else None) + (now("cc") if t == 0 else []))
            for k in range(1, 4):
                dma("sync", os4[k], ag2out_t[:, rows[k], :])
            for half in range(2):
                sl = slice(512 * half, 512 * (half + 1))
                v_op(lambda e, sl=sl, par=par, half=half: e.tensor_single_scalar(
                    out=ocat2[par][:, sl], in_=os4[2 * half], scalar=m0t[:], op=ALU.mult),
                    waits=(now("ld") + get("Wtp", t - 2, []) if half == 0 else ()))
                v_op(lambda e, sl=sl, par=par, half=half: e.scalar_tensor_tensor(
                    out=ocat2[par][:, sl], in0=os4[2 * half + 1], scalar=m1t[:],
                    in1=ocat2[par][:, sl], op0=ALU.mult, op1=ALU.add))
            cap("Wcmb", t, "v")
            dmaT(ocatT2[par], ocat2[par], waits=now("v") + get("Wmm", t - 2, []))
            cap("Wtp", t, "tp")
            dma("sync", xrow2[par][:], x1_t[:, t, :], waits=get("Wadd", t - 2, []))
            cap("Wxld", t, "ld")

        def wmm(t):
            par = t % 2
            w0 = get("Wtp", t) + get("Wadd", t - 1, [])
            for kc in range(8):
                for n in range(2):
                    t_op(lambda e, kc=kc, n=n, par=par: e.matmul(
                        out=pb[n][:, 0:512], lhsT=ocatT2[par][:, kc, :],
                        rhs=woT[:, kc, 512 * n:512 * (n + 1)],
                        start=(kc == 0), stop=(kc == 7)),
                        waits=(w0 if (kc == 0 and n == 0) else ()))
            cap("Wmm", t, "t")
            for n in range(2):
                v_op(lambda e, n=n, par=par: e.tensor_add(
                    out=xrow2[par][:, 512 * n:512 * (n + 1)], in0=pb[n][:, 0:512],
                    in1=xrow2[par][:, 512 * n:512 * (n + 1)]),
                    waits=(now("t") + get("Wxld", t) if n == 0 else ()))
            cap("Wadd", t, "v")
            dma("sync", x2_t[:, t, :], xrow2[par][:], waits=now("v"))
            cap("Wst", t, "ld")

        wprep(0)
        wprep(1)
        for t in range(NTT):
            wmm(t)
            if t + 2 < NTT:
                wprep(t + 2)
        barrier()

    if stage == "prep":
        pass
    else:
        ffn(xin_t, x1_t)
    if stage in ("attn", "full"):
        attention()
    if stage == "prep":
        pass
    elif stage == "full":
        dma("sync", wbuf[:, :], w1qt_st.ap(), waits=now("t", "s", "v", "tp", "g"))
        barrier()
        ffn(x2_t, out_t)
    elif stage != "prep":
        src_t = x1_t if stage == "ffn1" else x2_t
        for t in range(NTT):
            dma("sync", xrow[:], src_t[:, t, :], waits=now("v"))
            v_op(lambda e: e.tensor_copy(out=bigA[:, 0:D], in_=xrow[:]), waits=now("ld"))
            dma("sync", out_t[:, t, :], bigA[:, 0:D], waits=now("v"))
    else:
        for t in range(NTT):
            dma("sync", xrow[:], xin_t[:, t, :], waits=now("v"))
            v_op(lambda e: e.tensor_copy(out=bigA[:, 0:D], in_=xrow[:]), waits=now("ld"))
            dma("sync", out_t[:, t, :], bigA[:, 0:D], waits=now("v"))

    if dbg:
        dbgg = dt("dbg_g", [TOK, HID], BF16, kind="ExternalOutput")
        dbgw = dt("dbg_w", [128, 8 * HID], BF16, kind="ExternalOutput")
        dbgs = dt("dbg_s", [128, 64], F32, kind="ExternalOutput")
        dma("sync", dbgg.ap(), g_std.ap(), waits=dep_all())
        dma("sync", dbgw.ap(), w1qt_st.ap())
        v_op(lambda e: e.tensor_copy(out=bigA[:, 0:16], in_=sxs[:]), waits=now("ld"))
        v_op(lambda e: e.tensor_copy(out=bigA[:, 16:32], in_=rr1[:]))
        v_op(lambda e: e.tensor_copy(out=bigA[:, 32:48], in_=rr2[:]))
        v_op(lambda e: e.tensor_copy(out=bigA[:, 48:49], in_=wsc1[:]))
        v_op(lambda e: e.tensor_copy(out=bigA[:, 49:50], in_=wsc2[:]))
        v_op(lambda e: e.tensor_copy(out=bigA[:, 50:51], in_=wthr[:]))
        v_op(lambda e: e.tensor_copy(out=bigA[:, 51:52], in_=wnthr[:]))
        v_op(lambda e: e.tensor_copy(out=bigA[:, 52:53], in_=wsc1[:]))
        v_op(lambda e: e.tensor_copy(out=bigA[:, 0:16], in_=amdbg[:]))
        v_op(lambda e: e.tensor_copy(out=bigA[:, 16:32], in_=esdbg[:]))
        dma("sync", dbgs.ap(), bigA[:, 0:64], waits=now("v"))
        dbgp = dt("dbg_p", [128, 12288], BF16, kind="ExternalOutput")
        dma("sync", tlv(dbgp.ap())[:, 0, :] if False else dbgp.ap()[:, 0:4096], prepP[:, 0:4096], waits=now("v"))
        dma("sync", dbgp.ap()[:, 4096:8192], prepN[:, 0:4096])
        dma("sync", dbgp.ap()[:, 8192:12288], tern_ring[1][:, 0:4096])
    op("sync", lambda e: e.nop(), waits=dep_all())

    with contextlib.ExitStack() as fes:
        for n in CH:
            sems[n] = fes.enter_context(nc.semaphore("sem_" + n))
        block = fes.enter_context(nc.Block())
        for eng in ENGS:
            getattr(block, eng)(lambda e, _f=ops[eng]: [f(e) for f in _f])
    es.close()
    return nc


def shard_inputs(inputs):
    x = np.ascontiguousarray(np.asarray(inputs["L0_attn_x"], dtype=np.float32))
    maps = []
    for core in range(8):
        b, hg = core // 2, core % 2
        maps.append({
            "x_in": np.ascontiguousarray(x[b, hg * 2048:(hg + 1) * 2048, :]),
            "w1": np.ascontiguousarray(np.asarray(inputs["L0_ffn_w1"], np.float32)),
            "w2": np.ascontiguousarray(np.asarray(inputs["L0_ffn_w2"], np.float32)),
            "b1": np.asarray(inputs["L0_ffn_b1"], np.float32).reshape(1, HID),
            "b2": np.asarray(inputs["L0_ffn_b2"], np.float32).reshape(1, D),
            "lnw": np.asarray(inputs["L0_ffn_ln1_w"], np.float32).reshape(1, D),
            "lnb": np.asarray(inputs["L0_ffn_ln1_b"], np.float32).reshape(1, D),
            "ln2w": np.asarray(inputs["L0_ffn_ln2_w"], np.float32).reshape(1, HID),
            "ln2b": np.asarray(inputs["L0_ffn_ln2_b"], np.float32).reshape(1, HID),
            "wq": np.ascontiguousarray(np.asarray(inputs["L0_attn_Wq"], np.float32)[hg * 128:(hg + 1) * 128, :]),
            "wk": np.ascontiguousarray(np.asarray(inputs["L0_attn_Wk"], np.float32)[hg * 128:(hg + 1) * 128, :]),
            "wv": np.ascontiguousarray(np.asarray(inputs["L0_attn_Wv"], np.float32)[hg * 512:(hg + 1) * 512, :]),
            "wo": np.ascontiguousarray(np.asarray(inputs["L0_attn_Wo"], np.float32)),
            "gam": np.asarray(inputs["L0_attn_gamma"], np.float32).reshape(1, FD),
            "bet": np.asarray(inputs["L0_attn_beta"], np.float32).reshape(1, FD),
            "m0": np.full((128, 1), 1.0 if hg == 0 else 0.0, np.float32),
            "m1": np.full((128, 1), 0.0 if hg == 0 else 1.0, np.float32),
        })
    return maps


def assemble(results):
    out = np.empty((4, 4096, 1024), np.float32)
    for core in range(8):
        b, hg = core // 2, core % 2
        out[b, hg * 2048:(hg + 1) * 2048, :] = results[core]["out"]
    return out


def kernel(**inputs):
    nc = build("full")
    maps = shard_inputs(inputs)
    res = run_bass_kernel_spmd(nc, maps, core_ids=list(range(8)))
    return assemble(res.results)
